# version 09
# baseline (speedup 1.0000x reference)
import numpy as np
from contextlib import ExitStack

import concourse.bass as bass
import concourse.mybir as mybir
from concourse.bass_utils import run_bass_kernel_spmd

F32 = mybir.dt.float32
BF16 = mybir.dt.bfloat16
AF = mybir.ActivationFunctionType
OP = mybir.AluOpType
AX = mybir.AxisListType

f32 = np.float32

T = 4096
D = 1024
H = 8
DH = 128
FF = 4096
NT = 8
TT = 512
NCC = 4
CC = 128
NC_D = 8
NC_F = 32
SUB = 256
EPS_LN = 1e-5
EPS_L2 = 1e-6
LVLS = 7

_BUILD_CACHE = {}
DEBUG_DUMP = False
EARLY_CC = False

ENGINES = ("sync", "gpsimd", "tensor", "vector", "scalar")


class Plan:
    def __init__(self, hz_writers=None):
        self.steps = {e: [] for e in ENGINES}
        self.cnt = {}
        self.hw = {e: {} for e in ENGINES}
        self.bank_guard = {i: [] for i in range(8)}
        self.buf_guard = {}
        self.hz_writers = hz_writers or set()

    def val(self, sem):
        return self.cnt.get(sem, 0)

    def step(self, eng, fn, waits=None, inc=None, inc_by=1):
        w = []
        for s, v in (waits or []):
            if v is None or v <= 0:
                continue
            if self.hw[eng].get(s, 0) >= v:
                continue
            self.hw[eng][s] = v
            w.append((s, v))
        if inc is None and (eng, len(self.steps[eng])) in self.hz_writers:
            inc, inc_by = "hz_" + eng, 1
        if inc is not None:
            self.cnt[inc] = self.cnt.get(inc, 0) + inc_by
        self.steps[eng].append((fn, w, inc, inc_by))

    def dma(self, eng, fn, waits=None, inc=None):
        self.step(eng, fn, waits=waits, inc=inc, inc_by=16)

    def bank_w(self, *banks):
        out = []
        for b in banks:
            out.extend(self.bank_guard[b])
        return out

    def bank_consumed(self, b, sem):
        self.bank_guard[b] = [(sem, self.val(sem))]

    def buf_w(self, *names):
        out = []
        for nm in names:
            out.extend(self.buf_guard.get(nm, []))
        return out

    def buf_set(self, nm, *guards):
        self.buf_guard[nm] = [(s, v) for (s, v) in guards if v]


def _arg_names(args):
    out = set()
    for a in args:
        for attr in ("memref", "name"):
            n = getattr(a, attr, None)
            if isinstance(n, str):
                out.add(n)
                break
        else:
            t = getattr(a, "tensor", None)
            n = getattr(t, "name", None)
            if isinstance(n, str):
                out.add(n)
    return out


def _emit(plan, sems, blk, hazard_scan=None, hazard_fix=None):
    def mk(eng_name):
        check = eng_name in ("vector", "gpsimd") and hazard_scan is not None
        def closure(eng):
            recent = []
            for si, (fn, waits, inc, inc_by) in enumerate(plan.steps[eng_name]):
                if hazard_fix is not None:
                    for s, v in hazard_fix.get((eng_name, si), []):
                        eng.wait_ge(sems[s], v)
                for s, v in waits:
                    eng.wait_ge(sems[s], v)
                inst = fn(eng)
                insts = inst if isinstance(inst, list) else ([inst] if inst is not None else [])
                if check:
                    for one in insts:
                        ii = getattr(one, "ins", None)
                        m = getattr(ii, "ins", None)
                        mo = getattr(ii, "outs", None)
                        rd = _arg_names(m or [])
                        wr = _arg_names(mo or [])
                        touched = rd | wr
                        for wsi, wouts in recent:
                            if wsi != si and touched & wouts:
                                hazard_scan.setdefault((eng_name, si), set()).add(wsi)
                        recent.append((si, wr))
                        recent = recent[-2:]
                if inc is not None and insts:
                    insts[-1].then_inc(sems[inc], inc_by)
        return closure
    blk.sync(mk("sync"))
    blk.gpsimd(mk("gpsimd"))
    blk.tensor(mk("tensor"))
    blk.vector(mk("vector"))
    blk.scalar(mk("scalar"))


def build_program(_hazard_scan=None, _hazard_fix=None, _hz_writers=None):
    nc = bass.Bass("TRN2", target_bir_lowering=False, detect_race_conditions=False,
                   dynamic_dma_scratch_size=16384)

    d = {}
    d["xt"] = nc.dram_tensor("xt", [D, T], F32, kind="ExternalInput").ap()
    for nm, sh in [("wq", [D, D]), ("wk", [D, D]), ("wv", [D, D]), ("wo", [D, D]),
                   ("wab", [D, 16]), ("wg", [D, 8]), ("w1", [D, FF]), ("w2", [FF, D]),
                   ("wexp", [D, D])]:
        d[nm] = nc.dram_tensor(nm, sh, F32, kind="ExternalInput").ap()
    for nm, sh in [("ln1w", [128, NC_D]), ("ln1b", [128, NC_D]), ("ln2w", [128, NC_D]),
                   ("ln2b", [128, NC_D]), ("b1", [128, NC_F]), ("b2", [128, NC_D]),
                   ("bexp", [128, NC_D]), ("coremask", [1, 8])]:
        d[nm] = nc.dram_tensor("d_" + nm, sh, F32, kind="ExternalInput").ap()
    for nm in ["c_iu_f", "c_half_f", "c_ones_f", "c_id_f",
               "c_negsl_b", "c_negsu_b", "c_iu_b", "c_id_b", "c_ones_b"]:
        d[nm] = nc.dram_tensor("d_" + nm, [128, 128], F32, kind="ExternalInput").ap()
    d["c_onescol"] = nc.dram_tensor("d_c_onescol", [128, 1], F32, kind="ExternalInput").ap()
    d["c_sel"] = nc.dram_tensor("d_c_sel", [2, 256], F32, kind="ExternalInput").ap()
    d["c_iota999"] = nc.dram_tensor("d_c_iota999", [128, 32], F32,
                                    kind="ExternalInput").ap()
    d["out"] = nc.dram_tensor("out_t", [D, T], F32, kind="ExternalOutput").ap()

    xn32_dram = nc.dram_tensor("xn32_s", [D, T], F32).ap()
    base_dram = nc.dram_tensor("base_b", [1, 8], F32).ap()
    y1_dram = nc.dram_tensor("y1_s", [D, T], F32).ap()
    st2_dram = nc.dram_tensor("st2_s", [2, T], F32).ap()
    cnt_in = nc.dram_tensor("cnt_in", [1, 8], F32).ap()
    cnt_all = nc.dram_tensor("cnt_all", [8, 8], F32).ap()

    es_persist = ExitStack()
    E = es_persist.enter_context

    c_iu_f = E(nc.sbuf_tensor("c_iu_f", [128, 128], F32))
    c_half_f = E(nc.sbuf_tensor("c_half_f", [128, 128], F32))
    c_ones_f = E(nc.sbuf_tensor("c_ones_f", [128, 128], F32))
    c_id_f = E(nc.sbuf_tensor("c_id_f", [128, 128], F32))
    c_negsl_b = E(nc.sbuf_tensor("c_negsl_b", [128, 128], BF16))
    c_negsu_b = E(nc.sbuf_tensor("c_negsu_b", [128, 128], BF16))
    c_iu_b = E(nc.sbuf_tensor("c_iu_b", [128, 128], BF16))
    c_id_b = E(nc.sbuf_tensor("c_id_b", [128, 128], BF16))
    c_ones_b = E(nc.sbuf_tensor("c_ones_b", [128, 128], BF16))
    c_onescol = E(nc.sbuf_tensor("c_onescol", [128, 1], F32))
    c_sel = E(nc.sbuf_tensor("c_sel", [2, 256], F32))
    c_onescol_b = E(nc.sbuf_tensor("c_onescol_b", [128, 1], BF16))
    ln1w = E(nc.sbuf_tensor("ln1w", [128, NC_D], F32))
    ln1b = E(nc.sbuf_tensor("ln1b", [128, NC_D], F32))
    ln2w = E(nc.sbuf_tensor("ln2w", [128, NC_D], F32))
    ln2b = E(nc.sbuf_tensor("ln2b", [128, NC_D], F32))
    b1_sb = E(nc.sbuf_tensor("b1_sb", [128, NC_F], F32))
    b2_sb = E(nc.sbuf_tensor("b2_sb", [128, NC_D], F32))
    bexp_sb = E(nc.sbuf_tensor("bexp_sb", [128, NC_D], F32))
    coremask = E(nc.sbuf_tensor("coremask", [1, 8], F32))
    r_ohf = E(nc.sbuf_tensor("r_ohf", [128, 32 * 8], BF16))
    r_c = E(nc.sbuf_tensor("r_c", [128, 32], F32))
    r_l = E(nc.sbuf_tensor("r_l", [128, 32 * 8], F32))
    r_negm = E(nc.sbuf_tensor("r_negm", [128, 32], F32))
    r_g = E(nc.sbuf_tensor("r_g", [128, 32], F32))
    run_rep = E(nc.sbuf_tensor("run_rep", [128, 8], F32))
    base_rep = E(nc.sbuf_tensor("base_rep", [128, 8], F32))
    cnt_sb = E(nc.sbuf_tensor("cnt_sb", [1, 64], F32))
    base_row = E(nc.sbuf_tensor("base_row", [1, 72], F32))
    t8a = E(nc.sbuf_tensor("t8a", [128, 8], F32))
    t8b = E(nc.sbuf_tensor("t8b", [128, 8], F32))
    l_cc = E(nc.sbuf_tensor("l_cc", [128, 8], F32))
    m1c = E(nc.sbuf_tensor("m1c", [128, 1], F32))
    c_iota999 = E(nc.sbuf_tensor("c_iota999", [128, 32], F32))
    rt3 = E(nc.sbuf_tensor("rt3", [128, 32], F32))

    pb = [E(nc.psum_tensor(f"pb{i}", [128, 512], F32)) for i in range(8)]

    def pbf(i):
        return pb[i][:, :]

    def pbb(i):
        return pb[i][:, :].bitcast(BF16)

    def pg(i):
        return pb[i][:, :].rearrange("p (g n) -> p g n", n=DH)

    es_a = ExitStack()
    A = es_a.enter_context
    st_mu0 = A(nc.sbuf_tensor("st_mu0", [1, TT], F32))
    st_ss = A(nc.sbuf_tensor("st_ss", [1, TT], F32))
    y1h = A(nc.sbuf_tensor("y1h", [128, 4 * TT], F32))
    xnwo = A(nc.sbuf_tensor("xnwo", [128, 2 * TT], F32))
    o_fm = A(nc.sbuf_tensor("o_fm", [128, H * TT], BF16))
    s_st = A(nc.sbuf_tensor("s_st", [128, H * DH], F32))
    sig_bf = A(nc.sbuf_tensor("sig_bf", [128, H * DH], BF16))
    sq_sc = A(nc.sbuf_tensor("sq_sc", [128, H * DH], F32))
    kfm = A(nc.sbuf_tensor("kfm", [128, H * DH], BF16))
    nkfm = A(nc.sbuf_tensor("nkfm", [128, H * DH], BF16))
    kbfm = A(nc.sbuf_tensor("kbfm", [128, H * DH], BF16))
    qfm = A(nc.sbuf_tensor("qfm", [128, H * DH], BF16))
    p_cur = A(nc.sbuf_tensor("p_cur", [128, H * DH], BF16))
    p_nxt = A(nc.sbuf_tensor("p_nxt", [128, H * DH], BF16))
    q_cur = A(nc.sbuf_tensor("q_cur", [128, H * DH], BF16))
    q_nxt = A(nc.sbuf_tensor("q_nxt", [128, H * DH], BF16))
    mt_bf = A(nc.sbuf_tensor("mt_bf", [128, H * DH], BF16))
    x_a = A(nc.sbuf_tensor("x_a", [128, H * DH], BF16))
    x_b = A(nc.sbuf_tensor("x_b", [128, H * DH], BF16))
    wo_sb = A(nc.sbuf_tensor("wo_sb", [128, NC_D * D], BF16))
    wo1_sb = A(nc.sbuf_tensor("wo1_sb", [128, NC_D], F32))
    wo1_bf = A(nc.sbuf_tensor("wo1_bf", [128, NC_D], BF16))
    zab4 = A(nc.sbuf_tensor("zab4", [128, NCC * 16], F32))
    beta4 = A(nc.sbuf_tensor("beta4", [128, NCC * 8], F32))
    la8 = A(nc.sbuf_tensor("la8", [128, 8], F32))
    cumz = A(nc.sbuf_tensor("cumz", [128, 24], F32))
    dz = A(nc.sbuf_tensor("dz", [128, 8], F32))
    lam4 = A(nc.sbuf_tensor("lam4", [128, NCC * 8], F32))
    ginv4 = A(nc.sbuf_tensor("ginv4", [128, NCC * 8], F32))
    lamc4 = A(nc.sbuf_tensor("lamc4", [128, NCC * 8], F32))
    gref4 = A(nc.sbuf_tensor("gref4", [128, NCC * 8], F32))
    f18 = A(nc.sbuf_tensor("f18", [128, 8], F32))
    ssq4_q = A(nc.sbuf_tensor("ssq4_q", [128, NCC * 8], F32))
    ssq4_k = A(nc.sbuf_tensor("ssq4_k", [128, NCC * 8], F32))
    rsq_q = A(nc.sbuf_tensor("rsq_q", [128, 8], F32))
    rsq_k = A(nc.sbuf_tensor("rsq_k", [128, 8], F32))
    fq4 = A(nc.sbuf_tensor("fq4", [128, NCC * 8], F32))
    frk4 = A(nc.sbuf_tensor("frk4", [128, NCC * 8], F32))
    fkb4 = A(nc.sbuf_tensor("fkb4", [128, NCC * 8], F32))
    xn = A(nc.sbuf_tensor("xn", [128, NC_D * TT], F32))
    xsq = A(nc.sbuf_tensor("xsq", [128, 4 * TT], BF16))
    xin = A(nc.sbuf_tensor("xin", [128, NC_D * TT], F32))
    n_bf = A(nc.sbuf_tensor("n_bf", [128, NC_D * TT], BF16))
    wab_sb = A(nc.sbuf_tensor("wab_sb", [128, NC_D * 16], BF16))
    wg_sb = A(nc.sbuf_tensor("wg_sb", [128, NC_D * 8], F32))
    wq_sb = A(nc.sbuf_tensor("wq_sb", [128, NC_D * D], BF16))
    wk_sb = A(nc.sbuf_tensor("wk_sb", [128, NC_D * D], BF16))
    wv_sb = A(nc.sbuf_tensor("wv_sb", [128, NC_D * D], BF16))
    q_b4 = A(nc.sbuf_tensor("q_b4", [128, NCC * H * DH], BF16))
    k_b4 = A(nc.sbuf_tensor("k_b4", [128, NCC * H * DH], BF16))
    v_b4 = A(nc.sbuf_tensor("v_b4", [128, NCC * H * DH], BF16))
    qt_bf = A(nc.sbuf_tensor("qt_bf", [128, H * DH], BF16))
    kn_bf = A(nc.sbuf_tensor("kn_bf", [128, H * DH], BF16))
    vt_bf = A(nc.sbuf_tensor("vt_bf", [128, H * DH], BF16))
    kb_bf = [A(nc.sbuf_tensor(f"kb_bf{i}", [128, H * DH], BF16)) for i in range(2)]
    es_a.close()

    es_b = ExitStack()
    B = es_b.enter_context
    wexp_sb = B(nc.sbuf_tensor("wexp_sb", [128, NC_D * D], BF16))
    w2_sb = B(nc.sbuf_tensor("w2_sb", [128, NC_F * D], BF16))
    w1_sb = B(nc.sbuf_tensor("w1_sb", [128, NC_D * FF], BF16))
    by1 = [B(nc.sbuf_tensor(f"by1_{i}", [128, NC_D * SUB], F32)) for i in range(2)]
    bxnb = [B(nc.sbuf_tensor(f"bxnb_{i}", [128, NC_D * SUB], BF16)) for i in range(2)]
    bst2 = [B(nc.sbuf_tensor(f"bst2_{i}", [2, SUB], F32)) for i in range(2)]
    bh2 = B(nc.sbuf_tensor("bh2", [128, NC_D * SUB], BF16))
    bg = B(nc.sbuf_tensor("bg", [128, NC_F * SUB], BF16))
    bout = B(nc.sbuf_tensor("bout", [128, NC_D * SUB], F32))
    bgb = B(nc.sbuf_tensor("bgb", [128, SUB], F32))
    bdiag = B(nc.sbuf_tensor("bdiag", [128, 128], F32))
    es_b.close()

    sems = {}
    es_sem = ExitStack()
    for s in ["pe", "vv", "sc", "gp", "st", "cc", "hz_vector", "hz_gpsimd",
              "dx", "dy1", "dst2", "dxn", "dxw", "dcb", "dcs", "dwq", "dwk", "dwv",
              "dwab", "dwo", "dw1a", "dw1b", "dw1c", "dw1d", "dw2a", "dw2b",
              "dwexa", "dwexb", "dcnt", "dcall", "dsb", "dsg",
              "dxa", "dbs", "dbb"]:
        sems[s] = es_sem.enter_context(nc.semaphore(s))

    p = Plan(hz_writers=_hz_writers)

    def w3(t, m):
        return t[:, :].rearrange("p (c m) -> p c m", m=m)

    def tile3(t, n):
        return t[:, :].rearrange("p (c n) -> p c n", n=n)

    def dram3(ap):
        return ap.rearrange("(c p) n -> p c n", p=128)

    def h3(t):
        return t[:, :].rearrange("p (h e) -> p h e", e=DH)

    def h4(t, cc):
        return t[:, :].rearrange("p (s h e) -> p s h e", s=NCC, e=DH)[:, cc, :, :]

    def s4(t, cc):
        return t[:, cc * 8:(cc + 1) * 8]

    def bcast8s(t, cc):
        return s4(t, cc).unsqueeze(2).broadcast_to([128, 8, DH])

    def maskb(cmask):
        return cmask[:, :].unsqueeze(1).broadcast_to([128, 4, DH])

    p.dma("sync", lambda s: s.dma_start(out=w3(wg_sb, 8), in_=dram3(d["wg"])), inc="dcs")
    for nm, t in [("c_iu_f", c_iu_f), ("c_half_f", c_half_f), ("c_ones_f", c_ones_f),
                  ("c_id_f", c_id_f), ("c_onescol", c_onescol),
                  ("c_sel", c_sel), ("c_iota999", c_iota999),
                  ("ln1w", ln1w), ("ln1b", ln1b), ("ln2w", ln2w), ("ln2b", ln2b),
                  ("b1", b1_sb), ("b2", b2_sb), ("bexp", bexp_sb), ("coremask", coremask)]:
        p.dma("sync", lambda s, t=t, nm=nm: s.dma_start(out=t[:, :], in_=d[nm]), inc="dcs")
    ds_const = p.val("dcs")

    for nm, t in [("c_negsl_b", c_negsl_b), ("c_negsu_b", c_negsu_b), ("c_iu_b", c_iu_b),
                  ("c_id_b", c_id_b), ("c_ones_b", c_ones_b)]:
        p.dma("gpsimd", lambda g, t=t, nm=nm: g.dma_start(out=t[:, :], in_=d[nm]), inc="dcb")
    p.dma("gpsimd", lambda g: g.dma_start(out=c_onescol_b[:, :], in_=d["c_onescol"]), inc="dcb")
    dg_const = p.val("dcb")

    def load_cast(dst, src, m, sem, waits=None):
        p.dma("gpsimd", lambda g, dst=dst, src=src, m=m: g.dma_start(
            out=w3(dst, m), in_=dram3(src)), inc=sem, waits=waits)

    load_cast(wq_sb, d["wq"], D, "dwq")
    load_cast(wk_sb, d["wk"], D, "dwk")
    load_cast(wv_sb, d["wv"], D, "dwv")
    load_cast(wab_sb, d["wab"], 16, "dwab")
    load_cast(wo_sb, d["wo"], D, "dwo")

    p.step("vector", lambda v: v.tensor_reduce(wo1_sb[:, :], w3(wo_sb, D), AX.X, OP.add),
           waits=[("dwo", 16)], inc="vv")
    p.step("vector", lambda v: v.tensor_copy(wo1_bf[:, :], wo1_sb[:, :]), inc="vv")
    vv_wo1 = p.val("vv")
    p.step("vector", lambda v: v.memset(run_rep[:, :], 0.0), waits=[("dcs", ds_const)], inc="vv")
    p.step("gpsimd", lambda g: g.memset(s_st[:, :], 0.0), inc="gp")
    gp_szero = p.val("gp")

    st = {
        "pe_upd": {},
        "pe_tp": {},
        "pe_x0": {},
        "pe_proj": {},
        "sc_silu": {},
        "vv_zab": {},
        "vv_ssq": {},
        "vv_scal": {},
        "sc_n": {},
        "sc_lnexp": {},
        "ds_x": {},
        "dxn_t": {},
        "vv_ssq_prev": 0,
        "vv_folds_prev": 0,
        "sc_lnexp_prev": 0,
        "dst2_prev": 0,
        "pe_mu2_t": 0,
        "pe_ss2_t": 0,
    }

    def emit_xload(t_i):
        tk = t_i * TT
        p.dma("sync", (lambda s, tk=tk: s.dma_start(
            out=tile3(xin, TT), in_=dram3(d["xt"])[:, :, tk:tk + TT])),
            waits=p.buf_w("xin"), inc="dx")
        st["ds_x"][t_i] = p.val("dx")

    def emit_lnhead(t_i):
        tk = t_i * TT
        ds_x = st["ds_x"][t_i]
        for ic in range(NC_D):
            p.step("tensor", (lambda t, ic=ic: t.matmul(
                pb[6][0:1, 0:TT], c_onescol[:, 0:1], tile3(xin, TT)[:, ic, :],
                start=(ic == 0), stop=(ic == NC_D - 1))),
                waits=([("dx", ds_x), ("dcs", ds_const)] + p.bank_w(6)) if ic == 0 else [],
                inc="pe" if ic == NC_D - 1 else None)
        pe_mu0 = p.val("pe")
        sc_xsq = 0
        for hf in range(2):
            p.step("scalar", (lambda sc, hf=hf: sc.activation(
                tile3(xsq, TT), tile3(xin, TT)[:, hf * 4:hf * 4 + 4, :], AF.Square)),
                waits=(([("dx", ds_x)] + p.buf_w("xsq")) if hf == 0
                       else [("pe", p.val("pe"))]), inc="sc")
            sc_xsq = p.val("sc")
            for ic in range(4):
                p.step("tensor", (lambda t, ic=ic, hf=hf: t.matmul(
                    pb[7][0:1, 0:TT], c_onescol_b[:, 0:1], tile3(xsq, TT)[:, ic, :],
                    start=(ic == 0 and hf == 0), stop=(ic == 3 and hf == 1))),
                    waits=([("sc", sc_xsq), ("dcb", dg_const)] +
                           (p.bank_w(7) if hf == 0 else [])) if ic == 0 else [],
                    inc="pe" if ic == 3 else None)
        pe_ss0 = p.val("pe")

        p.step("vector", lambda v: v.tensor_scalar(
            st_mu0[0:1, :], pb[6][0:1, 0:TT], 1.0 / D, None, OP.mult),
            waits=[("pe", pe_mu0), ("dst2", st["dst2_prev"])], inc="vv")
        p.step("vector", lambda v: v.tensor_scalar(
            st_ss[0:1, :], pb[7][0:1, 0:TT], 1.0 / D, None, OP.mult),
            waits=[("pe", pe_ss0)], inc="vv")
        p.bank_consumed(6, "vv")
        p.bank_consumed(7, "vv")
        p.step("vector", lambda v: v.tensor_mul(xnwo[0:1, 0:TT], st_mu0[0:1, :],
                                                st_mu0[0:1, :]),
               waits=[("gp", st.get("gp_u", 0))], inc="vv")
        p.step("vector", lambda v: v.tensor_sub(st_ss[0:1, :], st_ss[0:1, :],
                                                xnwo[0:1, 0:TT]), inc="vv")
        vv_var0 = p.val("vv")
        p.buf_set("xnwo", ("vv", vv_var0))
        p.step("scalar", lambda sc: sc.activation(st_ss[0:1, :], st_ss[0:1, :], AF.Ln,
                                                  bias=f32(EPS_LN)),
               waits=[("vv", vv_var0)], inc="sc")
        p.step("scalar", lambda sc: sc.activation(st_ss[0:1, :], st_ss[0:1, :], AF.Exp,
                                                  scale=f32(-0.5)), inc="sc")
        sc_r0 = p.val("sc")
        p.step("vector", lambda v: v.tensor_mul(st_mu0[0:1, :], st_mu0[0:1, :],
                                                st_ss[0:1, :]),
               waits=[("sc", sc_r0)], inc="vv")
        vv_r0 = p.val("vv")

        p.step("tensor", lambda t: t.matmul(pbf(0), c_ones_f[0:1, :], st_ss[0:1, :],
                                            start=True, stop=True),
               waits=[("vv", vv_r0)] + p.bank_w(0), inc=None)
        p.step("tensor", lambda t: t.matmul(pbf(1), c_ones_f[0:1, :], st_mu0[0:1, :],
                                            start=True, stop=True),
               waits=p.bank_w(1), inc="pe")
        pe_bc0 = p.val("pe")

        p.step("vector", lambda v: v.tensor_copy(xnwo[:, 0:TT], pb[0][:, 0:TT]),
               waits=[("pe", pe_bc0)], inc="vv")
        p.step("vector", lambda v: v.tensor_copy(xnwo[:, TT:2 * TT], pb[1][:, 0:TT]),
               inc="vv")
        vv_rep = p.val("vv")
        p.bank_consumed(0, "vv")
        p.bank_consumed(1, "vv")

        def repb(c0):
            return xnwo[:, c0:c0 + TT].unsqueeze(1).broadcast_to([128, NC_D, TT])

        p.step("gpsimd", lambda g: g.tensor_tensor(
            tile3(xn, TT), tile3(xin, TT), repb(0), OP.mult),
            waits=[("vv", vv_rep)] + p.buf_w("xn"), inc="gp")
        p.step("gpsimd", lambda g: g.tensor_tensor(
            tile3(xn, TT), tile3(xn, TT), repb(TT), OP.subtract), inc="gp")
        gp_xn = p.val("gp")
        p.dma("sync", (lambda s, tk=tk: s.dma_start(
            out=dram3(xn32_dram)[:, :, tk:tk + TT], in_=tile3(xn, TT))),
            waits=[("gp", gp_xn)], inc="dxn")
        st["dxn_t"][t_i] = p.val("dxn")

        for cc in range(NCC):
            for ic in range(NC_D):
                p.step("tensor", (lambda t, cc=cc, ic=ic: t.matmul(
                    pb[7][:, cc * 8:(cc + 1) * 8],
                    tile3(xn, TT)[:, ic, cc * CC:(cc + 1) * CC],
                    w3(wg_sb, 8)[:, ic, :],
                    start=(ic == 0), stop=(ic == NC_D - 1))),
                    waits=([("gp", gp_xn)] + p.bank_w(7)) if (cc == 0 and ic == 0) else [],
                    inc="pe" if ic == NC_D - 1 else None)
        pe_log = p.val("pe")

        tb = t_i * NCC

        def g48(t, col0=0):
            return t[:, col0:col0 + 32].rearrange("p (c e) -> p c e", e=8)

        def gb4(t, col0):
            return t[:, col0:col0 + 4].unsqueeze(2).broadcast_to([128, 4, 8])

        rt1 = ssq4_q
        rt2 = ssq4_k
        p.step("vector", (lambda v: v.tensor_copy(
            r_l[:, tb * 8:tb * 8 + 32], pb[7][:, 0:32])),
            waits=[("pe", pe_log)], inc="vv")
        p.step("vector", (lambda v: v.tensor_reduce(
            r_negm[:, tb:tb + 4], g48(r_l, tb * 8), AX.X, OP.max, negate=True)), inc="vv")
        p.step("vector", (lambda v: v.tensor_tensor(
            g48(rt1), g48(r_l, tb * 8), gb4(r_negm, tb), OP.add)),
            waits=[("sc", st["sc_lnexp_prev"])], inc="vv")
        p.step("vector", lambda v: v.tensor_scalar(
            rt1[:, :], rt1[:, :], 0.0, None, OP.is_ge), inc="vv")
        p.step("vector", lambda v: v.scalar_tensor_tensor(
            rt2[:, :], rt1[:, :], -999.0, c_iota999[:, :], OP.mult, OP.add), inc="vv")
        p.step("vector", lambda v: v.tensor_reduce(
            rt3[:, 0:4], g48(rt2), AX.X, OP.min), inc="vv")
        p.step("vector", (lambda v: v.tensor_tensor(
            g48(r_ohf, tb * 8), g48(rt2), gb4(rt3, 0), OP.is_equal)), inc="vv")
        vv_ohf = p.val("vv")
        p.step("tensor", (lambda t: t.matmul(
            pb[6][:, 0:32], c_iu_b[:, :], r_ohf[:, tb * 8:tb * 8 + 32],
            start=True, stop=True)),
            waits=[("vv", vv_ohf)] + p.bank_w(6), inc=None)
        p.step("tensor", (lambda t: t.matmul(
            pb[6][:, 32:64], c_ones_b[:, :], r_ohf[:, tb * 8:tb * 8 + 32],
            start=True, stop=True)), inc="pe")
        pe_cnt = p.val("pe")
        p.step("vector", lambda v: v.tensor_copy(rt1[:, :], pb[6][:, 0:32]),
               waits=[("pe", pe_cnt)], inc="vv")
        p.step("vector", lambda v: v.tensor_copy(rt2[:, :], pb[6][:, 32:64]), inc="vv")
        p.bank_consumed(6, "vv")
        p.step("vector", lambda v: v.tensor_copy(rt3[:, 0:8], run_rep[:, :]), inc="vv")
        p.step("vector", lambda v: v.tensor_add(rt3[:, 8:16], rt3[:, 0:8], rt2[:, 0:8]),
               inc="vv")
        p.step("vector", lambda v: v.tensor_add(rt3[:, 16:24], rt3[:, 8:16], rt2[:, 8:16]),
               inc="vv")
        p.step("vector", lambda v: v.tensor_add(rt3[:, 24:32], rt3[:, 16:24],
                                                rt2[:, 16:24]), inc="vv")
        p.step("vector", lambda v: v.tensor_add(run_rep[:, :], rt3[:, 24:32],
                                                rt2[:, 24:32]), inc="vv")
        p.step("vector", lambda v: v.tensor_add(rt1[:, :], rt1[:, :], rt3[:, :]), inc="vv")
        p.step("vector", (lambda v: v.tensor_tensor(
            g48(rt1), g48(rt1), g48(r_ohf, tb * 8), OP.mult)), inc="vv")
        p.step("vector", (lambda v: v.tensor_reduce(
            r_c[:, tb:tb + 4], g48(rt1), AX.X, OP.add)), inc="vv")

        sc_xnsq = 0
        for hf in range(2):
            p.step("scalar", (lambda sc, hf=hf: sc.activation(
                tile3(xsq, TT), tile3(xn, TT)[:, hf * 4:hf * 4 + 4, :], AF.Square)),
                waits=([("gp", gp_xn), ("pe", pe_ss0)] if hf == 0
                       else [("pe", p.val("pe"))]), inc="sc")
            sc_xnsq = p.val("sc")
            for ic in range(4):
                p.step("tensor", (lambda t, ic=ic, hf=hf: t.matmul(
                    pb[7][0:1, 0:TT], c_onescol_b[:, 0:1], tile3(xsq, TT)[:, ic, :],
                    start=(ic == 0 and hf == 0), stop=(ic == 3 and hf == 1))),
                    waits=([("sc", sc_xnsq)] + (p.bank_w(7) if hf == 0 else []))
                    if ic == 0 else [],
                    inc="pe" if ic == 3 else None)
        pe_ss1 = p.val("pe")
        p.step("vector", lambda v: v.tensor_scalar(
            st_ss[0:1, :], pb[7][0:1, 0:TT], 1.0 / D, None, OP.mult),
            waits=[("pe", pe_ss1)], inc="vv")
        vv_var1 = p.val("vv")
        p.bank_consumed(7, "vv")
        p.step("scalar", lambda sc: sc.activation(st_ss[0:1, :], st_ss[0:1, :], AF.Ln,
                                                  bias=f32(EPS_LN)),
               waits=[("vv", vv_var1)], inc="sc")
        p.step("scalar", lambda sc: sc.activation(st_ss[0:1, :], st_ss[0:1, :], AF.Exp,
                                                  scale=f32(-0.5)), inc="sc")
        sc_r1 = p.val("sc")
        p.step("tensor", lambda t: t.matmul(pbf(0), c_ones_f[0:1, :], st_ss[0:1, :],
                                            start=True, stop=True),
               waits=[("sc", sc_r1)] + p.bank_w(0), inc="pe")
        pe_bc1 = p.val("pe")
        p.step("vector", lambda v: v.tensor_copy(xnwo[:, 0:TT], pb[0][:, 0:TT]),
               waits=[("pe", pe_bc1), ("gp", gp_xn)], inc="vv")
        vv_r1rep = p.val("vv")
        p.bank_consumed(0, "vv")
        p.step("gpsimd", lambda g: g.tensor_tensor(
            tile3(xin, TT), tile3(xn, TT), repb(0), OP.mult),
            waits=[("vv", vv_r1rep), ("sc", sc_xsq)], inc="gp")
        gp_u = p.val("gp")
        st["gp_u"] = gp_u
        p.buf_set("xnwo", ("gp", gp_u))
        for ic in range(NC_D):
            p.step("scalar", (lambda sc, ic=ic: sc.activation(
                tile3(n_bf, TT)[:, ic, :], tile3(xin, TT)[:, ic, :], AF.Identity,
                bias=ln1b[:, ic:ic + 1], scale=ln1w[:, ic:ic + 1])),
                waits=([("gp", gp_u)] + p.buf_w("n_bf")) if ic == 0 else [],
                inc="sc" if ic == NC_D - 1 else None)
        sc_n = p.val("sc")
        st["sc_n"][t_i] = sc_n
        p.buf_set("xin", ("sc", sc_n))
        p.buf_set("xn", ("dxn", st["dxn_t"][t_i]), ("pe", pe_ss1), ("pe", pe_log))
        p.buf_set("xsq", ("pe", pe_ss1))

    def emit_proj(t_i, cc):
        for ic in range(NC_D):
            w_first = ([("sc", st["sc_n"][t_i]), ("dwq", 16), ("dwk", 16), ("dwv", 16),
                        ("dwab", 16)] + p.bank_w(0, 1, 2, 3, 4, 5, 7)) if ic == 0 else []
            p.step("tensor", (lambda t, ic=ic, cc=cc: t.matmul(
                pb[0][:, :], tile3(n_bf, TT)[:, ic, cc * CC:(cc + 1) * CC],
                w3(wq_sb, D)[:, ic, 0:512],
                start=(ic == 0), stop=(ic == NC_D - 1))), waits=w_first, inc=None)
            p.step("tensor", (lambda t, ic=ic, cc=cc: t.matmul(
                pb[1][:, :], tile3(n_bf, TT)[:, ic, cc * CC:(cc + 1) * CC],
                w3(wq_sb, D)[:, ic, 512:1024],
                start=(ic == 0), stop=(ic == NC_D - 1))), inc=None)
            p.step("tensor", (lambda t, ic=ic, cc=cc: t.matmul(
                pb[2][:, :], tile3(n_bf, TT)[:, ic, cc * CC:(cc + 1) * CC],
                w3(wk_sb, D)[:, ic, 0:512],
                start=(ic == 0), stop=(ic == NC_D - 1))), inc=None)
            p.step("tensor", (lambda t, ic=ic, cc=cc: t.matmul(
                pb[3][:, :], tile3(n_bf, TT)[:, ic, cc * CC:(cc + 1) * CC],
                w3(wk_sb, D)[:, ic, 512:1024],
                start=(ic == 0), stop=(ic == NC_D - 1))), inc=None)
            p.step("tensor", (lambda t, ic=ic, cc=cc: t.matmul(
                pb[4][:, :], tile3(n_bf, TT)[:, ic, cc * CC:(cc + 1) * CC],
                w3(wv_sb, D)[:, ic, 0:512],
                start=(ic == 0), stop=(ic == NC_D - 1))), inc=None)
            p.step("tensor", (lambda t, ic=ic, cc=cc: t.matmul(
                pb[5][:, :], tile3(n_bf, TT)[:, ic, cc * CC:(cc + 1) * CC],
                w3(wv_sb, D)[:, ic, 512:1024],
                start=(ic == 0), stop=(ic == NC_D - 1))), inc=None)
            p.step("tensor", (lambda t, ic=ic, cc=cc: t.matmul(
                pb[7][:, 0:16], tile3(n_bf, TT)[:, ic, cc * CC:(cc + 1) * CC],
                w3(wab_sb, 16)[:, ic, :],
                start=(ic == 0), stop=(ic == NC_D - 1))),
                inc="pe" if ic == NC_D - 1 else None)
        pe_proj = p.val("pe")
        st["pe_proj"][(t_i, cc)] = pe_proj
        if cc == NCC - 1:
            p.buf_set("n_bf", ("pe", pe_proj))

        for dsti, (dst, bank0) in enumerate([(q_b4, 0), (k_b4, 2), (v_b4, 4)]):
            for half in range(2):
                p.step("scalar", (lambda sc, dst=dst, bank0=bank0, half=half, cc=cc:
                       sc.activation(
                           dst[:, cc * 1024 + half * 512:cc * 1024 + (half + 1) * 512],
                           pb[bank0 + half][:, :], AF.Silu)),
                       waits=([("pe", pe_proj)] + p.buf_w(f"qs{cc}"))
                       if (dsti == 0 and half == 0) else [],
                       inc="sc")
        sc_silu = p.val("sc")
        st["sc_silu"][(t_i, cc)] = sc_silu
        for b in (0, 1, 2, 3, 4, 5):
            p.bank_consumed(b, "sc")

        p.step("vector", (lambda v, cc=cc: v.tensor_copy(
            zab4[:, cc * 16:(cc + 1) * 16], pb[7][:, 0:16])),
            waits=[("pe", pe_proj)], inc="vv")
        st["vv_zab"][(t_i, cc)] = p.val("vv")
        p.bank_consumed(7, "vv")

        p.step("gpsimd", (lambda g, cc=cc: g.tensor_mul(
            sq_sc[:, :], q_b4[:, cc * 1024:(cc + 1) * 1024],
            q_b4[:, cc * 1024:(cc + 1) * 1024])),
            waits=[("sc", sc_silu), ("vv", st["vv_ssq_prev"])], inc="gp")
        p.step("vector", (lambda v, cc=cc: v.tensor_reduce(
            s4(ssq4_q, cc), h3(sq_sc), AX.X, OP.add)),
            waits=[("gp", p.val("gp")), ("sc", st["sc_lnexp_prev"])], inc="vv")
        p.step("gpsimd", (lambda g, cc=cc: g.tensor_mul(
            sq_sc[:, :], k_b4[:, cc * 1024:(cc + 1) * 1024],
            k_b4[:, cc * 1024:(cc + 1) * 1024])),
            waits=[("vv", p.val("vv"))], inc="gp")
        p.step("vector", (lambda v, cc=cc: v.tensor_reduce(
            s4(ssq4_k, cc), h3(sq_sc), AX.X, OP.add)),
            waits=[("gp", p.val("gp"))], inc="vv")
        st["vv_ssq"][(t_i, cc)] = p.val("vv")
        st["vv_ssq_prev"] = p.val("vv")

    def emit_lnexp(t_i, ccs):
        for cc in ccs:
            p.step("scalar", (lambda sc, cc=cc: sc.activation(
                zab4[:, cc * 16:(cc + 1) * 16], zab4[:, cc * 16:(cc + 1) * 16],
                AF.Exp, scale=f32(-1.0))),
                waits=[("vv", st["vv_zab"][(t_i, cc)])], inc="sc")
            sc_abe = p.val("sc")
            p.step("vector", (lambda v, cc=cc: v.tensor_scalar(
                zab4[:, cc * 16:(cc + 1) * 16], zab4[:, cc * 16:(cc + 1) * 16],
                1.0, None, OP.add)),
                waits=[("sc", sc_abe)], inc="vv")
            p.step("vector", (lambda v, cc=cc: v.reciprocal(
                s4(beta4, cc), zab4[:, cc * 16:cc * 16 + 8])), inc="vv")
            vv_beta = p.val("vv")
            p.step("scalar", (lambda sc, cc=cc: sc.activation(
                la8[:, :], zab4[:, cc * 16 + 8:cc * 16 + 16], AF.Ln)),
                waits=[("vv", vv_beta), ("pe", st.get("pe_cum_prev", 0))], inc="sc")
            sc_la = p.val("sc")
            p.step("tensor", lambda t: t.matmul(pb[6][:, 0:8], c_iu_f[:, :], la8[:, :],
                                                start=True, stop=True),
                   waits=[("sc", sc_la)] + p.bank_w(6), inc=None)
            p.step("tensor", lambda t: t.matmul(pb[6][:, 8:16], c_half_f[:, :], la8[:, :],
                                                start=True, stop=True), inc=None)
            p.step("tensor", lambda t: t.matmul(pb[6][:, 16:24], c_ones_f[:, :], la8[:, :],
                                                start=True, stop=True), inc="pe")
            pe_cum = p.val("pe")
            st["pe_cum_prev"] = pe_cum
            p.step("vector", lambda v: v.tensor_copy(cumz[:, :], pb[6][:, 0:24]),
                   waits=[("pe", pe_cum)], inc="vv")
            p.bank_consumed(6, "vv")
            p.step("vector", lambda v: v.tensor_sub(dz[:, :], cumz[:, 8:16], cumz[:, 0:8]),
                   inc="vv")
            p.step("vector", lambda v: v.tensor_sub(f18[:, :], cumz[:, 8:16], cumz[:, 16:24]),
                   inc="vv")
            vv_dz = p.val("vv")
            p.step("scalar", (lambda sc, cc=cc: sc.activation(s4(lam4, cc), dz[:, :], AF.Exp)),
                   waits=[("vv", vv_dz)], inc="sc")
            p.step("scalar", (lambda sc, cc=cc: sc.activation(s4(ginv4, cc), dz[:, :],
                                                              AF.Exp, scale=f32(-1.0))),
                   inc="sc")
            p.step("scalar", (lambda sc, cc=cc: sc.activation(s4(lamc4, cc), f18[:, :],
                                                              AF.Exp)), inc="sc")
            p.step("scalar", (lambda sc, cc=cc: sc.activation(s4(gref4, cc), cumz[:, 8:16],
                                                              AF.Exp, scale=f32(-1.0))),
                   inc="sc")
            p.step("scalar", (lambda sc, cc=cc: sc.activation(rsq_q[:, :], s4(ssq4_q, cc),
                                                              AF.Ln, bias=f32(EPS_L2))),
                   waits=[("vv", st["vv_ssq"][(t_i, cc)]),
                          ("vv", st["vv_folds_prev"])], inc="sc")
            p.step("scalar", lambda sc: sc.activation(rsq_q[:, :], rsq_q[:, :], AF.Exp,
                                                      scale=f32(-0.5)), inc="sc")
            p.step("scalar", (lambda sc, cc=cc: sc.activation(rsq_k[:, :], s4(ssq4_k, cc),
                                                              AF.Ln, bias=f32(EPS_L2))),
                   inc="sc")
            p.step("scalar", lambda sc: sc.activation(rsq_k[:, :], rsq_k[:, :], AF.Exp,
                                                      scale=f32(-0.5)), inc="sc")
            sc_lx = p.val("sc")
            p.step("vector", (lambda v, cc=cc: v.tensor_mul(
                s4(fq4, cc), rsq_q[:, :], s4(lam4, cc))),
                waits=[("sc", sc_lx)], inc="vv")
            p.step("vector", (lambda v, cc=cc: v.tensor_copy(s4(frk4, cc), rsq_k[:, :])),
                   inc="vv")
            p.step("vector", (lambda v, cc=cc: v.tensor_mul(
                s4(fkb4, cc), rsq_k[:, :], s4(beta4, cc))), inc="vv")
            st["vv_folds_prev"] = p.val("vv")
            st["sc_lnexp"][(t_i, cc)] = p.val("sc")
        st["sc_lnexp_prev"] = p.val("sc")

    def emit_scalings(t_i, cc):
        par = cc % 2
        ccg = t_i * NCC + cc
        g_pe = st["pe_x0"].get(ccg - 1, 0)
        p.step("vector", (lambda v, cc=cc: v.tensor_tensor(
            h3(qt_bf), h4(q_b4, cc), bcast8s(fq4, cc), OP.mult)),
            waits=[("sc", st["sc_lnexp"][(t_i, cc)]), ("pe", g_pe)], inc="vv")
        p.step("vector", (lambda v, cc=cc: v.tensor_tensor(
            h3(kn_bf), h4(k_b4, cc), bcast8s(frk4, cc), OP.mult)), inc="vv")
        p.step("vector", (lambda v, cc=cc, par=par: v.tensor_tensor(
            h3(kb_bf[par]), h4(k_b4, cc), bcast8s(fkb4, cc), OP.mult)), inc="vv")
        p.step("vector", (lambda v, cc=cc: v.tensor_tensor(
            h3(vt_bf), h4(v_b4, cc), bcast8s(ginv4, cc), OP.mult)), inc="vv")
        st["vv_scal"][(t_i, cc)] = p.val("vv")
        p.buf_set(f"qs{cc}", ("vv", p.val("vv")))

    def emit_delta(t_i, cc):
        par = cc % 2
        ccg = t_i * NCC + cc
        vv_tm = st["vv_scal"][(t_i, cc)]
        p.step("vector", (lambda v, cc=cc: v.tensor_tensor(
            h3(sig_bf), h3(s_st), bcast8s(gref4, cc), OP.mult)),
            waits=[("gp", gp_szero), ("pe", st["pe_upd"].get(ccg - 1, 0))], inc="vv")
        vv_sig = p.val("vv")
        for srcb, bank in [(kn_bf, 0), (kb_bf[par], 1), (qt_bf, 2)]:
            for h in range(H):
                p.step("tensor", (lambda t, srcb=srcb, bank=bank, h=h: t.transpose(
                    pbb(bank)[:, h * DH:(h + 1) * DH], h3(srcb)[:, h, :], c_id_b[:, :])),
                    waits=([("vv", vv_tm)] + p.bank_w(bank)) if h == 0 else [],
                    inc="pe" if h == H - 1 else None)
        pe_tp = p.val("pe")
        st["pe_tp"][ccg] = pe_tp
        p.step("scalar", lambda sc: sc.copy(kfm[:, :], pbb(0)[:, :]),
               waits=[("pe", pe_tp)], inc="sc")
        p.step("scalar", lambda sc: sc.activation(nkfm[:, :], pbb(0)[:, :], AF.Identity,
                                                  scale=f32(-1.0)), inc="sc")
        sc_tpe = p.val("sc")
        p.step("vector", lambda v: v.tensor_copy(kbfm[:, :], pbb(1)[:, :]),
               waits=[("pe", pe_tp)], inc="vv")
        p.step("vector", lambda v: v.tensor_copy(qfm[:, :], pbb(2)[:, :]), inc="vv")
        vv_tpe = p.val("vv")
        p.bank_consumed(0, "sc")
        p.bank_consumed(1, "vv")
        p.bank_consumed(2, "vv")

        for h in range(H):
            p.step("tensor", (lambda t, h=h: t.matmul(
                pg(2 + h // 4)[:, h % 4, :], h3(kfm)[:, h, :], h3(kbfm)[:, h, :],
                start=True, stop=True)),
                waits=([("sc", sc_tpe), ("vv", vv_tpe)] + p.bank_w(2, 3)) if h == 0 else [],
                inc="pe" if h == H - 1 else None)
        for h in range(H):
            p.step("tensor", (lambda t, h=h: t.matmul(
                pg(4 + h // 4)[:, h % 4, :], h3(kbfm)[:, h, :], h3(kfm)[:, h, :],
                start=True, stop=True)),
                waits=p.bank_w(4, 5) if h == 0 else [],
                inc="pe" if h == H - 1 else None)
        for h in range(H):
            p.step("tensor", (lambda t, h=h: t.matmul(
                pg(6 + h // 4)[:, h % 4, :], h3(kbfm)[:, h, :], h3(qfm)[:, h, :],
                start=True, stop=True)),
                waits=p.bank_w(6, 7) if h == 0 else [],
                inc="pe" if h == H - 1 else None)
        pe_afam = p.val("pe")

        def mask_evac(dst, b0, cmask):
            def fn(e, dst=dst, b0=b0, cmask=cmask):
                i1 = e.tensor_tensor(h3(dst)[:, 0:4, :], pg(b0), maskb(cmask), OP.mult)
                i2 = e.tensor_tensor(h3(dst)[:, 4:8, :], pg(b0 + 1), maskb(cmask), OP.mult)
                return [i1, i2]
            return fn
        p.step("vector", mask_evac(q_cur, 2, c_negsl_b), waits=[("pe", pe_afam)], inc="vv")
        p.step("vector", mask_evac(p_cur, 4, c_negsu_b), inc="vv")
        p.step("vector", mask_evac(mt_bf, 6, c_iu_b), inc="vv")
        vv_q = p.val("vv")
        sc_p = 0
        p.bank_consumed(2, "vv")
        p.bank_consumed(3, "vv")
        p.bank_consumed(4, "vv")
        p.bank_consumed(5, "vv")
        p.bank_consumed(6, "vv")
        p.bank_consumed(7, "vv")

        for h in range(H):
            p.step("tensor", (lambda t, h=h: t.matmul(
                pg(h // 4)[:, h % 4, :], h3(nkfm)[:, h, :], h3(sig_bf)[:, h, :],
                start=True, stop=False)),
                waits=([("vv", vv_sig), ("sc", sc_tpe)] + p.bank_w(0, 1)) if h == 0 else [],
                inc=None)
            p.step("tensor", (lambda t, h=h: t.matmul(
                pg(h // 4)[:, h % 4, :], c_id_b[:, :], h3(vt_bf)[:, h, :],
                start=False, stop=True)),
                inc="pe" if h == H - 1 else None)
        pe_x0 = p.val("pe")
        st["pe_x0"][ccg] = pe_x0

        def x0_evac(v):
            i1 = v.tensor_copy(h3(x_a)[:, 0:4, :], pg(0))
            i2 = v.tensor_copy(h3(x_a)[:, 4:8, :], pg(1))
            return [i1, i2]
        p.step("vector", x0_evac, waits=[("pe", pe_x0)], inc="vv")
        vv_x = p.val("vv")
        p.bank_consumed(0, "vv")
        p.bank_consumed(1, "vv")

        xcur, xnxt = x_a, x_b
        pcur, pnxt = p_cur, p_nxt
        qcur, qnxt = q_cur, q_nxt
        vvA = vv_x
        vvB = vv_x
        scA = 0
        scB = 0
        scQA = 0
        vvQB = vv_q
        for lvl in range(LVLS):
            for g, (bank, h0) in enumerate([(0, 0), (1, 4)]):
                w0 = ([("vv", (vvA if g == 0 else vvB)), ("vv", vv_q),
                       ("sc", (scA if g == 0 else scB))] + p.bank_w(bank))
                for j in range(4):
                    p.step("tensor", (lambda t, bank=bank, j=j, h0=h0, pcur=pcur,
                           xcur=xcur: t.matmul(
                               pg(bank)[:, j, :], h3(pcur)[:, h0 + j, :],
                               h3(xcur)[:, h0 + j, :], start=True, stop=True)),
                        waits=w0 if j == 0 else [],
                        inc="pe" if j == 3 else None)
                pe_ap = p.val("pe")

                def xadd(v, bank=bank, h0=h0, xcur=xcur, xnxt=xnxt):
                    return v.tensor_add(h3(xnxt)[:, h0:h0 + 4, :], pg(bank),
                                        h3(xcur)[:, h0:h0 + 4, :])
                p.step("vector", xadd, waits=[("pe", pe_ap)], inc="vv")
                if g == 0:
                    vvA = p.val("vv")
                else:
                    vvB = p.val("vv")
                p.bank_consumed(bank, "vv")
            if lvl < LVLS - 1:
                for g, (bank, h0) in enumerate([(2, 0), (3, 4)]):
                    w0 = ([("sc", (scA if g == 0 else scB)),
                           ("sc", scQA if g == 0 else 0),
                           ("vv", vvQB if g == 1 else 0), ("vv", vv_q)] +
                          p.bank_w(bank))
                    for j in range(4):
                        p.step("tensor", (lambda t, bank=bank, j=j, h0=h0, qcur=qcur,
                               pcur=pcur: t.matmul(
                                   pg(bank)[:, j, :], h3(qcur)[:, h0 + j, :],
                                   h3(pcur)[:, h0 + j, :], start=True, stop=True)),
                            waits=w0 if j == 0 else [],
                            inc="pe" if j == 3 else None)
                    pe_p = p.val("pe")

                    def pevac(sc, bank=bank, h0=h0, pnxt=pnxt):
                        return sc.copy(h3(pnxt)[:, h0:h0 + 4, :], pg(bank))
                    p.step("scalar", pevac, waits=[("pe", pe_p)], inc="sc")
                    if g == 0:
                        scA = p.val("sc")
                    else:
                        scB = p.val("sc")
                    p.bank_consumed(bank, "sc")
            if lvl < LVLS - 2:
                for g, (bank, h0) in enumerate([(4, 0), (5, 4)]):
                    for j in range(4):
                        p.step("tensor", (lambda t, bank=bank, j=j, h0=h0, qcur=qcur,
                               pcur=pcur: t.matmul(
                                   pg(bank)[:, j, :], h3(pcur)[:, h0 + j, :],
                                   h3(qcur)[:, h0 + j, :], start=True, stop=True)),
                            waits=p.bank_w(bank) if j == 0 else [],
                            inc="pe" if j == 3 else None)
                    pe_q = p.val("pe")
                    if g == 0:
                        def qevacA(sc, h0=h0, qnxt=qnxt):
                            return sc.copy(h3(qnxt)[:, h0:h0 + 4, :], pg(4))
                        p.step("scalar", qevacA, waits=[("pe", pe_q)], inc="sc")
                        scQA = p.val("sc")
                        p.bank_consumed(4, "sc")
                    else:
                        def qevacB(v, h0=h0, qnxt=qnxt):
                            return v.tensor_copy(h3(qnxt)[:, h0:h0 + 4, :], pg(5))
                        p.step("vector", qevacB, waits=[("pe", pe_q)], inc="vv")
                        vvQB = p.val("vv")
                        p.bank_consumed(5, "vv")
            xcur, xnxt = xnxt, xcur
            if lvl < LVLS - 1:
                pcur, pnxt = pnxt, pcur
            if lvl < LVLS - 2:
                qcur, qnxt = qnxt, qcur
        rbar = xcur
        vv_rbar = max(vvA, vvB)

        for h in range(H):
            p.step("tensor", (lambda t, h=h: t.matmul(
                pg(2 + h // 4)[:, h % 4, :], h3(sig_bf)[:, h, :], h3(qfm)[:, h, :],
                start=True, stop=False)),
                waits=([("vv", vv_rbar)] + p.bank_w(2, 3)) if h == 0 else [],
                inc=None)
            p.step("tensor", (lambda t, h=h, rbar=rbar: t.matmul(
                pg(2 + h // 4)[:, h % 4, :], h3(rbar)[:, h, :], h3(mt_bf)[:, h, :],
                start=False, stop=True)),
                inc="pe" if h == H - 1 else None)
        pe_o = p.val("pe")
        for h in range(H):
            p.step("tensor", (lambda t, h=h: t.matmul(
                pg(4 + h // 4)[:, h % 4, :], c_id_b[:, :], h3(sig_bf)[:, h, :],
                start=True, stop=False)),
                waits=p.bank_w(4, 5) if h == 0 else [], inc=None)
            p.step("tensor", (lambda t, h=h, rbar=rbar, par=par: t.matmul(
                pg(4 + h // 4)[:, h % 4, :], h3(kb_bf[par])[:, h, :], h3(rbar)[:, h, :],
                start=False, stop=True)),
                inc="pe" if h == H - 1 else None)
        pe_upd = p.val("pe")
        st["pe_upd"][ccg] = pe_upd

        def oevac(sc, cc=cc):
            o3 = o_fm[:, :].rearrange("p (h n) -> p h n", n=TT)
            i1 = sc.copy(o3[:, 0:4, cc * CC:(cc + 1) * CC], pg(2))
            i2 = sc.copy(o3[:, 4:8, cc * CC:(cc + 1) * CC], pg(3))
            return [i1, i2]
        p.step("scalar", oevac,
               waits=[("pe", pe_o)] + (p.buf_w("o_fm") if cc == 0 else []), inc="sc")
        p.bank_consumed(2, "sc")
        p.bank_consumed(3, "sc")

        def sevac(v, cc=cc):
            i1 = v.tensor_tensor(h3(s_st)[:, 0:4, :], pg(4), bcast8s(lamc4, cc)[:, 0:4, :],
                                 OP.mult)
            i2 = v.tensor_tensor(h3(s_st)[:, 4:8, :], pg(5), bcast8s(lamc4, cc)[:, 4:8, :],
                                 OP.mult)
            return [i1, i2]
        p.step("vector", sevac, waits=[("pe", pe_upd)], inc="vv")
        p.bank_consumed(4, "vv")
        p.bank_consumed(5, "vv")

    def emit_wo(t_i):
        tk = t_i * TT
        sc_oall = p.val("sc")
        for half in range(2):
            for oc in range(4):
                for hv in range(NC_D):
                    p.step("tensor", (lambda t, half=half, oc=oc, hv=hv: t.matmul(
                        pb[oc][:, :],
                        w3(wo_sb, D)[:, hv, (half * 4 + oc) * 128:(half * 4 + oc + 1) * 128],
                        o_fm[:, :].rearrange("p (h n) -> p h n", n=TT)[:, hv, :],
                        start=(hv == 0), stop=(hv == NC_D - 1))),
                        waits=([("sc", sc_oall), ("dwo", 16)] + p.bank_w(0, 1, 2, 3))
                        if (hv == 0 and oc == 0) else [],
                        inc="pe" if hv == NC_D - 1 else None)
            pe_woh = p.val("pe")
            vv_y1 = 0
            for q in range(2):
                c0 = half * 4 + 2 * q
                p.dma("sync", (lambda s, tk=tk, c0=c0: s.dma_start(
                    out=tile3(xnwo, TT),
                    in_=dram3(xn32_dram)[:, c0:c0 + 2, tk:tk + TT])),
                    waits=[("dxn", st["dxn_t"][t_i])] + p.buf_w("xnwo"), inc="dxw")
                ds_xw = p.val("dxw")

                def y1add(v, q=q):
                    i1 = v.tensor_add(tile3(y1h, TT)[:, 2 * q, :], pb[2 * q][:, :],
                                      tile3(xnwo, TT)[:, 0, :])
                    i2 = v.tensor_add(tile3(y1h, TT)[:, 2 * q + 1, :], pb[2 * q + 1][:, :],
                                      tile3(xnwo, TT)[:, 1, :])
                    return [i1, i2]
                p.step("vector", y1add,
                       waits=[("pe", pe_woh), ("dxw", ds_xw)] + p.buf_w("y1h"), inc="vv")
                vv_y1 = p.val("vv")
                p.buf_set("xnwo", ("vv", vv_y1))
                p.bank_consumed(2 * q, "vv")
                p.bank_consumed(2 * q + 1, "vv")
            p.step("scalar", (lambda sc: sc.activation(
                xsq[:, 0:4 * TT], y1h[:, :], AF.Square)),
                waits=[("vv", vv_y1)] + p.buf_w("xsq"), inc="sc")
            sc_sq = p.val("sc")
            for ic in range(4):
                p.step("tensor", (lambda t, ic=ic, half=half: t.matmul(
                    pb[7][0:1, 0:TT], c_onescol_b[:, 0:1], tile3(xsq, TT)[:, ic, :],
                    start=(ic == 0 and half == 0),
                    stop=(ic == 3 and half == 1))),
                    waits=([("sc", sc_sq)] + (p.bank_w(7) if half == 0 else []))
                    if ic == 0 else [],
                    inc="pe" if ic == 3 else None)
            pe_ss2 = p.val("pe")
            p.dma("sync", (lambda s, tk=tk, half=half: s.dma_start(
                out=dram3(y1_dram)[:, half * 4:(half + 1) * 4, tk:tk + TT],
                in_=tile3(y1h, TT))),
                waits=[("vv", vv_y1)], inc="dy1")
            p.buf_set("y1h", ("dy1", p.val("dy1")), ("sc", sc_sq))
            p.buf_set("xsq", ("pe", pe_ss2))
        for hv in range(NC_D):
            p.step("tensor", (lambda t, hv=hv: t.matmul(
                pb[6][0:1, 0:TT], wo1_bf[:, hv:hv + 1],
                o_fm[:, :].rearrange("p (h n) -> p h n", n=TT)[:, hv, :],
                start=(hv == 0), stop=(hv == NC_D - 1))),
                waits=([("vv", vv_wo1), ("sc", sc_oall)] + p.bank_w(6)) if hv == 0 else [],
                inc="pe" if hv == NC_D - 1 else None)
        st["pe_mu2_t"] = p.val("pe")
        st["pe_ss2_t"] = pe_ss2
        p.buf_set("o_fm", ("pe", p.val("pe")))

    def emit_ln2fin(t_i):
        tk = t_i * TT
        p.step("vector", lambda v: v.tensor_scalar(
            st_ss[0:1, :], pb[6][0:1, 0:TT], 1.0 / D, None, OP.mult),
            waits=[("pe", st["pe_mu2_t"]), ("dst2", st["dst2_prev"])], inc="vv")
        p.step("vector", lambda v: v.tensor_scalar(
            st_mu0[0:1, :], pb[7][0:1, 0:TT], 1.0 / D, None, OP.mult),
            waits=[("pe", st["pe_ss2_t"])], inc="vv")
        p.bank_consumed(6, "vv")
        p.bank_consumed(7, "vv")
        p.step("vector", lambda v: v.tensor_mul(xnwo[0:1, 0:TT], st_ss[0:1, :],
                                                st_ss[0:1, :]),
               waits=[("gp", st.get("gp_u", 0))], inc="vv")
        p.step("vector", lambda v: v.tensor_sub(st_mu0[0:1, :], st_mu0[0:1, :],
                                                xnwo[0:1, 0:TT]), inc="vv")
        vv_var2 = p.val("vv")
        p.buf_set("xnwo", ("vv", vv_var2))
        p.step("scalar", lambda sc: sc.activation(st_mu0[0:1, :], st_mu0[0:1, :], AF.Ln,
                                                  bias=f32(EPS_LN)),
               waits=[("vv", vv_var2)], inc="sc")
        p.step("scalar", lambda sc: sc.activation(st_mu0[0:1, :], st_mu0[0:1, :], AF.Exp,
                                                  scale=f32(-0.5)), inc="sc")
        sc_r2 = p.val("sc")
        p.step("vector", lambda v: v.tensor_mul(st_ss[0:1, :], st_ss[0:1, :],
                                                st_mu0[0:1, :]),
               waits=[("sc", sc_r2)], inc="vv")
        vv_st2 = p.val("vv")
        p.dma("sync", (lambda s, tk=tk: s.dma_start(out=st2_dram[0:1, tk:tk + TT],
                                                    in_=st_ss[0:1, :])),
              waits=[("vv", vv_st2)], inc="dst2")
        p.dma("sync", (lambda s, tk=tk: s.dma_start(out=st2_dram[1:2, tk:tk + TT],
                                                    in_=st_mu0[0:1, :])), inc="dst2")
        st["dst2_prev"] = p.val("dst2")

    emit_xload(0)
    emit_lnhead(0)
    for cc in range(NCC):
        emit_proj(0, cc)
    emit_lnexp(0, [0, 1, 2, 3])
    for t_i in range(NT):
        nxt = t_i + 1
        if nxt < NT:
            emit_xload(nxt)
        emit_scalings(t_i, 0)
        emit_delta(t_i, 0)
        if nxt < NT:
            emit_lnhead(nxt)
        emit_scalings(t_i, 1)
        emit_delta(t_i, 1)
        if nxt < NT:
            emit_proj(nxt, 0)
            emit_proj(nxt, 1)
        emit_scalings(t_i, 2)
        emit_delta(t_i, 2)
        if nxt < NT:
            emit_proj(nxt, 2)
        emit_scalings(t_i, 3)
        if nxt < NT:
            emit_proj(nxt, 3)
            emit_lnexp(nxt, [0, 1, 2])
        emit_delta(t_i, 3)
        emit_wo(t_i)
        emit_ln2fin(t_i)
        if nxt < NT:
            emit_lnexp(nxt, [3])
        if t_i == (NT - 2 if EARLY_CC else NT - 1):
            p.dma("gpsimd", lambda g: g.dma_start(out=cnt_in, in_=run_rep[0:1, 0:8]),
                  waits=[("vv", p.val("vv"))], inc="dcnt")
            p.step("gpsimd", lambda g: g.collective_compute(
                "AllGather", OP.bypass, replica_groups=[list(range(8))],
                ins=[cnt_in.opt()], outs=[cnt_all.opt()]),
                waits=[("dcnt", p.val("dcnt"))], inc="cc")
            p.dma("gpsimd", lambda g: g.dma_start(out=cnt_sb[0:1, 0:64],
                                                  in_=cnt_all[0:8, 0:8]),
                  waits=[("cc", p.val("cc"))], inc="dcall")

    dg_xn_all = p.val("dxn")
    pe_afinal = p.val("pe")
    vv_afinal = p.val("vv")
    sc_afinal = p.val("sc")
    alias_w = [("pe", pe_afinal), ("vv", vv_afinal), ("sc", sc_afinal),
               ("dst2", p.val("dst2")), ("dy1", p.val("dy1")),
               ("dxw", p.val("dxw")), ("dx", p.val("dx"))]

    for ccg in range(32):
        p.step("scalar", (lambda sc, ccg=ccg: sc.activation(
            r_l[:, ccg * 8:(ccg + 1) * 8], r_l[:, ccg * 8:(ccg + 1) * 8], AF.Exp,
            bias=r_negm[:, ccg:ccg + 1])),
            waits=[("vv", vv_afinal)] if ccg == 0 else [], inc="sc")
    sc_exp = p.val("sc")
    dg_call = p.val("dcall")
    p.step("vector", lambda v: v.tensor_tensor(
        base_row[0:1, 8:72].rearrange("p (e j) -> p e j", j=8),
        cnt_sb[0:1, 0:64].rearrange("p (j e) -> p j e", e=8).transpose([0, 2, 1]),
        coremask[0:1, 0:8].unsqueeze(1).broadcast_to([1, 8, 8]),
        OP.mult), waits=[("dcall", dg_call)], inc="vv")
    p.step("vector", lambda v: v.reduce_sum(
        base_row[0:1, 0:8], base_row[0:1, 8:72].rearrange("p (e j) -> p e j", j=8), AX.X),
        inc="vv")
    vv_base = p.val("vv")

    for gq in range(8):
        p.dma("gpsimd", (lambda g, gq=gq: g.dma_start(
            out=w3(w1_sb, FF)[:, :, gq * 512:(gq + 1) * 512],
            in_=dram3(d["w1"])[:, :, gq * 512:(gq + 1) * 512])),
            waits=alias_w if gq == 0 else [], inc="dw1" + "abcd"[gq // 2])
    for gq in range(8):
        p.dma("gpsimd", (lambda g, gq=gq: g.dma_start(
            out=w3(w2_sb, D)[:, :, gq * 128:(gq + 1) * 128],
            in_=dram3(d["w2"])[:, :, gq * 128:(gq + 1) * 128])),
            inc="dw2" + "ab"[gq // 4])
    for gh in range(2):
        p.dma("gpsimd", (lambda g, gh=gh: g.dma_start(
            out=w3(wexp_sb, D)[:, :, gh * 512:(gh + 1) * 512],
            in_=dram3(d["wexp"])[:, :, gh * 512:(gh + 1) * 512])),
            inc="dwex" + "ab"[gh])

    NSUB = T // SUB

    def emit_bload(su, first=False):
        par = su % 2
        j0 = su * SUB
        p.dma("sync", (lambda s, j0=j0, par=par: s.dma_start(
            out=tile3(by1[par], SUB), in_=dram3(y1_dram)[:, :, j0:j0 + SUB])),
            waits=(alias_w if first else []) + p.buf_w(f"by1{par}") +
                  [("dy1", p.val("dy1"))], inc="dsb")
        p.dma("gpsimd", (lambda g, j0=j0, par=par: g.dma_start(
            out=tile3(bxnb[par], SUB), in_=dram3(xn32_dram)[:, :, j0:j0 + SUB])),
            waits=p.buf_w(f"bxnb{par}") + [("dxn", dg_xn_all)], inc="dsg")
        p.dma("sync", (lambda s, j0=j0, par=par: s.dma_start(
            out=bst2[par][0:2, :], in_=st2_dram[0:2, j0:j0 + SUB])),
            waits=[("dst2", p.val("dst2"))] + p.buf_w(f"bst{par}"), inc="dsb")
        return (p.val("dsb"), p.val("dsg"))

    ds_b = [(0, 0)] * (NSUB + 1)
    ds_b[0] = emit_bload(0, first=True)
    ds_b[1] = emit_bload(1, first=True)
    p.dma("sync", lambda s: s.dma_start(out=base_dram, in_=base_row[0:1, 0:8]),
          waits=[("vv", vv_base)], inc="dbs")
    p.dma("sync", lambda s: s.dma_start(
        out=base_rep[:, :],
        in_=base_dram.unsqueeze(0).broadcast_to([128, 1, 8]).rearrange("p a e -> p (a e)")),
        waits=[("dbs", p.val("dbs"))], inc="dbb")
    dg_bb = p.val("dbb")
    for su in range(NSUB):
        par = su % 2
        j0 = su * SUB

        p.step("tensor", (lambda t, par=par: t.matmul(
            pb[6][:, 0:SUB], c_sel[0:2, 128:256], bst2[par][0:2, :],
            start=True, stop=True)),
            waits=[("dsb", ds_b[su][0])] + p.bank_w(6), inc=None)
        p.step("tensor", (lambda t, par=par: t.matmul(
            pb[6][:, SUB:2 * SUB], c_sel[0:2, 0:128], bst2[par][0:2, :],
            start=True, stop=True)), inc="pe")
        pe_bc2 = p.val("pe")
        p.step("vector", (lambda v, par=par: v.tensor_tensor(
            tile3(bout, SUB), tile3(by1[par], SUB),
            pb[6][:, 0:SUB].unsqueeze(1).broadcast_to([128, NC_D, SUB]), OP.mult)),
            waits=[("pe", pe_bc2)] + p.buf_w("bout"), inc="vv")
        p.step("vector", lambda v: v.tensor_tensor(
            tile3(bout, SUB), tile3(bout, SUB),
            pb[6][:, SUB:2 * SUB].unsqueeze(1).broadcast_to([128, NC_D, SUB]), OP.subtract),
            inc="vv")
        vv_u2 = p.val("vv")
        p.bank_consumed(6, "vv")
        for ic in range(NC_D):
            p.step("scalar", (lambda sc, ic=ic: sc.activation(
                tile3(bh2, SUB)[:, ic, :], tile3(bout, SUB)[:, ic, :], AF.Identity,
                bias=ln2b[:, ic:ic + 1], scale=ln2w[:, ic:ic + 1])),
                waits=([("vv", vv_u2)] + p.buf_w("bh2")) if ic == 0 else [],
                inc="sc" if ic == NC_D - 1 else None)
        sc_h2 = p.val("sc")
        p.buf_set(f"bst{par}", ("pe", pe_bc2))

        for g4 in range(8):
            bp = (g4 % 2) * 2
            for j in range(4):
                oc = g4 * 4 + j
                bank = bp + j // 2
                colr = (j % 2) * SUB
                for ic in range(NC_D):
                    p.step("tensor", (lambda t, oc=oc, bank=bank, colr=colr, ic=ic: t.matmul(
                        pb[bank][:, colr:colr + SUB],
                        w3(w1_sb, FF)[:, ic, oc * 128:(oc + 1) * 128],
                        tile3(bh2, SUB)[:, ic, :],
                        start=(ic == 0), stop=(ic == NC_D - 1))),
                        waits=([("sc", sc_h2), ("dw1" + "abcd"[g4 // 2], 32)] +
                               p.bank_w(bp, bp + 1))
                        if (ic == 0 and j == 0) else [],
                        inc="pe" if (ic == NC_D - 1 and j == 3) else None)
            pe_g = p.val("pe")
            for j in range(4):
                oc = g4 * 4 + j
                bank = bp + j // 2
                colr = (j % 2) * SUB
                p.step("scalar", (lambda sc, oc=oc, bank=bank, colr=colr: sc.activation(
                    tile3(bg, SUB)[:, oc, :], pb[bank][:, colr:colr + SUB], AF.Gelu,
                    bias=b1_sb[:, oc:oc + 1])),
                    waits=([("pe", pe_g)] + (p.buf_w("bg") if oc == 0 else [])), inc="sc")
            p.bank_consumed(bp, "sc")
            p.bank_consumed(bp + 1, "sc")
        sc_gelu = p.val("sc")

        for q in range(2):
            ccg = j0 // CC + q
            p.step("vector", (lambda v, ccg=ccg: v.reduce_sum(
                m1c[:, :], r_l[:, ccg * 8:(ccg + 1) * 8], AX.X)),
                waits=([("sc", sc_exp), ("dbb", dg_bb)] if (su == 0 and q == 0) else []),
                inc="vv")
            p.step("vector", (lambda v: v.reciprocal(t8a[:, 0:1], m1c[:, :])), inc="vv")
            p.step("vector", (lambda v, ccg=ccg: v.tensor_tensor(
                t8b[:, :], r_ohf[:, ccg * 8:(ccg + 1) * 8], base_rep[:, :], OP.mult)),
                inc="vv")
            p.step("vector", (lambda v: v.reduce_sum(m1c[:, :], t8b[:, :], AX.X)), inc="vv")
            p.step("vector", (lambda v, ccg=ccg: v.tensor_add(
                m1c[:, :], m1c[:, :], r_c[:, ccg:ccg + 1])), inc="vv")
            p.step("vector", (lambda v: v.tensor_scalar(
                m1c[:, :], m1c[:, :], 101.0, None, OP.is_lt)), inc="vv")
            p.step("vector", (lambda v, ccg=ccg: v.tensor_mul(
                r_g[:, ccg:ccg + 1], t8a[:, 0:1], m1c[:, :])), inc="vv")
            p.step("vector", (lambda v, ccg=ccg: v.tensor_scalar(
                bdiag[:, :], c_id_f[:, :], r_g[:, ccg:ccg + 1], None, OP.mult)),
                waits=[("pe", st.get("pe_gate_prev", 0) if q == 0 else p.val("pe"))],
                inc="vv")
            vv_dg2 = p.val("vv")
            p.step("tensor", (lambda t, q=q: t.matmul(
                pb[6][:, q * 128:(q + 1) * 128], c_ones_f[:, :], bdiag[:, :],
                start=True, stop=True)),
                waits=[("vv", vv_dg2)] + (p.bank_w(6) if q == 0 else []), inc="pe")
        pe_gb = p.val("pe")
        st["pe_gate_prev"] = pe_gb
        p.step("vector", lambda v: v.tensor_copy(bgb[:, :], pb[6][:, 0:SUB]),
               waits=[("pe", pe_gb)], inc="vv")
        vv_gb = p.val("vv")
        p.bank_consumed(6, "vv")


        for g4 in range(2):
            bp = 4 + g4 * 2
            for j in range(4):
                oc = g4 * 4 + j
                bank = bp + j // 2
                colr = (j % 2) * SUB
                for ic2 in range(NC_F):
                    p.step("tensor", (lambda t, oc=oc, bank=bank, colr=colr, ic2=ic2: t.matmul(
                        pb[bank][:, colr:colr + SUB],
                        w3(w2_sb, D)[:, ic2, oc * 128:(oc + 1) * 128],
                        tile3(bg, SUB)[:, ic2, :],
                        start=(ic2 == 0), stop=(ic2 == NC_F - 1))),
                        waits=([("sc", sc_gelu), ("dw2" + "ab"[g4], 64)] +
                               p.bank_w(bp, bp + 1))
                        if (ic2 == 0 and j == 0) else [],
                        inc="pe" if (ic2 == NC_F - 1 and j == 3) else None)
            pe_y2 = p.val("pe")
            for j in range(4):
                oc = g4 * 4 + j
                bank = bp + j // 2
                colr = (j % 2) * SUB
                p.step("vector", (lambda v, oc=oc, bank=bank, colr=colr, par=par:
                       v.scalar_tensor_tensor(
                           tile3(by1[par], SUB)[:, oc, :], pb[bank][:, colr:colr + SUB],
                           b2_sb[:, oc:oc + 1], tile3(by1[par], SUB)[:, oc, :],
                           OP.add, OP.add)),
                    waits=[("pe", pe_y2)], inc="vv")
            p.bank_consumed(bp, "vv")
            p.bank_consumed(bp + 1, "vv")
        vv_s1 = p.val("vv")

        for g4 in range(2):
            bp = g4 * 2
            for j in range(4):
                oc = g4 * 4 + j
                bank = bp + j // 2
                colr = (j % 2) * SUB
                for ic in range(NC_D):
                    p.step("tensor", (lambda t, oc=oc, bank=bank, colr=colr, ic=ic, par=par:
                           t.matmul(
                               pb[bank][:, colr:colr + SUB],
                               w3(wexp_sb, D)[:, ic, oc * 128:(oc + 1) * 128],
                               tile3(bxnb[par], SUB)[:, ic, :],
                               start=(ic == 0), stop=(ic == NC_D - 1))),
                        waits=([("dwex" + "ab"[g4], 16), ("dsg", ds_b[su][1])] +
                               p.bank_w(bp, bp + 1))
                        if (ic == 0 and j == 0) else [],
                        inc="pe" if (ic == NC_D - 1 and j == 3) else None)
            pe_m = p.val("pe")
            for j in range(4):
                oc = g4 * 4 + j
                bank = bp + j // 2
                colr = (j % 2) * SUB
                p.step("vector", (lambda v, oc=oc, bank=bank, colr=colr: v.scalar_tensor_tensor(
                    tile3(bout, SUB)[:, oc, :], pb[bank][:, colr:colr + SUB],
                    bexp_sb[:, oc:oc + 1], bgb[:, :], OP.add, OP.mult)),
                    waits=([("pe", pe_m), ("vv", vv_gb), ("sc", sc_h2)]
                           if (g4 == 0 and j == 0) else [("pe", pe_m)]),
                    inc="vv")
            p.bank_consumed(bp, "vv")
            p.bank_consumed(bp + 1, "vv")
        vv_m1 = p.val("vv")
        p.buf_set(f"bxnb{par}", ("pe", p.val("pe")))

        p.step("gpsimd", (lambda g, par=par: g.tensor_add(
            bout[:, :], bout[:, :], by1[par][:, :])),
            waits=[("vv", max(vv_m1, vv_s1))], inc="gp")
        gp_out = p.val("gp")
        p.buf_set(f"by1{par}", ("gp", gp_out))
        p.dma("gpsimd", (lambda g, j0=j0: g.dma_start(
            out=tile3(bout, SUB), in_=dram3(d["xt"])[:, :, j0:j0 + SUB],
            accum_op=OP.add)),
            waits=[("gp", gp_out)], inc="dxa")
        dg_xadd = p.val("dxa")
        p.dma("sync", (lambda s, j0=j0: s.dma_start(
            out=dram3(d["out"])[:, :, j0:j0 + SUB], in_=tile3(bout, SUB))),
            waits=[("dxa", dg_xadd)], inc="st")
        p.buf_set("bout", ("st", p.val("st")))
        p.buf_set("bh2", ("pe", p.val("pe")))
        p.buf_set("bg", ("pe", p.val("pe")))
        if su + 2 <= NSUB - 1:
            ds_b[su + 2] = emit_bload(su + 2)

    if DEBUG_DUMP:
        d["dbg_st2"] = nc.dram_tensor("dbg_st2", [2, T], F32, kind="ExternalOutput").ap()
        d["dbg_y1"] = nc.dram_tensor("dbg_y1", [D, T], F32, kind="ExternalOutput").ap()
        d["dbg_xn"] = nc.dram_tensor("dbg_xn", [D, T], F32, kind="ExternalOutput").ap()
        p.dma("sync", lambda s: s.dma_start(out=d["dbg_st2"], in_=st2_dram[:, :]),
              waits=[("st", p.val("st"))], inc="st")
        p.dma("sync", lambda s: s.dma_start(out=d["dbg_y1"], in_=y1_dram[:, :]), inc="st")
        p.dma("sync", lambda s: s.dma_start(out=d["dbg_xn"], in_=xn32_dram[:, :]), inc="st")
    p.step("sync", lambda s: None, waits=[("st", p.val("st"))])

    _BUILD_CACHE["dbg"] = dict(r_g=r_g, r_c=r_c, r_ohf=r_ohf, r_l=r_l, r_negm=r_negm,
                               run_rep=run_rep, base_rep=base_rep, cnt_sb=cnt_sb,
                               base_row=base_row)
    with nc.Block() as blk:
        _emit(p, sems, blk, hazard_scan=_hazard_scan, hazard_fix=_hazard_fix)
    es_sem.close()
    es_persist.close()
    return nc, p


def build_fixed():
    scan = {}
    nc1, p1 = build_program(_hazard_scan=scan)
    del nc1
    hz_writers = set()
    fix = {}
    for eng in ("vector", "gpsimd"):
        flagged = set()
        for (e, rsi), ws in scan.items():
            if e == eng:
                flagged.update(ws)
        if not flagged:
            continue
        for si in flagged:
            fn, waits, inc, inc_by = p1.steps[eng][si]
            if inc is None:
                hz_writers.add((eng, si))
        cnt = {}
        vals = {}
        for si, (fn, waits, inc, inc_by) in enumerate(p1.steps[eng]):
            if inc is None and (eng, si) in hz_writers:
                inc, inc_by = "hz_" + eng, 1
            if inc is not None:
                cnt[inc] = cnt.get(inc, 0) + inc_by
                vals[si] = (inc, cnt[inc])
            else:
                vals[si] = None
        for (e, rsi), ws in scan.items():
            if e != eng:
                continue
            for wsi in ws:
                wv = vals.get(wsi)
                if wv is not None:
                    fix.setdefault((eng, rsi), []).append(wv)
    nc2, _ = build_program(_hazard_fix=fix, _hz_writers=hz_writers)
    return nc2


def _consts():
    i = np.arange(128)
    iu = (i[:, None] <= i[None, :]).astype(np.float32)
    half = np.zeros((128, 128), np.float32)
    half[0:64, :] = 1.0
    ones = np.ones((128, 128), np.float32)
    ident = np.eye(128, dtype=np.float32)
    sl = (i[:, None] > i[None, :]).astype(np.float32)
    su = (i[:, None] < i[None, :]).astype(np.float32)
    return {
        "d_c_iu_f": iu, "d_c_half_f": half, "d_c_ones_f": ones, "d_c_id_f": ident,
        "d_c_negsl_b": -sl, "d_c_negsu_b": -su, "d_c_iu_b": iu, "d_c_id_b": ident,
        "d_c_ones_b": ones,
        "d_c_onescol": np.ones((128, 1), np.float32),
        "d_c_sel": np.concatenate([np.repeat(np.eye(2, dtype=np.float32)[:, 0:1], 128, 1),
                                   np.repeat(np.eye(2, dtype=np.float32)[:, 1:2], 128, 1)],
                                  axis=1),
        "d_c_iota999": np.broadcast_to(
            (np.arange(32) % 8 + 999.0).astype(np.float32), (128, 32)).copy(),
    }


def kernel(**inputs):
    if "prog" not in _BUILD_CACHE:
        _BUILD_CACHE["prog"] = build_fixed()
    nc = _BUILD_CACHE["prog"]

    x = np.asarray(inputs["L0_attn_x"])

    def col128(vec, ncol):
        return np.ascontiguousarray(np.asarray(vec).reshape(ncol, 128).T)

    shared = {
        "wq": np.asarray(inputs["L0_attn_Wq"]),
        "wk": np.asarray(inputs["L0_attn_Wk"]),
        "wv": np.asarray(inputs["L0_attn_Wv"]),
        "wo": np.asarray(inputs["L0_attn_Wo"]),
        "w1": np.asarray(inputs["L0_attn_W1"]),
        "w2": np.asarray(inputs["L0_attn_W2"]),
        "wexp": np.asarray(inputs["L0_ffn_w_expert"]),
        "wg": np.asarray(inputs["L0_ffn_w_gate"]),
        "wab": np.ascontiguousarray(np.concatenate(
            [np.asarray(inputs["L0_attn_Wb"]), np.asarray(inputs["L0_attn_Wa"])], axis=1)),
        "d_ln1w": col128(inputs["L0_attn_ln1_w"], 8),
        "d_ln1b": col128(inputs["L0_attn_ln1_b"], 8),
        "d_ln2w": col128(inputs["L0_attn_ln2_w"], 8),
        "d_ln2b": col128(inputs["L0_attn_ln2_b"], 8),
        "d_b1": col128(inputs["L0_attn_b1"], 32),
        "d_b2": col128(inputs["L0_attn_b2"], 8),
        "d_bexp": col128(inputs["L0_ffn_b_expert"], 8),
    }
    shared.update(_consts())

    in_maps = []
    for i in range(8):
        m = dict(shared)
        m["xt"] = np.ascontiguousarray(x[i].T)
        m["d_coremask"] = (np.arange(8) < i).astype(np.float32).reshape(1, 8)
        in_maps.append(m)

    res = run_bass_kernel_spmd(nc, in_maps, core_ids=list(range(8)))
    out = np.stack([np.ascontiguousarray(np.asarray(res.results[i]["out_t"]).T)
                    for i in range(8)])
    return out.astype(np.float32)


# version 10
# speedup vs baseline: 1.0218x; 1.0218x over previous
import numpy as np
from contextlib import ExitStack

import concourse.bass as bass
import concourse.mybir as mybir
from concourse.bass_utils import run_bass_kernel_spmd

F32 = mybir.dt.float32
BF16 = mybir.dt.bfloat16
AF = mybir.ActivationFunctionType
OP = mybir.AluOpType
AX = mybir.AxisListType

f32 = np.float32

T = 4096
D = 1024
H = 8
DH = 128
FF = 4096
NT = 8
TT = 512
NCC = 4
CC = 128
NC_D = 8
NC_F = 32
SUB = 256
EPS_LN = 1e-5
EPS_L2 = 1e-6
LVLS = 7

_BUILD_CACHE = {}
DEBUG_DUMP = False
EARLY_CC = False

ENGINES = ("sync", "gpsimd", "tensor", "vector", "scalar")


class Plan:
    def __init__(self, hz_writers=None):
        self.steps = {e: [] for e in ENGINES}
        self.cnt = {}
        self.hw = {e: {} for e in ENGINES}
        self.bank_guard = {i: [] for i in range(8)}
        self.buf_guard = {}
        self.hz_writers = hz_writers or set()

    def val(self, sem):
        return self.cnt.get(sem, 0)

    def step(self, eng, fn, waits=None, inc=None, inc_by=1):
        w = []
        for s, v in (waits or []):
            if v is None or v <= 0:
                continue
            if self.hw[eng].get(s, 0) >= v:
                continue
            self.hw[eng][s] = v
            w.append((s, v))
        if inc is None and (eng, len(self.steps[eng])) in self.hz_writers:
            inc, inc_by = "hz_" + eng, 1
        if inc is not None:
            self.cnt[inc] = self.cnt.get(inc, 0) + inc_by
        self.steps[eng].append((fn, w, inc, inc_by))

    def dma(self, eng, fn, waits=None, inc=None):
        self.step(eng, fn, waits=waits, inc=inc, inc_by=16)

    def bank_w(self, *banks):
        out = []
        for b in banks:
            out.extend(self.bank_guard[b])
        return out

    def bank_consumed(self, b, sem):
        self.bank_guard[b] = [(sem, self.val(sem))]

    def buf_w(self, *names):
        out = []
        for nm in names:
            out.extend(self.buf_guard.get(nm, []))
        return out

    def buf_set(self, nm, *guards):
        self.buf_guard[nm] = [(s, v) for (s, v) in guards if v]


def _arg_names(args):
    out = set()
    for a in args:
        for attr in ("memref", "name"):
            n = getattr(a, attr, None)
            if isinstance(n, str):
                out.add(n)
                break
        else:
            t = getattr(a, "tensor", None)
            n = getattr(t, "name", None)
            if isinstance(n, str):
                out.add(n)
    return out


def _emit(plan, sems, blk, hazard_scan=None, hazard_fix=None):
    def mk(eng_name):
        check = eng_name in ("vector", "gpsimd") and hazard_scan is not None
        def closure(eng):
            recent = []
            for si, (fn, waits, inc, inc_by) in enumerate(plan.steps[eng_name]):
                if hazard_fix is not None:
                    for s, v in hazard_fix.get((eng_name, si), []):
                        eng.wait_ge(sems[s], v)
                for s, v in waits:
                    eng.wait_ge(sems[s], v)
                inst = fn(eng)
                insts = inst if isinstance(inst, list) else ([inst] if inst is not None else [])
                if check:
                    for one in insts:
                        ii = getattr(one, "ins", None)
                        m = getattr(ii, "ins", None)
                        mo = getattr(ii, "outs", None)
                        rd = _arg_names(m or [])
                        wr = _arg_names(mo or [])
                        touched = rd | wr
                        for wsi, wouts in recent:
                            if wsi != si and touched & wouts:
                                hazard_scan.setdefault((eng_name, si), set()).add(wsi)
                        recent.append((si, wr))
                        recent = recent[-2:]
                if inc is not None and insts:
                    insts[-1].then_inc(sems[inc], inc_by)
        return closure
    blk.sync(mk("sync"))
    blk.gpsimd(mk("gpsimd"))
    blk.tensor(mk("tensor"))
    blk.vector(mk("vector"))
    blk.scalar(mk("scalar"))


def build_program(_hazard_scan=None, _hazard_fix=None, _hz_writers=None):
    nc = bass.Bass("TRN2", target_bir_lowering=False, detect_race_conditions=False,
                   dynamic_dma_scratch_size=16384)

    d = {}
    d["xt"] = nc.dram_tensor("xt", [D, T], F32, kind="ExternalInput").ap()
    for nm, sh in [("wq", [D, D]), ("wk", [D, D]), ("wv", [D, D]), ("wo", [D, D]),
                   ("wab", [D, 16]), ("wg", [D, 8]), ("w1", [D, FF]), ("w2", [FF, D]),
                   ("wexp", [D, D])]:
        d[nm] = nc.dram_tensor(nm, sh, F32, kind="ExternalInput").ap()
    for nm, sh in [("ln1w", [128, NC_D]), ("ln1b", [128, NC_D]), ("ln2w", [128, NC_D]),
                   ("ln2b", [128, NC_D]), ("b1", [128, NC_F]), ("b2", [128, NC_D]),
                   ("bexp", [128, NC_D]), ("coremask", [1, 8])]:
        d[nm] = nc.dram_tensor("d_" + nm, sh, F32, kind="ExternalInput").ap()
    for nm in ["c_iu_f", "c_half_f", "c_ones_f", "c_id_f",
               "c_negsl_b", "c_negsu_b", "c_iu_b", "c_id_b", "c_ones_b"]:
        d[nm] = nc.dram_tensor("d_" + nm, [128, 128], F32, kind="ExternalInput").ap()
    d["c_onescol"] = nc.dram_tensor("d_c_onescol", [128, 1], F32, kind="ExternalInput").ap()
    d["c_sel"] = nc.dram_tensor("d_c_sel", [2, 256], F32, kind="ExternalInput").ap()
    d["c_iota999"] = nc.dram_tensor("d_c_iota999", [128, 32], F32,
                                    kind="ExternalInput").ap()
    d["out"] = nc.dram_tensor("out_t", [D, T], F32, kind="ExternalOutput").ap()

    xn32_dram = nc.dram_tensor("xn32_s", [D, T], F32).ap()
    base_dram = nc.dram_tensor("base_b", [1, 8], F32).ap()
    y1_dram = nc.dram_tensor("y1_s", [D, T], F32).ap()
    st2_dram = nc.dram_tensor("st2_s", [2, T], F32).ap()
    cnt_in = nc.dram_tensor("cnt_in", [1, 8], F32).ap()
    cnt_all = nc.dram_tensor("cnt_all", [8, 8], F32).ap()

    es_persist = ExitStack()
    E = es_persist.enter_context

    c_iu_f = E(nc.sbuf_tensor("c_iu_f", [128, 128], F32))
    c_half_f = E(nc.sbuf_tensor("c_half_f", [128, 128], F32))
    c_ones_f = E(nc.sbuf_tensor("c_ones_f", [128, 128], F32))
    c_id_f = E(nc.sbuf_tensor("c_id_f", [128, 128], F32))
    c_negsl_b = E(nc.sbuf_tensor("c_negsl_b", [128, 128], BF16))
    c_negsu_b = E(nc.sbuf_tensor("c_negsu_b", [128, 128], BF16))
    c_iu_b = E(nc.sbuf_tensor("c_iu_b", [128, 128], BF16))
    c_id_b = E(nc.sbuf_tensor("c_id_b", [128, 128], BF16))
    c_ones_b = E(nc.sbuf_tensor("c_ones_b", [128, 128], BF16))
    c_onescol = E(nc.sbuf_tensor("c_onescol", [128, 1], F32))
    c_sel = E(nc.sbuf_tensor("c_sel", [2, 256], F32))
    c_onescol_b = E(nc.sbuf_tensor("c_onescol_b", [128, 1], BF16))
    ln1w = E(nc.sbuf_tensor("ln1w", [128, NC_D], F32))
    ln1b = E(nc.sbuf_tensor("ln1b", [128, NC_D], F32))
    ln2w = E(nc.sbuf_tensor("ln2w", [128, NC_D], F32))
    ln2b = E(nc.sbuf_tensor("ln2b", [128, NC_D], F32))
    b1_sb = E(nc.sbuf_tensor("b1_sb", [128, NC_F], F32))
    b2_sb = E(nc.sbuf_tensor("b2_sb", [128, NC_D], F32))
    bexp_sb = E(nc.sbuf_tensor("bexp_sb", [128, NC_D], F32))
    coremask = E(nc.sbuf_tensor("coremask", [1, 8], F32))
    r_ohf = E(nc.sbuf_tensor("r_ohf", [128, 32 * 8], BF16))
    r_c = E(nc.sbuf_tensor("r_c", [128, 32], F32))
    r_l = E(nc.sbuf_tensor("r_l", [128, 32 * 8], F32))
    r_negm = E(nc.sbuf_tensor("r_negm", [128, 32], F32))
    r_g = E(nc.sbuf_tensor("r_g", [128, 32], F32))
    run_rep = E(nc.sbuf_tensor("run_rep", [128, 8], F32))
    base_rep = E(nc.sbuf_tensor("base_rep", [128, 8], F32))
    cnt_sb = E(nc.sbuf_tensor("cnt_sb", [1, 64], F32))
    base_row = E(nc.sbuf_tensor("base_row", [1, 72], F32))
    t8a = E(nc.sbuf_tensor("t8a", [128, 8], F32))
    t8b = E(nc.sbuf_tensor("t8b", [128, 8], F32))
    l_cc = E(nc.sbuf_tensor("l_cc", [128, 8], F32))
    m1c = E(nc.sbuf_tensor("m1c", [128, 1], F32))
    c_iota999 = E(nc.sbuf_tensor("c_iota999", [128, 32], F32))
    rt3 = E(nc.sbuf_tensor("rt3", [128, 32], F32))

    pb = [E(nc.psum_tensor(f"pb{i}", [128, 512], F32)) for i in range(8)]

    def pbf(i):
        return pb[i][:, :]

    def pbb(i):
        return pb[i][:, :].bitcast(BF16)

    def pg(i):
        return pb[i][:, :].rearrange("p (g n) -> p g n", n=DH)

    es_a = ExitStack()
    A = es_a.enter_context
    st_mu0 = A(nc.sbuf_tensor("st_mu0", [1, TT], F32))
    st_ss = A(nc.sbuf_tensor("st_ss", [1, TT], F32))
    y1h = A(nc.sbuf_tensor("y1h", [128, 4 * TT], F32))
    xnwo = A(nc.sbuf_tensor("xnwo", [128, 2 * TT], F32))
    o_fm = A(nc.sbuf_tensor("o_fm", [128, H * TT], BF16))
    s_st = A(nc.sbuf_tensor("s_st", [128, H * DH], F32))
    sig_bf = A(nc.sbuf_tensor("sig_bf", [128, H * DH], BF16))
    sq_sc = A(nc.sbuf_tensor("sq_sc", [128, H * DH], F32))
    kfm = A(nc.sbuf_tensor("kfm", [128, H * DH], BF16))
    nkfm = A(nc.sbuf_tensor("nkfm", [128, H * DH], BF16))
    kbfm = A(nc.sbuf_tensor("kbfm", [128, H * DH], BF16))
    qfm = A(nc.sbuf_tensor("qfm", [128, H * DH], BF16))
    p_cur = A(nc.sbuf_tensor("p_cur", [128, H * DH], BF16))
    p_nxt = A(nc.sbuf_tensor("p_nxt", [128, H * DH], BF16))
    q_cur = A(nc.sbuf_tensor("q_cur", [128, H * DH], BF16))
    q_nxt = A(nc.sbuf_tensor("q_nxt", [128, H * DH], BF16))
    mt_bf = A(nc.sbuf_tensor("mt_bf", [128, H * DH], BF16))
    x_a = A(nc.sbuf_tensor("x_a", [128, H * DH], BF16))
    x_b = A(nc.sbuf_tensor("x_b", [128, H * DH], BF16))
    wo_sb = A(nc.sbuf_tensor("wo_sb", [128, NC_D * D], BF16))
    wo1_sb = A(nc.sbuf_tensor("wo1_sb", [128, NC_D], F32))
    wo1_bf = A(nc.sbuf_tensor("wo1_bf", [128, NC_D], BF16))
    zab4 = A(nc.sbuf_tensor("zab4", [128, NCC * 16], F32))
    beta4 = A(nc.sbuf_tensor("beta4", [128, NCC * 8], F32))
    la8 = A(nc.sbuf_tensor("la8", [128, 8], F32))
    cumz = A(nc.sbuf_tensor("cumz", [128, 24], F32))
    dz = A(nc.sbuf_tensor("dz", [128, 8], F32))
    lam4 = A(nc.sbuf_tensor("lam4", [128, NCC * 8], F32))
    ginv4 = A(nc.sbuf_tensor("ginv4", [128, NCC * 8], F32))
    lamc4 = A(nc.sbuf_tensor("lamc4", [128, NCC * 8], F32))
    gref4 = A(nc.sbuf_tensor("gref4", [128, NCC * 8], F32))
    f18 = A(nc.sbuf_tensor("f18", [128, 8], F32))
    ssq4_q = A(nc.sbuf_tensor("ssq4_q", [128, NCC * 8], F32))
    ssq4_k = A(nc.sbuf_tensor("ssq4_k", [128, NCC * 8], F32))
    rsq_q = A(nc.sbuf_tensor("rsq_q", [128, 8], F32))
    rsq_k = A(nc.sbuf_tensor("rsq_k", [128, 8], F32))
    fq4 = A(nc.sbuf_tensor("fq4", [128, NCC * 8], F32))
    frk4 = A(nc.sbuf_tensor("frk4", [128, NCC * 8], F32))
    fkb4 = A(nc.sbuf_tensor("fkb4", [128, NCC * 8], F32))
    xn = A(nc.sbuf_tensor("xn", [128, NC_D * TT], F32))
    xsq = A(nc.sbuf_tensor("xsq", [128, 4 * TT], BF16))
    xin = A(nc.sbuf_tensor("xin", [128, NC_D * TT], F32))
    n_bf = A(nc.sbuf_tensor("n_bf", [128, NC_D * TT], BF16))
    wab_sb = A(nc.sbuf_tensor("wab_sb", [128, NC_D * 16], BF16))
    wg_sb = A(nc.sbuf_tensor("wg_sb", [128, NC_D * 8], F32))
    wq_sb = A(nc.sbuf_tensor("wq_sb", [128, NC_D * D], BF16))
    wk_sb = A(nc.sbuf_tensor("wk_sb", [128, NC_D * D], BF16))
    wv_sb = A(nc.sbuf_tensor("wv_sb", [128, NC_D * D], BF16))
    q_b4 = A(nc.sbuf_tensor("q_b4", [128, NCC * H * DH], BF16))
    k_b4 = A(nc.sbuf_tensor("k_b4", [128, NCC * H * DH], BF16))
    v_b4 = A(nc.sbuf_tensor("v_b4", [128, NCC * H * DH], BF16))
    qt_bf = A(nc.sbuf_tensor("qt_bf", [128, H * DH], BF16))
    kn_bf = A(nc.sbuf_tensor("kn_bf", [128, H * DH], BF16))
    vt_bf = A(nc.sbuf_tensor("vt_bf", [128, H * DH], BF16))
    kb_bf = [A(nc.sbuf_tensor(f"kb_bf{i}", [128, H * DH], BF16)) for i in range(2)]
    es_a.close()

    es_b = ExitStack()
    B = es_b.enter_context
    wexp_sb = B(nc.sbuf_tensor("wexp_sb", [128, NC_D * D], BF16))
    w2_sb = B(nc.sbuf_tensor("w2_sb", [128, NC_F * D], BF16))
    w1_sb = B(nc.sbuf_tensor("w1_sb", [128, NC_D * FF], BF16))
    by1 = [B(nc.sbuf_tensor(f"by1_{i}", [128, NC_D * SUB], F32)) for i in range(2)]
    bxnb = [B(nc.sbuf_tensor(f"bxnb_{i}", [128, NC_D * SUB], BF16)) for i in range(2)]
    bst2 = [B(nc.sbuf_tensor(f"bst2_{i}", [2, SUB], F32)) for i in range(2)]
    bh2 = B(nc.sbuf_tensor("bh2", [128, NC_D * SUB], BF16))
    bg = B(nc.sbuf_tensor("bg", [128, NC_F * SUB], BF16))
    bout = B(nc.sbuf_tensor("bout", [128, NC_D * SUB], F32))
    bgb = B(nc.sbuf_tensor("bgb", [128, SUB], F32))
    bdiag = B(nc.sbuf_tensor("bdiag", [128, 128], F32))
    es_b.close()

    sems = {}
    es_sem = ExitStack()
    for s in ["pe", "vv", "sc", "gp", "st", "cc", "hz_vector", "hz_gpsimd",
              "dx", "dy1", "dst2", "dxn", "dxw", "dcb", "dcs", "dwq", "dwk", "dwv",
              "dwab", "dwo", "dw1a", "dw1b", "dw1c", "dw1d", "dw2a", "dw2b",
              "dwexa", "dwexb", "dcnt", "dcall", "dsb", "dsg",
              "dxa", "dbs", "dbb"]:
        sems[s] = es_sem.enter_context(nc.semaphore(s))

    p = Plan(hz_writers=_hz_writers)

    def w3(t, m):
        return t[:, :].rearrange("p (c m) -> p c m", m=m)

    def tile3(t, n):
        return t[:, :].rearrange("p (c n) -> p c n", n=n)

    def dram3(ap):
        return ap.rearrange("(c p) n -> p c n", p=128)

    def h3(t):
        return t[:, :].rearrange("p (h e) -> p h e", e=DH)

    def h4(t, cc):
        return t[:, :].rearrange("p (s h e) -> p s h e", s=NCC, e=DH)[:, cc, :, :]

    def s4(t, cc):
        return t[:, cc * 8:(cc + 1) * 8]

    def bcast8s(t, cc):
        return s4(t, cc).unsqueeze(2).broadcast_to([128, 8, DH])

    def maskb(cmask):
        return cmask[:, :].unsqueeze(1).broadcast_to([128, 4, DH])

    p.dma("sync", lambda s: s.dma_start(out=w3(wg_sb, 8), in_=dram3(d["wg"])), inc="dcs")
    for nm, t in [("c_iu_f", c_iu_f), ("c_half_f", c_half_f), ("c_ones_f", c_ones_f),
                  ("c_id_f", c_id_f), ("c_onescol", c_onescol),
                  ("c_sel", c_sel), ("c_iota999", c_iota999),
                  ("ln1w", ln1w), ("ln1b", ln1b), ("ln2w", ln2w), ("ln2b", ln2b),
                  ("b1", b1_sb), ("b2", b2_sb), ("bexp", bexp_sb), ("coremask", coremask)]:
        p.dma("sync", lambda s, t=t, nm=nm: s.dma_start(out=t[:, :], in_=d[nm]), inc="dcs")
    ds_const = p.val("dcs")

    for nm, t in [("c_negsl_b", c_negsl_b), ("c_negsu_b", c_negsu_b), ("c_iu_b", c_iu_b),
                  ("c_id_b", c_id_b), ("c_ones_b", c_ones_b)]:
        p.dma("gpsimd", lambda g, t=t, nm=nm: g.dma_start(out=t[:, :], in_=d[nm]), inc="dcb")
    p.dma("gpsimd", lambda g: g.dma_start(out=c_onescol_b[:, :], in_=d["c_onescol"]), inc="dcb")
    dg_const = p.val("dcb")

    def load_cast(dst, src, m, sem, waits=None):
        p.dma("gpsimd", lambda g, dst=dst, src=src, m=m: g.dma_start(
            out=w3(dst, m), in_=dram3(src)), inc=sem, waits=waits)

    load_cast(wq_sb, d["wq"], D, "dwq")
    load_cast(wk_sb, d["wk"], D, "dwk")
    load_cast(wv_sb, d["wv"], D, "dwv")
    load_cast(wab_sb, d["wab"], 16, "dwab")
    load_cast(wo_sb, d["wo"], D, "dwo")

    p.step("vector", lambda v: v.tensor_reduce(wo1_sb[:, :], w3(wo_sb, D), AX.X, OP.add),
           waits=[("dwo", 16)], inc="vv")
    p.step("vector", lambda v: v.tensor_copy(wo1_bf[:, :], wo1_sb[:, :]), inc="vv")
    vv_wo1 = p.val("vv")
    p.step("vector", lambda v: v.memset(run_rep[:, :], 0.0), waits=[("dcs", ds_const)], inc="vv")
    p.step("gpsimd", lambda g: g.memset(s_st[:, :], 0.0), inc="gp")
    gp_szero = p.val("gp")

    st = {
        "pe_upd": {},
        "pe_tp": {},
        "pe_x0": {},
        "pe_proj": {},
        "sc_silu": {},
        "vv_zab": {},
        "vv_ssq": {},
        "vv_scal": {},
        "sc_n": {},
        "sc_lnexp": {},
        "ds_x": {},
        "dxn_t": {},
        "vv_ssq_prev": 0,
        "vv_folds_prev": 0,
        "sc_lnexp_prev": 0,
        "dst2_prev": 0,
        "pe_mu2_t": 0,
        "pe_ss2_t": 0,
    }

    def emit_xload(t_i):
        tk = t_i * TT
        p.dma("sync", (lambda s, tk=tk: s.dma_start(
            out=tile3(xin, TT), in_=dram3(d["xt"])[:, :, tk:tk + TT])),
            waits=p.buf_w("xin"), inc="dx")
        st["ds_x"][t_i] = p.val("dx")

    def emit_lnhead(t_i):
        tk = t_i * TT
        ds_x = st["ds_x"][t_i]
        for ic in range(NC_D):
            p.step("tensor", (lambda t, ic=ic: t.matmul(
                pb[6][0:1, 0:TT], c_onescol[:, 0:1], tile3(xin, TT)[:, ic, :],
                start=(ic == 0), stop=(ic == NC_D - 1))),
                waits=([("dx", ds_x), ("dcs", ds_const)] + p.bank_w(6)) if ic == 0 else [],
                inc="pe" if ic == NC_D - 1 else None)
        pe_mu0 = p.val("pe")
        sc_xsq = 0
        for hf in range(2):
            p.step("scalar", (lambda sc, hf=hf: sc.activation(
                tile3(xsq, TT), tile3(xin, TT)[:, hf * 4:hf * 4 + 4, :], AF.Square)),
                waits=(([("dx", ds_x)] + p.buf_w("xsq")) if hf == 0
                       else [("pe", p.val("pe"))]), inc="sc")
            sc_xsq = p.val("sc")
            for ic in range(4):
                p.step("tensor", (lambda t, ic=ic, hf=hf: t.matmul(
                    pb[7][0:1, 0:TT], c_onescol_b[:, 0:1], tile3(xsq, TT)[:, ic, :],
                    start=(ic == 0 and hf == 0), stop=(ic == 3 and hf == 1))),
                    waits=([("sc", sc_xsq), ("dcb", dg_const)] +
                           (p.bank_w(7) if hf == 0 else [])) if ic == 0 else [],
                    inc="pe" if ic == 3 else None)
        pe_ss0 = p.val("pe")

        p.step("vector", lambda v: v.tensor_scalar(
            st_mu0[0:1, :], pb[6][0:1, 0:TT], 1.0 / D, None, OP.mult),
            waits=[("pe", pe_mu0), ("dst2", st["dst2_prev"])], inc="vv")
        p.step("vector", lambda v: v.tensor_scalar(
            st_ss[0:1, :], pb[7][0:1, 0:TT], 1.0 / D, None, OP.mult),
            waits=[("pe", pe_ss0)], inc="vv")
        p.bank_consumed(6, "vv")
        p.bank_consumed(7, "vv")
        p.step("vector", lambda v: v.tensor_mul(xnwo[0:1, 0:TT], st_mu0[0:1, :],
                                                st_mu0[0:1, :]), inc="vv")
        p.step("vector", lambda v: v.tensor_sub(st_ss[0:1, :], st_ss[0:1, :],
                                                xnwo[0:1, 0:TT]), inc="vv")
        vv_var0 = p.val("vv")
        p.buf_set("xnwo", ("vv", vv_var0))
        p.step("scalar", lambda sc: sc.activation(st_ss[0:1, :], st_ss[0:1, :], AF.Ln,
                                                  bias=f32(EPS_LN)),
               waits=[("vv", vv_var0)], inc="sc")
        p.step("scalar", lambda sc: sc.activation(st_ss[0:1, :], st_ss[0:1, :], AF.Exp,
                                                  scale=f32(-0.5)), inc="sc")
        sc_r0 = p.val("sc")
        p.step("vector", lambda v: v.tensor_mul(st_mu0[0:1, :], st_mu0[0:1, :],
                                                st_ss[0:1, :]),
               waits=[("sc", sc_r0)], inc="vv")
        vv_r0 = p.val("vv")

        p.step("tensor", lambda t: t.matmul(pbf(0), c_ones_f[0:1, :], st_ss[0:1, :],
                                            start=True, stop=True),
               waits=[("vv", vv_r0)] + p.bank_w(0), inc=None)
        p.step("tensor", lambda t: t.matmul(pbf(1), c_ones_f[0:1, :], st_mu0[0:1, :],
                                            start=True, stop=True),
               waits=p.bank_w(1), inc="pe")
        pe_bc0 = p.val("pe")

        def bcast_tile(bank):
            return pb[bank][:, 0:TT].unsqueeze(1).broadcast_to([128, NC_D, TT])

        p.step("vector", lambda v: v.tensor_tensor(
            tile3(xn, TT), tile3(xin, TT), bcast_tile(0), OP.mult),
            waits=[("pe", pe_bc0)] + p.buf_w("xn"), inc="vv")
        p.step("vector", lambda v: v.tensor_tensor(
            tile3(xn, TT), tile3(xn, TT), bcast_tile(1), OP.subtract), inc="vv")
        vv_xn = p.val("vv")
        p.bank_consumed(0, "vv")
        p.bank_consumed(1, "vv")
        p.dma("sync", (lambda s, tk=tk: s.dma_start(
            out=dram3(xn32_dram)[:, :, tk:tk + TT], in_=tile3(xn, TT))),
            waits=[("vv", vv_xn)], inc="dxn")
        st["dxn_t"][t_i] = p.val("dxn")

        for cc in range(NCC):
            for ic in range(NC_D):
                p.step("tensor", (lambda t, cc=cc, ic=ic: t.matmul(
                    pb[7][:, cc * 8:(cc + 1) * 8],
                    tile3(xn, TT)[:, ic, cc * CC:(cc + 1) * CC],
                    w3(wg_sb, 8)[:, ic, :],
                    start=(ic == 0), stop=(ic == NC_D - 1))),
                    waits=([("vv", vv_xn)] + p.bank_w(7)) if (cc == 0 and ic == 0) else [],
                    inc="pe" if ic == NC_D - 1 else None)
        pe_log = p.val("pe")

        tb = t_i * NCC

        def g48(t, col0=0):
            return t[:, col0:col0 + 32].rearrange("p (c e) -> p c e", e=8)

        def gb4(t, col0):
            return t[:, col0:col0 + 4].unsqueeze(2).broadcast_to([128, 4, 8])

        rt1 = ssq4_q
        rt2 = ssq4_k
        p.step("vector", (lambda v: v.tensor_copy(
            r_l[:, tb * 8:tb * 8 + 32], pb[7][:, 0:32])),
            waits=[("pe", pe_log)], inc="vv")
        p.step("vector", (lambda v: v.tensor_reduce(
            r_negm[:, tb:tb + 4], g48(r_l, tb * 8), AX.X, OP.max, negate=True)), inc="vv")
        p.step("vector", (lambda v: v.tensor_tensor(
            g48(rt1), g48(r_l, tb * 8), gb4(r_negm, tb), OP.add)),
            waits=[("sc", st["sc_lnexp_prev"])], inc="vv")
        p.step("vector", lambda v: v.tensor_scalar(
            rt1[:, :], rt1[:, :], 0.0, None, OP.is_ge), inc="vv")
        p.step("vector", lambda v: v.scalar_tensor_tensor(
            rt2[:, :], rt1[:, :], -999.0, c_iota999[:, :], OP.mult, OP.add), inc="vv")
        p.step("vector", lambda v: v.tensor_reduce(
            rt3[:, 0:4], g48(rt2), AX.X, OP.min), inc="vv")
        p.step("vector", (lambda v: v.tensor_tensor(
            g48(r_ohf, tb * 8), g48(rt2), gb4(rt3, 0), OP.is_equal)), inc="vv")
        vv_ohf = p.val("vv")
        p.step("tensor", (lambda t: t.matmul(
            pb[6][:, 0:32], c_iu_b[:, :], r_ohf[:, tb * 8:tb * 8 + 32],
            start=True, stop=True)),
            waits=[("vv", vv_ohf)] + p.bank_w(6), inc=None)
        p.step("tensor", (lambda t: t.matmul(
            pb[6][:, 32:64], c_ones_b[:, :], r_ohf[:, tb * 8:tb * 8 + 32],
            start=True, stop=True)), inc="pe")
        pe_cnt = p.val("pe")
        p.step("vector", lambda v: v.tensor_copy(rt1[:, :], pb[6][:, 0:32]),
               waits=[("pe", pe_cnt)], inc="vv")
        p.step("vector", lambda v: v.tensor_copy(rt2[:, :], pb[6][:, 32:64]), inc="vv")
        p.bank_consumed(6, "vv")
        p.step("vector", lambda v: v.tensor_copy(rt3[:, 0:8], run_rep[:, :]), inc="vv")
        p.step("vector", lambda v: v.tensor_add(rt3[:, 8:16], rt3[:, 0:8], rt2[:, 0:8]),
               inc="vv")
        p.step("vector", lambda v: v.tensor_add(rt3[:, 16:24], rt3[:, 8:16], rt2[:, 8:16]),
               inc="vv")
        p.step("vector", lambda v: v.tensor_add(rt3[:, 24:32], rt3[:, 16:24],
                                                rt2[:, 16:24]), inc="vv")
        p.step("vector", lambda v: v.tensor_add(run_rep[:, :], rt3[:, 24:32],
                                                rt2[:, 24:32]), inc="vv")
        p.step("vector", lambda v: v.tensor_add(rt1[:, :], rt1[:, :], rt3[:, :]), inc="vv")
        p.step("vector", (lambda v: v.tensor_tensor(
            g48(rt1), g48(rt1), g48(r_ohf, tb * 8), OP.mult)), inc="vv")
        p.step("vector", (lambda v: v.tensor_reduce(
            r_c[:, tb:tb + 4], g48(rt1), AX.X, OP.add)), inc="vv")

        sc_xnsq = 0
        for hf in range(2):
            p.step("scalar", (lambda sc, hf=hf: sc.activation(
                tile3(xsq, TT), tile3(xn, TT)[:, hf * 4:hf * 4 + 4, :], AF.Square)),
                waits=([("vv", vv_xn), ("pe", pe_ss0)] if hf == 0
                       else [("pe", p.val("pe"))]), inc="sc")
            sc_xnsq = p.val("sc")
            for ic in range(4):
                p.step("tensor", (lambda t, ic=ic, hf=hf: t.matmul(
                    pb[7][0:1, 0:TT], c_onescol_b[:, 0:1], tile3(xsq, TT)[:, ic, :],
                    start=(ic == 0 and hf == 0), stop=(ic == 3 and hf == 1))),
                    waits=([("sc", sc_xnsq)] + (p.bank_w(7) if hf == 0 else []))
                    if ic == 0 else [],
                    inc="pe" if ic == 3 else None)
        pe_ss1 = p.val("pe")
        p.step("vector", lambda v: v.tensor_scalar(
            st_ss[0:1, :], pb[7][0:1, 0:TT], 1.0 / D, None, OP.mult),
            waits=[("pe", pe_ss1)], inc="vv")
        vv_var1 = p.val("vv")
        p.bank_consumed(7, "vv")
        p.step("scalar", lambda sc: sc.activation(st_ss[0:1, :], st_ss[0:1, :], AF.Ln,
                                                  bias=f32(EPS_LN)),
               waits=[("vv", vv_var1)], inc="sc")
        p.step("scalar", lambda sc: sc.activation(st_ss[0:1, :], st_ss[0:1, :], AF.Exp,
                                                  scale=f32(-0.5)), inc="sc")
        sc_r1 = p.val("sc")
        p.step("tensor", lambda t: t.matmul(pbf(0), c_ones_f[0:1, :], st_ss[0:1, :],
                                            start=True, stop=True),
               waits=[("sc", sc_r1)] + p.bank_w(0), inc="pe")
        pe_bc1 = p.val("pe")
        p.step("vector", lambda v: v.tensor_tensor(
            tile3(xin, TT), tile3(xn, TT), bcast_tile(0), OP.mult),
            waits=[("pe", pe_bc1), ("sc", sc_xsq)], inc="vv")
        vv_u1 = p.val("vv")
        p.bank_consumed(0, "vv")
        for ic in range(NC_D):
            p.step("scalar", (lambda sc, ic=ic: sc.activation(
                tile3(n_bf, TT)[:, ic, :], tile3(xin, TT)[:, ic, :], AF.Identity,
                bias=ln1b[:, ic:ic + 1], scale=ln1w[:, ic:ic + 1])),
                waits=([("vv", vv_u1)] + p.buf_w("n_bf")) if ic == 0 else [],
                inc="sc" if ic == NC_D - 1 else None)
        sc_n = p.val("sc")
        st["sc_n"][t_i] = sc_n
        p.buf_set("xin", ("sc", sc_n))
        p.buf_set("xn", ("dxn", st["dxn_t"][t_i]), ("pe", pe_ss1), ("pe", pe_log))
        p.buf_set("xsq", ("pe", pe_ss1))

    def emit_proj(t_i, cc):
        for ic in range(NC_D):
            w_first = ([("sc", st["sc_n"][t_i]), ("dwq", 16), ("dwk", 16), ("dwv", 16),
                        ("dwab", 16)] + p.bank_w(0, 1, 2, 3, 4, 5, 7)) if ic == 0 else []
            p.step("tensor", (lambda t, ic=ic, cc=cc: t.matmul(
                pb[0][:, :], tile3(n_bf, TT)[:, ic, cc * CC:(cc + 1) * CC],
                w3(wq_sb, D)[:, ic, 0:512],
                start=(ic == 0), stop=(ic == NC_D - 1))), waits=w_first, inc=None)
            p.step("tensor", (lambda t, ic=ic, cc=cc: t.matmul(
                pb[1][:, :], tile3(n_bf, TT)[:, ic, cc * CC:(cc + 1) * CC],
                w3(wq_sb, D)[:, ic, 512:1024],
                start=(ic == 0), stop=(ic == NC_D - 1))), inc=None)
            p.step("tensor", (lambda t, ic=ic, cc=cc: t.matmul(
                pb[2][:, :], tile3(n_bf, TT)[:, ic, cc * CC:(cc + 1) * CC],
                w3(wk_sb, D)[:, ic, 0:512],
                start=(ic == 0), stop=(ic == NC_D - 1))), inc=None)
            p.step("tensor", (lambda t, ic=ic, cc=cc: t.matmul(
                pb[3][:, :], tile3(n_bf, TT)[:, ic, cc * CC:(cc + 1) * CC],
                w3(wk_sb, D)[:, ic, 512:1024],
                start=(ic == 0), stop=(ic == NC_D - 1))), inc=None)
            p.step("tensor", (lambda t, ic=ic, cc=cc: t.matmul(
                pb[4][:, :], tile3(n_bf, TT)[:, ic, cc * CC:(cc + 1) * CC],
                w3(wv_sb, D)[:, ic, 0:512],
                start=(ic == 0), stop=(ic == NC_D - 1))), inc=None)
            p.step("tensor", (lambda t, ic=ic, cc=cc: t.matmul(
                pb[5][:, :], tile3(n_bf, TT)[:, ic, cc * CC:(cc + 1) * CC],
                w3(wv_sb, D)[:, ic, 512:1024],
                start=(ic == 0), stop=(ic == NC_D - 1))), inc=None)
            p.step("tensor", (lambda t, ic=ic, cc=cc: t.matmul(
                pb[7][:, 0:16], tile3(n_bf, TT)[:, ic, cc * CC:(cc + 1) * CC],
                w3(wab_sb, 16)[:, ic, :],
                start=(ic == 0), stop=(ic == NC_D - 1))),
                inc="pe" if ic == NC_D - 1 else None)
        pe_proj = p.val("pe")
        st["pe_proj"][(t_i, cc)] = pe_proj
        if cc == NCC - 1:
            p.buf_set("n_bf", ("pe", pe_proj))

        for dsti, (dst, bank0) in enumerate([(q_b4, 0), (k_b4, 2), (v_b4, 4)]):
            for half in range(2):
                p.step("scalar", (lambda sc, dst=dst, bank0=bank0, half=half, cc=cc:
                       sc.activation(
                           dst[:, cc * 1024 + half * 512:cc * 1024 + (half + 1) * 512],
                           pb[bank0 + half][:, :], AF.Silu)),
                       waits=([("pe", pe_proj)] + p.buf_w(f"qs{cc}"))
                       if (dsti == 0 and half == 0) else [],
                       inc="sc")
        sc_silu = p.val("sc")
        st["sc_silu"][(t_i, cc)] = sc_silu
        for b in (0, 1, 2, 3, 4, 5):
            p.bank_consumed(b, "sc")

        p.step("vector", (lambda v, cc=cc: v.tensor_copy(
            zab4[:, cc * 16:(cc + 1) * 16], pb[7][:, 0:16])),
            waits=[("pe", pe_proj)], inc="vv")
        st["vv_zab"][(t_i, cc)] = p.val("vv")
        p.bank_consumed(7, "vv")

        p.step("gpsimd", (lambda g, cc=cc: g.tensor_mul(
            sq_sc[:, :], q_b4[:, cc * 1024:(cc + 1) * 1024],
            q_b4[:, cc * 1024:(cc + 1) * 1024])),
            waits=[("sc", sc_silu), ("vv", st["vv_ssq_prev"])], inc="gp")
        p.step("vector", (lambda v, cc=cc: v.tensor_reduce(
            s4(ssq4_q, cc), h3(sq_sc), AX.X, OP.add)),
            waits=[("gp", p.val("gp")), ("sc", st["sc_lnexp_prev"])], inc="vv")
        p.step("gpsimd", (lambda g, cc=cc: g.tensor_mul(
            sq_sc[:, :], k_b4[:, cc * 1024:(cc + 1) * 1024],
            k_b4[:, cc * 1024:(cc + 1) * 1024])),
            waits=[("vv", p.val("vv"))], inc="gp")
        p.step("vector", (lambda v, cc=cc: v.tensor_reduce(
            s4(ssq4_k, cc), h3(sq_sc), AX.X, OP.add)),
            waits=[("gp", p.val("gp"))], inc="vv")
        st["vv_ssq"][(t_i, cc)] = p.val("vv")
        st["vv_ssq_prev"] = p.val("vv")

    def emit_lnexp(t_i, ccs):
        for cc in ccs:
            p.step("scalar", (lambda sc, cc=cc: sc.activation(
                zab4[:, cc * 16:(cc + 1) * 16], zab4[:, cc * 16:(cc + 1) * 16],
                AF.Exp, scale=f32(-1.0))),
                waits=[("vv", st["vv_zab"][(t_i, cc)])], inc="sc")
            sc_abe = p.val("sc")
            p.step("vector", (lambda v, cc=cc: v.tensor_scalar(
                zab4[:, cc * 16:(cc + 1) * 16], zab4[:, cc * 16:(cc + 1) * 16],
                1.0, None, OP.add)),
                waits=[("sc", sc_abe)], inc="vv")
            p.step("vector", (lambda v, cc=cc: v.reciprocal(
                s4(beta4, cc), zab4[:, cc * 16:cc * 16 + 8])), inc="vv")
            vv_beta = p.val("vv")
            p.step("scalar", (lambda sc, cc=cc: sc.activation(
                la8[:, :], zab4[:, cc * 16 + 8:cc * 16 + 16], AF.Ln)),
                waits=[("vv", vv_beta), ("pe", st.get("pe_cum_prev", 0))], inc="sc")
            sc_la = p.val("sc")
            p.step("tensor", lambda t: t.matmul(pb[6][:, 0:8], c_iu_f[:, :], la8[:, :],
                                                start=True, stop=True),
                   waits=[("sc", sc_la)] + p.bank_w(6), inc=None)
            p.step("tensor", lambda t: t.matmul(pb[6][:, 8:16], c_half_f[:, :], la8[:, :],
                                                start=True, stop=True), inc=None)
            p.step("tensor", lambda t: t.matmul(pb[6][:, 16:24], c_ones_f[:, :], la8[:, :],
                                                start=True, stop=True), inc="pe")
            pe_cum = p.val("pe")
            st["pe_cum_prev"] = pe_cum
            p.step("vector", lambda v: v.tensor_copy(cumz[:, :], pb[6][:, 0:24]),
                   waits=[("pe", pe_cum)], inc="vv")
            p.bank_consumed(6, "vv")
            p.step("vector", lambda v: v.tensor_sub(dz[:, :], cumz[:, 8:16], cumz[:, 0:8]),
                   inc="vv")
            p.step("vector", lambda v: v.tensor_sub(f18[:, :], cumz[:, 8:16], cumz[:, 16:24]),
                   inc="vv")
            vv_dz = p.val("vv")
            p.step("scalar", (lambda sc, cc=cc: sc.activation(s4(lam4, cc), dz[:, :], AF.Exp)),
                   waits=[("vv", vv_dz)], inc="sc")
            p.step("scalar", (lambda sc, cc=cc: sc.activation(s4(ginv4, cc), dz[:, :],
                                                              AF.Exp, scale=f32(-1.0))),
                   inc="sc")
            p.step("scalar", (lambda sc, cc=cc: sc.activation(s4(lamc4, cc), f18[:, :],
                                                              AF.Exp)), inc="sc")
            p.step("scalar", (lambda sc, cc=cc: sc.activation(s4(gref4, cc), cumz[:, 8:16],
                                                              AF.Exp, scale=f32(-1.0))),
                   inc="sc")
            p.step("scalar", (lambda sc, cc=cc: sc.activation(rsq_q[:, :], s4(ssq4_q, cc),
                                                              AF.Ln, bias=f32(EPS_L2))),
                   waits=[("vv", st["vv_ssq"][(t_i, cc)]),
                          ("vv", st["vv_folds_prev"])], inc="sc")
            p.step("scalar", lambda sc: sc.activation(rsq_q[:, :], rsq_q[:, :], AF.Exp,
                                                      scale=f32(-0.5)), inc="sc")
            p.step("scalar", (lambda sc, cc=cc: sc.activation(rsq_k[:, :], s4(ssq4_k, cc),
                                                              AF.Ln, bias=f32(EPS_L2))),
                   inc="sc")
            p.step("scalar", lambda sc: sc.activation(rsq_k[:, :], rsq_k[:, :], AF.Exp,
                                                      scale=f32(-0.5)), inc="sc")
            sc_lx = p.val("sc")
            p.step("vector", (lambda v, cc=cc: v.tensor_mul(
                s4(fq4, cc), rsq_q[:, :], s4(lam4, cc))),
                waits=[("sc", sc_lx)], inc="vv")
            p.step("vector", (lambda v, cc=cc: v.tensor_copy(s4(frk4, cc), rsq_k[:, :])),
                   inc="vv")
            p.step("vector", (lambda v, cc=cc: v.tensor_mul(
                s4(fkb4, cc), rsq_k[:, :], s4(beta4, cc))), inc="vv")
            st["vv_folds_prev"] = p.val("vv")
            st["sc_lnexp"][(t_i, cc)] = p.val("sc")
        st["sc_lnexp_prev"] = p.val("sc")

    def emit_scalings(t_i, cc):
        par = cc % 2
        ccg = t_i * NCC + cc
        g_pe = st["pe_x0"].get(ccg - 1, 0)
        p.step("vector", (lambda v, cc=cc: v.tensor_tensor(
            h3(qt_bf), h4(q_b4, cc), bcast8s(fq4, cc), OP.mult)),
            waits=[("sc", st["sc_lnexp"][(t_i, cc)]), ("pe", g_pe)], inc="vv")
        p.step("vector", (lambda v, cc=cc: v.tensor_tensor(
            h3(kn_bf), h4(k_b4, cc), bcast8s(frk4, cc), OP.mult)), inc="vv")
        p.step("vector", (lambda v, cc=cc, par=par: v.tensor_tensor(
            h3(kb_bf[par]), h4(k_b4, cc), bcast8s(fkb4, cc), OP.mult)), inc="vv")
        p.step("vector", (lambda v, cc=cc: v.tensor_tensor(
            h3(vt_bf), h4(v_b4, cc), bcast8s(ginv4, cc), OP.mult)), inc="vv")
        st["vv_scal"][(t_i, cc)] = p.val("vv")
        p.buf_set(f"qs{cc}", ("vv", p.val("vv")))

    def emit_delta(t_i, cc):
        par = cc % 2
        ccg = t_i * NCC + cc
        vv_tm = st["vv_scal"][(t_i, cc)]
        p.step("vector", (lambda v, cc=cc: v.tensor_tensor(
            h3(sig_bf), h3(s_st), bcast8s(gref4, cc), OP.mult)),
            waits=[("gp", gp_szero), ("pe", st["pe_upd"].get(ccg - 1, 0))], inc="vv")
        vv_sig = p.val("vv")
        for srcb, bank in [(kn_bf, 0), (kb_bf[par], 1), (qt_bf, 2)]:
            for h in range(H):
                p.step("tensor", (lambda t, srcb=srcb, bank=bank, h=h: t.transpose(
                    pbb(bank)[:, h * DH:(h + 1) * DH], h3(srcb)[:, h, :], c_id_b[:, :])),
                    waits=([("vv", vv_tm)] + p.bank_w(bank)) if h == 0 else [],
                    inc="pe" if h == H - 1 else None)
        pe_tp = p.val("pe")
        st["pe_tp"][ccg] = pe_tp
        p.step("scalar", lambda sc: sc.copy(kfm[:, :], pbb(0)[:, :]),
               waits=[("pe", pe_tp)], inc="sc")
        p.step("scalar", lambda sc: sc.activation(nkfm[:, :], pbb(0)[:, :], AF.Identity,
                                                  scale=f32(-1.0)), inc="sc")
        sc_tpe = p.val("sc")
        p.step("vector", lambda v: v.tensor_copy(kbfm[:, :], pbb(1)[:, :]),
               waits=[("pe", pe_tp)], inc="vv")
        p.step("vector", lambda v: v.tensor_copy(qfm[:, :], pbb(2)[:, :]), inc="vv")
        vv_tpe = p.val("vv")
        p.bank_consumed(0, "sc")
        p.bank_consumed(1, "vv")
        p.bank_consumed(2, "vv")

        for h in range(H):
            p.step("tensor", (lambda t, h=h: t.matmul(
                pg(2 + h // 4)[:, h % 4, :], h3(kfm)[:, h, :], h3(kbfm)[:, h, :],
                start=True, stop=True)),
                waits=([("sc", sc_tpe), ("vv", vv_tpe)] + p.bank_w(2, 3)) if h == 0 else [],
                inc="pe" if h == H - 1 else None)
        for h in range(H):
            p.step("tensor", (lambda t, h=h: t.matmul(
                pg(4 + h // 4)[:, h % 4, :], h3(kbfm)[:, h, :], h3(kfm)[:, h, :],
                start=True, stop=True)),
                waits=p.bank_w(4, 5) if h == 0 else [],
                inc="pe" if h == H - 1 else None)
        for h in range(H):
            p.step("tensor", (lambda t, h=h: t.matmul(
                pg(6 + h // 4)[:, h % 4, :], h3(kbfm)[:, h, :], h3(qfm)[:, h, :],
                start=True, stop=True)),
                waits=p.bank_w(6, 7) if h == 0 else [],
                inc="pe" if h == H - 1 else None)
        pe_afam = p.val("pe")

        def mask_evac(dst, b0, cmask):
            def fn(e, dst=dst, b0=b0, cmask=cmask):
                i1 = e.tensor_tensor(h3(dst)[:, 0:4, :], pg(b0), maskb(cmask), OP.mult)
                i2 = e.tensor_tensor(h3(dst)[:, 4:8, :], pg(b0 + 1), maskb(cmask), OP.mult)
                return [i1, i2]
            return fn
        p.step("vector", mask_evac(q_cur, 2, c_negsl_b), waits=[("pe", pe_afam)], inc="vv")
        p.step("vector", mask_evac(p_cur, 4, c_negsu_b), inc="vv")
        p.step("vector", mask_evac(mt_bf, 6, c_iu_b), inc="vv")
        vv_q = p.val("vv")
        sc_p = 0
        p.bank_consumed(2, "vv")
        p.bank_consumed(3, "vv")
        p.bank_consumed(4, "vv")
        p.bank_consumed(5, "vv")
        p.bank_consumed(6, "vv")
        p.bank_consumed(7, "vv")

        for h in range(H):
            p.step("tensor", (lambda t, h=h: t.matmul(
                pg(h // 4)[:, h % 4, :], h3(nkfm)[:, h, :], h3(sig_bf)[:, h, :],
                start=True, stop=False)),
                waits=([("vv", vv_sig), ("sc", sc_tpe)] + p.bank_w(0, 1)) if h == 0 else [],
                inc=None)
            p.step("tensor", (lambda t, h=h: t.matmul(
                pg(h // 4)[:, h % 4, :], c_id_b[:, :], h3(vt_bf)[:, h, :],
                start=False, stop=True)),
                inc="pe" if h == H - 1 else None)
        pe_x0 = p.val("pe")
        st["pe_x0"][ccg] = pe_x0

        def x0_evac(v):
            i1 = v.tensor_copy(h3(x_a)[:, 0:4, :], pg(0))
            i2 = v.tensor_copy(h3(x_a)[:, 4:8, :], pg(1))
            return [i1, i2]
        p.step("vector", x0_evac, waits=[("pe", pe_x0)], inc="vv")
        vv_x = p.val("vv")
        p.bank_consumed(0, "vv")
        p.bank_consumed(1, "vv")

        xcur, xnxt = x_a, x_b
        pcur, pnxt = p_cur, p_nxt
        qcur, qnxt = q_cur, q_nxt
        vvA = vv_x
        vvB = vv_x
        scA = 0
        scB = 0
        scQA = 0
        vvQB = vv_q
        for lvl in range(LVLS):
            for g, (bank, h0) in enumerate([(0, 0), (1, 4)]):
                w0 = ([("vv", (vvA if g == 0 else vvB)), ("vv", vv_q),
                       ("sc", (scA if g == 0 else scB))] + p.bank_w(bank))
                for j in range(4):
                    p.step("tensor", (lambda t, bank=bank, j=j, h0=h0, pcur=pcur,
                           xcur=xcur: t.matmul(
                               pg(bank)[:, j, :], h3(pcur)[:, h0 + j, :],
                               h3(xcur)[:, h0 + j, :], start=True, stop=True)),
                        waits=w0 if j == 0 else [],
                        inc="pe" if j == 3 else None)
                pe_ap = p.val("pe")

                def xadd(v, bank=bank, h0=h0, xcur=xcur, xnxt=xnxt):
                    return v.tensor_add(h3(xnxt)[:, h0:h0 + 4, :], pg(bank),
                                        h3(xcur)[:, h0:h0 + 4, :])
                p.step("vector", xadd, waits=[("pe", pe_ap)], inc="vv")
                if g == 0:
                    vvA = p.val("vv")
                else:
                    vvB = p.val("vv")
                p.bank_consumed(bank, "vv")
            if lvl < LVLS - 1:
                for g, (bank, h0) in enumerate([(2, 0), (3, 4)]):
                    w0 = ([("sc", (scA if g == 0 else scB)),
                           ("sc", scQA if g == 0 else 0),
                           ("vv", vvQB if g == 1 else 0), ("vv", vv_q)] +
                          p.bank_w(bank))
                    for j in range(4):
                        p.step("tensor", (lambda t, bank=bank, j=j, h0=h0, qcur=qcur,
                               pcur=pcur: t.matmul(
                                   pg(bank)[:, j, :], h3(qcur)[:, h0 + j, :],
                                   h3(pcur)[:, h0 + j, :], start=True, stop=True)),
                            waits=w0 if j == 0 else [],
                            inc="pe" if j == 3 else None)
                    pe_p = p.val("pe")

                    def pevac(sc, bank=bank, h0=h0, pnxt=pnxt):
                        return sc.copy(h3(pnxt)[:, h0:h0 + 4, :], pg(bank))
                    p.step("scalar", pevac, waits=[("pe", pe_p)], inc="sc")
                    if g == 0:
                        scA = p.val("sc")
                    else:
                        scB = p.val("sc")
                    p.bank_consumed(bank, "sc")
            if lvl < LVLS - 2:
                for g, (bank, h0) in enumerate([(4, 0), (5, 4)]):
                    for j in range(4):
                        p.step("tensor", (lambda t, bank=bank, j=j, h0=h0, qcur=qcur,
                               pcur=pcur: t.matmul(
                                   pg(bank)[:, j, :], h3(pcur)[:, h0 + j, :],
                                   h3(qcur)[:, h0 + j, :], start=True, stop=True)),
                            waits=p.bank_w(bank) if j == 0 else [],
                            inc="pe" if j == 3 else None)
                    pe_q = p.val("pe")
                    if g == 0:
                        def qevacA(sc, h0=h0, qnxt=qnxt):
                            return sc.copy(h3(qnxt)[:, h0:h0 + 4, :], pg(4))
                        p.step("scalar", qevacA, waits=[("pe", pe_q)], inc="sc")
                        scQA = p.val("sc")
                        p.bank_consumed(4, "sc")
                    else:
                        def qevacB(v, h0=h0, qnxt=qnxt):
                            return v.tensor_copy(h3(qnxt)[:, h0:h0 + 4, :], pg(5))
                        p.step("vector", qevacB, waits=[("pe", pe_q)], inc="vv")
                        vvQB = p.val("vv")
                        p.bank_consumed(5, "vv")
            xcur, xnxt = xnxt, xcur
            if lvl < LVLS - 1:
                pcur, pnxt = pnxt, pcur
            if lvl < LVLS - 2:
                qcur, qnxt = qnxt, qcur
        rbar = xcur
        vv_rbar = max(vvA, vvB)

        for h in range(H):
            p.step("tensor", (lambda t, h=h: t.matmul(
                pg(2 + h // 4)[:, h % 4, :], h3(sig_bf)[:, h, :], h3(qfm)[:, h, :],
                start=True, stop=False)),
                waits=([("vv", vv_rbar)] + p.bank_w(2, 3)) if h == 0 else [],
                inc=None)
            p.step("tensor", (lambda t, h=h, rbar=rbar: t.matmul(
                pg(2 + h // 4)[:, h % 4, :], h3(rbar)[:, h, :], h3(mt_bf)[:, h, :],
                start=False, stop=True)),
                inc="pe" if h == H - 1 else None)
        pe_o = p.val("pe")
        for h in range(H):
            p.step("tensor", (lambda t, h=h: t.matmul(
                pg(4 + h // 4)[:, h % 4, :], c_id_b[:, :], h3(sig_bf)[:, h, :],
                start=True, stop=False)),
                waits=p.bank_w(4, 5) if h == 0 else [], inc=None)
            p.step("tensor", (lambda t, h=h, rbar=rbar, par=par: t.matmul(
                pg(4 + h // 4)[:, h % 4, :], h3(kb_bf[par])[:, h, :], h3(rbar)[:, h, :],
                start=False, stop=True)),
                inc="pe" if h == H - 1 else None)
        pe_upd = p.val("pe")
        st["pe_upd"][ccg] = pe_upd

        def oevac(sc, cc=cc):
            o3 = o_fm[:, :].rearrange("p (h n) -> p h n", n=TT)
            i1 = sc.copy(o3[:, 0:4, cc * CC:(cc + 1) * CC], pg(2))
            i2 = sc.copy(o3[:, 4:8, cc * CC:(cc + 1) * CC], pg(3))
            return [i1, i2]
        p.step("scalar", oevac,
               waits=[("pe", pe_o)] + (p.buf_w("o_fm") if cc == 0 else []), inc="sc")
        p.bank_consumed(2, "sc")
        p.bank_consumed(3, "sc")

        def sevac(v, cc=cc):
            i1 = v.tensor_tensor(h3(s_st)[:, 0:4, :], pg(4), bcast8s(lamc4, cc)[:, 0:4, :],
                                 OP.mult)
            i2 = v.tensor_tensor(h3(s_st)[:, 4:8, :], pg(5), bcast8s(lamc4, cc)[:, 4:8, :],
                                 OP.mult)
            return [i1, i2]
        p.step("vector", sevac, waits=[("pe", pe_upd)], inc="vv")
        p.bank_consumed(4, "vv")
        p.bank_consumed(5, "vv")

    def emit_wo(t_i):
        tk = t_i * TT
        sc_oall = p.val("sc")
        for half in range(2):
            for oc in range(4):
                for hv in range(NC_D):
                    p.step("tensor", (lambda t, half=half, oc=oc, hv=hv: t.matmul(
                        pb[oc][:, :],
                        w3(wo_sb, D)[:, hv, (half * 4 + oc) * 128:(half * 4 + oc + 1) * 128],
                        o_fm[:, :].rearrange("p (h n) -> p h n", n=TT)[:, hv, :],
                        start=(hv == 0), stop=(hv == NC_D - 1))),
                        waits=([("sc", sc_oall), ("dwo", 16)] + p.bank_w(0, 1, 2, 3))
                        if (hv == 0 and oc == 0) else [],
                        inc="pe" if hv == NC_D - 1 else None)
            pe_woh = p.val("pe")
            vv_y1 = 0
            for q in range(2):
                c0 = half * 4 + 2 * q
                p.dma("sync", (lambda s, tk=tk, c0=c0: s.dma_start(
                    out=tile3(xnwo, TT),
                    in_=dram3(xn32_dram)[:, c0:c0 + 2, tk:tk + TT])),
                    waits=[("dxn", st["dxn_t"][t_i])] + p.buf_w("xnwo"), inc="dxw")
                ds_xw = p.val("dxw")

                def y1add(v, q=q):
                    i1 = v.tensor_add(tile3(y1h, TT)[:, 2 * q, :], pb[2 * q][:, :],
                                      tile3(xnwo, TT)[:, 0, :])
                    i2 = v.tensor_add(tile3(y1h, TT)[:, 2 * q + 1, :], pb[2 * q + 1][:, :],
                                      tile3(xnwo, TT)[:, 1, :])
                    return [i1, i2]
                p.step("vector", y1add,
                       waits=[("pe", pe_woh), ("dxw", ds_xw)] + p.buf_w("y1h"), inc="vv")
                vv_y1 = p.val("vv")
                p.buf_set("xnwo", ("vv", vv_y1))
                p.bank_consumed(2 * q, "vv")
                p.bank_consumed(2 * q + 1, "vv")
            p.step("scalar", (lambda sc: sc.activation(
                xsq[:, 0:4 * TT], y1h[:, :], AF.Square)),
                waits=[("vv", vv_y1)] + p.buf_w("xsq"), inc="sc")
            sc_sq = p.val("sc")
            for ic in range(4):
                p.step("tensor", (lambda t, ic=ic, half=half: t.matmul(
                    pb[7][0:1, 0:TT], c_onescol_b[:, 0:1], tile3(xsq, TT)[:, ic, :],
                    start=(ic == 0 and half == 0),
                    stop=(ic == 3 and half == 1))),
                    waits=([("sc", sc_sq)] + (p.bank_w(7) if half == 0 else []))
                    if ic == 0 else [],
                    inc="pe" if ic == 3 else None)
            pe_ss2 = p.val("pe")
            p.dma("sync", (lambda s, tk=tk, half=half: s.dma_start(
                out=dram3(y1_dram)[:, half * 4:(half + 1) * 4, tk:tk + TT],
                in_=tile3(y1h, TT))),
                waits=[("vv", vv_y1)], inc="dy1")
            p.buf_set("y1h", ("dy1", p.val("dy1")), ("sc", sc_sq))
            p.buf_set("xsq", ("pe", pe_ss2))
        for hv in range(NC_D):
            p.step("tensor", (lambda t, hv=hv: t.matmul(
                pb[6][0:1, 0:TT], wo1_bf[:, hv:hv + 1],
                o_fm[:, :].rearrange("p (h n) -> p h n", n=TT)[:, hv, :],
                start=(hv == 0), stop=(hv == NC_D - 1))),
                waits=([("vv", vv_wo1), ("sc", sc_oall)] + p.bank_w(6)) if hv == 0 else [],
                inc="pe" if hv == NC_D - 1 else None)
        st["pe_mu2_t"] = p.val("pe")
        st["pe_ss2_t"] = pe_ss2
        p.buf_set("o_fm", ("pe", p.val("pe")))

    def emit_ln2fin(t_i):
        tk = t_i * TT
        p.step("vector", lambda v: v.tensor_scalar(
            st_ss[0:1, :], pb[6][0:1, 0:TT], 1.0 / D, None, OP.mult),
            waits=[("pe", st["pe_mu2_t"]), ("dst2", st["dst2_prev"])], inc="vv")
        p.step("vector", lambda v: v.tensor_scalar(
            st_mu0[0:1, :], pb[7][0:1, 0:TT], 1.0 / D, None, OP.mult),
            waits=[("pe", st["pe_ss2_t"])], inc="vv")
        p.bank_consumed(6, "vv")
        p.bank_consumed(7, "vv")
        p.step("vector", lambda v: v.tensor_mul(xnwo[0:1, 0:TT], st_ss[0:1, :],
                                                st_ss[0:1, :]), inc="vv")
        p.step("vector", lambda v: v.tensor_sub(st_mu0[0:1, :], st_mu0[0:1, :],
                                                xnwo[0:1, 0:TT]), inc="vv")
        vv_var2 = p.val("vv")
        p.buf_set("xnwo", ("vv", vv_var2))
        p.step("scalar", lambda sc: sc.activation(st_mu0[0:1, :], st_mu0[0:1, :], AF.Ln,
                                                  bias=f32(EPS_LN)),
               waits=[("vv", vv_var2)], inc="sc")
        p.step("scalar", lambda sc: sc.activation(st_mu0[0:1, :], st_mu0[0:1, :], AF.Exp,
                                                  scale=f32(-0.5)), inc="sc")
        sc_r2 = p.val("sc")
        p.step("vector", lambda v: v.tensor_mul(st_ss[0:1, :], st_ss[0:1, :],
                                                st_mu0[0:1, :]),
               waits=[("sc", sc_r2)], inc="vv")
        vv_st2 = p.val("vv")
        p.dma("sync", (lambda s, tk=tk: s.dma_start(out=st2_dram[0:1, tk:tk + TT],
                                                    in_=st_ss[0:1, :])),
              waits=[("vv", vv_st2)], inc="dst2")
        p.dma("sync", (lambda s, tk=tk: s.dma_start(out=st2_dram[1:2, tk:tk + TT],
                                                    in_=st_mu0[0:1, :])), inc="dst2")
        st["dst2_prev"] = p.val("dst2")

    emit_xload(0)
    emit_lnhead(0)
    for cc in range(NCC):
        emit_proj(0, cc)
    emit_lnexp(0, [0, 1, 2, 3])
    for t_i in range(NT):
        nxt = t_i + 1
        if nxt < NT:
            emit_xload(nxt)
        emit_scalings(t_i, 0)
        emit_delta(t_i, 0)
        if nxt < NT:
            emit_lnhead(nxt)
        emit_scalings(t_i, 1)
        emit_delta(t_i, 1)
        if nxt < NT:
            emit_proj(nxt, 0)
            emit_proj(nxt, 1)
        emit_scalings(t_i, 2)
        emit_delta(t_i, 2)
        if nxt < NT:
            emit_proj(nxt, 2)
        emit_scalings(t_i, 3)
        if nxt < NT:
            emit_proj(nxt, 3)
            emit_lnexp(nxt, [0, 1, 2])
        emit_delta(t_i, 3)
        emit_wo(t_i)
        emit_ln2fin(t_i)
        if nxt < NT:
            emit_lnexp(nxt, [3])
        if t_i == (NT - 2 if EARLY_CC else NT - 1):
            p.dma("gpsimd", lambda g: g.dma_start(out=cnt_in, in_=run_rep[0:1, 0:8]),
                  waits=[("vv", p.val("vv"))], inc="dcnt")
            p.step("gpsimd", lambda g: g.collective_compute(
                "AllGather", OP.bypass, replica_groups=[list(range(8))],
                ins=[cnt_in.opt()], outs=[cnt_all.opt()]),
                waits=[("dcnt", p.val("dcnt"))], inc="cc")
            p.dma("gpsimd", lambda g: g.dma_start(out=cnt_sb[0:1, 0:64],
                                                  in_=cnt_all[0:8, 0:8]),
                  waits=[("cc", p.val("cc"))], inc="dcall")

    dg_xn_all = p.val("dxn")
    pe_afinal = p.val("pe")
    vv_afinal = p.val("vv")
    sc_afinal = p.val("sc")
    alias_w = [("pe", pe_afinal), ("vv", vv_afinal), ("sc", sc_afinal),
               ("dst2", p.val("dst2")), ("dy1", p.val("dy1")),
               ("dxw", p.val("dxw")), ("dx", p.val("dx"))]

    for ccg in range(32):
        p.step("scalar", (lambda sc, ccg=ccg: sc.activation(
            r_l[:, ccg * 8:(ccg + 1) * 8], r_l[:, ccg * 8:(ccg + 1) * 8], AF.Exp,
            bias=r_negm[:, ccg:ccg + 1])),
            waits=[("vv", vv_afinal)] if ccg == 0 else [], inc="sc")
    sc_exp = p.val("sc")
    dg_call = p.val("dcall")
    p.step("vector", lambda v: v.tensor_tensor(
        base_row[0:1, 8:72].rearrange("p (e j) -> p e j", j=8),
        cnt_sb[0:1, 0:64].rearrange("p (j e) -> p j e", e=8).transpose([0, 2, 1]),
        coremask[0:1, 0:8].unsqueeze(1).broadcast_to([1, 8, 8]),
        OP.mult), waits=[("dcall", dg_call)], inc="vv")
    p.step("vector", lambda v: v.reduce_sum(
        base_row[0:1, 0:8], base_row[0:1, 8:72].rearrange("p (e j) -> p e j", j=8), AX.X),
        inc="vv")
    vv_base = p.val("vv")

    for gq in range(8):
        p.dma("gpsimd", (lambda g, gq=gq: g.dma_start(
            out=w3(w1_sb, FF)[:, :, gq * 512:(gq + 1) * 512],
            in_=dram3(d["w1"])[:, :, gq * 512:(gq + 1) * 512])),
            waits=alias_w if gq == 0 else [], inc="dw1" + "abcd"[gq // 2])
    for gq in range(8):
        p.dma("gpsimd", (lambda g, gq=gq: g.dma_start(
            out=w3(w2_sb, D)[:, :, gq * 128:(gq + 1) * 128],
            in_=dram3(d["w2"])[:, :, gq * 128:(gq + 1) * 128])),
            inc="dw2" + "ab"[gq // 4])
    for gh in range(2):
        p.dma("gpsimd", (lambda g, gh=gh: g.dma_start(
            out=w3(wexp_sb, D)[:, :, gh * 512:(gh + 1) * 512],
            in_=dram3(d["wexp"])[:, :, gh * 512:(gh + 1) * 512])),
            inc="dwex" + "ab"[gh])

    NSUB = T // SUB

    def emit_bload(su, first=False):
        par = su % 2
        j0 = su * SUB
        p.dma("sync", (lambda s, j0=j0, par=par: s.dma_start(
            out=tile3(by1[par], SUB), in_=dram3(y1_dram)[:, :, j0:j0 + SUB])),
            waits=(alias_w if first else []) + p.buf_w(f"by1{par}") +
                  [("dy1", p.val("dy1"))], inc="dsb")
        p.dma("gpsimd", (lambda g, j0=j0, par=par: g.dma_start(
            out=tile3(bxnb[par], SUB), in_=dram3(xn32_dram)[:, :, j0:j0 + SUB])),
            waits=p.buf_w(f"bxnb{par}") + [("dxn", dg_xn_all)], inc="dsg")
        p.dma("sync", (lambda s, j0=j0, par=par: s.dma_start(
            out=bst2[par][0:2, :], in_=st2_dram[0:2, j0:j0 + SUB])),
            waits=[("dst2", p.val("dst2"))] + p.buf_w(f"bst{par}"), inc="dsb")
        return (p.val("dsb"), p.val("dsg"))

    ds_b = [(0, 0)] * (NSUB + 1)
    ds_b[0] = emit_bload(0, first=True)
    ds_b[1] = emit_bload(1, first=True)
    p.dma("sync", lambda s: s.dma_start(out=base_dram, in_=base_row[0:1, 0:8]),
          waits=[("vv", vv_base)], inc="dbs")
    p.dma("sync", lambda s: s.dma_start(
        out=base_rep[:, :],
        in_=base_dram.unsqueeze(0).broadcast_to([128, 1, 8]).rearrange("p a e -> p (a e)")),
        waits=[("dbs", p.val("dbs"))], inc="dbb")
    dg_bb = p.val("dbb")
    for su in range(NSUB):
        par = su % 2
        j0 = su * SUB

        p.step("tensor", (lambda t, par=par: t.matmul(
            pb[6][:, 0:SUB], c_sel[0:2, 128:256], bst2[par][0:2, :],
            start=True, stop=True)),
            waits=[("dsb", ds_b[su][0])] + p.bank_w(6), inc=None)
        p.step("tensor", (lambda t, par=par: t.matmul(
            pb[6][:, SUB:2 * SUB], c_sel[0:2, 0:128], bst2[par][0:2, :],
            start=True, stop=True)), inc="pe")
        pe_bc2 = p.val("pe")
        p.step("vector", (lambda v, par=par: v.tensor_tensor(
            tile3(bout, SUB), tile3(by1[par], SUB),
            pb[6][:, 0:SUB].unsqueeze(1).broadcast_to([128, NC_D, SUB]), OP.mult)),
            waits=[("pe", pe_bc2)] + p.buf_w("bout"), inc="vv")
        p.step("vector", lambda v: v.tensor_tensor(
            tile3(bout, SUB), tile3(bout, SUB),
            pb[6][:, SUB:2 * SUB].unsqueeze(1).broadcast_to([128, NC_D, SUB]), OP.subtract),
            inc="vv")
        vv_u2 = p.val("vv")
        p.bank_consumed(6, "vv")
        for ic in range(NC_D):
            p.step("scalar", (lambda sc, ic=ic: sc.activation(
                tile3(bh2, SUB)[:, ic, :], tile3(bout, SUB)[:, ic, :], AF.Identity,
                bias=ln2b[:, ic:ic + 1], scale=ln2w[:, ic:ic + 1])),
                waits=([("vv", vv_u2)] + p.buf_w("bh2")) if ic == 0 else [],
                inc="sc" if ic == NC_D - 1 else None)
        sc_h2 = p.val("sc")
        p.buf_set(f"bst{par}", ("pe", pe_bc2))

        for g4 in range(8):
            bp = (g4 % 2) * 2
            for j in range(4):
                oc = g4 * 4 + j
                bank = bp + j // 2
                colr = (j % 2) * SUB
                for ic in range(NC_D):
                    p.step("tensor", (lambda t, oc=oc, bank=bank, colr=colr, ic=ic: t.matmul(
                        pb[bank][:, colr:colr + SUB],
                        w3(w1_sb, FF)[:, ic, oc * 128:(oc + 1) * 128],
                        tile3(bh2, SUB)[:, ic, :],
                        start=(ic == 0), stop=(ic == NC_D - 1))),
                        waits=([("sc", sc_h2), ("dw1" + "abcd"[g4 // 2], 32)] +
                               p.bank_w(bp, bp + 1))
                        if (ic == 0 and j == 0) else [],
                        inc="pe" if (ic == NC_D - 1 and j == 3) else None)
            pe_g = p.val("pe")
            for j in range(4):
                oc = g4 * 4 + j
                bank = bp + j // 2
                colr = (j % 2) * SUB
                p.step("scalar", (lambda sc, oc=oc, bank=bank, colr=colr: sc.activation(
                    tile3(bg, SUB)[:, oc, :], pb[bank][:, colr:colr + SUB], AF.Gelu,
                    bias=b1_sb[:, oc:oc + 1])),
                    waits=([("pe", pe_g)] + (p.buf_w("bg") if oc == 0 else [])), inc="sc")
            p.bank_consumed(bp, "sc")
            p.bank_consumed(bp + 1, "sc")
        sc_gelu = p.val("sc")

        for q in range(2):
            ccg = j0 // CC + q
            p.step("vector", (lambda v, ccg=ccg: v.reduce_sum(
                m1c[:, :], r_l[:, ccg * 8:(ccg + 1) * 8], AX.X)),
                waits=([("sc", sc_exp), ("dbb", dg_bb)] if (su == 0 and q == 0) else []),
                inc="vv")
            p.step("vector", (lambda v: v.reciprocal(t8a[:, 0:1], m1c[:, :])), inc="vv")
            p.step("vector", (lambda v, ccg=ccg: v.tensor_tensor(
                t8b[:, :], r_ohf[:, ccg * 8:(ccg + 1) * 8], base_rep[:, :], OP.mult)),
                inc="vv")
            p.step("vector", (lambda v: v.reduce_sum(m1c[:, :], t8b[:, :], AX.X)), inc="vv")
            p.step("vector", (lambda v, ccg=ccg: v.tensor_add(
                m1c[:, :], m1c[:, :], r_c[:, ccg:ccg + 1])), inc="vv")
            p.step("vector", (lambda v: v.tensor_scalar(
                m1c[:, :], m1c[:, :], 101.0, None, OP.is_lt)), inc="vv")
            p.step("vector", (lambda v, ccg=ccg: v.tensor_mul(
                r_g[:, ccg:ccg + 1], t8a[:, 0:1], m1c[:, :])), inc="vv")
            p.step("vector", (lambda v, ccg=ccg: v.tensor_scalar(
                bdiag[:, :], c_id_f[:, :], r_g[:, ccg:ccg + 1], None, OP.mult)),
                waits=[("pe", st.get("pe_gate_prev", 0) if q == 0 else p.val("pe"))],
                inc="vv")
            vv_dg2 = p.val("vv")
            p.step("tensor", (lambda t, q=q: t.matmul(
                pb[6][:, q * 128:(q + 1) * 128], c_ones_f[:, :], bdiag[:, :],
                start=True, stop=True)),
                waits=[("vv", vv_dg2)] + (p.bank_w(6) if q == 0 else []), inc="pe")
        pe_gb = p.val("pe")
        st["pe_gate_prev"] = pe_gb
        p.step("vector", lambda v: v.tensor_copy(bgb[:, :], pb[6][:, 0:SUB]),
               waits=[("pe", pe_gb)], inc="vv")
        vv_gb = p.val("vv")
        p.bank_consumed(6, "vv")


        for g4 in range(2):
            bp = 4 + g4 * 2
            for j in range(4):
                oc = g4 * 4 + j
                bank = bp + j // 2
                colr = (j % 2) * SUB
                for ic2 in range(NC_F):
                    p.step("tensor", (lambda t, oc=oc, bank=bank, colr=colr, ic2=ic2: t.matmul(
                        pb[bank][:, colr:colr + SUB],
                        w3(w2_sb, D)[:, ic2, oc * 128:(oc + 1) * 128],
                        tile3(bg, SUB)[:, ic2, :],
                        start=(ic2 == 0), stop=(ic2 == NC_F - 1))),
                        waits=([("sc", sc_gelu), ("dw2" + "ab"[g4], 64)] +
                               p.bank_w(bp, bp + 1))
                        if (ic2 == 0 and j == 0) else [],
                        inc="pe" if (ic2 == NC_F - 1 and j == 3) else None)
            pe_y2 = p.val("pe")
            for j in range(4):
                oc = g4 * 4 + j
                bank = bp + j // 2
                colr = (j % 2) * SUB
                p.step("vector", (lambda v, oc=oc, bank=bank, colr=colr, par=par:
                       v.scalar_tensor_tensor(
                           tile3(by1[par], SUB)[:, oc, :], pb[bank][:, colr:colr + SUB],
                           b2_sb[:, oc:oc + 1], tile3(by1[par], SUB)[:, oc, :],
                           OP.add, OP.add)),
                    waits=[("pe", pe_y2)], inc="vv")
            p.bank_consumed(bp, "vv")
            p.bank_consumed(bp + 1, "vv")
        vv_s1 = p.val("vv")

        for g4 in range(2):
            bp = g4 * 2
            for j in range(4):
                oc = g4 * 4 + j
                bank = bp + j // 2
                colr = (j % 2) * SUB
                for ic in range(NC_D):
                    p.step("tensor", (lambda t, oc=oc, bank=bank, colr=colr, ic=ic, par=par:
                           t.matmul(
                               pb[bank][:, colr:colr + SUB],
                               w3(wexp_sb, D)[:, ic, oc * 128:(oc + 1) * 128],
                               tile3(bxnb[par], SUB)[:, ic, :],
                               start=(ic == 0), stop=(ic == NC_D - 1))),
                        waits=([("dwex" + "ab"[g4], 16), ("dsg", ds_b[su][1])] +
                               p.bank_w(bp, bp + 1))
                        if (ic == 0 and j == 0) else [],
                        inc="pe" if (ic == NC_D - 1 and j == 3) else None)
            pe_m = p.val("pe")
            for j in range(4):
                oc = g4 * 4 + j
                bank = bp + j // 2
                colr = (j % 2) * SUB
                p.step("vector", (lambda v, oc=oc, bank=bank, colr=colr: v.scalar_tensor_tensor(
                    tile3(bout, SUB)[:, oc, :], pb[bank][:, colr:colr + SUB],
                    bexp_sb[:, oc:oc + 1], bgb[:, :], OP.add, OP.mult)),
                    waits=([("pe", pe_m), ("vv", vv_gb), ("sc", sc_h2)]
                           if (g4 == 0 and j == 0) else [("pe", pe_m)]),
                    inc="vv")
            p.bank_consumed(bp, "vv")
            p.bank_consumed(bp + 1, "vv")
        vv_m1 = p.val("vv")
        p.buf_set(f"bxnb{par}", ("pe", p.val("pe")))

        p.step("gpsimd", (lambda g, par=par: g.tensor_add(
            bout[:, :], bout[:, :], by1[par][:, :])),
            waits=[("vv", max(vv_m1, vv_s1))], inc="gp")
        gp_out = p.val("gp")
        p.buf_set(f"by1{par}", ("gp", gp_out))
        p.dma("gpsimd", (lambda g, j0=j0: g.dma_start(
            out=tile3(bout, SUB), in_=dram3(d["xt"])[:, :, j0:j0 + SUB],
            accum_op=OP.add)),
            waits=[("gp", gp_out)], inc="dxa")
        dg_xadd = p.val("dxa")
        p.dma("sync", (lambda s, j0=j0: s.dma_start(
            out=dram3(d["out"])[:, :, j0:j0 + SUB], in_=tile3(bout, SUB))),
            waits=[("dxa", dg_xadd)], inc="st")
        p.buf_set("bout", ("st", p.val("st")))
        p.buf_set("bh2", ("pe", p.val("pe")))
        p.buf_set("bg", ("pe", p.val("pe")))
        if su + 2 <= NSUB - 1:
            ds_b[su + 2] = emit_bload(su + 2)

    if DEBUG_DUMP:
        d["dbg_st2"] = nc.dram_tensor("dbg_st2", [2, T], F32, kind="ExternalOutput").ap()
        d["dbg_y1"] = nc.dram_tensor("dbg_y1", [D, T], F32, kind="ExternalOutput").ap()
        d["dbg_xn"] = nc.dram_tensor("dbg_xn", [D, T], F32, kind="ExternalOutput").ap()
        p.dma("sync", lambda s: s.dma_start(out=d["dbg_st2"], in_=st2_dram[:, :]),
              waits=[("st", p.val("st"))], inc="st")
        p.dma("sync", lambda s: s.dma_start(out=d["dbg_y1"], in_=y1_dram[:, :]), inc="st")
        p.dma("sync", lambda s: s.dma_start(out=d["dbg_xn"], in_=xn32_dram[:, :]), inc="st")
    p.step("sync", lambda s: None, waits=[("st", p.val("st"))])

    _BUILD_CACHE["dbg"] = dict(r_g=r_g, r_c=r_c, r_ohf=r_ohf, r_l=r_l, r_negm=r_negm,
                               run_rep=run_rep, base_rep=base_rep, cnt_sb=cnt_sb,
                               base_row=base_row)
    with nc.Block() as blk:
        _emit(p, sems, blk, hazard_scan=_hazard_scan, hazard_fix=_hazard_fix)
    es_sem.close()
    es_persist.close()
    return nc, p


def build_fixed():
    scan = {}
    nc1, p1 = build_program(_hazard_scan=scan)
    del nc1
    hz_writers = set()
    fix = {}
    for eng in ("vector", "gpsimd"):
        flagged = set()
        for (e, rsi), ws in scan.items():
            if e == eng:
                flagged.update(ws)
        if not flagged:
            continue
        for si in flagged:
            fn, waits, inc, inc_by = p1.steps[eng][si]
            if inc is None:
                hz_writers.add((eng, si))
        cnt = {}
        vals = {}
        for si, (fn, waits, inc, inc_by) in enumerate(p1.steps[eng]):
            if inc is None and (eng, si) in hz_writers:
                inc, inc_by = "hz_" + eng, 1
            if inc is not None:
                cnt[inc] = cnt.get(inc, 0) + inc_by
                vals[si] = (inc, cnt[inc])
            else:
                vals[si] = None
        for (e, rsi), ws in scan.items():
            if e != eng:
                continue
            for wsi in ws:
                wv = vals.get(wsi)
                if wv is not None:
                    fix.setdefault((eng, rsi), []).append(wv)
    nc2, _ = build_program(_hazard_fix=fix, _hz_writers=hz_writers)
    return nc2


def _consts():
    i = np.arange(128)
    iu = (i[:, None] <= i[None, :]).astype(np.float32)
    half = np.zeros((128, 128), np.float32)
    half[0:64, :] = 1.0
    ones = np.ones((128, 128), np.float32)
    ident = np.eye(128, dtype=np.float32)
    sl = (i[:, None] > i[None, :]).astype(np.float32)
    su = (i[:, None] < i[None, :]).astype(np.float32)
    return {
        "d_c_iu_f": iu, "d_c_half_f": half, "d_c_ones_f": ones, "d_c_id_f": ident,
        "d_c_negsl_b": -sl, "d_c_negsu_b": -su, "d_c_iu_b": iu, "d_c_id_b": ident,
        "d_c_ones_b": ones,
        "d_c_onescol": np.ones((128, 1), np.float32),
        "d_c_sel": np.concatenate([np.repeat(np.eye(2, dtype=np.float32)[:, 0:1], 128, 1),
                                   np.repeat(np.eye(2, dtype=np.float32)[:, 1:2], 128, 1)],
                                  axis=1),
        "d_c_iota999": np.broadcast_to(
            (np.arange(32) % 8 + 999.0).astype(np.float32), (128, 32)).copy(),
    }


def kernel(**inputs):
    if "prog" not in _BUILD_CACHE:
        _BUILD_CACHE["prog"] = build_fixed()
    nc = _BUILD_CACHE["prog"]

    x = np.asarray(inputs["L0_attn_x"])

    def col128(vec, ncol):
        return np.ascontiguousarray(np.asarray(vec).reshape(ncol, 128).T)

    shared = {
        "wq": np.asarray(inputs["L0_attn_Wq"]),
        "wk": np.asarray(inputs["L0_attn_Wk"]),
        "wv": np.asarray(inputs["L0_attn_Wv"]),
        "wo": np.asarray(inputs["L0_attn_Wo"]),
        "w1": np.asarray(inputs["L0_attn_W1"]),
        "w2": np.asarray(inputs["L0_attn_W2"]),
        "wexp": np.asarray(inputs["L0_ffn_w_expert"]),
        "wg": np.asarray(inputs["L0_ffn_w_gate"]),
        "wab": np.ascontiguousarray(np.concatenate(
            [np.asarray(inputs["L0_attn_Wb"]), np.asarray(inputs["L0_attn_Wa"])], axis=1)),
        "d_ln1w": col128(inputs["L0_attn_ln1_w"], 8),
        "d_ln1b": col128(inputs["L0_attn_ln1_b"], 8),
        "d_ln2w": col128(inputs["L0_attn_ln2_w"], 8),
        "d_ln2b": col128(inputs["L0_attn_ln2_b"], 8),
        "d_b1": col128(inputs["L0_attn_b1"], 32),
        "d_b2": col128(inputs["L0_attn_b2"], 8),
        "d_bexp": col128(inputs["L0_ffn_b_expert"], 8),
    }
    shared.update(_consts())

    in_maps = []
    for i in range(8):
        m = dict(shared)
        m["xt"] = np.ascontiguousarray(x[i].T)
        m["d_coremask"] = (np.arange(8) < i).astype(np.float32).reshape(1, 8)
        in_maps.append(m)

    res = run_bass_kernel_spmd(nc, in_maps, core_ids=list(range(8)))
    out = np.stack([np.ascontiguousarray(np.asarray(res.results[i]["out_t"]).T)
                    for i in range(8)])
    return out.astype(np.float32)


# version 11
# speedup vs baseline: 1.0782x; 1.0552x over previous
import numpy as np
from contextlib import ExitStack

import concourse.bass as bass
import concourse.mybir as mybir
from concourse.bass_utils import run_bass_kernel_spmd

F32 = mybir.dt.float32
BF16 = mybir.dt.bfloat16
AF = mybir.ActivationFunctionType
OP = mybir.AluOpType
AX = mybir.AxisListType

f32 = np.float32

T = 4096
D = 1024
H = 8
DH = 128
FF = 4096
NT = 8
TT = 512
NCC = 4
CC = 128
NC_D = 8
NC_F = 32
SUB = 256
EPS_LN = 1e-5
EPS_L2 = 1e-6
LVLS = 7

_BUILD_CACHE = {}
DEBUG_DUMP = False
EARLY_CC = False

ENGINES = ("sync", "gpsimd", "tensor", "vector", "scalar")


class Plan:
    def __init__(self, hz_writers=None):
        self.steps = {e: [] for e in ENGINES}
        self.cnt = {}
        self.hw = {e: {} for e in ENGINES}
        self.bank_guard = {i: [] for i in range(8)}
        self.buf_guard = {}
        self.hz_writers = hz_writers or set()

    def val(self, sem):
        return self.cnt.get(sem, 0)

    def step(self, eng, fn, waits=None, inc=None, inc_by=1):
        w = []
        for s, v in (waits or []):
            if v is None or v <= 0:
                continue
            if self.hw[eng].get(s, 0) >= v:
                continue
            self.hw[eng][s] = v
            w.append((s, v))
        if inc is None and (eng, len(self.steps[eng])) in self.hz_writers:
            inc, inc_by = "hz_" + eng, 1
        if inc is not None:
            self.cnt[inc] = self.cnt.get(inc, 0) + inc_by
        self.steps[eng].append((fn, w, inc, inc_by))

    def dma(self, eng, fn, waits=None, inc=None):
        self.step(eng, fn, waits=waits, inc=inc, inc_by=16)

    def bank_w(self, *banks):
        out = []
        for b in banks:
            out.extend(self.bank_guard[b])
        return out

    def bank_consumed(self, b, sem):
        self.bank_guard[b] = [(sem, self.val(sem))]

    def buf_w(self, *names):
        out = []
        for nm in names:
            out.extend(self.buf_guard.get(nm, []))
        return out

    def buf_set(self, nm, *guards):
        self.buf_guard[nm] = [(s, v) for (s, v) in guards if v]


def _arg_names(args):
    out = set()
    for a in args:
        for attr in ("memref", "name"):
            n = getattr(a, attr, None)
            if isinstance(n, str):
                out.add(n)
                break
        else:
            t = getattr(a, "tensor", None)
            n = getattr(t, "name", None)
            if isinstance(n, str):
                out.add(n)
    return out


def _emit(plan, sems, blk, hazard_scan=None, hazard_fix=None):
    def mk(eng_name):
        check = eng_name in ("vector", "gpsimd") and hazard_scan is not None
        def closure(eng):
            recent = []
            for si, (fn, waits, inc, inc_by) in enumerate(plan.steps[eng_name]):
                if hazard_fix is not None:
                    for s, v in hazard_fix.get((eng_name, si), []):
                        eng.wait_ge(sems[s], v)
                for s, v in waits:
                    eng.wait_ge(sems[s], v)
                inst = fn(eng)
                insts = inst if isinstance(inst, list) else ([inst] if inst is not None else [])
                if check:
                    for one in insts:
                        ii = getattr(one, "ins", None)
                        m = getattr(ii, "ins", None)
                        mo = getattr(ii, "outs", None)
                        rd = _arg_names(m or [])
                        wr = _arg_names(mo or [])
                        touched = rd | wr
                        for wsi, wouts in recent:
                            if wsi != si and touched & wouts:
                                hazard_scan.setdefault((eng_name, si), set()).add(wsi)
                        recent.append((si, wr))
                        recent = recent[-2:]
                if inc is not None and insts:
                    insts[-1].then_inc(sems[inc], inc_by)
        return closure
    blk.sync(mk("sync"))
    blk.gpsimd(mk("gpsimd"))
    blk.tensor(mk("tensor"))
    blk.vector(mk("vector"))
    blk.scalar(mk("scalar"))


def build_program(_hazard_scan=None, _hazard_fix=None, _hz_writers=None):
    nc = bass.Bass("TRN2", target_bir_lowering=False, detect_race_conditions=False,
                   dynamic_dma_scratch_size=16384)

    d = {}
    d["xt"] = nc.dram_tensor("xt", [D, T], F32, kind="ExternalInput").ap()
    for nm, sh in [("wq", [D, D]), ("wk", [D, D]), ("wv", [D, D]), ("wo", [D, D]),
                   ("wab", [D, 16]), ("wg", [D, 8]), ("w1", [D, FF]), ("w2", [FF, D]),
                   ("wexp", [D, D])]:
        d[nm] = nc.dram_tensor(nm, sh, F32, kind="ExternalInput").ap()
    for nm, sh in [("ln1w", [128, NC_D]), ("ln1b", [128, NC_D]), ("ln2w", [128, NC_D]),
                   ("ln2b", [128, NC_D]), ("b1", [128, NC_F]), ("b2", [128, NC_D]),
                   ("bexp", [128, NC_D]), ("coremask", [1, 8])]:
        d[nm] = nc.dram_tensor("d_" + nm, sh, F32, kind="ExternalInput").ap()
    for nm in ["c_iu_f", "c_half_f", "c_ones_f", "c_id_f",
               "c_negsl_b", "c_negsu_b", "c_iu_b", "c_id_b", "c_ones_b"]:
        d[nm] = nc.dram_tensor("d_" + nm, [128, 128], F32, kind="ExternalInput").ap()
    d["c_onescol"] = nc.dram_tensor("d_c_onescol", [128, 1], F32, kind="ExternalInput").ap()
    d["c_sel"] = nc.dram_tensor("d_c_sel", [2, 256], F32, kind="ExternalInput").ap()
    d["c_iota999"] = nc.dram_tensor("d_c_iota999", [128, 32], F32,
                                    kind="ExternalInput").ap()
    d["out"] = nc.dram_tensor("out_t", [D, T], F32, kind="ExternalOutput").ap()

    xn32_dram = nc.dram_tensor("xn32_s", [D, T], F32).ap()
    base_dram = nc.dram_tensor("base_b", [1, 8], F32).ap()
    y1_dram = nc.dram_tensor("y1_s", [D, T], F32).ap()
    st2_dram = nc.dram_tensor("st2_s", [2, T], F32).ap()
    cnt_in = nc.dram_tensor("cnt_in", [1, 8], F32).ap()
    cnt_all = nc.dram_tensor("cnt_all", [8, 8], F32).ap()

    es_persist = ExitStack()
    E = es_persist.enter_context

    c_iu_f = E(nc.sbuf_tensor("c_iu_f", [128, 128], F32))
    c_half_f = E(nc.sbuf_tensor("c_half_f", [128, 128], F32))
    c_ones_f = E(nc.sbuf_tensor("c_ones_f", [128, 128], F32))
    c_id_f = E(nc.sbuf_tensor("c_id_f", [128, 128], F32))
    c_negsl_b = E(nc.sbuf_tensor("c_negsl_b", [128, 128], BF16))
    c_negsu_b = E(nc.sbuf_tensor("c_negsu_b", [128, 128], BF16))
    c_iu_b = E(nc.sbuf_tensor("c_iu_b", [128, 128], BF16))
    c_id_b = E(nc.sbuf_tensor("c_id_b", [128, 128], BF16))
    c_ones_b = E(nc.sbuf_tensor("c_ones_b", [128, 128], BF16))
    c_onescol = E(nc.sbuf_tensor("c_onescol", [128, 1], F32))
    c_sel = E(nc.sbuf_tensor("c_sel", [2, 256], F32))
    c_onescol_b = E(nc.sbuf_tensor("c_onescol_b", [128, 1], BF16))
    ln1w = E(nc.sbuf_tensor("ln1w", [128, NC_D], F32))
    ln1b = E(nc.sbuf_tensor("ln1b", [128, NC_D], F32))
    ln2w = E(nc.sbuf_tensor("ln2w", [128, NC_D], F32))
    ln2b = E(nc.sbuf_tensor("ln2b", [128, NC_D], F32))
    b1_sb = E(nc.sbuf_tensor("b1_sb", [128, NC_F], F32))
    b2_sb = E(nc.sbuf_tensor("b2_sb", [128, NC_D], F32))
    bexp_sb = E(nc.sbuf_tensor("bexp_sb", [128, NC_D], F32))
    coremask = E(nc.sbuf_tensor("coremask", [1, 8], F32))
    r_ohf = E(nc.sbuf_tensor("r_ohf", [128, 32 * 8], BF16))
    r_c = E(nc.sbuf_tensor("r_c", [128, 32], F32))
    r_l = E(nc.sbuf_tensor("r_l", [128, 32 * 8], F32))
    r_negm = E(nc.sbuf_tensor("r_negm", [128, 32], F32))
    r_g = E(nc.sbuf_tensor("r_g", [128, 32], F32))
    run_rep = E(nc.sbuf_tensor("run_rep", [128, 8], F32))
    base_rep = E(nc.sbuf_tensor("base_rep", [128, 8], F32))
    cnt_sb = E(nc.sbuf_tensor("cnt_sb", [1, 64], F32))
    base_row = E(nc.sbuf_tensor("base_row", [1, 72], F32))
    t8a = E(nc.sbuf_tensor("t8a", [128, 8], F32))
    t8b = E(nc.sbuf_tensor("t8b", [128, 8], F32))
    l_cc = E(nc.sbuf_tensor("l_cc", [128, 8], F32))
    m1c = E(nc.sbuf_tensor("m1c", [128, 1], F32))
    c_iota999 = E(nc.sbuf_tensor("c_iota999", [128, 32], F32))
    rt3 = E(nc.sbuf_tensor("rt3", [128, 32], F32))

    pb = [E(nc.psum_tensor(f"pb{i}", [128, 512], F32)) for i in range(8)]

    def pbf(i):
        return pb[i][:, :]

    def pbb(i):
        return pb[i][:, :].bitcast(BF16)

    def pg(i):
        return pb[i][:, :].rearrange("p (g n) -> p g n", n=DH)

    es_a = ExitStack()
    A = es_a.enter_context
    st_mu0 = A(nc.sbuf_tensor("st_mu0", [1, TT], F32))
    st_ss = A(nc.sbuf_tensor("st_ss", [1, TT], F32))
    y1h = A(nc.sbuf_tensor("y1h", [128, 4 * TT], F32))
    xnwo = A(nc.sbuf_tensor("xnwo", [128, 2 * TT], F32))
    o_fm = A(nc.sbuf_tensor("o_fm", [128, H * TT], BF16))
    s_st = A(nc.sbuf_tensor("s_st", [128, H * DH], F32))
    sig_bf = A(nc.sbuf_tensor("sig_bf", [128, H * DH], BF16))
    sq_sc = A(nc.sbuf_tensor("sq_sc", [128, H * DH], F32))
    kfm = A(nc.sbuf_tensor("kfm", [128, H * DH], BF16))
    nkfm = A(nc.sbuf_tensor("nkfm", [128, H * DH], BF16))
    kbfm = A(nc.sbuf_tensor("kbfm", [128, H * DH], BF16))
    qfm = A(nc.sbuf_tensor("qfm", [128, H * DH], BF16))
    p_cur = A(nc.sbuf_tensor("p_cur", [128, H * DH], BF16))
    p_nxt = A(nc.sbuf_tensor("p_nxt", [128, H * DH], BF16))
    q_cur = A(nc.sbuf_tensor("q_cur", [128, H * DH], BF16))
    q_nxt = A(nc.sbuf_tensor("q_nxt", [128, H * DH], BF16))
    mt_bf = A(nc.sbuf_tensor("mt_bf", [128, H * DH], BF16))
    x_a = A(nc.sbuf_tensor("x_a", [128, H * DH], BF16))
    x_b = A(nc.sbuf_tensor("x_b", [128, H * DH], BF16))
    wo_sb = A(nc.sbuf_tensor("wo_sb", [128, NC_D * D], BF16))
    wo1_sb = A(nc.sbuf_tensor("wo1_sb", [128, NC_D], F32))
    wo1_bf = A(nc.sbuf_tensor("wo1_bf", [128, NC_D], BF16))
    zab4 = A(nc.sbuf_tensor("zab4", [128, NCC * 16], F32))
    beta4 = A(nc.sbuf_tensor("beta4", [128, NCC * 8], F32))
    la8 = A(nc.sbuf_tensor("la8", [128, 8], F32))
    cumz = A(nc.sbuf_tensor("cumz", [128, 24], F32))
    dz = A(nc.sbuf_tensor("dz", [128, 8], F32))
    lam4 = A(nc.sbuf_tensor("lam4", [128, NCC * 8], F32))
    ginv4 = A(nc.sbuf_tensor("ginv4", [128, NCC * 8], F32))
    lamc4 = A(nc.sbuf_tensor("lamc4", [128, NCC * 8], F32))
    gref4 = A(nc.sbuf_tensor("gref4", [128, NCC * 8], F32))
    f18 = A(nc.sbuf_tensor("f18", [128, 8], F32))
    ssq4_q = A(nc.sbuf_tensor("ssq4_q", [128, NCC * 8], F32))
    ssq4_k = A(nc.sbuf_tensor("ssq4_k", [128, NCC * 8], F32))
    rsq_q = A(nc.sbuf_tensor("rsq_q", [128, 8], F32))
    rsq_k = A(nc.sbuf_tensor("rsq_k", [128, 8], F32))
    fq4 = A(nc.sbuf_tensor("fq4", [128, NCC * 8], F32))
    frk4 = A(nc.sbuf_tensor("frk4", [128, NCC * 8], F32))
    fkb4 = A(nc.sbuf_tensor("fkb4", [128, NCC * 8], F32))
    xn = A(nc.sbuf_tensor("xn", [128, NC_D * TT], F32))
    xsq = A(nc.sbuf_tensor("xsq", [128, 4 * TT], BF16))
    xin = A(nc.sbuf_tensor("xin", [128, NC_D * TT], F32))
    n_bf = A(nc.sbuf_tensor("n_bf", [128, NC_D * TT], BF16))
    wab_sb = A(nc.sbuf_tensor("wab_sb", [128, NC_D * 16], BF16))
    wg_sb = A(nc.sbuf_tensor("wg_sb", [128, NC_D * 8], F32))
    wq_sb = A(nc.sbuf_tensor("wq_sb", [128, NC_D * D], BF16))
    wk_sb = A(nc.sbuf_tensor("wk_sb", [128, NC_D * D], BF16))
    wv_sb = A(nc.sbuf_tensor("wv_sb", [128, NC_D * D], BF16))
    q_b4 = A(nc.sbuf_tensor("q_b4", [128, NCC * H * DH], BF16))
    k_b4 = A(nc.sbuf_tensor("k_b4", [128, NCC * H * DH], BF16))
    v_b4 = A(nc.sbuf_tensor("v_b4", [128, NCC * H * DH], BF16))
    qt_bf = A(nc.sbuf_tensor("qt_bf", [128, H * DH], BF16))
    kn_bf = A(nc.sbuf_tensor("kn_bf", [128, H * DH], BF16))
    vt_bf = A(nc.sbuf_tensor("vt_bf", [128, H * DH], BF16))
    kb_bf = [A(nc.sbuf_tensor(f"kb_bf{i}", [128, H * DH], BF16)) for i in range(2)]
    es_a.close()

    es_b = ExitStack()
    B = es_b.enter_context
    wexp_sb = B(nc.sbuf_tensor("wexp_sb", [128, NC_D * D], BF16))
    w2_sb = B(nc.sbuf_tensor("w2_sb", [128, NC_F * D], BF16))
    w1_sb = B(nc.sbuf_tensor("w1_sb", [128, NC_D * FF], BF16))
    by1 = [B(nc.sbuf_tensor(f"by1_{i}", [128, NC_D * SUB], F32)) for i in range(2)]
    bxnb = [B(nc.sbuf_tensor(f"bxnb_{i}", [128, NC_D * SUB], BF16)) for i in range(2)]
    bst2 = [B(nc.sbuf_tensor(f"bst2_{i}", [2, SUB], F32)) for i in range(2)]
    bh2 = B(nc.sbuf_tensor("bh2", [128, NC_D * SUB], BF16))
    bg = B(nc.sbuf_tensor("bg", [128, NC_F * SUB], BF16))
    bout = B(nc.sbuf_tensor("bout", [128, NC_D * SUB], F32))
    bgb = B(nc.sbuf_tensor("bgb", [128, SUB], F32))
    bdiag = B(nc.sbuf_tensor("bdiag", [128, 128], F32))
    es_b.close()

    sems = {}
    es_sem = ExitStack()
    for s in ["pe", "vv", "sc", "gp", "st", "cc", "hz_vector", "hz_gpsimd",
              "dx", "dy1", "dst2", "dxn", "dxw", "dcb", "dcs", "dwq", "dwk", "dwv",
              "dwab", "dwo", "dw1a", "dw1b", "dw1c", "dw1d", "dw2a", "dw2b",
              "dwexa", "dwexb", "dcnt", "dcall", "dsb", "dsg",
              "dxa", "dbs", "dbb"]:
        sems[s] = es_sem.enter_context(nc.semaphore(s))

    p = Plan(hz_writers=_hz_writers)

    def w3(t, m):
        return t[:, :].rearrange("p (c m) -> p c m", m=m)

    def tile3(t, n):
        return t[:, :].rearrange("p (c n) -> p c n", n=n)

    def dram3(ap):
        return ap.rearrange("(c p) n -> p c n", p=128)

    def h3(t):
        return t[:, :].rearrange("p (h e) -> p h e", e=DH)

    def h4(t, cc):
        return t[:, :].rearrange("p (s h e) -> p s h e", s=NCC, e=DH)[:, cc, :, :]

    def s4(t, cc):
        return t[:, cc * 8:(cc + 1) * 8]

    def bcast8s(t, cc):
        return s4(t, cc).unsqueeze(2).broadcast_to([128, 8, DH])

    def maskb(cmask):
        return cmask[:, :].unsqueeze(1).broadcast_to([128, 4, DH])

    p.dma("sync", lambda s: s.dma_start(out=w3(wg_sb, 8), in_=dram3(d["wg"])), inc="dcs")
    for nm, t in [("c_iu_f", c_iu_f), ("c_half_f", c_half_f), ("c_ones_f", c_ones_f),
                  ("c_id_f", c_id_f), ("c_onescol", c_onescol),
                  ("c_sel", c_sel), ("c_iota999", c_iota999),
                  ("ln1w", ln1w), ("ln1b", ln1b), ("ln2w", ln2w), ("ln2b", ln2b),
                  ("b1", b1_sb), ("b2", b2_sb), ("bexp", bexp_sb), ("coremask", coremask)]:
        p.dma("sync", lambda s, t=t, nm=nm: s.dma_start(out=t[:, :], in_=d[nm]), inc="dcs")
    ds_const = p.val("dcs")

    for nm, t in [("c_negsl_b", c_negsl_b), ("c_negsu_b", c_negsu_b), ("c_iu_b", c_iu_b),
                  ("c_id_b", c_id_b), ("c_ones_b", c_ones_b)]:
        p.dma("gpsimd", lambda g, t=t, nm=nm: g.dma_start(out=t[:, :], in_=d[nm]), inc="dcb")
    p.dma("gpsimd", lambda g: g.dma_start(out=c_onescol_b[:, :], in_=d["c_onescol"]), inc="dcb")
    dg_const = p.val("dcb")

    def load_cast(dst, src, m, sem, waits=None):
        p.dma("gpsimd", lambda g, dst=dst, src=src, m=m: g.dma_start(
            out=w3(dst, m), in_=dram3(src)), inc=sem, waits=waits)

    load_cast(wq_sb, d["wq"], D, "dwq")
    load_cast(wk_sb, d["wk"], D, "dwk")
    load_cast(wv_sb, d["wv"], D, "dwv")
    load_cast(wab_sb, d["wab"], 16, "dwab")
    load_cast(wo_sb, d["wo"], D, "dwo")

    p.step("vector", lambda v: v.tensor_reduce(wo1_sb[:, :], w3(wo_sb, D), AX.X, OP.add),
           waits=[("dwo", 16)], inc="vv")
    p.step("vector", lambda v: v.tensor_copy(wo1_bf[:, :], wo1_sb[:, :]), inc="vv")
    vv_wo1 = p.val("vv")
    p.step("vector", lambda v: v.memset(run_rep[:, :], 0.0), waits=[("dcs", ds_const)], inc="vv")
    p.step("gpsimd", lambda g: g.memset(s_st[:, :], 0.0), inc="gp")
    gp_szero = p.val("gp")

    st = {
        "pe_upd": {},
        "pe_tp": {},
        "pe_x0": {},
        "pe_proj": {},
        "sc_silu": {},
        "vv_zab": {},
        "vv_ssq": {},
        "vv_scal": {},
        "sc_n": {},
        "sc_lnexp": {},
        "ds_x": {},
        "dxn_t": {},
        "vv_ssq_prev": 0,
        "vv_folds_prev": 0,
        "sc_lnexp_prev": 0,
        "dst2_prev": 0,
        "pe_mu2_t": 0,
        "pe_ss2_t": 0,
    }

    def emit_xload(t_i):
        tk = t_i * TT
        p.dma("sync", (lambda s, tk=tk: s.dma_start(
            out=tile3(xin, TT), in_=dram3(d["xt"])[:, :, tk:tk + TT])),
            waits=p.buf_w("xin"), inc="dx")
        st["ds_x"][t_i] = p.val("dx")

    def emit_lnhead(t_i):
        tk = t_i * TT
        ds_x = st["ds_x"][t_i]
        for ic in range(NC_D):
            p.step("tensor", (lambda t, ic=ic: t.matmul(
                pb[6][0:1, 0:TT], c_onescol[:, 0:1], tile3(xin, TT)[:, ic, :],
                start=(ic == 0), stop=(ic == NC_D - 1))),
                waits=([("dx", ds_x), ("dcs", ds_const)] + p.bank_w(6)) if ic == 0 else [],
                inc="pe" if ic == NC_D - 1 else None)
        pe_mu0 = p.val("pe")
        sc_xsq = 0
        for hf in range(2):
            p.step("scalar", (lambda sc, hf=hf: sc.activation(
                tile3(xsq, TT), tile3(xin, TT)[:, hf * 4:hf * 4 + 4, :], AF.Square)),
                waits=(([("dx", ds_x)] + p.buf_w("xsq")) if hf == 0
                       else [("pe", p.val("pe"))]), inc="sc")
            sc_xsq = p.val("sc")
            for ic in range(4):
                p.step("tensor", (lambda t, ic=ic, hf=hf: t.matmul(
                    pb[7][0:1, 0:TT], c_onescol_b[:, 0:1], tile3(xsq, TT)[:, ic, :],
                    start=(ic == 0 and hf == 0), stop=(ic == 3 and hf == 1))),
                    waits=([("sc", sc_xsq), ("dcb", dg_const)] +
                           (p.bank_w(7) if hf == 0 else [])) if ic == 0 else [],
                    inc="pe" if ic == 3 else None)
        pe_ss0 = p.val("pe")

        p.step("vector", lambda v: v.tensor_scalar(
            st_mu0[0:1, :], pb[6][0:1, 0:TT], 1.0 / D, None, OP.mult),
            waits=[("pe", pe_mu0), ("dst2", st["dst2_prev"])], inc="vv")
        p.step("vector", lambda v: v.tensor_scalar(
            st_ss[0:1, :], pb[7][0:1, 0:TT], 1.0 / D, None, OP.mult),
            waits=[("pe", pe_ss0)], inc="vv")
        p.bank_consumed(6, "vv")
        p.bank_consumed(7, "vv")
        p.step("vector", lambda v: v.tensor_mul(xnwo[0:1, 0:TT], st_mu0[0:1, :],
                                                st_mu0[0:1, :]), inc="vv")
        p.step("vector", lambda v: v.tensor_sub(st_ss[0:1, :], st_ss[0:1, :],
                                                xnwo[0:1, 0:TT]), inc="vv")
        vv_var0 = p.val("vv")
        p.buf_set("xnwo", ("vv", vv_var0))
        p.step("scalar", lambda sc: sc.activation(st_ss[0:1, :], st_ss[0:1, :], AF.Ln,
                                                  bias=f32(EPS_LN)),
               waits=[("vv", vv_var0)], inc="sc")
        p.step("scalar", lambda sc: sc.activation(st_ss[0:1, :], st_ss[0:1, :], AF.Exp,
                                                  scale=f32(-0.5)), inc="sc")
        sc_r0 = p.val("sc")
        p.step("vector", lambda v: v.tensor_mul(st_mu0[0:1, :], st_mu0[0:1, :],
                                                st_ss[0:1, :]),
               waits=[("sc", sc_r0)], inc="vv")
        vv_r0 = p.val("vv")

        p.step("tensor", lambda t: t.matmul(pbf(0), c_ones_f[0:1, :], st_ss[0:1, :],
                                            start=True, stop=True),
               waits=[("vv", vv_r0)] + p.bank_w(0), inc=None)
        p.step("tensor", lambda t: t.matmul(pbf(1), c_ones_f[0:1, :], st_mu0[0:1, :],
                                            start=True, stop=True),
               waits=p.bank_w(1), inc="pe")
        pe_bc0 = p.val("pe")

        def bcast_tile(bank):
            return pb[bank][:, 0:TT].unsqueeze(1).broadcast_to([128, NC_D, TT])

        p.step("vector", lambda v: v.tensor_tensor(
            tile3(xn, TT), tile3(xin, TT), bcast_tile(0), OP.mult),
            waits=[("pe", pe_bc0)] + p.buf_w("xn"), inc="vv")
        p.step("vector", lambda v: v.tensor_tensor(
            tile3(xn, TT), tile3(xn, TT), bcast_tile(1), OP.subtract), inc="vv")
        vv_xn = p.val("vv")
        p.bank_consumed(0, "vv")
        p.bank_consumed(1, "vv")
        p.dma("sync", (lambda s, tk=tk: s.dma_start(
            out=dram3(xn32_dram)[:, :, tk:tk + TT], in_=tile3(xn, TT))),
            waits=[("vv", vv_xn)], inc="dxn")
        st["dxn_t"][t_i] = p.val("dxn")

        for cc in range(NCC):
            for ic in range(NC_D):
                p.step("tensor", (lambda t, cc=cc, ic=ic: t.matmul(
                    pb[7][:, cc * 8:(cc + 1) * 8],
                    tile3(xn, TT)[:, ic, cc * CC:(cc + 1) * CC],
                    w3(wg_sb, 8)[:, ic, :],
                    start=(ic == 0), stop=(ic == NC_D - 1))),
                    waits=([("vv", vv_xn)] + p.bank_w(7)) if (cc == 0 and ic == 0) else [],
                    inc="pe" if ic == NC_D - 1 else None)
        pe_log = p.val("pe")

        tb = t_i * NCC

        def g48(t, col0=0):
            return t[:, col0:col0 + 32].rearrange("p (c e) -> p c e", e=8)

        def gb4(t, col0):
            return t[:, col0:col0 + 4].unsqueeze(2).broadcast_to([128, 4, 8])

        rt1 = ssq4_q
        rt2 = ssq4_k
        p.step("vector", (lambda v: v.tensor_copy(
            r_l[:, tb * 8:tb * 8 + 32], pb[7][:, 0:32])),
            waits=[("pe", pe_log)], inc="vv")
        p.step("vector", (lambda v: v.tensor_reduce(
            r_negm[:, tb:tb + 4], g48(r_l, tb * 8), AX.X, OP.max, negate=True)), inc="vv")
        p.step("vector", (lambda v: v.tensor_tensor(
            g48(rt1), g48(r_l, tb * 8), gb4(r_negm, tb), OP.add)),
            waits=[("sc", st["sc_lnexp_prev"])], inc="vv")
        p.step("vector", lambda v: v.tensor_scalar(
            rt1[:, :], rt1[:, :], 0.0, None, OP.is_ge), inc="vv")
        p.step("vector", lambda v: v.scalar_tensor_tensor(
            rt2[:, :], rt1[:, :], -999.0, c_iota999[:, :], OP.mult, OP.add), inc="vv")
        p.step("vector", lambda v: v.tensor_reduce(
            rt3[:, 0:4], g48(rt2), AX.X, OP.min), inc="vv")
        p.step("vector", (lambda v: v.tensor_tensor(
            g48(r_ohf, tb * 8), g48(rt2), gb4(rt3, 0), OP.is_equal)), inc="vv")
        vv_ohf = p.val("vv")
        p.step("tensor", (lambda t: t.matmul(
            pb[6][:, 0:32], c_iu_b[:, :], r_ohf[:, tb * 8:tb * 8 + 32],
            start=True, stop=True)),
            waits=[("vv", vv_ohf)] + p.bank_w(6), inc=None)
        p.step("tensor", (lambda t: t.matmul(
            pb[6][:, 32:64], c_ones_b[:, :], r_ohf[:, tb * 8:tb * 8 + 32],
            start=True, stop=True)), inc="pe")
        pe_cnt = p.val("pe")
        p.step("vector", lambda v: v.tensor_copy(rt1[:, :], pb[6][:, 0:32]),
               waits=[("pe", pe_cnt)], inc="vv")
        p.step("vector", lambda v: v.tensor_copy(rt2[:, :], pb[6][:, 32:64]), inc="vv")
        p.bank_consumed(6, "vv")
        p.step("vector", lambda v: v.tensor_copy(rt3[:, 0:8], run_rep[:, :]), inc="vv")
        p.step("vector", lambda v: v.tensor_add(rt3[:, 8:16], rt3[:, 0:8], rt2[:, 0:8]),
               inc="vv")
        p.step("vector", lambda v: v.tensor_add(rt3[:, 16:24], rt3[:, 8:16], rt2[:, 8:16]),
               inc="vv")
        p.step("vector", lambda v: v.tensor_add(rt3[:, 24:32], rt3[:, 16:24],
                                                rt2[:, 16:24]), inc="vv")
        p.step("vector", lambda v: v.tensor_add(run_rep[:, :], rt3[:, 24:32],
                                                rt2[:, 24:32]), inc="vv")
        p.step("vector", lambda v: v.tensor_add(rt1[:, :], rt1[:, :], rt3[:, :]), inc="vv")
        p.step("vector", (lambda v: v.tensor_tensor(
            g48(rt1), g48(rt1), g48(r_ohf, tb * 8), OP.mult)), inc="vv")
        p.step("vector", (lambda v: v.tensor_reduce(
            r_c[:, tb:tb + 4], g48(rt1), AX.X, OP.add)), inc="vv")

        sc_xnsq = 0
        for hf in range(2):
            p.step("scalar", (lambda sc, hf=hf: sc.activation(
                tile3(xsq, TT), tile3(xn, TT)[:, hf * 4:hf * 4 + 4, :], AF.Square)),
                waits=([("vv", vv_xn), ("pe", pe_ss0)] if hf == 0
                       else [("pe", p.val("pe"))]), inc="sc")
            sc_xnsq = p.val("sc")
            for ic in range(4):
                p.step("tensor", (lambda t, ic=ic, hf=hf: t.matmul(
                    pb[7][0:1, 0:TT], c_onescol_b[:, 0:1], tile3(xsq, TT)[:, ic, :],
                    start=(ic == 0 and hf == 0), stop=(ic == 3 and hf == 1))),
                    waits=([("sc", sc_xnsq)] + (p.bank_w(7) if hf == 0 else []))
                    if ic == 0 else [],
                    inc="pe" if ic == 3 else None)
        pe_ss1 = p.val("pe")
        p.step("vector", lambda v: v.tensor_scalar(
            st_ss[0:1, :], pb[7][0:1, 0:TT], 1.0 / D, None, OP.mult),
            waits=[("pe", pe_ss1)], inc="vv")
        vv_var1 = p.val("vv")
        p.bank_consumed(7, "vv")
        p.step("scalar", lambda sc: sc.activation(st_ss[0:1, :], st_ss[0:1, :], AF.Ln,
                                                  bias=f32(EPS_LN)),
               waits=[("vv", vv_var1)], inc="sc")
        p.step("scalar", lambda sc: sc.activation(st_ss[0:1, :], st_ss[0:1, :], AF.Exp,
                                                  scale=f32(-0.5)), inc="sc")
        sc_r1 = p.val("sc")
        p.step("tensor", lambda t: t.matmul(pbf(0), c_ones_f[0:1, :], st_ss[0:1, :],
                                            start=True, stop=True),
               waits=[("sc", sc_r1)] + p.bank_w(0), inc="pe")
        pe_bc1 = p.val("pe")
        p.step("vector", lambda v: v.tensor_tensor(
            tile3(xin, TT), tile3(xn, TT), bcast_tile(0), OP.mult),
            waits=[("pe", pe_bc1), ("sc", sc_xsq)], inc="vv")
        vv_u1 = p.val("vv")
        p.bank_consumed(0, "vv")
        for ic in range(NC_D):
            p.step("scalar", (lambda sc, ic=ic: sc.activation(
                tile3(n_bf, TT)[:, ic, :], tile3(xin, TT)[:, ic, :], AF.Identity,
                bias=ln1b[:, ic:ic + 1], scale=ln1w[:, ic:ic + 1])),
                waits=([("vv", vv_u1)] + p.buf_w("n_bf")) if ic == 0 else [],
                inc="sc" if ic == NC_D - 1 else None)
        sc_n = p.val("sc")
        st["sc_n"][t_i] = sc_n
        p.buf_set("xin", ("sc", sc_n))
        p.buf_set("xn", ("dxn", st["dxn_t"][t_i]), ("pe", pe_ss1), ("pe", pe_log))
        p.buf_set("xsq", ("pe", pe_ss1))

    def emit_proj(t_i, cc):
        for ic in range(NC_D):
            w_first = ([("sc", st["sc_n"][t_i]), ("dwq", 16), ("dwk", 16), ("dwv", 16),
                        ("dwab", 16)] + p.bank_w(0, 1, 2, 3, 4, 5, 7)) if ic == 0 else []
            p.step("tensor", (lambda t, ic=ic, cc=cc: t.matmul(
                pb[0][:, :], tile3(n_bf, TT)[:, ic, cc * CC:(cc + 1) * CC],
                w3(wq_sb, D)[:, ic, 0:512],
                start=(ic == 0), stop=(ic == NC_D - 1))), waits=w_first, inc=None)
            p.step("tensor", (lambda t, ic=ic, cc=cc: t.matmul(
                pb[1][:, :], tile3(n_bf, TT)[:, ic, cc * CC:(cc + 1) * CC],
                w3(wq_sb, D)[:, ic, 512:1024],
                start=(ic == 0), stop=(ic == NC_D - 1))), inc=None)
            p.step("tensor", (lambda t, ic=ic, cc=cc: t.matmul(
                pb[2][:, :], tile3(n_bf, TT)[:, ic, cc * CC:(cc + 1) * CC],
                w3(wk_sb, D)[:, ic, 0:512],
                start=(ic == 0), stop=(ic == NC_D - 1))), inc=None)
            p.step("tensor", (lambda t, ic=ic, cc=cc: t.matmul(
                pb[3][:, :], tile3(n_bf, TT)[:, ic, cc * CC:(cc + 1) * CC],
                w3(wk_sb, D)[:, ic, 512:1024],
                start=(ic == 0), stop=(ic == NC_D - 1))), inc=None)
            p.step("tensor", (lambda t, ic=ic, cc=cc: t.matmul(
                pb[4][:, :], tile3(n_bf, TT)[:, ic, cc * CC:(cc + 1) * CC],
                w3(wv_sb, D)[:, ic, 0:512],
                start=(ic == 0), stop=(ic == NC_D - 1))), inc=None)
            p.step("tensor", (lambda t, ic=ic, cc=cc: t.matmul(
                pb[5][:, :], tile3(n_bf, TT)[:, ic, cc * CC:(cc + 1) * CC],
                w3(wv_sb, D)[:, ic, 512:1024],
                start=(ic == 0), stop=(ic == NC_D - 1))), inc=None)
            p.step("tensor", (lambda t, ic=ic, cc=cc: t.matmul(
                pb[7][:, 0:16], tile3(n_bf, TT)[:, ic, cc * CC:(cc + 1) * CC],
                w3(wab_sb, 16)[:, ic, :],
                start=(ic == 0), stop=(ic == NC_D - 1))),
                inc="pe" if ic == NC_D - 1 else None)
        pe_proj = p.val("pe")
        st["pe_proj"][(t_i, cc)] = pe_proj
        if cc == NCC - 1:
            p.buf_set("n_bf", ("pe", pe_proj))

        for dsti, (dst, bank0) in enumerate([(q_b4, 0), (k_b4, 2), (v_b4, 4)]):
            for half in range(2):
                p.step("scalar", (lambda sc, dst=dst, bank0=bank0, half=half, cc=cc:
                       sc.activation(
                           dst[:, cc * 1024 + half * 512:cc * 1024 + (half + 1) * 512],
                           pb[bank0 + half][:, :], AF.Silu)),
                       waits=([("pe", pe_proj)] + p.buf_w(f"qs{cc}"))
                       if (dsti == 0 and half == 0) else [],
                       inc="sc")
        sc_silu = p.val("sc")
        st["sc_silu"][(t_i, cc)] = sc_silu
        for b in (0, 1, 2, 3, 4, 5):
            p.bank_consumed(b, "sc")

        p.step("vector", (lambda v, cc=cc: v.tensor_copy(
            zab4[:, cc * 16:(cc + 1) * 16], pb[7][:, 0:16])),
            waits=[("pe", pe_proj)], inc="vv")
        st["vv_zab"][(t_i, cc)] = p.val("vv")
        p.bank_consumed(7, "vv")

        p.step("gpsimd", (lambda g, cc=cc: g.tensor_mul(
            sq_sc[:, :], q_b4[:, cc * 1024:(cc + 1) * 1024],
            q_b4[:, cc * 1024:(cc + 1) * 1024])),
            waits=[("sc", sc_silu), ("vv", st["vv_ssq_prev"])], inc="gp")
        p.step("vector", (lambda v, cc=cc: v.tensor_reduce(
            s4(ssq4_q, cc), h3(sq_sc), AX.X, OP.add)),
            waits=[("gp", p.val("gp")), ("sc", st["sc_lnexp_prev"])], inc="vv")
        p.step("gpsimd", (lambda g, cc=cc: g.tensor_mul(
            sq_sc[:, :], k_b4[:, cc * 1024:(cc + 1) * 1024],
            k_b4[:, cc * 1024:(cc + 1) * 1024])),
            waits=[("vv", p.val("vv"))], inc="gp")
        p.step("vector", (lambda v, cc=cc: v.tensor_reduce(
            s4(ssq4_k, cc), h3(sq_sc), AX.X, OP.add)),
            waits=[("gp", p.val("gp"))], inc="vv")
        st["vv_ssq"][(t_i, cc)] = p.val("vv")
        st["vv_ssq_prev"] = p.val("vv")

    def emit_lnexp(t_i, ccs):
        for cc in ccs:
            p.step("scalar", (lambda sc, cc=cc: sc.activation(
                zab4[:, cc * 16:(cc + 1) * 16], zab4[:, cc * 16:(cc + 1) * 16],
                AF.Exp, scale=f32(-1.0))),
                waits=[("vv", st["vv_zab"][(t_i, cc)])], inc="sc")
            sc_abe = p.val("sc")
            p.step("vector", (lambda v, cc=cc: v.tensor_scalar(
                zab4[:, cc * 16:(cc + 1) * 16], zab4[:, cc * 16:(cc + 1) * 16],
                1.0, None, OP.add)),
                waits=[("sc", sc_abe)], inc="vv")
            p.step("vector", (lambda v, cc=cc: v.reciprocal(
                s4(beta4, cc), zab4[:, cc * 16:cc * 16 + 8])), inc="vv")
            vv_beta = p.val("vv")
            p.step("scalar", (lambda sc, cc=cc: sc.activation(
                la8[:, :], zab4[:, cc * 16 + 8:cc * 16 + 16], AF.Ln)),
                waits=[("vv", vv_beta), ("pe", st.get("pe_cum_prev", 0))], inc="sc")
            sc_la = p.val("sc")
            p.step("tensor", lambda t: t.matmul(pb[6][:, 0:8], c_iu_f[:, :], la8[:, :],
                                                start=True, stop=True),
                   waits=[("sc", sc_la)] + p.bank_w(6), inc=None)
            p.step("tensor", lambda t: t.matmul(pb[6][:, 8:16], c_half_f[:, :], la8[:, :],
                                                start=True, stop=True), inc=None)
            p.step("tensor", lambda t: t.matmul(pb[6][:, 16:24], c_ones_f[:, :], la8[:, :],
                                                start=True, stop=True), inc="pe")
            pe_cum = p.val("pe")
            st["pe_cum_prev"] = pe_cum
            p.step("vector", lambda v: v.tensor_copy(cumz[:, :], pb[6][:, 0:24]),
                   waits=[("pe", pe_cum)], inc="vv")
            p.bank_consumed(6, "vv")
            p.step("vector", lambda v: v.tensor_sub(dz[:, :], cumz[:, 8:16], cumz[:, 0:8]),
                   inc="vv")
            p.step("vector", lambda v: v.tensor_sub(f18[:, :], cumz[:, 8:16], cumz[:, 16:24]),
                   inc="vv")
            vv_dz = p.val("vv")
            p.step("scalar", (lambda sc, cc=cc: sc.activation(s4(lam4, cc), dz[:, :], AF.Exp)),
                   waits=[("vv", vv_dz)], inc="sc")
            p.step("scalar", (lambda sc, cc=cc: sc.activation(s4(ginv4, cc), dz[:, :],
                                                              AF.Exp, scale=f32(-1.0))),
                   inc="sc")
            p.step("scalar", (lambda sc, cc=cc: sc.activation(s4(lamc4, cc), f18[:, :],
                                                              AF.Exp)), inc="sc")
            p.step("scalar", (lambda sc, cc=cc: sc.activation(s4(gref4, cc), cumz[:, 8:16],
                                                              AF.Exp, scale=f32(-1.0))),
                   inc="sc")
            p.step("scalar", (lambda sc, cc=cc: sc.activation(rsq_q[:, :], s4(ssq4_q, cc),
                                                              AF.Ln, bias=f32(EPS_L2))),
                   waits=[("vv", st["vv_ssq"][(t_i, cc)]),
                          ("vv", st["vv_folds_prev"])], inc="sc")
            p.step("scalar", lambda sc: sc.activation(rsq_q[:, :], rsq_q[:, :], AF.Exp,
                                                      scale=f32(-0.5)), inc="sc")
            p.step("scalar", (lambda sc, cc=cc: sc.activation(rsq_k[:, :], s4(ssq4_k, cc),
                                                              AF.Ln, bias=f32(EPS_L2))),
                   inc="sc")
            p.step("scalar", lambda sc: sc.activation(rsq_k[:, :], rsq_k[:, :], AF.Exp,
                                                      scale=f32(-0.5)), inc="sc")
            sc_lx = p.val("sc")
            p.step("vector", (lambda v, cc=cc: v.tensor_mul(
                s4(fq4, cc), rsq_q[:, :], s4(lam4, cc))),
                waits=[("sc", sc_lx)], inc="vv")
            p.step("vector", (lambda v, cc=cc: v.tensor_copy(s4(frk4, cc), rsq_k[:, :])),
                   inc="vv")
            p.step("vector", (lambda v, cc=cc: v.tensor_mul(
                s4(fkb4, cc), rsq_k[:, :], s4(beta4, cc))), inc="vv")
            st["vv_folds_prev"] = p.val("vv")
            st["sc_lnexp"][(t_i, cc)] = p.val("sc")
        st["sc_lnexp_prev"] = p.val("sc")

    def emit_scalings(t_i, cc):
        par = cc % 2
        ccg = t_i * NCC + cc
        g_pe = st["pe_x0"].get(ccg - 1, 0)
        p.step("vector", (lambda v, cc=cc: v.tensor_tensor(
            h3(qt_bf), h4(q_b4, cc), bcast8s(fq4, cc), OP.mult)),
            waits=[("sc", st["sc_lnexp"][(t_i, cc)]), ("pe", g_pe)], inc="vv")
        p.step("vector", (lambda v, cc=cc: v.tensor_tensor(
            h3(kn_bf), h4(k_b4, cc), bcast8s(frk4, cc), OP.mult)), inc="vv")
        p.step("vector", (lambda v, cc=cc, par=par: v.tensor_tensor(
            h3(kb_bf[par]), h4(k_b4, cc), bcast8s(fkb4, cc), OP.mult)), inc="vv")
        p.step("vector", (lambda v, cc=cc: v.tensor_tensor(
            h3(vt_bf), h4(v_b4, cc), bcast8s(ginv4, cc), OP.mult)), inc="vv")
        st["vv_scal"][(t_i, cc)] = p.val("vv")
        p.buf_set(f"qs{cc}", ("vv", p.val("vv")))

    def emit_delta(t_i, cc):
        par = cc % 2
        ccg = t_i * NCC + cc
        vv_tm = st["vv_scal"][(t_i, cc)]
        p.step("vector", (lambda v, cc=cc: v.tensor_tensor(
            h3(sig_bf), h3(s_st), bcast8s(gref4, cc), OP.mult)),
            waits=[("gp", gp_szero), ("pe", st["pe_upd"].get(ccg - 1, 0))], inc="vv")
        vv_sig = p.val("vv")
        for srcb, bank in [(kn_bf, 0), (kb_bf[par], 1), (qt_bf, 2)]:
            for h in range(H):
                p.step("tensor", (lambda t, srcb=srcb, bank=bank, h=h: t.transpose(
                    pbb(bank)[:, h * DH:(h + 1) * DH], h3(srcb)[:, h, :], c_id_b[:, :])),
                    waits=([("vv", vv_tm)] + p.bank_w(bank)) if h == 0 else [],
                    inc="pe" if h == H - 1 else None)
        pe_tp = p.val("pe")
        st["pe_tp"][ccg] = pe_tp
        p.step("scalar", lambda sc: sc.copy(kfm[:, :], pbb(0)[:, :]),
               waits=[("pe", pe_tp)], inc="sc")
        p.step("scalar", lambda sc: sc.activation(nkfm[:, :], pbb(0)[:, :], AF.Identity,
                                                  scale=f32(-1.0)), inc="sc")
        sc_tpe = p.val("sc")
        p.step("vector", lambda v: v.tensor_copy(kbfm[:, :], pbb(1)[:, :]),
               waits=[("pe", pe_tp)], inc="vv")
        p.step("vector", lambda v: v.tensor_copy(qfm[:, :], pbb(2)[:, :]), inc="vv")
        vv_tpe = p.val("vv")
        p.bank_consumed(0, "sc")
        p.bank_consumed(1, "vv")
        p.bank_consumed(2, "vv")

        for h in range(H):
            p.step("tensor", (lambda t, h=h: t.matmul(
                pg(2 + h // 4)[:, h % 4, :], h3(kfm)[:, h, :], h3(kbfm)[:, h, :],
                start=True, stop=True)),
                waits=([("sc", sc_tpe), ("vv", vv_tpe)] + p.bank_w(2, 3)) if h == 0 else [],
                inc="pe" if h == H - 1 else None)
        for h in range(H):
            p.step("tensor", (lambda t, h=h: t.matmul(
                pg(4 + h // 4)[:, h % 4, :], h3(kbfm)[:, h, :], h3(kfm)[:, h, :],
                start=True, stop=True)),
                waits=p.bank_w(4, 5) if h == 0 else [],
                inc="pe" if h == H - 1 else None)
        for h in range(H):
            p.step("tensor", (lambda t, h=h: t.matmul(
                pg(6 + h // 4)[:, h % 4, :], h3(kbfm)[:, h, :], h3(qfm)[:, h, :],
                start=True, stop=True)),
                waits=p.bank_w(6, 7) if h == 0 else [],
                inc="pe" if h == H - 1 else None)
        pe_afam = p.val("pe")

        def mask_evac(dst, b0, cmask):
            def fn(e, dst=dst, b0=b0, cmask=cmask):
                i1 = e.tensor_tensor(h3(dst)[:, 0:4, :], pg(b0), maskb(cmask), OP.mult)
                i2 = e.tensor_tensor(h3(dst)[:, 4:8, :], pg(b0 + 1), maskb(cmask), OP.mult)
                return [i1, i2]
            return fn
        p.step("vector", mask_evac(q_cur, 2, c_negsl_b), waits=[("pe", pe_afam)], inc="vv")
        p.step("vector", mask_evac(p_cur, 4, c_negsu_b), inc="vv")
        p.step("vector", mask_evac(mt_bf, 6, c_iu_b), inc="vv")
        vv_q = p.val("vv")
        sc_p = 0
        p.bank_consumed(2, "vv")
        p.bank_consumed(3, "vv")
        p.bank_consumed(4, "vv")
        p.bank_consumed(5, "vv")
        p.bank_consumed(6, "vv")
        p.bank_consumed(7, "vv")

        for h in range(H):
            p.step("tensor", (lambda t, h=h: t.matmul(
                pg(h // 4)[:, h % 4, :], h3(nkfm)[:, h, :], h3(sig_bf)[:, h, :],
                start=True, stop=False)),
                waits=([("vv", vv_sig), ("sc", sc_tpe)] + p.bank_w(0, 1)) if h == 0 else [],
                inc=None)
            p.step("tensor", (lambda t, h=h: t.matmul(
                pg(h // 4)[:, h % 4, :], c_id_b[:, :], h3(vt_bf)[:, h, :],
                start=False, stop=True)),
                inc="pe" if h == H - 1 else None)
        pe_x0 = p.val("pe")
        st["pe_x0"][ccg] = pe_x0

        def x0_evac(v):
            i1 = v.tensor_copy(h3(x_a)[:, 0:4, :], pg(0))
            i2 = v.tensor_copy(h3(x_a)[:, 4:8, :], pg(1))
            return [i1, i2]
        p.step("vector", x0_evac, waits=[("pe", pe_x0)], inc="vv")
        vv_x = p.val("vv")
        p.bank_consumed(0, "vv")
        p.bank_consumed(1, "vv")

        xcur, xnxt = x_a, x_b
        pcur, pnxt = p_cur, p_nxt
        qcur, qnxt = q_cur, q_nxt
        vvA = vv_x
        vvB = vv_x
        scA = 0
        scB = 0
        scQA = 0
        vvQB = vv_q
        for lvl in range(LVLS):
            for g, (bank, h0) in enumerate([(0, 0), (1, 4)]):
                w0 = ([("vv", (vvA if g == 0 else vvB)), ("vv", vv_q),
                       ("sc", (scA if g == 0 else scB))] + p.bank_w(bank))
                for j in range(4):
                    p.step("tensor", (lambda t, bank=bank, j=j, h0=h0, pcur=pcur,
                           xcur=xcur: t.matmul(
                               pg(bank)[:, j, :], h3(pcur)[:, h0 + j, :],
                               h3(xcur)[:, h0 + j, :], start=True, stop=True)),
                        waits=w0 if j == 0 else [],
                        inc="pe" if j == 3 else None)
                pe_ap = p.val("pe")

                def xadd(v, bank=bank, h0=h0, xcur=xcur, xnxt=xnxt):
                    return v.tensor_add(h3(xnxt)[:, h0:h0 + 4, :], pg(bank),
                                        h3(xcur)[:, h0:h0 + 4, :])
                p.step("vector", xadd, waits=[("pe", pe_ap)], inc="vv")
                if g == 0:
                    vvA = p.val("vv")
                else:
                    vvB = p.val("vv")
                p.bank_consumed(bank, "vv")
            if lvl < LVLS - 1:
                for g, (bank, h0) in enumerate([(2, 0), (3, 4)]):
                    w0 = ([("sc", (scA if g == 0 else scB)),
                           ("sc", scQA if g == 0 else 0),
                           ("vv", vvQB if g == 1 else 0), ("vv", vv_q)] +
                          p.bank_w(bank))
                    for j in range(4):
                        p.step("tensor", (lambda t, bank=bank, j=j, h0=h0, qcur=qcur,
                               pcur=pcur: t.matmul(
                                   pg(bank)[:, j, :], h3(qcur)[:, h0 + j, :],
                                   h3(pcur)[:, h0 + j, :], start=True, stop=True)),
                            waits=w0 if j == 0 else [],
                            inc="pe" if j == 3 else None)
                    pe_p = p.val("pe")

                    def pevac(sc, bank=bank, h0=h0, pnxt=pnxt):
                        return sc.copy(h3(pnxt)[:, h0:h0 + 4, :], pg(bank))
                    p.step("scalar", pevac, waits=[("pe", pe_p)], inc="sc")
                    if g == 0:
                        scA = p.val("sc")
                    else:
                        scB = p.val("sc")
                    p.bank_consumed(bank, "sc")
            if lvl < LVLS - 2:
                for g, (bank, h0) in enumerate([(4, 0), (5, 4)]):
                    for j in range(4):
                        p.step("tensor", (lambda t, bank=bank, j=j, h0=h0, qcur=qcur,
                               pcur=pcur: t.matmul(
                                   pg(bank)[:, j, :], h3(pcur)[:, h0 + j, :],
                                   h3(qcur)[:, h0 + j, :], start=True, stop=True)),
                            waits=p.bank_w(bank) if j == 0 else [],
                            inc="pe" if j == 3 else None)
                    pe_q = p.val("pe")
                    if g == 0:
                        def qevacA(sc, h0=h0, qnxt=qnxt):
                            return sc.copy(h3(qnxt)[:, h0:h0 + 4, :], pg(4))
                        p.step("scalar", qevacA, waits=[("pe", pe_q)], inc="sc")
                        scQA = p.val("sc")
                        p.bank_consumed(4, "sc")
                    else:
                        def qevacB(v, h0=h0, qnxt=qnxt):
                            return v.tensor_copy(h3(qnxt)[:, h0:h0 + 4, :], pg(5))
                        p.step("vector", qevacB, waits=[("pe", pe_q)], inc="vv")
                        vvQB = p.val("vv")
                        p.bank_consumed(5, "vv")
            xcur, xnxt = xnxt, xcur
            if lvl < LVLS - 1:
                pcur, pnxt = pnxt, pcur
            if lvl < LVLS - 2:
                qcur, qnxt = qnxt, qcur
        rbar = xcur
        vv_rbar = max(vvA, vvB)

        for h in range(H):
            p.step("tensor", (lambda t, h=h: t.matmul(
                pg(2 + h // 4)[:, h % 4, :], h3(sig_bf)[:, h, :], h3(qfm)[:, h, :],
                start=True, stop=False)),
                waits=([("vv", vv_rbar)] + p.bank_w(2, 3)) if h == 0 else [],
                inc=None)
            p.step("tensor", (lambda t, h=h, rbar=rbar: t.matmul(
                pg(2 + h // 4)[:, h % 4, :], h3(rbar)[:, h, :], h3(mt_bf)[:, h, :],
                start=False, stop=True)),
                inc="pe" if h == H - 1 else None)
        pe_o = p.val("pe")
        for h in range(H):
            p.step("tensor", (lambda t, h=h: t.matmul(
                pg(4 + h // 4)[:, h % 4, :], c_id_b[:, :], h3(sig_bf)[:, h, :],
                start=True, stop=False)),
                waits=p.bank_w(4, 5) if h == 0 else [], inc=None)
            p.step("tensor", (lambda t, h=h, rbar=rbar, par=par: t.matmul(
                pg(4 + h // 4)[:, h % 4, :], h3(kb_bf[par])[:, h, :], h3(rbar)[:, h, :],
                start=False, stop=True)),
                inc="pe" if h == H - 1 else None)
        pe_upd = p.val("pe")
        st["pe_upd"][ccg] = pe_upd

        def oevac(sc, cc=cc):
            o3 = o_fm[:, :].rearrange("p (h n) -> p h n", n=TT)
            i1 = sc.copy(o3[:, 0:4, cc * CC:(cc + 1) * CC], pg(2))
            i2 = sc.copy(o3[:, 4:8, cc * CC:(cc + 1) * CC], pg(3))
            return [i1, i2]
        p.step("scalar", oevac,
               waits=[("pe", pe_o)] + (p.buf_w("o_fm") if cc == 0 else []), inc="sc")
        p.bank_consumed(2, "sc")
        p.bank_consumed(3, "sc")

        def sevac(v, cc=cc):
            i1 = v.tensor_tensor(h3(s_st)[:, 0:4, :], pg(4), bcast8s(lamc4, cc)[:, 0:4, :],
                                 OP.mult)
            i2 = v.tensor_tensor(h3(s_st)[:, 4:8, :], pg(5), bcast8s(lamc4, cc)[:, 4:8, :],
                                 OP.mult)
            return [i1, i2]
        p.step("vector", sevac, waits=[("pe", pe_upd)], inc="vv")
        p.bank_consumed(4, "vv")
        p.bank_consumed(5, "vv")

    def emit_wo(t_i):
        tk = t_i * TT
        sc_oall = p.val("sc")
        for half in range(2):
            for oc in range(4):
                for hv in range(NC_D):
                    p.step("tensor", (lambda t, half=half, oc=oc, hv=hv: t.matmul(
                        pb[oc][:, :],
                        w3(wo_sb, D)[:, hv, (half * 4 + oc) * 128:(half * 4 + oc + 1) * 128],
                        o_fm[:, :].rearrange("p (h n) -> p h n", n=TT)[:, hv, :],
                        start=(hv == 0), stop=(hv == NC_D - 1))),
                        waits=([("sc", sc_oall), ("dwo", 16)] + p.bank_w(0, 1, 2, 3))
                        if (hv == 0 and oc == 0) else [],
                        inc="pe" if hv == NC_D - 1 else None)
            pe_woh = p.val("pe")
            vv_y1 = 0
            for q in range(2):
                c0 = half * 4 + 2 * q
                p.dma("sync", (lambda s, tk=tk, c0=c0: s.dma_start(
                    out=tile3(xnwo, TT),
                    in_=dram3(xn32_dram)[:, c0:c0 + 2, tk:tk + TT])),
                    waits=[("dxn", st["dxn_t"][t_i])] + p.buf_w("xnwo"), inc="dxw")
                ds_xw = p.val("dxw")

                def y1add(v, q=q):
                    i1 = v.tensor_add(tile3(y1h, TT)[:, 2 * q, :], pb[2 * q][:, :],
                                      tile3(xnwo, TT)[:, 0, :])
                    i2 = v.tensor_add(tile3(y1h, TT)[:, 2 * q + 1, :], pb[2 * q + 1][:, :],
                                      tile3(xnwo, TT)[:, 1, :])
                    return [i1, i2]
                p.step("vector", y1add,
                       waits=[("pe", pe_woh), ("dxw", ds_xw)] + p.buf_w("y1h"), inc="vv")
                vv_y1 = p.val("vv")
                p.buf_set("xnwo", ("vv", vv_y1))
                p.bank_consumed(2 * q, "vv")
                p.bank_consumed(2 * q + 1, "vv")
            p.step("scalar", (lambda sc: sc.activation(
                xsq[:, 0:4 * TT], y1h[:, :], AF.Square)),
                waits=[("vv", vv_y1)] + p.buf_w("xsq"), inc="sc")
            sc_sq = p.val("sc")
            for ic in range(4):
                p.step("tensor", (lambda t, ic=ic, half=half: t.matmul(
                    pb[7][0:1, 0:TT], c_onescol_b[:, 0:1], tile3(xsq, TT)[:, ic, :],
                    start=(ic == 0 and half == 0),
                    stop=(ic == 3 and half == 1))),
                    waits=([("sc", sc_sq)] + (p.bank_w(7) if half == 0 else []))
                    if ic == 0 else [],
                    inc="pe" if ic == 3 else None)
            pe_ss2 = p.val("pe")
            p.dma("sync", (lambda s, tk=tk, half=half: s.dma_start(
                out=dram3(y1_dram)[:, half * 4:(half + 1) * 4, tk:tk + TT],
                in_=tile3(y1h, TT))),
                waits=[("vv", vv_y1)], inc="dy1")
            p.buf_set("y1h", ("dy1", p.val("dy1")), ("sc", sc_sq))
            p.buf_set("xsq", ("pe", pe_ss2))
        for hv in range(NC_D):
            p.step("tensor", (lambda t, hv=hv: t.matmul(
                pb[6][0:1, 0:TT], wo1_bf[:, hv:hv + 1],
                o_fm[:, :].rearrange("p (h n) -> p h n", n=TT)[:, hv, :],
                start=(hv == 0), stop=(hv == NC_D - 1))),
                waits=([("vv", vv_wo1), ("sc", sc_oall)] + p.bank_w(6)) if hv == 0 else [],
                inc="pe" if hv == NC_D - 1 else None)
        st["pe_mu2_t"] = p.val("pe")
        st["pe_ss2_t"] = pe_ss2
        p.buf_set("o_fm", ("pe", p.val("pe")))

    def emit_ln2fin(t_i):
        tk = t_i * TT
        p.step("vector", lambda v: v.tensor_scalar(
            st_ss[0:1, :], pb[6][0:1, 0:TT], 1.0 / D, None, OP.mult),
            waits=[("pe", st["pe_mu2_t"]), ("dst2", st["dst2_prev"])], inc="vv")
        p.step("vector", lambda v: v.tensor_scalar(
            st_mu0[0:1, :], pb[7][0:1, 0:TT], 1.0 / D, None, OP.mult),
            waits=[("pe", st["pe_ss2_t"])], inc="vv")
        p.bank_consumed(6, "vv")
        p.bank_consumed(7, "vv")
        p.step("vector", lambda v: v.tensor_mul(xnwo[0:1, 0:TT], st_ss[0:1, :],
                                                st_ss[0:1, :]), inc="vv")
        p.step("vector", lambda v: v.tensor_sub(st_mu0[0:1, :], st_mu0[0:1, :],
                                                xnwo[0:1, 0:TT]), inc="vv")
        vv_var2 = p.val("vv")
        p.buf_set("xnwo", ("vv", vv_var2))
        p.step("scalar", lambda sc: sc.activation(st_mu0[0:1, :], st_mu0[0:1, :], AF.Ln,
                                                  bias=f32(EPS_LN)),
               waits=[("vv", vv_var2)], inc="sc")
        p.step("scalar", lambda sc: sc.activation(st_mu0[0:1, :], st_mu0[0:1, :], AF.Exp,
                                                  scale=f32(-0.5)), inc="sc")
        sc_r2 = p.val("sc")
        p.step("vector", lambda v: v.tensor_mul(st_ss[0:1, :], st_ss[0:1, :],
                                                st_mu0[0:1, :]),
               waits=[("sc", sc_r2)], inc="vv")
        vv_st2 = p.val("vv")
        p.dma("sync", (lambda s, tk=tk: s.dma_start(out=st2_dram[0:1, tk:tk + TT],
                                                    in_=st_ss[0:1, :])),
              waits=[("vv", vv_st2)], inc="dst2")
        p.dma("sync", (lambda s, tk=tk: s.dma_start(out=st2_dram[1:2, tk:tk + TT],
                                                    in_=st_mu0[0:1, :])), inc="dst2")
        st["dst2_prev"] = p.val("dst2")

    emit_xload(0)
    emit_lnhead(0)
    for cc in range(NCC):
        emit_proj(0, cc)
    emit_lnexp(0, [0, 1, 2, 3])
    for t_i in range(NT):
        nxt = t_i + 1
        if nxt < NT:
            emit_xload(nxt)
        emit_scalings(t_i, 0)
        emit_delta(t_i, 0)
        if nxt < NT:
            emit_lnhead(nxt)
        emit_scalings(t_i, 1)
        emit_delta(t_i, 1)
        if nxt < NT:
            emit_proj(nxt, 0)
            emit_proj(nxt, 1)
        emit_scalings(t_i, 2)
        emit_delta(t_i, 2)
        if nxt < NT:
            emit_proj(nxt, 2)
        emit_scalings(t_i, 3)
        if nxt < NT:
            emit_proj(nxt, 3)
            emit_lnexp(nxt, [0, 1, 2])
        emit_delta(t_i, 3)
        emit_wo(t_i)
        emit_ln2fin(t_i)
        if nxt < NT:
            emit_lnexp(nxt, [3])
        if t_i == (NT - 2 if EARLY_CC else NT - 1):
            p.dma("gpsimd", lambda g: g.dma_start(out=cnt_in, in_=run_rep[0:1, 0:8]),
                  waits=[("vv", p.val("vv"))], inc="dcnt")
            p.step("gpsimd", lambda g: g.collective_compute(
                "AllGather", OP.bypass, replica_groups=[list(range(8))],
                ins=[cnt_in.opt()], outs=[cnt_all.opt()]),
                waits=[("dcnt", p.val("dcnt"))], inc="cc")
            p.dma("gpsimd", lambda g: g.dma_start(out=cnt_sb[0:1, 0:64],
                                                  in_=cnt_all[0:8, 0:8]),
                  waits=[("cc", p.val("cc"))], inc="dcall")

    dg_xn_all = p.val("dxn")
    pe_afinal = p.val("pe")
    vv_afinal = p.val("vv")
    sc_afinal = p.val("sc")
    alias_w = [("pe", pe_afinal), ("vv", vv_afinal), ("sc", sc_afinal),
               ("dst2", p.val("dst2")), ("dy1", p.val("dy1")),
               ("dxw", p.val("dxw")), ("dx", p.val("dx"))]

    for ccg in range(32):
        p.step("scalar", (lambda sc, ccg=ccg: sc.activation(
            r_l[:, ccg * 8:(ccg + 1) * 8], r_l[:, ccg * 8:(ccg + 1) * 8], AF.Exp,
            bias=r_negm[:, ccg:ccg + 1])),
            waits=[("vv", vv_afinal)] if ccg == 0 else [], inc="sc")
    sc_exp = p.val("sc")
    dg_call = p.val("dcall")
    p.step("vector", lambda v: v.tensor_tensor(
        base_row[0:1, 8:72].rearrange("p (e j) -> p e j", j=8),
        cnt_sb[0:1, 0:64].rearrange("p (j e) -> p j e", e=8).transpose([0, 2, 1]),
        coremask[0:1, 0:8].unsqueeze(1).broadcast_to([1, 8, 8]),
        OP.mult), waits=[("dcall", dg_call)], inc="vv")
    p.step("vector", lambda v: v.reduce_sum(
        base_row[0:1, 0:8], base_row[0:1, 8:72].rearrange("p (e j) -> p e j", j=8), AX.X),
        inc="vv")
    vv_base = p.val("vv")

    for gq in range(8):
        p.dma("gpsimd", (lambda g, gq=gq: g.dma_start(
            out=w3(w1_sb, FF)[:, :, gq * 512:(gq + 1) * 512],
            in_=dram3(d["w1"])[:, :, gq * 512:(gq + 1) * 512])),
            waits=alias_w if gq == 0 else [], inc="dw1" + "abcd"[gq // 2])
    for gq in range(8):
        p.dma("gpsimd", (lambda g, gq=gq: g.dma_start(
            out=w3(w2_sb, D)[:, :, gq * 128:(gq + 1) * 128],
            in_=dram3(d["w2"])[:, :, gq * 128:(gq + 1) * 128])),
            inc="dw2" + "ab"[gq // 4])
    for gh in range(2):
        p.dma("gpsimd", (lambda g, gh=gh: g.dma_start(
            out=w3(wexp_sb, D)[:, :, gh * 512:(gh + 1) * 512],
            in_=dram3(d["wexp"])[:, :, gh * 512:(gh + 1) * 512])),
            inc="dwex" + "ab"[gh])

    NSUB = T // SUB

    def emit_bload(su, first=False):
        par = su % 2
        j0 = su * SUB
        p.dma("sync", (lambda s, j0=j0, par=par: s.dma_start(
            out=tile3(by1[par], SUB), in_=dram3(y1_dram)[:, :, j0:j0 + SUB])),
            waits=(alias_w if first else []) + p.buf_w(f"by1{par}") +
                  [("dy1", p.val("dy1"))], inc="dsb")
        p.dma("gpsimd", (lambda g, j0=j0, par=par: g.dma_start(
            out=tile3(bxnb[par], SUB), in_=dram3(xn32_dram)[:, :, j0:j0 + SUB])),
            waits=p.buf_w(f"bxnb{par}") + [("dxn", dg_xn_all)], inc="dsg")
        p.dma("sync", (lambda s, j0=j0, par=par: s.dma_start(
            out=bst2[par][0:2, :], in_=st2_dram[0:2, j0:j0 + SUB])),
            waits=[("dst2", p.val("dst2"))] + p.buf_w(f"bst{par}"), inc="dsb")
        return (p.val("dsb"), p.val("dsg"))

    ds_b = [(0, 0)] * (NSUB + 1)
    ds_b[0] = emit_bload(0, first=True)
    ds_b[1] = emit_bload(1, first=True)
    p.dma("sync", lambda s: s.dma_start(out=base_dram, in_=base_row[0:1, 0:8]),
          waits=[("vv", vv_base)], inc="dbs")
    p.dma("sync", lambda s: s.dma_start(
        out=base_rep[:, :],
        in_=base_dram.unsqueeze(0).broadcast_to([128, 1, 8]).rearrange("p a e -> p (a e)")),
        waits=[("dbs", p.val("dbs"))], inc="dbb")
    dg_bb = p.val("dbb")
    for su in range(NSUB):
        par = su % 2
        j0 = su * SUB

        p.step("tensor", (lambda t, par=par: t.matmul(
            pb[6][:, 0:SUB], c_sel[0:2, 128:256], bst2[par][0:2, :],
            start=True, stop=True)),
            waits=[("dsb", ds_b[su][0])] + p.bank_w(6), inc=None)
        p.step("tensor", (lambda t, par=par: t.matmul(
            pb[6][:, SUB:2 * SUB], c_sel[0:2, 0:128], bst2[par][0:2, :],
            start=True, stop=True)), inc="pe")
        pe_bc2 = p.val("pe")
        p.step("vector", (lambda v, par=par: v.tensor_tensor(
            tile3(bout, SUB), tile3(by1[par], SUB),
            pb[6][:, 0:SUB].unsqueeze(1).broadcast_to([128, NC_D, SUB]), OP.mult)),
            waits=[("pe", pe_bc2)] + p.buf_w("bout"), inc="vv")
        p.step("vector", lambda v: v.tensor_tensor(
            tile3(bout, SUB), tile3(bout, SUB),
            pb[6][:, SUB:2 * SUB].unsqueeze(1).broadcast_to([128, NC_D, SUB]), OP.subtract),
            inc="vv")
        vv_u2 = p.val("vv")
        p.bank_consumed(6, "vv")
        for ic in range(NC_D):
            p.step("scalar", (lambda sc, ic=ic: sc.activation(
                tile3(bh2, SUB)[:, ic, :], tile3(bout, SUB)[:, ic, :], AF.Identity,
                bias=ln2b[:, ic:ic + 1], scale=ln2w[:, ic:ic + 1])),
                waits=([("vv", vv_u2)] + p.buf_w("bh2")) if ic == 0 else [],
                inc="sc" if ic == NC_D - 1 else None)
        sc_h2 = p.val("sc")
        p.buf_set(f"bst{par}", ("pe", pe_bc2))

        for g4 in range(8):
            bp = (g4 % 2) * 2
            for j in range(4):
                oc = g4 * 4 + j
                bank = bp + j // 2
                colr = (j % 2) * SUB
                for ic in range(NC_D):
                    p.step("tensor", (lambda t, oc=oc, bank=bank, colr=colr, ic=ic: t.matmul(
                        pb[bank][:, colr:colr + SUB],
                        w3(w1_sb, FF)[:, ic, oc * 128:(oc + 1) * 128],
                        tile3(bh2, SUB)[:, ic, :],
                        start=(ic == 0), stop=(ic == NC_D - 1))),
                        waits=([("sc", sc_h2), ("dw1" + "abcd"[g4 // 2], 32)] +
                               p.bank_w(bp, bp + 1))
                        if (ic == 0 and j == 0) else [],
                        inc="pe" if (ic == NC_D - 1 and j == 3) else None)
            pe_g = p.val("pe")
            for j in range(4):
                oc = g4 * 4 + j
                bank = bp + j // 2
                colr = (j % 2) * SUB
                p.step("scalar", (lambda sc, oc=oc, bank=bank, colr=colr: sc.activation(
                    tile3(bg, SUB)[:, oc, :], pb[bank][:, colr:colr + SUB], AF.Gelu,
                    bias=b1_sb[:, oc:oc + 1])),
                    waits=([("pe", pe_g)] + (p.buf_w("bg") if oc == 0 else [])), inc="sc")
            p.bank_consumed(bp, "sc")
            p.bank_consumed(bp + 1, "sc")
        sc_gelu = p.val("sc")

        for q in range(2):
            ccg = j0 // CC + q
            p.step("vector", (lambda v, ccg=ccg: v.reduce_sum(
                m1c[:, :], r_l[:, ccg * 8:(ccg + 1) * 8], AX.X)),
                waits=([("sc", sc_exp), ("dbb", dg_bb)] if (su == 0 and q == 0) else []),
                inc="vv")
            p.step("vector", (lambda v: v.reciprocal(t8a[:, 0:1], m1c[:, :])), inc="vv")
            p.step("vector", (lambda v, ccg=ccg: v.tensor_tensor(
                t8b[:, :], r_ohf[:, ccg * 8:(ccg + 1) * 8], base_rep[:, :], OP.mult)),
                inc="vv")
            p.step("vector", (lambda v: v.reduce_sum(m1c[:, :], t8b[:, :], AX.X)), inc="vv")
            p.step("vector", (lambda v, ccg=ccg: v.tensor_add(
                m1c[:, :], m1c[:, :], r_c[:, ccg:ccg + 1])), inc="vv")
            p.step("vector", (lambda v: v.tensor_scalar(
                m1c[:, :], m1c[:, :], 101.0, None, OP.is_lt)), inc="vv")
            p.step("vector", (lambda v, ccg=ccg: v.tensor_mul(
                r_g[:, ccg:ccg + 1], t8a[:, 0:1], m1c[:, :])), inc="vv")
            p.step("vector", (lambda v, ccg=ccg: v.tensor_scalar(
                bdiag[:, :], c_id_f[:, :], r_g[:, ccg:ccg + 1], None, OP.mult)),
                waits=[("pe", st.get("pe_gate_prev", 0) if q == 0 else p.val("pe"))],
                inc="vv")
            vv_dg2 = p.val("vv")
            p.step("tensor", (lambda t, q=q: t.matmul(
                pb[6][:, q * 128:(q + 1) * 128], c_ones_f[:, :], bdiag[:, :],
                start=True, stop=True)),
                waits=[("vv", vv_dg2)] + (p.bank_w(6) if q == 0 else []), inc="pe")
        pe_gb = p.val("pe")
        st["pe_gate_prev"] = pe_gb
        p.step("vector", lambda v: v.tensor_copy(bgb[:, :], pb[6][:, 0:SUB]),
               waits=[("pe", pe_gb)], inc="vv")
        vv_gb = p.val("vv")
        p.bank_consumed(6, "vv")


        for g4 in range(2):
            bp = 4 + g4 * 2
            for j in range(4):
                oc = g4 * 4 + j
                bank = bp + j // 2
                colr = (j % 2) * SUB
                for ic2 in range(NC_F):
                    p.step("tensor", (lambda t, oc=oc, bank=bank, colr=colr, ic2=ic2: t.matmul(
                        pb[bank][:, colr:colr + SUB],
                        w3(w2_sb, D)[:, ic2, oc * 128:(oc + 1) * 128],
                        tile3(bg, SUB)[:, ic2, :],
                        start=(ic2 == 0), stop=(ic2 == NC_F - 1))),
                        waits=([("sc", sc_gelu), ("dw2" + "ab"[g4], 64)] +
                               p.bank_w(bp, bp + 1))
                        if (ic2 == 0 and j == 0) else [],
                        inc="pe" if (ic2 == NC_F - 1 and j == 3) else None)
            pe_y2 = p.val("pe")
            for j in range(4):
                oc = g4 * 4 + j
                bank = bp + j // 2
                colr = (j % 2) * SUB
                p.step("vector", (lambda v, oc=oc, bank=bank, colr=colr, par=par:
                       v.scalar_tensor_tensor(
                           tile3(by1[par], SUB)[:, oc, :], pb[bank][:, colr:colr + SUB],
                           b2_sb[:, oc:oc + 1], tile3(by1[par], SUB)[:, oc, :],
                           OP.add, OP.add)),
                    waits=[("pe", pe_y2)], inc="vv")
            p.bank_consumed(bp, "vv")
            p.bank_consumed(bp + 1, "vv")
        vv_s1 = p.val("vv")

        for g4 in range(2):
            bp = g4 * 2
            for j in range(4):
                oc = g4 * 4 + j
                bank = bp + j // 2
                colr = (j % 2) * SUB
                for ic in range(NC_D):
                    p.step("tensor", (lambda t, oc=oc, bank=bank, colr=colr, ic=ic, par=par:
                           t.matmul(
                               pb[bank][:, colr:colr + SUB],
                               w3(wexp_sb, D)[:, ic, oc * 128:(oc + 1) * 128],
                               tile3(bxnb[par], SUB)[:, ic, :],
                               start=(ic == 0), stop=(ic == NC_D - 1))),
                        waits=([("dwex" + "ab"[g4], 16), ("dsg", ds_b[su][1])] +
                               p.bank_w(bp, bp + 1))
                        if (ic == 0 and j == 0) else [],
                        inc="pe" if (ic == NC_D - 1 and j == 3) else None)
            pe_m = p.val("pe")
            for j in range(4):
                oc = g4 * 4 + j
                bank = bp + j // 2
                colr = (j % 2) * SUB
                p.step("vector", (lambda v, oc=oc, bank=bank, colr=colr: v.scalar_tensor_tensor(
                    tile3(bout, SUB)[:, oc, :], pb[bank][:, colr:colr + SUB],
                    bexp_sb[:, oc:oc + 1], bgb[:, :], OP.add, OP.mult)),
                    waits=([("pe", pe_m), ("vv", vv_gb), ("sc", sc_h2)]
                           if (g4 == 0 and j == 0) else [("pe", pe_m)]),
                    inc="vv")
            p.bank_consumed(bp, "vv")
            p.bank_consumed(bp + 1, "vv")
        vv_m1 = p.val("vv")
        p.buf_set(f"bxnb{par}", ("pe", p.val("pe")))

        p.step("gpsimd", (lambda g, par=par: g.tensor_add(
            by1[par][:, :], by1[par][:, :], bout[:, :])),
            waits=[("vv", max(vv_m1, vv_s1))], inc="gp")
        gp_out = p.val("gp")
        p.dma("gpsimd", (lambda g, j0=j0, par=par: g.dma_start(
            out=tile3(by1[par], SUB), in_=dram3(d["xt"])[:, :, j0:j0 + SUB],
            accum_op=OP.add)),
            waits=[("gp", gp_out)], inc="dxa")
        dg_xadd = p.val("dxa")
        p.dma("sync", (lambda s, j0=j0, par=par: s.dma_start(
            out=dram3(d["out"])[:, :, j0:j0 + SUB], in_=tile3(by1[par], SUB))),
            waits=[("dxa", dg_xadd)], inc="st")
        p.buf_set(f"by1{par}", ("st", p.val("st")))
        p.buf_set("bout", ("gp", gp_out))
        p.buf_set("bh2", ("pe", p.val("pe")))
        p.buf_set("bg", ("pe", p.val("pe")))
        if su + 2 <= NSUB - 1:
            ds_b[su + 2] = emit_bload(su + 2)

    if DEBUG_DUMP:
        d["dbg_st2"] = nc.dram_tensor("dbg_st2", [2, T], F32, kind="ExternalOutput").ap()
        d["dbg_y1"] = nc.dram_tensor("dbg_y1", [D, T], F32, kind="ExternalOutput").ap()
        d["dbg_xn"] = nc.dram_tensor("dbg_xn", [D, T], F32, kind="ExternalOutput").ap()
        p.dma("sync", lambda s: s.dma_start(out=d["dbg_st2"], in_=st2_dram[:, :]),
              waits=[("st", p.val("st"))], inc="st")
        p.dma("sync", lambda s: s.dma_start(out=d["dbg_y1"], in_=y1_dram[:, :]), inc="st")
        p.dma("sync", lambda s: s.dma_start(out=d["dbg_xn"], in_=xn32_dram[:, :]), inc="st")
    p.step("sync", lambda s: None, waits=[("st", p.val("st"))])

    _BUILD_CACHE["dbg"] = dict(r_g=r_g, r_c=r_c, r_ohf=r_ohf, r_l=r_l, r_negm=r_negm,
                               run_rep=run_rep, base_rep=base_rep, cnt_sb=cnt_sb,
                               base_row=base_row)
    with nc.Block() as blk:
        _emit(p, sems, blk, hazard_scan=_hazard_scan, hazard_fix=_hazard_fix)
    es_sem.close()
    es_persist.close()
    return nc, p


def build_fixed():
    scan = {}
    nc1, p1 = build_program(_hazard_scan=scan)
    del nc1
    hz_writers = set()
    fix = {}
    for eng in ("vector", "gpsimd"):
        flagged = set()
        for (e, rsi), ws in scan.items():
            if e == eng:
                flagged.update(ws)
        if not flagged:
            continue
        for si in flagged:
            fn, waits, inc, inc_by = p1.steps[eng][si]
            if inc is None:
                hz_writers.add((eng, si))
        cnt = {}
        vals = {}
        for si, (fn, waits, inc, inc_by) in enumerate(p1.steps[eng]):
            if inc is None and (eng, si) in hz_writers:
                inc, inc_by = "hz_" + eng, 1
            if inc is not None:
                cnt[inc] = cnt.get(inc, 0) + inc_by
                vals[si] = (inc, cnt[inc])
            else:
                vals[si] = None
        for (e, rsi), ws in scan.items():
            if e != eng:
                continue
            for wsi in ws:
                wv = vals.get(wsi)
                if wv is not None:
                    fix.setdefault((eng, rsi), []).append(wv)
    nc2, _ = build_program(_hazard_fix=fix, _hz_writers=hz_writers)
    return nc2


def _consts():
    i = np.arange(128)
    iu = (i[:, None] <= i[None, :]).astype(np.float32)
    half = np.zeros((128, 128), np.float32)
    half[0:64, :] = 1.0
    ones = np.ones((128, 128), np.float32)
    ident = np.eye(128, dtype=np.float32)
    sl = (i[:, None] > i[None, :]).astype(np.float32)
    su = (i[:, None] < i[None, :]).astype(np.float32)
    return {
        "d_c_iu_f": iu, "d_c_half_f": half, "d_c_ones_f": ones, "d_c_id_f": ident,
        "d_c_negsl_b": -sl, "d_c_negsu_b": -su, "d_c_iu_b": iu, "d_c_id_b": ident,
        "d_c_ones_b": ones,
        "d_c_onescol": np.ones((128, 1), np.float32),
        "d_c_sel": np.concatenate([np.repeat(np.eye(2, dtype=np.float32)[:, 0:1], 128, 1),
                                   np.repeat(np.eye(2, dtype=np.float32)[:, 1:2], 128, 1)],
                                  axis=1),
        "d_c_iota999": np.broadcast_to(
            (np.arange(32) % 8 + 999.0).astype(np.float32), (128, 32)).copy(),
    }


def kernel(**inputs):
    if "prog" not in _BUILD_CACHE:
        _BUILD_CACHE["prog"] = build_fixed()
    nc = _BUILD_CACHE["prog"]

    x = np.asarray(inputs["L0_attn_x"])

    def col128(vec, ncol):
        return np.ascontiguousarray(np.asarray(vec).reshape(ncol, 128).T)

    shared = {
        "wq": np.asarray(inputs["L0_attn_Wq"]),
        "wk": np.asarray(inputs["L0_attn_Wk"]),
        "wv": np.asarray(inputs["L0_attn_Wv"]),
        "wo": np.asarray(inputs["L0_attn_Wo"]),
        "w1": np.asarray(inputs["L0_attn_W1"]),
        "w2": np.asarray(inputs["L0_attn_W2"]),
        "wexp": np.asarray(inputs["L0_ffn_w_expert"]),
        "wg": np.asarray(inputs["L0_ffn_w_gate"]),
        "wab": np.ascontiguousarray(np.concatenate(
            [np.asarray(inputs["L0_attn_Wb"]), np.asarray(inputs["L0_attn_Wa"])], axis=1)),
        "d_ln1w": col128(inputs["L0_attn_ln1_w"], 8),
        "d_ln1b": col128(inputs["L0_attn_ln1_b"], 8),
        "d_ln2w": col128(inputs["L0_attn_ln2_w"], 8),
        "d_ln2b": col128(inputs["L0_attn_ln2_b"], 8),
        "d_b1": col128(inputs["L0_attn_b1"], 32),
        "d_b2": col128(inputs["L0_attn_b2"], 8),
        "d_bexp": col128(inputs["L0_ffn_b_expert"], 8),
    }
    shared.update(_consts())

    in_maps = []
    for i in range(8):
        m = dict(shared)
        m["xt"] = np.ascontiguousarray(x[i].T)
        m["d_coremask"] = (np.arange(8) < i).astype(np.float32).reshape(1, 8)
        in_maps.append(m)

    res = run_bass_kernel_spmd(nc, in_maps, core_ids=list(range(8)))
    out = np.stack([np.ascontiguousarray(np.asarray(res.results[i]["out_t"]).T)
                    for i in range(8)])
    return out.astype(np.float32)


# version 13
# speedup vs baseline: 1.0989x; 1.0192x over previous
import numpy as np
from contextlib import ExitStack

import concourse.bass as bass
import concourse.mybir as mybir
from concourse.bass_utils import run_bass_kernel_spmd

F32 = mybir.dt.float32
BF16 = mybir.dt.bfloat16
AF = mybir.ActivationFunctionType
OP = mybir.AluOpType
AX = mybir.AxisListType

f32 = np.float32

T = 4096
D = 1024
H = 8
DH = 128
FF = 4096
NT = 8
TT = 512
NCC = 4
CC = 128
NC_D = 8
NC_F = 32
SUB = 256
EPS_LN = 1e-5
EPS_L2 = 1e-6
LVLS = 7

_BUILD_CACHE = {}
DEBUG_DUMP = False
EARLY_CC = False

ENGINES = ("sync", "gpsimd", "tensor", "vector", "scalar")


class Plan:
    def __init__(self, hz_writers=None):
        self.steps = {e: [] for e in ENGINES}
        self.cnt = {}
        self.hw = {e: {} for e in ENGINES}
        self.bank_guard = {i: [] for i in range(8)}
        self.buf_guard = {}
        self.hz_writers = hz_writers or set()

    def val(self, sem):
        return self.cnt.get(sem, 0)

    def step(self, eng, fn, waits=None, inc=None, inc_by=1):
        w = []
        for s, v in (waits or []):
            if v is None or v <= 0:
                continue
            if self.hw[eng].get(s, 0) >= v:
                continue
            self.hw[eng][s] = v
            w.append((s, v))
        if inc is None and (eng, len(self.steps[eng])) in self.hz_writers:
            inc, inc_by = "hz_" + eng, 1
        if inc is not None:
            self.cnt[inc] = self.cnt.get(inc, 0) + inc_by
        self.steps[eng].append((fn, w, inc, inc_by))

    def dma(self, eng, fn, waits=None, inc=None):
        self.step(eng, fn, waits=waits, inc=inc, inc_by=16)

    def bank_w(self, *banks):
        out = []
        for b in banks:
            out.extend(self.bank_guard[b])
        return out

    def bank_consumed(self, b, sem):
        self.bank_guard[b] = [(sem, self.val(sem))]

    def buf_w(self, *names):
        out = []
        for nm in names:
            out.extend(self.buf_guard.get(nm, []))
        return out

    def buf_set(self, nm, *guards):
        self.buf_guard[nm] = [(s, v) for (s, v) in guards if v]


def _arg_names(args):
    out = set()
    for a in args:
        for attr in ("memref", "name"):
            n = getattr(a, attr, None)
            if isinstance(n, str):
                out.add(n)
                break
        else:
            t = getattr(a, "tensor", None)
            n = getattr(t, "name", None)
            if isinstance(n, str):
                out.add(n)
    return out


def _emit(plan, sems, blk, hazard_scan=None, hazard_fix=None):
    def mk(eng_name):
        check = eng_name in ("vector", "gpsimd") and hazard_scan is not None
        def closure(eng):
            recent = []
            for si, (fn, waits, inc, inc_by) in enumerate(plan.steps[eng_name]):
                if hazard_fix is not None:
                    for s, v in hazard_fix.get((eng_name, si), []):
                        eng.wait_ge(sems[s], v)
                for s, v in waits:
                    eng.wait_ge(sems[s], v)
                inst = fn(eng)
                insts = inst if isinstance(inst, list) else ([inst] if inst is not None else [])
                if check:
                    for one in insts:
                        ii = getattr(one, "ins", None)
                        m = getattr(ii, "ins", None)
                        mo = getattr(ii, "outs", None)
                        rd = _arg_names(m or [])
                        wr = _arg_names(mo or [])
                        touched = rd | wr
                        for wsi, wouts in recent:
                            if wsi != si and touched & wouts:
                                hazard_scan.setdefault((eng_name, si), set()).add(wsi)
                        recent.append((si, wr))
                        recent = recent[-2:]
                if inc is not None and insts:
                    insts[-1].then_inc(sems[inc], inc_by)
        return closure
    blk.sync(mk("sync"))
    blk.gpsimd(mk("gpsimd"))
    blk.tensor(mk("tensor"))
    blk.vector(mk("vector"))
    blk.scalar(mk("scalar"))


def build_program(_hazard_scan=None, _hazard_fix=None, _hz_writers=None):
    nc = bass.Bass("TRN2", target_bir_lowering=False, detect_race_conditions=False,
                   dynamic_dma_scratch_size=16384)

    d = {}
    d["xt"] = nc.dram_tensor("xt", [D, T], F32, kind="ExternalInput").ap()
    for nm, sh in [("wq", [D, D]), ("wk", [D, D]), ("wv", [D, D]), ("wo", [D, D]),
                   ("wab", [D, 16]), ("wg", [D, 8]), ("w1", [D, FF]), ("w2", [FF, D]),
                   ("wexp", [D, D])]:
        d[nm] = nc.dram_tensor(nm, sh, F32, kind="ExternalInput").ap()
    for nm, sh in [("ln1w", [128, NC_D]), ("ln1b", [128, NC_D]), ("ln2w", [128, NC_D]),
                   ("ln2b", [128, NC_D]), ("b1", [128, NC_F]), ("b2", [128, NC_D]),
                   ("bexp", [128, NC_D]), ("coremask", [1, 8])]:
        d[nm] = nc.dram_tensor("d_" + nm, sh, F32, kind="ExternalInput").ap()
    for nm in ["c_iu_f", "c_half_f", "c_ones_f", "c_id_f",
               "c_negsl_b", "c_negsu_b", "c_iu_b", "c_id_b", "c_ones_b"]:
        d[nm] = nc.dram_tensor("d_" + nm, [128, 128], F32, kind="ExternalInput").ap()
    d["c_onescol"] = nc.dram_tensor("d_c_onescol", [128, 1], F32, kind="ExternalInput").ap()
    d["c_sel"] = nc.dram_tensor("d_c_sel", [2, 256], F32, kind="ExternalInput").ap()
    d["c_iota999"] = nc.dram_tensor("d_c_iota999", [128, 32], F32,
                                    kind="ExternalInput").ap()
    d["out"] = nc.dram_tensor("out_t", [D, T], F32, kind="ExternalOutput").ap()

    xn32_dram = nc.dram_tensor("xn32_s", [D, T], F32).ap()
    base_dram = nc.dram_tensor("base_b", [1, 8], F32).ap()
    y1_dram = nc.dram_tensor("y1_s", [D, T], F32).ap()
    st2_dram = nc.dram_tensor("st2_s", [2, T], F32).ap()
    cnt_in = nc.dram_tensor("cnt_in", [1, 8], F32).ap()
    cnt_all = nc.dram_tensor("cnt_all", [8, 8], F32).ap()

    es_persist = ExitStack()
    E = es_persist.enter_context

    c_iu_f = E(nc.sbuf_tensor("c_iu_f", [128, 128], F32))
    c_half_f = E(nc.sbuf_tensor("c_half_f", [128, 128], F32))
    c_ones_f = E(nc.sbuf_tensor("c_ones_f", [128, 128], F32))
    c_id_f = E(nc.sbuf_tensor("c_id_f", [128, 128], F32))
    c_negsl_b = E(nc.sbuf_tensor("c_negsl_b", [128, 128], BF16))
    c_negsu_b = E(nc.sbuf_tensor("c_negsu_b", [128, 128], BF16))
    c_iu_b = E(nc.sbuf_tensor("c_iu_b", [128, 128], BF16))
    c_id_b = E(nc.sbuf_tensor("c_id_b", [128, 128], BF16))
    c_ones_b = E(nc.sbuf_tensor("c_ones_b", [128, 128], BF16))
    c_onescol = E(nc.sbuf_tensor("c_onescol", [128, 1], F32))
    c_sel = E(nc.sbuf_tensor("c_sel", [2, 256], F32))
    c_onescol_b = E(nc.sbuf_tensor("c_onescol_b", [128, 1], BF16))
    ln1w = E(nc.sbuf_tensor("ln1w", [128, NC_D], F32))
    ln1b = E(nc.sbuf_tensor("ln1b", [128, NC_D], F32))
    ln2w = E(nc.sbuf_tensor("ln2w", [128, NC_D], F32))
    ln2b = E(nc.sbuf_tensor("ln2b", [128, NC_D], F32))
    b1_sb = E(nc.sbuf_tensor("b1_sb", [128, NC_F], F32))
    b2_sb = E(nc.sbuf_tensor("b2_sb", [128, NC_D], F32))
    bexp_sb = E(nc.sbuf_tensor("bexp_sb", [128, NC_D], F32))
    coremask = E(nc.sbuf_tensor("coremask", [1, 8], F32))
    r_ohf = E(nc.sbuf_tensor("r_ohf", [128, 32 * 8], BF16))
    r_c = E(nc.sbuf_tensor("r_c", [128, 32], F32))
    r_l = E(nc.sbuf_tensor("r_l", [128, 32 * 8], F32))
    r_negm = E(nc.sbuf_tensor("r_negm", [128, 32], F32))
    r_g = E(nc.sbuf_tensor("r_g", [128, 32], F32))
    run_rep = E(nc.sbuf_tensor("run_rep", [128, 8], F32))
    base_rep = E(nc.sbuf_tensor("base_rep", [128, 8], F32))
    cnt_sb = E(nc.sbuf_tensor("cnt_sb", [1, 64], F32))
    base_row = E(nc.sbuf_tensor("base_row", [1, 72], F32))
    t8a = E(nc.sbuf_tensor("t8a", [128, 1], F32))
    t8b = E(nc.sbuf_tensor("t8b", [128, 8], F32))
    m1c = E(nc.sbuf_tensor("m1c", [128, 1], F32))
    c_iota999 = E(nc.sbuf_tensor("c_iota999", [128, 32], BF16))
    rt3 = E(nc.sbuf_tensor("rt3", [128, 32], F32))

    pb = [E(nc.psum_tensor(f"pb{i}", [128, 512], F32)) for i in range(8)]

    def pbf(i):
        return pb[i][:, :]

    def pbb(i):
        return pb[i][:, :].bitcast(BF16)

    def pg(i):
        return pb[i][:, :].rearrange("p (g n) -> p g n", n=DH)

    es_a = ExitStack()
    A = es_a.enter_context
    st_mu0 = A(nc.sbuf_tensor("st_mu0", [1, TT], F32))
    st_ss = A(nc.sbuf_tensor("st_ss", [1, TT], F32))
    y1h = A(nc.sbuf_tensor("y1h", [128, 4 * TT], F32))
    xnwo = A(nc.sbuf_tensor("xnwo", [128, 2 * TT], F32))
    o_fm = A(nc.sbuf_tensor("o_fm", [128, H * TT], BF16))
    s_st = A(nc.sbuf_tensor("s_st", [128, H * DH], F32))
    sig_bf = A(nc.sbuf_tensor("sig_bf", [128, H * DH], BF16))
    sq_sc = A(nc.sbuf_tensor("sq_sc", [128, H * DH], F32))
    kfm = A(nc.sbuf_tensor("kfm", [128, H * DH], BF16))
    nkfm = A(nc.sbuf_tensor("nkfm", [128, H * DH], BF16))
    kbfm = A(nc.sbuf_tensor("kbfm", [128, H * DH], BF16))
    qfm = A(nc.sbuf_tensor("qfm", [128, H * DH], BF16))
    p_cur = A(nc.sbuf_tensor("p_cur", [128, H * DH], BF16))
    p_nxt = A(nc.sbuf_tensor("p_nxt", [128, H * DH], BF16))
    q_cur = A(nc.sbuf_tensor("q_cur", [128, H * DH], BF16))
    q_nxt = A(nc.sbuf_tensor("q_nxt", [128, H * DH], BF16))
    mt_bf = A(nc.sbuf_tensor("mt_bf", [128, H * DH], BF16))
    x_a = A(nc.sbuf_tensor("x_a", [128, H * DH], BF16))
    x_b = A(nc.sbuf_tensor("x_b", [128, H * DH], BF16))
    wo_sb = A(nc.sbuf_tensor("wo_sb", [128, NC_D * D], BF16))
    wo1_sb = A(nc.sbuf_tensor("wo1_sb", [128, NC_D], F32))
    wo1_bf = A(nc.sbuf_tensor("wo1_bf", [128, NC_D], BF16))
    zab4 = A(nc.sbuf_tensor("zab4", [128, NCC * 16], F32))
    beta4 = A(nc.sbuf_tensor("beta4", [128, NCC * 8], F32))
    la8_4 = A(nc.sbuf_tensor("la8_4", [128, NCC * 8], F32))
    cumz = A(nc.sbuf_tensor("cumz", [128, 24], F32))
    dz = A(nc.sbuf_tensor("dz", [128, 8], F32))
    lam4 = A(nc.sbuf_tensor("lam4", [128, NCC * 8], F32))
    ginv4 = A(nc.sbuf_tensor("ginv4", [128, NCC * 8], F32))
    lamc4 = A(nc.sbuf_tensor("lamc4", [128, NCC * 8], F32))
    gref4 = A(nc.sbuf_tensor("gref4", [128, NCC * 8], F32))
    f18 = A(nc.sbuf_tensor("f18", [128, 8], F32))
    ssq4_q = A(nc.sbuf_tensor("ssq4_q", [128, NCC * 8], F32))
    ssq4_k = A(nc.sbuf_tensor("ssq4_k", [128, NCC * 8], F32))
    rsq_q = A(nc.sbuf_tensor("rsq_q", [128, 8], F32))
    rsq_k = A(nc.sbuf_tensor("rsq_k", [128, 8], F32))
    fq4 = A(nc.sbuf_tensor("fq4", [128, NCC * 8], F32))
    frk4 = A(nc.sbuf_tensor("frk4", [128, NCC * 8], F32))
    fkb4 = A(nc.sbuf_tensor("fkb4", [128, NCC * 8], F32))
    xn = A(nc.sbuf_tensor("xn", [128, NC_D * TT], F32))
    xsq = A(nc.sbuf_tensor("xsq", [128, 4 * TT], BF16))
    xin = A(nc.sbuf_tensor("xin", [128, NC_D * TT], F32))
    n_bf = A(nc.sbuf_tensor("n_bf", [128, NC_D * TT], BF16))
    wab_sb = A(nc.sbuf_tensor("wab_sb", [128, NC_D * 16], BF16))
    wg_sb = A(nc.sbuf_tensor("wg_sb", [128, NC_D * 8], F32))
    wq_sb = A(nc.sbuf_tensor("wq_sb", [128, NC_D * D], BF16))
    wk_sb = A(nc.sbuf_tensor("wk_sb", [128, NC_D * D], BF16))
    wv_sb = A(nc.sbuf_tensor("wv_sb", [128, NC_D * D], BF16))
    q_b4 = A(nc.sbuf_tensor("q_b4", [128, NCC * H * DH], BF16))
    k_b4 = A(nc.sbuf_tensor("k_b4", [128, NCC * H * DH], BF16))
    v_b4 = A(nc.sbuf_tensor("v_b4", [128, NCC * H * DH], BF16))
    qt_bf = A(nc.sbuf_tensor("qt_bf", [128, H * DH], BF16))
    kn_bf = A(nc.sbuf_tensor("kn_bf", [128, H * DH], BF16))
    vt_bf = A(nc.sbuf_tensor("vt_bf", [128, H * DH], BF16))
    kb_bf = [A(nc.sbuf_tensor(f"kb_bf{i}", [128, H * DH], BF16)) for i in range(2)]
    es_a.close()

    es_b = ExitStack()
    B = es_b.enter_context
    wexp_sb = B(nc.sbuf_tensor("wexp_sb", [128, NC_D * D], BF16))
    w2_sb = B(nc.sbuf_tensor("w2_sb", [128, NC_F * D], BF16))
    w1_sb = B(nc.sbuf_tensor("w1_sb", [128, NC_D * FF], BF16))
    by1 = [B(nc.sbuf_tensor(f"by1_{i}", [128, NC_D * SUB], F32)) for i in range(2)]
    bxnb = [B(nc.sbuf_tensor(f"bxnb_{i}", [128, NC_D * SUB], BF16)) for i in range(2)]
    bst2 = [B(nc.sbuf_tensor(f"bst2_{i}", [2, SUB], F32)) for i in range(2)]
    bh2 = B(nc.sbuf_tensor("bh2", [128, NC_D * SUB], BF16))
    bg = B(nc.sbuf_tensor("bg", [128, NC_F * SUB], BF16))
    bout = B(nc.sbuf_tensor("bout", [128, NC_D * SUB], F32))
    bgb = B(nc.sbuf_tensor("bgb", [128, SUB], F32))
    bdiag = B(nc.sbuf_tensor("bdiag", [128, 128], F32))
    es_b.close()

    sems = {}
    es_sem = ExitStack()
    for s in ["pe", "vv", "sc", "gp", "st", "cc", "hz_vector", "hz_gpsimd",
              "dx", "dy1", "dst2", "dxn", "dxw", "dcb", "dcs", "dwq", "dwk", "dwv",
              "dwab", "dwo", "dw1a", "dw1b", "dw1c", "dw1d", "dw2a", "dw2b",
              "dwexa", "dwexb", "dcnt", "dcall", "dsb", "dsg",
              "dxa", "dbs", "dbb"]:
        sems[s] = es_sem.enter_context(nc.semaphore(s))

    p = Plan(hz_writers=_hz_writers)

    def w3(t, m):
        return t[:, :].rearrange("p (c m) -> p c m", m=m)

    def tile3(t, n):
        return t[:, :].rearrange("p (c n) -> p c n", n=n)

    def dram3(ap):
        return ap.rearrange("(c p) n -> p c n", p=128)

    def h3(t):
        return t[:, :].rearrange("p (h e) -> p h e", e=DH)

    def h4(t, cc):
        return t[:, :].rearrange("p (s h e) -> p s h e", s=NCC, e=DH)[:, cc, :, :]

    def s4(t, cc):
        return t[:, cc * 8:(cc + 1) * 8]

    def bcast8s(t, cc):
        return s4(t, cc).unsqueeze(2).broadcast_to([128, 8, DH])

    def maskb(cmask):
        return cmask[:, :].unsqueeze(1).broadcast_to([128, 4, DH])

    p.dma("sync", lambda s: s.dma_start(out=w3(wg_sb, 8), in_=dram3(d["wg"])), inc="dcs")
    for nm, t in [("c_iu_f", c_iu_f), ("c_half_f", c_half_f), ("c_ones_f", c_ones_f),
                  ("c_id_f", c_id_f), ("c_onescol", c_onescol),
                  ("c_sel", c_sel),
                  ("ln1w", ln1w), ("ln1b", ln1b), ("ln2w", ln2w), ("ln2b", ln2b),
                  ("b1", b1_sb), ("b2", b2_sb), ("bexp", bexp_sb), ("coremask", coremask)]:
        p.dma("sync", lambda s, t=t, nm=nm: s.dma_start(out=t[:, :], in_=d[nm]), inc="dcs")
    ds_const = p.val("dcs")

    for nm, t in [("c_negsl_b", c_negsl_b), ("c_negsu_b", c_negsu_b), ("c_iu_b", c_iu_b),
                  ("c_id_b", c_id_b), ("c_ones_b", c_ones_b)]:
        p.dma("gpsimd", lambda g, t=t, nm=nm: g.dma_start(out=t[:, :], in_=d[nm]), inc="dcb")
    p.dma("gpsimd", lambda g: g.dma_start(out=c_onescol_b[:, :], in_=d["c_onescol"]), inc="dcb")
    p.dma("gpsimd", lambda g: g.dma_start(out=c_iota999[:, :], in_=d["c_iota999"]), inc="dcb")
    dg_const = p.val("dcb")

    def load_cast(dst, src, m, sem, waits=None):
        p.dma("gpsimd", lambda g, dst=dst, src=src, m=m: g.dma_start(
            out=w3(dst, m), in_=dram3(src)), inc=sem, waits=waits)

    load_cast(wq_sb, d["wq"], D, "dwq")
    load_cast(wk_sb, d["wk"], D, "dwk")
    load_cast(wv_sb, d["wv"], D, "dwv")
    load_cast(wab_sb, d["wab"], 16, "dwab")
    load_cast(wo_sb, d["wo"], D, "dwo")

    p.step("vector", lambda v: v.tensor_reduce(wo1_sb[:, :], w3(wo_sb, D), AX.X, OP.add),
           waits=[("dwo", 16)], inc="vv")
    p.step("vector", lambda v: v.tensor_copy(wo1_bf[:, :], wo1_sb[:, :]), inc="vv")
    vv_wo1 = p.val("vv")
    p.step("vector", lambda v: v.memset(run_rep[:, :], 0.0), waits=[("dcs", ds_const)], inc="vv")
    p.step("gpsimd", lambda g: g.memset(s_st[:, :], 0.0), inc="gp")
    gp_szero = p.val("gp")

    st = {
        "pe_upd": {},
        "pe_tp": {},
        "pe_x0": {},
        "pe_proj": {},
        "sc_silu": {},
        "vv_zab": {},
        "vv_ssq": {},
        "vv_scal": {},
        "sc_n": {},
        "sc_lnexp": {},
        "ds_x": {},
        "dxn_t": {},
        "vv_ssq_prev": 0,
        "vv_folds_prev": 0,
        "sc_lnexp_prev": 0,
        "dst2_prev": 0,
        "pe_mu2_t": 0,
        "pe_ss2_t": 0,
    }

    def emit_xload(t_i):
        tk = t_i * TT
        p.dma("sync", (lambda s, tk=tk: s.dma_start(
            out=tile3(xin, TT), in_=dram3(d["xt"])[:, :, tk:tk + TT])),
            waits=p.buf_w("xin"), inc="dx")
        st["ds_x"][t_i] = p.val("dx")

    def emit_lnhead(t_i):
        tk = t_i * TT
        ds_x = st["ds_x"][t_i]
        for ic in range(NC_D):
            p.step("tensor", (lambda t, ic=ic: t.matmul(
                pb[6][0:1, 0:TT], c_onescol[:, 0:1], tile3(xin, TT)[:, ic, :],
                start=(ic == 0), stop=(ic == NC_D - 1))),
                waits=([("dx", ds_x), ("dcs", ds_const)] + p.bank_w(6)) if ic == 0 else [],
                inc="pe" if ic == NC_D - 1 else None)
        pe_mu0 = p.val("pe")
        sc_xsq = 0
        for hf in range(2):
            p.step("scalar", (lambda sc, hf=hf: sc.activation(
                tile3(xsq, TT), tile3(xin, TT)[:, hf * 4:hf * 4 + 4, :], AF.Square)),
                waits=(([("dx", ds_x)] + p.buf_w("xsq")) if hf == 0
                       else [("pe", p.val("pe"))]), inc="sc")
            sc_xsq = p.val("sc")
            for ic in range(4):
                p.step("tensor", (lambda t, ic=ic, hf=hf: t.matmul(
                    pb[7][0:1, 0:TT], c_onescol_b[:, 0:1], tile3(xsq, TT)[:, ic, :],
                    start=(ic == 0 and hf == 0), stop=(ic == 3 and hf == 1))),
                    waits=([("sc", sc_xsq), ("dcb", dg_const)] +
                           (p.bank_w(7) if hf == 0 else [])) if ic == 0 else [],
                    inc="pe" if ic == 3 else None)
        pe_ss0 = p.val("pe")

        p.step("vector", lambda v: v.tensor_scalar(
            st_mu0[0:1, :], pb[6][0:1, 0:TT], 1.0 / D, None, OP.mult),
            waits=[("pe", pe_mu0), ("dst2", st["dst2_prev"])], inc="vv")
        p.step("vector", lambda v: v.tensor_scalar(
            st_ss[0:1, :], pb[7][0:1, 0:TT], 1.0 / D, None, OP.mult),
            waits=[("pe", pe_ss0)], inc="vv")
        p.bank_consumed(6, "vv")
        p.bank_consumed(7, "vv")
        p.step("vector", lambda v: v.tensor_mul(xnwo[0:1, 0:TT], st_mu0[0:1, :],
                                                st_mu0[0:1, :]), inc="vv")
        p.step("vector", lambda v: v.tensor_sub(st_ss[0:1, :], st_ss[0:1, :],
                                                xnwo[0:1, 0:TT]), inc="vv")
        vv_var0 = p.val("vv")
        p.buf_set("xnwo", ("vv", vv_var0))
        p.step("scalar", lambda sc: sc.activation(st_ss[0:1, :], st_ss[0:1, :], AF.Ln,
                                                  bias=f32(EPS_LN)),
               waits=[("vv", vv_var0)], inc="sc")
        p.step("scalar", lambda sc: sc.activation(st_ss[0:1, :], st_ss[0:1, :], AF.Exp,
                                                  scale=f32(-0.5)), inc="sc")
        sc_r0 = p.val("sc")
        p.step("vector", lambda v: v.tensor_mul(st_mu0[0:1, :], st_mu0[0:1, :],
                                                st_ss[0:1, :]),
               waits=[("sc", sc_r0)], inc="vv")
        vv_r0 = p.val("vv")

        p.step("tensor", lambda t: t.matmul(pbf(0), c_ones_f[0:1, :], st_ss[0:1, :],
                                            start=True, stop=True),
               waits=[("vv", vv_r0)] + p.bank_w(0), inc=None)
        p.step("tensor", lambda t: t.matmul(pbf(1), c_ones_f[0:1, :], st_mu0[0:1, :],
                                            start=True, stop=True),
               waits=p.bank_w(1), inc="pe")
        pe_bc0 = p.val("pe")

        def bcast_tile(bank):
            return pb[bank][:, 0:TT].unsqueeze(1).broadcast_to([128, NC_D, TT])

        p.step("vector", lambda v: v.tensor_tensor(
            tile3(xn, TT), tile3(xin, TT), bcast_tile(0), OP.mult),
            waits=[("pe", pe_bc0)] + p.buf_w("xn"), inc="vv")
        p.step("vector", lambda v: v.tensor_tensor(
            tile3(xn, TT), tile3(xn, TT), bcast_tile(1), OP.subtract), inc="vv")
        vv_xn = p.val("vv")
        p.bank_consumed(0, "vv")
        p.bank_consumed(1, "vv")
        p.dma("sync", (lambda s, tk=tk: s.dma_start(
            out=dram3(xn32_dram)[:, :, tk:tk + TT], in_=tile3(xn, TT))),
            waits=[("vv", vv_xn)], inc="dxn")
        st["dxn_t"][t_i] = p.val("dxn")

        for cc in range(NCC):
            for ic in range(NC_D):
                p.step("tensor", (lambda t, cc=cc, ic=ic: t.matmul(
                    pb[7][:, cc * 8:(cc + 1) * 8],
                    tile3(xn, TT)[:, ic, cc * CC:(cc + 1) * CC],
                    w3(wg_sb, 8)[:, ic, :],
                    start=(ic == 0), stop=(ic == NC_D - 1))),
                    waits=([("vv", vv_xn)] + p.bank_w(7)) if (cc == 0 and ic == 0) else [],
                    inc="pe" if ic == NC_D - 1 else None)
        pe_log = p.val("pe")

        tb = t_i * NCC

        def g48(t, col0=0):
            return t[:, col0:col0 + 32].rearrange("p (c e) -> p c e", e=8)

        def gb4(t, col0):
            return t[:, col0:col0 + 4].unsqueeze(2).broadcast_to([128, 4, 8])

        rt1 = ssq4_q
        rt2 = ssq4_k
        p.step("vector", (lambda v: v.tensor_copy(
            r_l[:, tb * 8:tb * 8 + 32], pb[7][:, 0:32])),
            waits=[("pe", pe_log)], inc="vv")
        p.step("vector", (lambda v: v.tensor_reduce(
            r_negm[:, tb:tb + 4], g48(r_l, tb * 8), AX.X, OP.max, negate=True)), inc="vv")
        p.step("vector", (lambda v: v.tensor_tensor(
            g48(rt1), g48(r_l, tb * 8), gb4(r_negm, tb), OP.add)),
            waits=[("sc", st["sc_lnexp_prev"])], inc="vv")
        p.step("vector", lambda v: v.tensor_scalar(
            rt1[:, :], rt1[:, :], 0.0, None, OP.is_ge), inc="vv")
        p.step("vector", lambda v: v.scalar_tensor_tensor(
            rt2[:, :], rt1[:, :], -16.0, c_iota999[:, :], OP.mult, OP.add), inc="vv")
        p.step("vector", lambda v: v.tensor_reduce(
            rt3[:, 0:4], g48(rt2), AX.X, OP.min), inc="vv")
        p.step("vector", (lambda v: v.tensor_tensor(
            g48(r_ohf, tb * 8), g48(rt2), gb4(rt3, 0), OP.is_equal)), inc="vv")
        vv_ohf = p.val("vv")
        p.step("tensor", (lambda t: t.matmul(
            pb[6][:, 0:32], c_iu_b[:, :], r_ohf[:, tb * 8:tb * 8 + 32],
            start=True, stop=True)),
            waits=[("vv", vv_ohf)] + p.bank_w(6), inc=None)
        p.step("tensor", (lambda t: t.matmul(
            pb[6][:, 32:64], c_ones_b[:, :], r_ohf[:, tb * 8:tb * 8 + 32],
            start=True, stop=True)), inc="pe")
        pe_cnt = p.val("pe")
        p.step("vector", lambda v: v.tensor_copy(rt1[:, :], pb[6][:, 0:32]),
               waits=[("pe", pe_cnt)], inc="vv")
        p.step("vector", lambda v: v.tensor_copy(rt2[:, :], pb[6][:, 32:64]), inc="vv")
        p.bank_consumed(6, "vv")
        p.step("vector", lambda v: v.tensor_copy(rt3[:, 0:8], run_rep[:, :]), inc="vv")
        p.step("vector", lambda v: v.tensor_add(rt3[:, 8:16], rt3[:, 0:8], rt2[:, 0:8]),
               inc="vv")
        p.step("vector", lambda v: v.tensor_add(rt3[:, 16:24], rt3[:, 8:16], rt2[:, 8:16]),
               inc="vv")
        p.step("vector", lambda v: v.tensor_add(rt3[:, 24:32], rt3[:, 16:24],
                                                rt2[:, 16:24]), inc="vv")
        p.step("vector", lambda v: v.tensor_add(run_rep[:, :], rt3[:, 24:32],
                                                rt2[:, 24:32]), inc="vv")
        p.step("vector", lambda v: v.tensor_add(rt1[:, :], rt1[:, :], rt3[:, :]), inc="vv")
        p.step("vector", (lambda v: v.tensor_tensor(
            g48(rt1), g48(rt1), g48(r_ohf, tb * 8), OP.mult)), inc="vv")
        p.step("vector", (lambda v: v.tensor_reduce(
            r_c[:, tb:tb + 4], g48(rt1), AX.X, OP.add)), inc="vv")

        sc_xnsq = 0
        for hf in range(2):
            p.step("scalar", (lambda sc, hf=hf: sc.activation(
                tile3(xsq, TT), tile3(xn, TT)[:, hf * 4:hf * 4 + 4, :], AF.Square)),
                waits=([("vv", vv_xn), ("pe", pe_ss0)] if hf == 0
                       else [("pe", p.val("pe"))]), inc="sc")
            sc_xnsq = p.val("sc")
            for ic in range(4):
                p.step("tensor", (lambda t, ic=ic, hf=hf: t.matmul(
                    pb[7][0:1, 0:TT], c_onescol_b[:, 0:1], tile3(xsq, TT)[:, ic, :],
                    start=(ic == 0 and hf == 0), stop=(ic == 3 and hf == 1))),
                    waits=([("sc", sc_xnsq)] + (p.bank_w(7) if hf == 0 else []))
                    if ic == 0 else [],
                    inc="pe" if ic == 3 else None)
        pe_ss1 = p.val("pe")
        p.step("vector", lambda v: v.tensor_scalar(
            st_ss[0:1, :], pb[7][0:1, 0:TT], 1.0 / D, None, OP.mult),
            waits=[("pe", pe_ss1)], inc="vv")
        vv_var1 = p.val("vv")
        p.bank_consumed(7, "vv")
        p.step("scalar", lambda sc: sc.activation(st_ss[0:1, :], st_ss[0:1, :], AF.Ln,
                                                  bias=f32(EPS_LN)),
               waits=[("vv", vv_var1)], inc="sc")
        p.step("scalar", lambda sc: sc.activation(st_ss[0:1, :], st_ss[0:1, :], AF.Exp,
                                                  scale=f32(-0.5)), inc="sc")
        sc_r1 = p.val("sc")
        p.step("tensor", lambda t: t.matmul(pbf(0), c_ones_f[0:1, :], st_ss[0:1, :],
                                            start=True, stop=True),
               waits=[("sc", sc_r1)] + p.bank_w(0), inc="pe")
        pe_bc1 = p.val("pe")
        p.step("vector", lambda v: v.tensor_tensor(
            tile3(xin, TT), tile3(xn, TT), bcast_tile(0), OP.mult),
            waits=[("pe", pe_bc1), ("sc", sc_xsq)], inc="vv")
        vv_u1 = p.val("vv")
        p.bank_consumed(0, "vv")
        for ic in range(NC_D):
            p.step("scalar", (lambda sc, ic=ic: sc.activation(
                tile3(n_bf, TT)[:, ic, :], tile3(xin, TT)[:, ic, :], AF.Identity,
                bias=ln1b[:, ic:ic + 1], scale=ln1w[:, ic:ic + 1])),
                waits=([("vv", vv_u1)] + p.buf_w("n_bf")) if ic == 0 else [],
                inc="sc" if ic == NC_D - 1 else None)
        sc_n = p.val("sc")
        st["sc_n"][t_i] = sc_n
        p.buf_set("xin", ("sc", sc_n))
        p.buf_set("xn", ("dxn", st["dxn_t"][t_i]), ("pe", pe_ss1), ("pe", pe_log))
        p.buf_set("xsq", ("pe", pe_ss1))

    def emit_proj(t_i, cc):
        for ic in range(NC_D):
            w_first = ([("sc", st["sc_n"][t_i]), ("dwq", 16), ("dwk", 16), ("dwv", 16),
                        ("dwab", 16)] + p.bank_w(0, 1, 2, 3, 4, 5, 7)) if ic == 0 else []
            p.step("tensor", (lambda t, ic=ic, cc=cc: t.matmul(
                pb[0][:, :], tile3(n_bf, TT)[:, ic, cc * CC:(cc + 1) * CC],
                w3(wq_sb, D)[:, ic, 0:512],
                start=(ic == 0), stop=(ic == NC_D - 1))), waits=w_first, inc=None)
            p.step("tensor", (lambda t, ic=ic, cc=cc: t.matmul(
                pb[1][:, :], tile3(n_bf, TT)[:, ic, cc * CC:(cc + 1) * CC],
                w3(wq_sb, D)[:, ic, 512:1024],
                start=(ic == 0), stop=(ic == NC_D - 1))), inc=None)
            p.step("tensor", (lambda t, ic=ic, cc=cc: t.matmul(
                pb[2][:, :], tile3(n_bf, TT)[:, ic, cc * CC:(cc + 1) * CC],
                w3(wk_sb, D)[:, ic, 0:512],
                start=(ic == 0), stop=(ic == NC_D - 1))), inc=None)
            p.step("tensor", (lambda t, ic=ic, cc=cc: t.matmul(
                pb[3][:, :], tile3(n_bf, TT)[:, ic, cc * CC:(cc + 1) * CC],
                w3(wk_sb, D)[:, ic, 512:1024],
                start=(ic == 0), stop=(ic == NC_D - 1))), inc=None)
            p.step("tensor", (lambda t, ic=ic, cc=cc: t.matmul(
                pb[4][:, :], tile3(n_bf, TT)[:, ic, cc * CC:(cc + 1) * CC],
                w3(wv_sb, D)[:, ic, 0:512],
                start=(ic == 0), stop=(ic == NC_D - 1))), inc=None)
            p.step("tensor", (lambda t, ic=ic, cc=cc: t.matmul(
                pb[5][:, :], tile3(n_bf, TT)[:, ic, cc * CC:(cc + 1) * CC],
                w3(wv_sb, D)[:, ic, 512:1024],
                start=(ic == 0), stop=(ic == NC_D - 1))), inc=None)
            p.step("tensor", (lambda t, ic=ic, cc=cc: t.matmul(
                pb[7][:, 0:16], tile3(n_bf, TT)[:, ic, cc * CC:(cc + 1) * CC],
                w3(wab_sb, 16)[:, ic, :],
                start=(ic == 0), stop=(ic == NC_D - 1))),
                inc="pe" if ic == NC_D - 1 else None)
        pe_proj = p.val("pe")
        st["pe_proj"][(t_i, cc)] = pe_proj
        if cc == NCC - 1:
            p.buf_set("n_bf", ("pe", pe_proj))

        for dsti, (dst, bank0) in enumerate([(q_b4, 0), (k_b4, 2), (v_b4, 4)]):
            for half in range(2):
                p.step("scalar", (lambda sc, dst=dst, bank0=bank0, half=half, cc=cc:
                       sc.activation(
                           dst[:, cc * 1024 + half * 512:cc * 1024 + (half + 1) * 512],
                           pb[bank0 + half][:, :], AF.Silu)),
                       waits=([("pe", pe_proj)] + p.buf_w(f"qs{cc}"))
                       if (dsti == 0 and half == 0) else [],
                       inc="sc")
        sc_silu = p.val("sc")
        st["sc_silu"][(t_i, cc)] = sc_silu
        for b in (0, 1, 2, 3, 4, 5):
            p.bank_consumed(b, "sc")

        p.step("vector", (lambda v, cc=cc: v.tensor_copy(
            zab4[:, cc * 16:(cc + 1) * 16], pb[7][:, 0:16])),
            waits=[("pe", pe_proj)], inc="vv")
        st["vv_zab"][(t_i, cc)] = p.val("vv")
        p.bank_consumed(7, "vv")

        p.step("gpsimd", (lambda g, cc=cc: g.tensor_mul(
            sq_sc[:, :], q_b4[:, cc * 1024:(cc + 1) * 1024],
            q_b4[:, cc * 1024:(cc + 1) * 1024])),
            waits=[("sc", sc_silu), ("vv", st["vv_ssq_prev"])], inc="gp")
        p.step("vector", (lambda v, cc=cc: v.tensor_reduce(
            s4(ssq4_q, cc), h3(sq_sc), AX.X, OP.add)),
            waits=[("gp", p.val("gp")), ("sc", st["sc_lnexp_prev"])], inc="vv")
        p.step("gpsimd", (lambda g, cc=cc: g.tensor_mul(
            sq_sc[:, :], k_b4[:, cc * 1024:(cc + 1) * 1024],
            k_b4[:, cc * 1024:(cc + 1) * 1024])),
            waits=[("vv", p.val("vv"))], inc="gp")
        p.step("vector", (lambda v, cc=cc: v.tensor_reduce(
            s4(ssq4_k, cc), h3(sq_sc), AX.X, OP.add)),
            waits=[("gp", p.val("gp"))], inc="vv")
        st["vv_ssq"][(t_i, cc)] = p.val("vv")
        st["vv_ssq_prev"] = p.val("vv")

    def emit_lnexp_pre(t_i, ccs):
        for cc in ccs:
            p.step("scalar", (lambda sc, cc=cc: sc.activation(
                zab4[:, cc * 16:(cc + 1) * 16], zab4[:, cc * 16:(cc + 1) * 16],
                AF.Exp, scale=f32(-1.0))),
                waits=[("vv", st["vv_zab"][(t_i, cc)])], inc="sc")
            sc_abe = p.val("sc")
            p.step("vector", (lambda v, cc=cc: v.tensor_scalar(
                zab4[:, cc * 16:(cc + 1) * 16], zab4[:, cc * 16:(cc + 1) * 16],
                1.0, None, OP.add)),
                waits=[("sc", sc_abe)], inc="vv")
            p.step("vector", (lambda v, cc=cc: v.reciprocal(
                s4(beta4, cc), zab4[:, cc * 16:cc * 16 + 8])), inc="vv")
            vv_beta = p.val("vv")
            p.step("scalar", (lambda sc, cc=cc: sc.activation(
                s4(la8_4, cc), zab4[:, cc * 16 + 8:cc * 16 + 16], AF.Ln)),
                waits=[("vv", vv_beta)], inc="sc")
            st["sc_la"] = p.val("sc")

    def emit_lnexp(t_i, ccs):
        for cc in ccs:
            p.step("tensor", (lambda t, cc=cc: t.matmul(
                pb[6][:, 0:8], c_iu_f[:, :], s4(la8_4, cc), start=True, stop=True)),
                waits=[("sc", st["sc_la"])] + p.bank_w(6), inc=None)
            p.step("tensor", (lambda t, cc=cc: t.matmul(
                pb[6][:, 8:16], c_half_f[:, :], s4(la8_4, cc), start=True, stop=True)),
                inc=None)
            p.step("tensor", (lambda t, cc=cc: t.matmul(
                pb[6][:, 16:24], c_ones_f[:, :], s4(la8_4, cc), start=True, stop=True)),
                inc="pe")
            pe_cum = p.val("pe")
            p.step("vector", lambda v: v.tensor_copy(cumz[:, :], pb[6][:, 0:24]),
                   waits=[("pe", pe_cum)], inc="vv")
            p.bank_consumed(6, "vv")
            p.step("vector", lambda v: v.tensor_sub(dz[:, :], cumz[:, 8:16], cumz[:, 0:8]),
                   inc="vv")
            p.step("vector", lambda v: v.tensor_sub(f18[:, :], cumz[:, 8:16], cumz[:, 16:24]),
                   inc="vv")
            vv_dz = p.val("vv")
            p.step("scalar", (lambda sc, cc=cc: sc.activation(s4(lam4, cc), dz[:, :], AF.Exp)),
                   waits=[("vv", vv_dz)], inc="sc")
            p.step("scalar", (lambda sc, cc=cc: sc.activation(s4(ginv4, cc), dz[:, :],
                                                              AF.Exp, scale=f32(-1.0))),
                   inc="sc")
            p.step("scalar", (lambda sc, cc=cc: sc.activation(s4(lamc4, cc), f18[:, :],
                                                              AF.Exp)), inc="sc")
            p.step("scalar", (lambda sc, cc=cc: sc.activation(s4(gref4, cc), cumz[:, 8:16],
                                                              AF.Exp, scale=f32(-1.0))),
                   inc="sc")
            p.step("scalar", (lambda sc, cc=cc: sc.activation(rsq_q[:, :], s4(ssq4_q, cc),
                                                              AF.Ln, bias=f32(EPS_L2))),
                   waits=[("vv", st["vv_ssq"][(t_i, cc)]),
                          ("vv", st["vv_folds_prev"])], inc="sc")
            p.step("scalar", lambda sc: sc.activation(rsq_q[:, :], rsq_q[:, :], AF.Exp,
                                                      scale=f32(-0.5)), inc="sc")
            p.step("scalar", (lambda sc, cc=cc: sc.activation(rsq_k[:, :], s4(ssq4_k, cc),
                                                              AF.Ln, bias=f32(EPS_L2))),
                   inc="sc")
            p.step("scalar", lambda sc: sc.activation(rsq_k[:, :], rsq_k[:, :], AF.Exp,
                                                      scale=f32(-0.5)), inc="sc")
            sc_lx = p.val("sc")
            p.step("vector", (lambda v, cc=cc: v.tensor_mul(
                s4(fq4, cc), rsq_q[:, :], s4(lam4, cc))),
                waits=[("sc", sc_lx)], inc="vv")
            p.step("vector", (lambda v, cc=cc: v.tensor_copy(s4(frk4, cc), rsq_k[:, :])),
                   inc="vv")
            p.step("vector", (lambda v, cc=cc: v.tensor_mul(
                s4(fkb4, cc), rsq_k[:, :], s4(beta4, cc))), inc="vv")
            st["vv_folds_prev"] = p.val("vv")
            st["sc_lnexp"][(t_i, cc)] = p.val("sc")
        st["sc_lnexp_prev"] = p.val("sc")

    def emit_scalings(t_i, cc):
        par = cc % 2
        ccg = t_i * NCC + cc
        g_pe = st["pe_x0"].get(ccg - 1, 0)
        p.step("vector", (lambda v, cc=cc: v.tensor_tensor(
            h3(qt_bf), h4(q_b4, cc), bcast8s(fq4, cc), OP.mult)),
            waits=[("sc", st["sc_lnexp"][(t_i, cc)]), ("pe", g_pe)], inc="vv")
        p.step("vector", (lambda v, cc=cc: v.tensor_tensor(
            h3(kn_bf), h4(k_b4, cc), bcast8s(frk4, cc), OP.mult)), inc="vv")
        p.step("vector", (lambda v, cc=cc, par=par: v.tensor_tensor(
            h3(kb_bf[par]), h4(k_b4, cc), bcast8s(fkb4, cc), OP.mult)), inc="vv")
        p.step("vector", (lambda v, cc=cc: v.tensor_tensor(
            h3(vt_bf), h4(v_b4, cc), bcast8s(ginv4, cc), OP.mult)), inc="vv")
        st["vv_scal"][(t_i, cc)] = p.val("vv")
        p.buf_set(f"qs{cc}", ("vv", p.val("vv")))

    def emit_delta(t_i, cc):
        par = cc % 2
        ccg = t_i * NCC + cc
        vv_tm = st["vv_scal"][(t_i, cc)]
        p.step("vector", (lambda v, cc=cc: v.tensor_tensor(
            h3(sig_bf), h3(s_st), bcast8s(gref4, cc), OP.mult)),
            waits=[("gp", gp_szero), ("pe", st["pe_upd"].get(ccg - 1, 0))], inc="vv")
        vv_sig = p.val("vv")
        for srcb, bank in [(kn_bf, 0), (kb_bf[par], 1), (qt_bf, 2)]:
            for h in range(H):
                p.step("tensor", (lambda t, srcb=srcb, bank=bank, h=h: t.transpose(
                    pbb(bank)[:, h * DH:(h + 1) * DH], h3(srcb)[:, h, :], c_id_b[:, :])),
                    waits=([("vv", vv_tm)] + p.bank_w(bank)) if h == 0 else [],
                    inc="pe" if h == H - 1 else None)
        pe_tp = p.val("pe")
        st["pe_tp"][ccg] = pe_tp
        p.step("scalar", lambda sc: sc.copy(kfm[:, :], pbb(0)[:, :]),
               waits=[("pe", pe_tp)], inc="sc")
        p.step("scalar", lambda sc: sc.activation(nkfm[:, :], pbb(0)[:, :], AF.Identity,
                                                  scale=f32(-1.0)), inc="sc")
        sc_tpe = p.val("sc")
        p.step("vector", lambda v: v.tensor_copy(kbfm[:, :], pbb(1)[:, :]),
               waits=[("pe", pe_tp)], inc="vv")
        p.step("vector", lambda v: v.tensor_copy(qfm[:, :], pbb(2)[:, :]), inc="vv")
        vv_tpe = p.val("vv")
        p.bank_consumed(0, "sc")
        p.bank_consumed(1, "vv")
        p.bank_consumed(2, "vv")

        for h in range(H):
            p.step("tensor", (lambda t, h=h: t.matmul(
                pg(2 + h // 4)[:, h % 4, :], h3(kfm)[:, h, :], h3(kbfm)[:, h, :],
                start=True, stop=True)),
                waits=([("sc", sc_tpe), ("vv", vv_tpe)] + p.bank_w(2, 3)) if h == 0 else [],
                inc="pe" if h == H - 1 else None)
        for h in range(H):
            p.step("tensor", (lambda t, h=h: t.matmul(
                pg(4 + h // 4)[:, h % 4, :], h3(kbfm)[:, h, :], h3(kfm)[:, h, :],
                start=True, stop=True)),
                waits=p.bank_w(4, 5) if h == 0 else [],
                inc="pe" if h == H - 1 else None)
        for h in range(H):
            p.step("tensor", (lambda t, h=h: t.matmul(
                pg(6 + h // 4)[:, h % 4, :], h3(kbfm)[:, h, :], h3(qfm)[:, h, :],
                start=True, stop=True)),
                waits=p.bank_w(6, 7) if h == 0 else [],
                inc="pe" if h == H - 1 else None)
        pe_afam = p.val("pe")

        def mask_evac(dst, b0, cmask):
            def fn(e, dst=dst, b0=b0, cmask=cmask):
                i1 = e.tensor_tensor(h3(dst)[:, 0:4, :], pg(b0), maskb(cmask), OP.mult)
                i2 = e.tensor_tensor(h3(dst)[:, 4:8, :], pg(b0 + 1), maskb(cmask), OP.mult)
                return [i1, i2]
            return fn
        p.step("vector", mask_evac(q_cur, 2, c_negsl_b), waits=[("pe", pe_afam)], inc="vv")
        p.step("vector", mask_evac(p_cur, 4, c_negsu_b), inc="vv")
        p.step("vector", mask_evac(mt_bf, 6, c_iu_b), inc="vv")
        vv_q = p.val("vv")
        sc_p = 0
        p.bank_consumed(2, "vv")
        p.bank_consumed(3, "vv")
        p.bank_consumed(4, "vv")
        p.bank_consumed(5, "vv")
        p.bank_consumed(6, "vv")
        p.bank_consumed(7, "vv")

        for h in range(H):
            p.step("tensor", (lambda t, h=h: t.matmul(
                pg(h // 4)[:, h % 4, :], h3(nkfm)[:, h, :], h3(sig_bf)[:, h, :],
                start=True, stop=False)),
                waits=([("vv", vv_sig), ("sc", sc_tpe)] + p.bank_w(0, 1)) if h == 0 else [],
                inc=None)
            p.step("tensor", (lambda t, h=h: t.matmul(
                pg(h // 4)[:, h % 4, :], c_id_b[:, :], h3(vt_bf)[:, h, :],
                start=False, stop=True)),
                inc="pe" if h == H - 1 else None)
        pe_x0 = p.val("pe")
        st["pe_x0"][ccg] = pe_x0

        def x0_evac(v):
            i1 = v.tensor_copy(h3(x_a)[:, 0:4, :], pg(0))
            i2 = v.tensor_copy(h3(x_a)[:, 4:8, :], pg(1))
            return [i1, i2]
        p.step("vector", x0_evac, waits=[("pe", pe_x0)], inc="vv")
        vv_x = p.val("vv")
        p.bank_consumed(0, "vv")
        p.bank_consumed(1, "vv")

        xcur, xnxt = x_a, x_b
        pcur, pnxt = p_cur, p_nxt
        qcur, qnxt = q_cur, q_nxt
        vvA = vv_x
        vvB = vv_x
        scA = 0
        scB = 0
        scQA = 0
        vvQB = vv_q
        for lvl in range(LVLS):
            for g, (bank, h0) in enumerate([(0, 0), (1, 4)]):
                w0 = ([("vv", (vvA if g == 0 else vvB)), ("vv", vv_q),
                       ("sc", (scA if g == 0 else scB))] + p.bank_w(bank))
                for j in range(4):
                    p.step("tensor", (lambda t, bank=bank, j=j, h0=h0, pcur=pcur,
                           xcur=xcur: t.matmul(
                               pg(bank)[:, j, :], h3(pcur)[:, h0 + j, :],
                               h3(xcur)[:, h0 + j, :], start=True, stop=True)),
                        waits=w0 if j == 0 else [],
                        inc="pe" if j == 3 else None)
                pe_ap = p.val("pe")

                def xadd(v, bank=bank, h0=h0, xcur=xcur, xnxt=xnxt):
                    return v.tensor_add(h3(xnxt)[:, h0:h0 + 4, :], pg(bank),
                                        h3(xcur)[:, h0:h0 + 4, :])
                p.step("vector", xadd, waits=[("pe", pe_ap)], inc="vv")
                if g == 0:
                    vvA = p.val("vv")
                else:
                    vvB = p.val("vv")
                p.bank_consumed(bank, "vv")
            if lvl < LVLS - 1:
                for g, (bank, h0) in enumerate([(2, 0), (3, 4)]):
                    w0 = ([("sc", (scA if g == 0 else scB)),
                           ("sc", scQA if g == 0 else 0),
                           ("vv", vvQB if g == 1 else 0), ("vv", vv_q)] +
                          p.bank_w(bank))
                    for j in range(4):
                        p.step("tensor", (lambda t, bank=bank, j=j, h0=h0, qcur=qcur,
                               pcur=pcur: t.matmul(
                                   pg(bank)[:, j, :], h3(qcur)[:, h0 + j, :],
                                   h3(pcur)[:, h0 + j, :], start=True, stop=True)),
                            waits=w0 if j == 0 else [],
                            inc="pe" if j == 3 else None)
                    pe_p = p.val("pe")

                    def pevac(sc, bank=bank, h0=h0, pnxt=pnxt):
                        return sc.copy(h3(pnxt)[:, h0:h0 + 4, :], pg(bank))
                    p.step("scalar", pevac, waits=[("pe", pe_p)], inc="sc")
                    if g == 0:
                        scA = p.val("sc")
                    else:
                        scB = p.val("sc")
                    p.bank_consumed(bank, "sc")
            if lvl < LVLS - 2:
                for g, (bank, h0) in enumerate([(4, 0), (5, 4)]):
                    for j in range(4):
                        p.step("tensor", (lambda t, bank=bank, j=j, h0=h0, qcur=qcur,
                               pcur=pcur: t.matmul(
                                   pg(bank)[:, j, :], h3(pcur)[:, h0 + j, :],
                                   h3(qcur)[:, h0 + j, :], start=True, stop=True)),
                            waits=p.bank_w(bank) if j == 0 else [],
                            inc="pe" if j == 3 else None)
                    pe_q = p.val("pe")
                    if g == 0:
                        def qevacA(sc, h0=h0, qnxt=qnxt):
                            return sc.copy(h3(qnxt)[:, h0:h0 + 4, :], pg(4))
                        p.step("scalar", qevacA, waits=[("pe", pe_q)], inc="sc")
                        scQA = p.val("sc")
                        p.bank_consumed(4, "sc")
                    else:
                        def qevacB(v, h0=h0, qnxt=qnxt):
                            return v.tensor_copy(h3(qnxt)[:, h0:h0 + 4, :], pg(5))
                        p.step("vector", qevacB, waits=[("pe", pe_q)], inc="vv")
                        vvQB = p.val("vv")
                        p.bank_consumed(5, "vv")
            xcur, xnxt = xnxt, xcur
            if lvl < LVLS - 1:
                pcur, pnxt = pnxt, pcur
            if lvl < LVLS - 2:
                qcur, qnxt = qnxt, qcur
        rbar = xcur
        vv_rbar = max(vvA, vvB)

        for h in range(H):
            p.step("tensor", (lambda t, h=h: t.matmul(
                pg(2 + h // 4)[:, h % 4, :], h3(sig_bf)[:, h, :], h3(qfm)[:, h, :],
                start=True, stop=False)),
                waits=([("vv", vv_rbar)] + p.bank_w(2, 3)) if h == 0 else [],
                inc=None)
            p.step("tensor", (lambda t, h=h, rbar=rbar: t.matmul(
                pg(2 + h // 4)[:, h % 4, :], h3(rbar)[:, h, :], h3(mt_bf)[:, h, :],
                start=False, stop=True)),
                inc="pe" if h == H - 1 else None)
        pe_o = p.val("pe")
        for h in range(H):
            p.step("tensor", (lambda t, h=h: t.matmul(
                pg(4 + h // 4)[:, h % 4, :], c_id_b[:, :], h3(sig_bf)[:, h, :],
                start=True, stop=False)),
                waits=p.bank_w(4, 5) if h == 0 else [], inc=None)
            p.step("tensor", (lambda t, h=h, rbar=rbar, par=par: t.matmul(
                pg(4 + h // 4)[:, h % 4, :], h3(kb_bf[par])[:, h, :], h3(rbar)[:, h, :],
                start=False, stop=True)),
                inc="pe" if h == H - 1 else None)
        pe_upd = p.val("pe")
        st["pe_upd"][ccg] = pe_upd

        def oevac(sc, cc=cc):
            o3 = o_fm[:, :].rearrange("p (h n) -> p h n", n=TT)
            i1 = sc.copy(o3[:, 0:4, cc * CC:(cc + 1) * CC], pg(2))
            i2 = sc.copy(o3[:, 4:8, cc * CC:(cc + 1) * CC], pg(3))
            return [i1, i2]
        p.step("scalar", oevac,
               waits=[("pe", pe_o)] + (p.buf_w("o_fm") if cc == 0 else []), inc="sc")
        p.bank_consumed(2, "sc")
        p.bank_consumed(3, "sc")

        def sevac(v, cc=cc):
            i1 = v.tensor_tensor(h3(s_st)[:, 0:4, :], pg(4), bcast8s(lamc4, cc)[:, 0:4, :],
                                 OP.mult)
            i2 = v.tensor_tensor(h3(s_st)[:, 4:8, :], pg(5), bcast8s(lamc4, cc)[:, 4:8, :],
                                 OP.mult)
            return [i1, i2]
        p.step("vector", sevac, waits=[("pe", pe_upd)], inc="vv")
        p.bank_consumed(4, "vv")
        p.bank_consumed(5, "vv")

    def emit_wo(t_i):
        tk = t_i * TT
        sc_oall = p.val("sc")
        for half in range(2):
            for oc in range(4):
                for hv in range(NC_D):
                    p.step("tensor", (lambda t, half=half, oc=oc, hv=hv: t.matmul(
                        pb[oc][:, :],
                        w3(wo_sb, D)[:, hv, (half * 4 + oc) * 128:(half * 4 + oc + 1) * 128],
                        o_fm[:, :].rearrange("p (h n) -> p h n", n=TT)[:, hv, :],
                        start=(hv == 0), stop=(hv == NC_D - 1))),
                        waits=([("sc", sc_oall), ("dwo", 16)] + p.bank_w(0, 1, 2, 3))
                        if (hv == 0 and oc == 0) else [],
                        inc="pe" if hv == NC_D - 1 else None)
            pe_woh = p.val("pe")
            vv_y1 = 0
            for q in range(2):
                c0 = half * 4 + 2 * q
                p.dma("sync", (lambda s, tk=tk, c0=c0: s.dma_start(
                    out=tile3(xnwo, TT),
                    in_=dram3(xn32_dram)[:, c0:c0 + 2, tk:tk + TT])),
                    waits=[("dxn", st["dxn_t"][t_i])] + p.buf_w("xnwo"), inc="dxw")
                ds_xw = p.val("dxw")

                def y1add(v, q=q):
                    i1 = v.tensor_add(tile3(y1h, TT)[:, 2 * q, :], pb[2 * q][:, :],
                                      tile3(xnwo, TT)[:, 0, :])
                    i2 = v.tensor_add(tile3(y1h, TT)[:, 2 * q + 1, :], pb[2 * q + 1][:, :],
                                      tile3(xnwo, TT)[:, 1, :])
                    return [i1, i2]
                p.step("vector", y1add,
                       waits=[("pe", pe_woh), ("dxw", ds_xw)] + p.buf_w("y1h"), inc="vv")
                vv_y1 = p.val("vv")
                p.buf_set("xnwo", ("vv", vv_y1))
                p.bank_consumed(2 * q, "vv")
                p.bank_consumed(2 * q + 1, "vv")
            p.step("scalar", (lambda sc: sc.activation(
                xsq[:, 0:4 * TT], y1h[:, :], AF.Square)),
                waits=[("vv", vv_y1)] + p.buf_w("xsq"), inc="sc")
            sc_sq = p.val("sc")
            for ic in range(4):
                p.step("tensor", (lambda t, ic=ic, half=half: t.matmul(
                    pb[7][0:1, 0:TT], c_onescol_b[:, 0:1], tile3(xsq, TT)[:, ic, :],
                    start=(ic == 0 and half == 0),
                    stop=(ic == 3 and half == 1))),
                    waits=([("sc", sc_sq)] + (p.bank_w(7) if half == 0 else []))
                    if ic == 0 else [],
                    inc="pe" if ic == 3 else None)
            pe_ss2 = p.val("pe")
            p.dma("sync", (lambda s, tk=tk, half=half: s.dma_start(
                out=dram3(y1_dram)[:, half * 4:(half + 1) * 4, tk:tk + TT],
                in_=tile3(y1h, TT))),
                waits=[("vv", vv_y1)], inc="dy1")
            p.buf_set("y1h", ("dy1", p.val("dy1")), ("sc", sc_sq))
            p.buf_set("xsq", ("pe", pe_ss2))
        for hv in range(NC_D):
            p.step("tensor", (lambda t, hv=hv: t.matmul(
                pb[6][0:1, 0:TT], wo1_bf[:, hv:hv + 1],
                o_fm[:, :].rearrange("p (h n) -> p h n", n=TT)[:, hv, :],
                start=(hv == 0), stop=(hv == NC_D - 1))),
                waits=([("vv", vv_wo1), ("sc", sc_oall)] + p.bank_w(6)) if hv == 0 else [],
                inc="pe" if hv == NC_D - 1 else None)
        st["pe_mu2_t"] = p.val("pe")
        st["pe_ss2_t"] = pe_ss2
        p.buf_set("o_fm", ("pe", p.val("pe")))

    def emit_ln2fin(t_i):
        tk = t_i * TT
        p.step("vector", lambda v: v.tensor_scalar(
            st_ss[0:1, :], pb[6][0:1, 0:TT], 1.0 / D, None, OP.mult),
            waits=[("pe", st["pe_mu2_t"]), ("dst2", st["dst2_prev"])], inc="vv")
        p.step("vector", lambda v: v.tensor_scalar(
            st_mu0[0:1, :], pb[7][0:1, 0:TT], 1.0 / D, None, OP.mult),
            waits=[("pe", st["pe_ss2_t"])], inc="vv")
        p.bank_consumed(6, "vv")
        p.bank_consumed(7, "vv")
        p.step("vector", lambda v: v.tensor_mul(xnwo[0:1, 0:TT], st_ss[0:1, :],
                                                st_ss[0:1, :]), inc="vv")
        p.step("vector", lambda v: v.tensor_sub(st_mu0[0:1, :], st_mu0[0:1, :],
                                                xnwo[0:1, 0:TT]), inc="vv")
        vv_var2 = p.val("vv")
        p.buf_set("xnwo", ("vv", vv_var2))
        p.step("scalar", lambda sc: sc.activation(st_mu0[0:1, :], st_mu0[0:1, :], AF.Ln,
                                                  bias=f32(EPS_LN)),
               waits=[("vv", vv_var2)], inc="sc")
        p.step("scalar", lambda sc: sc.activation(st_mu0[0:1, :], st_mu0[0:1, :], AF.Exp,
                                                  scale=f32(-0.5)), inc="sc")
        sc_r2 = p.val("sc")
        p.step("vector", lambda v: v.tensor_mul(st_ss[0:1, :], st_ss[0:1, :],
                                                st_mu0[0:1, :]),
               waits=[("sc", sc_r2)], inc="vv")
        vv_st2 = p.val("vv")
        p.dma("sync", (lambda s, tk=tk: s.dma_start(out=st2_dram[0:1, tk:tk + TT],
                                                    in_=st_ss[0:1, :])),
              waits=[("vv", vv_st2)], inc="dst2")
        p.dma("sync", (lambda s, tk=tk: s.dma_start(out=st2_dram[1:2, tk:tk + TT],
                                                    in_=st_mu0[0:1, :])), inc="dst2")
        st["dst2_prev"] = p.val("dst2")

    emit_xload(0)
    emit_lnhead(0)
    for cc in range(NCC):
        emit_proj(0, cc)
    emit_lnexp_pre(0, [0, 1, 2, 3])
    emit_lnexp(0, [0, 1, 2, 3])
    for t_i in range(NT):
        nxt = t_i + 1
        if nxt < NT:
            emit_xload(nxt)
        emit_scalings(t_i, 0)
        emit_delta(t_i, 0)
        if nxt < NT:
            emit_lnhead(nxt)
        emit_scalings(t_i, 1)
        emit_delta(t_i, 1)
        if nxt < NT:
            emit_proj(nxt, 0)
            emit_proj(nxt, 1)
        emit_scalings(t_i, 2)
        emit_delta(t_i, 2)
        if nxt < NT:
            emit_proj(nxt, 2)
        emit_scalings(t_i, 3)
        if nxt < NT:
            emit_proj(nxt, 3)
            emit_lnexp_pre(nxt, [0, 1, 2])
        emit_delta(t_i, 3)
        if nxt < NT:
            emit_lnexp(nxt, [0, 1, 2])
        emit_wo(t_i)
        emit_ln2fin(t_i)
        if nxt < NT:
            emit_lnexp_pre(nxt, [3])
            emit_lnexp(nxt, [3])
        if t_i == (NT - 2 if EARLY_CC else NT - 1):
            p.dma("gpsimd", lambda g: g.dma_start(out=cnt_in, in_=run_rep[0:1, 0:8]),
                  waits=[("vv", p.val("vv"))], inc="dcnt")
            p.step("gpsimd", lambda g: g.collective_compute(
                "AllGather", OP.bypass, replica_groups=[list(range(8))],
                ins=[cnt_in.opt()], outs=[cnt_all.opt()]),
                waits=[("dcnt", p.val("dcnt"))], inc="cc")
            p.dma("gpsimd", lambda g: g.dma_start(out=cnt_sb[0:1, 0:64],
                                                  in_=cnt_all[0:8, 0:8]),
                  waits=[("cc", p.val("cc"))], inc="dcall")

    dg_xn_all = p.val("dxn")
    pe_afinal = p.val("pe")
    vv_afinal = p.val("vv")
    sc_afinal = p.val("sc")
    alias_w = [("pe", pe_afinal), ("vv", vv_afinal), ("sc", sc_afinal),
               ("dst2", p.val("dst2")), ("dy1", p.val("dy1")),
               ("dxw", p.val("dxw")), ("dx", p.val("dx"))]

    for ccg in range(32):
        p.step("scalar", (lambda sc, ccg=ccg: sc.activation(
            r_l[:, ccg * 8:(ccg + 1) * 8], r_l[:, ccg * 8:(ccg + 1) * 8], AF.Exp,
            bias=r_negm[:, ccg:ccg + 1])),
            waits=[("vv", vv_afinal)] if ccg == 0 else [], inc="sc")
    sc_exp = p.val("sc")
    dg_call = p.val("dcall")
    p.step("vector", lambda v: v.tensor_tensor(
        base_row[0:1, 8:72].rearrange("p (e j) -> p e j", j=8),
        cnt_sb[0:1, 0:64].rearrange("p (j e) -> p j e", e=8).transpose([0, 2, 1]),
        coremask[0:1, 0:8].unsqueeze(1).broadcast_to([1, 8, 8]),
        OP.mult), waits=[("dcall", dg_call)], inc="vv")
    p.step("vector", lambda v: v.reduce_sum(
        base_row[0:1, 0:8], base_row[0:1, 8:72].rearrange("p (e j) -> p e j", j=8), AX.X),
        inc="vv")
    vv_base = p.val("vv")

    for gq in range(8):
        p.dma("gpsimd", (lambda g, gq=gq: g.dma_start(
            out=w3(w1_sb, FF)[:, :, gq * 512:(gq + 1) * 512],
            in_=dram3(d["w1"])[:, :, gq * 512:(gq + 1) * 512])),
            waits=alias_w if gq == 0 else [], inc="dw1" + "abcd"[gq // 2])
    for gq in range(8):
        p.dma("gpsimd", (lambda g, gq=gq: g.dma_start(
            out=w3(w2_sb, D)[:, :, gq * 128:(gq + 1) * 128],
            in_=dram3(d["w2"])[:, :, gq * 128:(gq + 1) * 128])),
            inc="dw2" + "ab"[gq // 4])
    for gh in range(2):
        p.dma("gpsimd", (lambda g, gh=gh: g.dma_start(
            out=w3(wexp_sb, D)[:, :, gh * 512:(gh + 1) * 512],
            in_=dram3(d["wexp"])[:, :, gh * 512:(gh + 1) * 512])),
            inc="dwex" + "ab"[gh])

    NSUB = T // SUB

    def emit_bload(su, first=False):
        par = su % 2
        j0 = su * SUB
        p.dma("sync", (lambda s, j0=j0, par=par: s.dma_start(
            out=tile3(by1[par], SUB), in_=dram3(y1_dram)[:, :, j0:j0 + SUB])),
            waits=(alias_w if first else []) + p.buf_w(f"by1{par}") +
                  [("dy1", p.val("dy1"))], inc="dsb")
        p.dma("gpsimd", (lambda g, j0=j0, par=par: g.dma_start(
            out=tile3(bxnb[par], SUB), in_=dram3(xn32_dram)[:, :, j0:j0 + SUB])),
            waits=p.buf_w(f"bxnb{par}") + [("dxn", dg_xn_all)], inc="dsg")
        p.dma("sync", (lambda s, j0=j0, par=par: s.dma_start(
            out=bst2[par][0:2, :], in_=st2_dram[0:2, j0:j0 + SUB])),
            waits=[("dst2", p.val("dst2"))] + p.buf_w(f"bst{par}"), inc="dsb")
        return (p.val("dsb"), p.val("dsg"))

    ds_b = [(0, 0)] * (NSUB + 1)
    ds_b[0] = emit_bload(0, first=True)
    ds_b[1] = emit_bload(1, first=True)
    p.dma("sync", lambda s: s.dma_start(out=base_dram, in_=base_row[0:1, 0:8]),
          waits=[("vv", vv_base)], inc="dbs")
    p.dma("sync", lambda s: s.dma_start(
        out=base_rep[:, :],
        in_=base_dram.unsqueeze(0).broadcast_to([128, 1, 8]).rearrange("p a e -> p (a e)")),
        waits=[("dbs", p.val("dbs"))], inc="dbb")
    dg_bb = p.val("dbb")
    for su in range(NSUB):
        par = su % 2
        j0 = su * SUB

        p.step("tensor", (lambda t, par=par: t.matmul(
            pb[6][:, 0:SUB], c_sel[0:2, 128:256], bst2[par][0:2, :],
            start=True, stop=True)),
            waits=[("dsb", ds_b[su][0])] + p.bank_w(6), inc=None)
        p.step("tensor", (lambda t, par=par: t.matmul(
            pb[6][:, SUB:2 * SUB], c_sel[0:2, 0:128], bst2[par][0:2, :],
            start=True, stop=True)), inc="pe")
        pe_bc2 = p.val("pe")
        p.step("vector", (lambda v, par=par: v.tensor_tensor(
            tile3(bout, SUB), tile3(by1[par], SUB),
            pb[6][:, 0:SUB].unsqueeze(1).broadcast_to([128, NC_D, SUB]), OP.mult)),
            waits=[("pe", pe_bc2)] + p.buf_w("bout"), inc="vv")
        p.step("vector", lambda v: v.tensor_tensor(
            tile3(bout, SUB), tile3(bout, SUB),
            pb[6][:, SUB:2 * SUB].unsqueeze(1).broadcast_to([128, NC_D, SUB]), OP.subtract),
            inc="vv")
        vv_u2 = p.val("vv")
        p.bank_consumed(6, "vv")
        for ic in range(NC_D):
            p.step("scalar", (lambda sc, ic=ic: sc.activation(
                tile3(bh2, SUB)[:, ic, :], tile3(bout, SUB)[:, ic, :], AF.Identity,
                bias=ln2b[:, ic:ic + 1], scale=ln2w[:, ic:ic + 1])),
                waits=([("vv", vv_u2)] + p.buf_w("bh2")) if ic == 0 else [],
                inc="sc" if ic == NC_D - 1 else None)
        sc_h2 = p.val("sc")
        p.buf_set(f"bst{par}", ("pe", pe_bc2))

        for g4 in range(8):
            bp = (g4 % 2) * 2
            for j in range(4):
                oc = g4 * 4 + j
                bank = bp + j // 2
                colr = (j % 2) * SUB
                for ic in range(NC_D):
                    p.step("tensor", (lambda t, oc=oc, bank=bank, colr=colr, ic=ic: t.matmul(
                        pb[bank][:, colr:colr + SUB],
                        w3(w1_sb, FF)[:, ic, oc * 128:(oc + 1) * 128],
                        tile3(bh2, SUB)[:, ic, :],
                        start=(ic == 0), stop=(ic == NC_D - 1))),
                        waits=([("sc", sc_h2), ("dw1" + "abcd"[g4 // 2], 32)] +
                               p.bank_w(bp, bp + 1))
                        if (ic == 0 and j == 0) else [],
                        inc="pe" if (ic == NC_D - 1 and j == 3) else None)
            pe_g = p.val("pe")
            for j in range(4):
                oc = g4 * 4 + j
                bank = bp + j // 2
                colr = (j % 2) * SUB
                p.step("scalar", (lambda sc, oc=oc, bank=bank, colr=colr: sc.activation(
                    tile3(bg, SUB)[:, oc, :], pb[bank][:, colr:colr + SUB], AF.Gelu,
                    bias=b1_sb[:, oc:oc + 1])),
                    waits=([("pe", pe_g)] + (p.buf_w("bg") if oc == 0 else [])), inc="sc")
            p.bank_consumed(bp, "sc")
            p.bank_consumed(bp + 1, "sc")
        sc_gelu = p.val("sc")

        for q in range(2):
            ccg = j0 // CC + q
            p.step("vector", (lambda v, ccg=ccg: v.reduce_sum(
                m1c[:, :], r_l[:, ccg * 8:(ccg + 1) * 8], AX.X)),
                waits=([("sc", sc_exp), ("dbb", dg_bb)] if (su == 0 and q == 0) else []),
                inc="vv")
            p.step("vector", (lambda v: v.reciprocal(t8a[:, 0:1], m1c[:, :])), inc="vv")
            p.step("vector", (lambda v, ccg=ccg: v.tensor_tensor(
                t8b[:, :], r_ohf[:, ccg * 8:(ccg + 1) * 8], base_rep[:, :], OP.mult)),
                inc="vv")
            p.step("vector", (lambda v: v.reduce_sum(m1c[:, :], t8b[:, :], AX.X)), inc="vv")
            p.step("vector", (lambda v, ccg=ccg: v.tensor_add(
                m1c[:, :], m1c[:, :], r_c[:, ccg:ccg + 1])), inc="vv")
            p.step("vector", (lambda v: v.tensor_scalar(
                m1c[:, :], m1c[:, :], 101.0, None, OP.is_lt)), inc="vv")
            p.step("vector", (lambda v, ccg=ccg: v.tensor_mul(
                r_g[:, ccg:ccg + 1], t8a[:, 0:1], m1c[:, :])), inc="vv")
            p.step("vector", (lambda v, ccg=ccg: v.tensor_scalar(
                bdiag[:, :], c_id_f[:, :], r_g[:, ccg:ccg + 1], None, OP.mult)),
                waits=[("pe", st.get("pe_gate_prev", 0) if q == 0 else p.val("pe"))],
                inc="vv")
            vv_dg2 = p.val("vv")
            p.step("tensor", (lambda t, q=q: t.matmul(
                pb[6][:, q * 128:(q + 1) * 128], c_ones_f[:, :], bdiag[:, :],
                start=True, stop=True)),
                waits=[("vv", vv_dg2)] + (p.bank_w(6) if q == 0 else []), inc="pe")
        pe_gb = p.val("pe")
        st["pe_gate_prev"] = pe_gb
        p.step("vector", lambda v: v.tensor_copy(bgb[:, :], pb[6][:, 0:SUB]),
               waits=[("pe", pe_gb)], inc="vv")
        vv_gb = p.val("vv")
        p.bank_consumed(6, "vv")


        for g4 in range(2):
            bp = 4 + g4 * 2
            for j in range(4):
                oc = g4 * 4 + j
                bank = bp + j // 2
                colr = (j % 2) * SUB
                for ic2 in range(NC_F):
                    p.step("tensor", (lambda t, oc=oc, bank=bank, colr=colr, ic2=ic2: t.matmul(
                        pb[bank][:, colr:colr + SUB],
                        w3(w2_sb, D)[:, ic2, oc * 128:(oc + 1) * 128],
                        tile3(bg, SUB)[:, ic2, :],
                        start=(ic2 == 0), stop=(ic2 == NC_F - 1))),
                        waits=([("sc", sc_gelu), ("dw2" + "ab"[g4], 64)] +
                               p.bank_w(bp, bp + 1))
                        if (ic2 == 0 and j == 0) else [],
                        inc="pe" if (ic2 == NC_F - 1 and j == 3) else None)
            pe_y2 = p.val("pe")
            for j in range(4):
                oc = g4 * 4 + j
                bank = bp + j // 2
                colr = (j % 2) * SUB
                p.step("vector", (lambda v, oc=oc, bank=bank, colr=colr, par=par:
                       v.scalar_tensor_tensor(
                           tile3(by1[par], SUB)[:, oc, :], pb[bank][:, colr:colr + SUB],
                           b2_sb[:, oc:oc + 1], tile3(by1[par], SUB)[:, oc, :],
                           OP.add, OP.add)),
                    waits=[("pe", pe_y2)], inc="vv")
            p.bank_consumed(bp, "vv")
            p.bank_consumed(bp + 1, "vv")
        vv_s1 = p.val("vv")

        for g4 in range(2):
            bp = g4 * 2
            for j in range(4):
                oc = g4 * 4 + j
                bank = bp + j // 2
                colr = (j % 2) * SUB
                for ic in range(NC_D):
                    p.step("tensor", (lambda t, oc=oc, bank=bank, colr=colr, ic=ic, par=par:
                           t.matmul(
                               pb[bank][:, colr:colr + SUB],
                               w3(wexp_sb, D)[:, ic, oc * 128:(oc + 1) * 128],
                               tile3(bxnb[par], SUB)[:, ic, :],
                               start=(ic == 0), stop=(ic == NC_D - 1))),
                        waits=([("dwex" + "ab"[g4], 16), ("dsg", ds_b[su][1])] +
                               p.bank_w(bp, bp + 1))
                        if (ic == 0 and j == 0) else [],
                        inc="pe" if (ic == NC_D - 1 and j == 3) else None)
            pe_m = p.val("pe")
            for j in range(4):
                oc = g4 * 4 + j
                bank = bp + j // 2
                colr = (j % 2) * SUB
                p.step("vector", (lambda v, oc=oc, bank=bank, colr=colr: v.scalar_tensor_tensor(
                    tile3(bout, SUB)[:, oc, :], pb[bank][:, colr:colr + SUB],
                    bexp_sb[:, oc:oc + 1], bgb[:, :], OP.add, OP.mult)),
                    waits=([("pe", pe_m), ("vv", vv_gb), ("sc", sc_h2)]
                           if (g4 == 0 and j == 0) else [("pe", pe_m)]),
                    inc="vv")
            p.bank_consumed(bp, "vv")
            p.bank_consumed(bp + 1, "vv")
        vv_m1 = p.val("vv")
        p.buf_set(f"bxnb{par}", ("pe", p.val("pe")))

        p.step("gpsimd", (lambda g, par=par: g.tensor_add(
            by1[par][:, :], by1[par][:, :], bout[:, :])),
            waits=[("vv", max(vv_m1, vv_s1))], inc="gp")
        gp_out = ("gp", p.val("gp"))
        p.dma("gpsimd", (lambda g, j0=j0, par=par: g.dma_start(
            out=tile3(by1[par], SUB), in_=dram3(d["xt"])[:, :, j0:j0 + SUB],
            accum_op=OP.add)),
            waits=[gp_out], inc="dxa")
        dg_xadd = p.val("dxa")
        p.dma("sync", (lambda s, j0=j0, par=par: s.dma_start(
            out=dram3(d["out"])[:, :, j0:j0 + SUB], in_=tile3(by1[par], SUB))),
            waits=[("dxa", dg_xadd)], inc="st")
        p.buf_set(f"by1{par}", ("st", p.val("st")))
        p.buf_set("bout", gp_out)
        p.buf_set("bh2", ("pe", p.val("pe")))
        p.buf_set("bg", ("pe", p.val("pe")))
        if su + 2 <= NSUB - 1:
            ds_b[su + 2] = emit_bload(su + 2)

    if DEBUG_DUMP:
        d["dbg_st2"] = nc.dram_tensor("dbg_st2", [2, T], F32, kind="ExternalOutput").ap()
        d["dbg_y1"] = nc.dram_tensor("dbg_y1", [D, T], F32, kind="ExternalOutput").ap()
        d["dbg_xn"] = nc.dram_tensor("dbg_xn", [D, T], F32, kind="ExternalOutput").ap()
        p.dma("sync", lambda s: s.dma_start(out=d["dbg_st2"], in_=st2_dram[:, :]),
              waits=[("st", p.val("st"))], inc="st")
        p.dma("sync", lambda s: s.dma_start(out=d["dbg_y1"], in_=y1_dram[:, :]), inc="st")
        p.dma("sync", lambda s: s.dma_start(out=d["dbg_xn"], in_=xn32_dram[:, :]), inc="st")
    p.step("sync", lambda s: None, waits=[("st", p.val("st"))])

    _BUILD_CACHE["dbg"] = dict(r_g=r_g, r_c=r_c, r_ohf=r_ohf, r_l=r_l, r_negm=r_negm,
                               run_rep=run_rep, base_rep=base_rep, cnt_sb=cnt_sb,
                               base_row=base_row)
    with nc.Block() as blk:
        _emit(p, sems, blk, hazard_scan=_hazard_scan, hazard_fix=_hazard_fix)
    es_sem.close()
    es_persist.close()
    return nc, p


def build_fixed():
    scan = {}
    nc1, p1 = build_program(_hazard_scan=scan)
    del nc1
    hz_writers = set()
    fix = {}
    for eng in ("vector", "gpsimd"):
        flagged = set()
        for (e, rsi), ws in scan.items():
            if e == eng:
                flagged.update(ws)
        if not flagged:
            continue
        for si in flagged:
            fn, waits, inc, inc_by = p1.steps[eng][si]
            if inc is None:
                hz_writers.add((eng, si))
        cnt = {}
        vals = {}
        for si, (fn, waits, inc, inc_by) in enumerate(p1.steps[eng]):
            if inc is None and (eng, si) in hz_writers:
                inc, inc_by = "hz_" + eng, 1
            if inc is not None:
                cnt[inc] = cnt.get(inc, 0) + inc_by
                vals[si] = (inc, cnt[inc])
            else:
                vals[si] = None
        for (e, rsi), ws in scan.items():
            if e != eng:
                continue
            for wsi in ws:
                wv = vals.get(wsi)
                if wv is not None:
                    fix.setdefault((eng, rsi), []).append(wv)
    nc2, _ = build_program(_hazard_fix=fix, _hz_writers=hz_writers)
    return nc2


def _consts():
    i = np.arange(128)
    iu = (i[:, None] <= i[None, :]).astype(np.float32)
    half = np.zeros((128, 128), np.float32)
    half[0:64, :] = 1.0
    ones = np.ones((128, 128), np.float32)
    ident = np.eye(128, dtype=np.float32)
    sl = (i[:, None] > i[None, :]).astype(np.float32)
    su = (i[:, None] < i[None, :]).astype(np.float32)
    return {
        "d_c_iu_f": iu, "d_c_half_f": half, "d_c_ones_f": ones, "d_c_id_f": ident,
        "d_c_negsl_b": -sl, "d_c_negsu_b": -su, "d_c_iu_b": iu, "d_c_id_b": ident,
        "d_c_ones_b": ones,
        "d_c_onescol": np.ones((128, 1), np.float32),
        "d_c_sel": np.concatenate([np.repeat(np.eye(2, dtype=np.float32)[:, 0:1], 128, 1),
                                   np.repeat(np.eye(2, dtype=np.float32)[:, 1:2], 128, 1)],
                                  axis=1),
        "d_c_iota999": np.broadcast_to(
            (np.arange(32) % 8 + 16.0).astype(np.float32), (128, 32)).copy(),
    }


def kernel(**inputs):
    if "prog" not in _BUILD_CACHE:
        _BUILD_CACHE["prog"] = build_fixed()
    nc = _BUILD_CACHE["prog"]

    x = np.asarray(inputs["L0_attn_x"])

    def col128(vec, ncol):
        return np.ascontiguousarray(np.asarray(vec).reshape(ncol, 128).T)

    shared = {
        "wq": np.asarray(inputs["L0_attn_Wq"]),
        "wk": np.asarray(inputs["L0_attn_Wk"]),
        "wv": np.asarray(inputs["L0_attn_Wv"]),
        "wo": np.asarray(inputs["L0_attn_Wo"]),
        "w1": np.asarray(inputs["L0_attn_W1"]),
        "w2": np.asarray(inputs["L0_attn_W2"]),
        "wexp": np.asarray(inputs["L0_ffn_w_expert"]),
        "wg": np.asarray(inputs["L0_ffn_w_gate"]),
        "wab": np.ascontiguousarray(np.concatenate(
            [np.asarray(inputs["L0_attn_Wb"]), np.asarray(inputs["L0_attn_Wa"])], axis=1)),
        "d_ln1w": col128(inputs["L0_attn_ln1_w"], 8),
        "d_ln1b": col128(inputs["L0_attn_ln1_b"], 8),
        "d_ln2w": col128(inputs["L0_attn_ln2_w"], 8),
        "d_ln2b": col128(inputs["L0_attn_ln2_b"], 8),
        "d_b1": col128(inputs["L0_attn_b1"], 32),
        "d_b2": col128(inputs["L0_attn_b2"], 8),
        "d_bexp": col128(inputs["L0_ffn_b_expert"], 8),
    }
    shared.update(_consts())

    in_maps = []
    for i in range(8):
        m = dict(shared)
        m["xt"] = np.ascontiguousarray(x[i].T)
        m["d_coremask"] = (np.arange(8) < i).astype(np.float32).reshape(1, 8)
        in_maps.append(m)

    res = run_bass_kernel_spmd(nc, in_maps, core_ids=list(range(8)))
    out = np.stack([np.ascontiguousarray(np.asarray(res.results[i]["out_t"]).T)
                    for i in range(8)])
    return out.astype(np.float32)


# version 14
# speedup vs baseline: 1.1196x; 1.0188x over previous
import numpy as np
from contextlib import ExitStack

import concourse.bass as bass
import concourse.mybir as mybir
from concourse.bass_utils import run_bass_kernel_spmd

F32 = mybir.dt.float32
BF16 = mybir.dt.bfloat16
AF = mybir.ActivationFunctionType
OP = mybir.AluOpType
AX = mybir.AxisListType

f32 = np.float32

T = 4096
D = 1024
H = 8
DH = 128
FF = 4096
NT = 8
TT = 512
NCC = 4
CC = 128
NC_D = 8
NC_F = 32
SUB = 256
EPS_LN = 1e-5
EPS_L2 = 1e-6
LVLS = 7

_BUILD_CACHE = {}
DEBUG_DUMP = False
EARLY_CC = False

ENGINES = ("sync", "gpsimd", "tensor", "vector", "scalar")


class Plan:
    def __init__(self, hz_writers=None):
        self.steps = {e: [] for e in ENGINES}
        self.cnt = {}
        self.hw = {e: {} for e in ENGINES}
        self.bank_guard = {i: [] for i in range(8)}
        self.buf_guard = {}
        self.hz_writers = hz_writers or set()

    def val(self, sem):
        return self.cnt.get(sem, 0)

    def step(self, eng, fn, waits=None, inc=None, inc_by=1):
        w = []
        for s, v in (waits or []):
            if v is None or v <= 0:
                continue
            if self.hw[eng].get(s, 0) >= v:
                continue
            self.hw[eng][s] = v
            w.append((s, v))
        if inc is None and (eng, len(self.steps[eng])) in self.hz_writers:
            inc, inc_by = "hz_" + eng, 1
        if inc is not None:
            self.cnt[inc] = self.cnt.get(inc, 0) + inc_by
        self.steps[eng].append((fn, w, inc, inc_by))

    def dma(self, eng, fn, waits=None, inc=None):
        self.step(eng, fn, waits=waits, inc=inc, inc_by=16)

    def bank_w(self, *banks):
        out = []
        for b in banks:
            out.extend(self.bank_guard[b])
        return out

    def bank_consumed(self, b, sem):
        self.bank_guard[b] = [(sem, self.val(sem))]

    def buf_w(self, *names):
        out = []
        for nm in names:
            out.extend(self.buf_guard.get(nm, []))
        return out

    def buf_set(self, nm, *guards):
        self.buf_guard[nm] = [(s, v) for (s, v) in guards if v]


def _arg_names(args):
    out = set()
    for a in args:
        for attr in ("memref", "name"):
            n = getattr(a, attr, None)
            if isinstance(n, str):
                out.add(n)
                break
        else:
            t = getattr(a, "tensor", None)
            n = getattr(t, "name", None)
            if isinstance(n, str):
                out.add(n)
    return out


def _emit(plan, sems, blk, hazard_scan=None, hazard_fix=None):
    def mk(eng_name):
        check = eng_name in ("vector", "gpsimd") and hazard_scan is not None
        def closure(eng):
            recent = []
            for si, (fn, waits, inc, inc_by) in enumerate(plan.steps[eng_name]):
                if hazard_fix is not None:
                    for s, v in hazard_fix.get((eng_name, si), []):
                        eng.wait_ge(sems[s], v)
                for s, v in waits:
                    eng.wait_ge(sems[s], v)
                inst = fn(eng)
                insts = inst if isinstance(inst, list) else ([inst] if inst is not None else [])
                if check:
                    for one in insts:
                        ii = getattr(one, "ins", None)
                        m = getattr(ii, "ins", None)
                        mo = getattr(ii, "outs", None)
                        rd = _arg_names(m or [])
                        wr = _arg_names(mo or [])
                        touched = rd | wr
                        for wsi, wouts in recent:
                            if wsi != si and touched & wouts:
                                hazard_scan.setdefault((eng_name, si), set()).add(wsi)
                        recent.append((si, wr))
                        recent = recent[-2:]
                if inc is not None and insts:
                    insts[-1].then_inc(sems[inc], inc_by)
        return closure
    blk.sync(mk("sync"))
    blk.gpsimd(mk("gpsimd"))
    blk.tensor(mk("tensor"))
    blk.vector(mk("vector"))
    blk.scalar(mk("scalar"))


def build_program(_hazard_scan=None, _hazard_fix=None, _hz_writers=None):
    nc = bass.Bass("TRN2", target_bir_lowering=False, detect_race_conditions=False,
                   dynamic_dma_scratch_size=16384)

    d = {}
    d["xt"] = nc.dram_tensor("xt", [D, T], F32, kind="ExternalInput").ap()
    for nm, sh in [("wq", [D, D]), ("wk", [D, D]), ("wv", [D, D]), ("wo", [D, D]),
                   ("wab", [D, 16]), ("wg", [D, 8]), ("w1", [D, FF]), ("w2", [FF, D]),
                   ("wexp", [D, D])]:
        d[nm] = nc.dram_tensor(nm, sh, F32, kind="ExternalInput").ap()
    for nm, sh in [("ln1w", [128, NC_D]), ("ln1b", [128, NC_D]), ("ln2w", [128, NC_D]),
                   ("ln2b", [128, NC_D]), ("b1", [128, NC_F]), ("b2", [128, NC_D]),
                   ("bexp", [128, NC_D]), ("coremask", [1, 8])]:
        d[nm] = nc.dram_tensor("d_" + nm, sh, F32, kind="ExternalInput").ap()
    for nm in ["c_iu_f", "c_half_f", "c_ones_f", "c_id_f",
               "c_negsl_b", "c_negsu_b", "c_iu_b", "c_id_b", "c_ones_b"]:
        d[nm] = nc.dram_tensor("d_" + nm, [128, 128], F32, kind="ExternalInput").ap()
    d["c_onescol"] = nc.dram_tensor("d_c_onescol", [128, 1], F32, kind="ExternalInput").ap()
    d["c_sel"] = nc.dram_tensor("d_c_sel", [2, 256], F32, kind="ExternalInput").ap()
    d["c_iota999"] = nc.dram_tensor("d_c_iota999", [128, 32], F32,
                                    kind="ExternalInput").ap()
    d["out"] = nc.dram_tensor("out_t", [D, T], F32, kind="ExternalOutput").ap()

    xn32_dram = nc.dram_tensor("xn32_s", [D, T], F32).ap()
    base_dram = nc.dram_tensor("base_b", [1, 8], F32).ap()
    y1_dram = nc.dram_tensor("y1_s", [D, T], F32).ap()
    st2_dram = nc.dram_tensor("st2_s", [2, T], F32).ap()
    cnt_in = nc.dram_tensor("cnt_in", [1, 8], F32).ap()
    cnt_all = nc.dram_tensor("cnt_all", [8, 8], F32).ap()

    es_persist = ExitStack()
    E = es_persist.enter_context

    c_iu_f = E(nc.sbuf_tensor("c_iu_f", [128, 128], F32))
    c_half_f = E(nc.sbuf_tensor("c_half_f", [128, 128], F32))
    c_ones_f = E(nc.sbuf_tensor("c_ones_f", [128, 128], F32))
    c_id_f = E(nc.sbuf_tensor("c_id_f", [128, 128], F32))
    c_negsl_b = E(nc.sbuf_tensor("c_negsl_b", [128, 128], BF16))
    c_negsu_b = E(nc.sbuf_tensor("c_negsu_b", [128, 128], BF16))
    c_iu_b = E(nc.sbuf_tensor("c_iu_b", [128, 128], BF16))
    c_id_b = E(nc.sbuf_tensor("c_id_b", [128, 128], BF16))
    c_ones_b = E(nc.sbuf_tensor("c_ones_b", [128, 128], BF16))
    c_onescol = E(nc.sbuf_tensor("c_onescol", [128, 1], F32))
    c_sel = E(nc.sbuf_tensor("c_sel", [2, 256], F32))
    c_onescol_b = E(nc.sbuf_tensor("c_onescol_b", [128, 1], BF16))
    ln1w = E(nc.sbuf_tensor("ln1w", [128, NC_D], F32))
    ln1b = E(nc.sbuf_tensor("ln1b", [128, NC_D], F32))
    ln2w = E(nc.sbuf_tensor("ln2w", [128, NC_D], F32))
    ln2b = E(nc.sbuf_tensor("ln2b", [128, NC_D], F32))
    b1_sb = E(nc.sbuf_tensor("b1_sb", [128, NC_F], F32))
    b2_sb = E(nc.sbuf_tensor("b2_sb", [128, NC_D], F32))
    bexp_sb = E(nc.sbuf_tensor("bexp_sb", [128, NC_D], F32))
    coremask = E(nc.sbuf_tensor("coremask", [1, 8], F32))
    r_ohf = E(nc.sbuf_tensor("r_ohf", [128, 32 * 8], BF16))
    r_c = E(nc.sbuf_tensor("r_c", [128, 32], F32))
    r_l = E(nc.sbuf_tensor("r_l", [128, 32 * 8], F32))
    r_negm = E(nc.sbuf_tensor("r_negm", [128, 32], F32))
    r_g = E(nc.sbuf_tensor("r_g", [128, 32], F32))
    run_rep = E(nc.sbuf_tensor("run_rep", [128, 8], F32))
    base_rep = E(nc.sbuf_tensor("base_rep", [128, 8], F32))
    cnt_sb = E(nc.sbuf_tensor("cnt_sb", [1, 64], F32))
    base_row = E(nc.sbuf_tensor("base_row", [1, 72], F32))
    t8a = E(nc.sbuf_tensor("t8a", [128, 1], F32))
    t8b = E(nc.sbuf_tensor("t8b", [128, 8], F32))
    m1c = E(nc.sbuf_tensor("m1c", [128, 1], F32))
    c_iota999 = E(nc.sbuf_tensor("c_iota999", [128, 32], BF16))
    rt3 = E(nc.sbuf_tensor("rt3", [128, 32], F32))

    pb = [E(nc.psum_tensor(f"pb{i}", [128, 512], F32)) for i in range(8)]

    def pbf(i):
        return pb[i][:, :]

    def pbb(i):
        return pb[i][:, :].bitcast(BF16)

    def pg(i):
        return pb[i][:, :].rearrange("p (g n) -> p g n", n=DH)

    es_a = ExitStack()
    A = es_a.enter_context
    st_mu0 = A(nc.sbuf_tensor("st_mu0", [1, TT], F32))
    st_ss = A(nc.sbuf_tensor("st_ss", [1, TT], F32))
    y1h = A(nc.sbuf_tensor("y1h", [128, 4 * TT], F32))
    xnwo = A(nc.sbuf_tensor("xnwo", [128, 2 * TT], F32))
    o_fm = A(nc.sbuf_tensor("o_fm", [128, H * TT], BF16))
    s_st = A(nc.sbuf_tensor("s_st", [128, H * DH], F32))
    sig_bf = A(nc.sbuf_tensor("sig_bf", [128, H * DH], BF16))
    sq_sc = A(nc.sbuf_tensor("sq_sc", [128, H * DH], F32))
    kfm = A(nc.sbuf_tensor("kfm", [128, H * DH], BF16))
    nkfm = A(nc.sbuf_tensor("nkfm", [128, H * DH], BF16))
    kbfm = A(nc.sbuf_tensor("kbfm", [128, H * DH], BF16))
    qfm = A(nc.sbuf_tensor("qfm", [128, H * DH], BF16))
    p_cur = A(nc.sbuf_tensor("p_cur", [128, H * DH], BF16))
    p_nxt = A(nc.sbuf_tensor("p_nxt", [128, H * DH], BF16))
    q_cur = A(nc.sbuf_tensor("q_cur", [128, H * DH], BF16))
    q_nxt = A(nc.sbuf_tensor("q_nxt", [128, H * DH], BF16))
    mt_bf = A(nc.sbuf_tensor("mt_bf", [128, H * DH], BF16))
    x_a = A(nc.sbuf_tensor("x_a", [128, H * DH], BF16))
    x_b = A(nc.sbuf_tensor("x_b", [128, H * DH], BF16))
    wo_sb = A(nc.sbuf_tensor("wo_sb", [128, NC_D * D], BF16))
    wo1_sb = A(nc.sbuf_tensor("wo1_sb", [128, NC_D], F32))
    wo1_bf = A(nc.sbuf_tensor("wo1_bf", [128, NC_D], BF16))
    zab4 = A(nc.sbuf_tensor("zab4", [128, NCC * 16], F32))
    beta4 = A(nc.sbuf_tensor("beta4", [128, NCC * 8], F32))
    la8_4 = A(nc.sbuf_tensor("la8_4", [128, NCC * 8], F32))
    cumz = A(nc.sbuf_tensor("cumz", [128, 24], F32))
    dz = A(nc.sbuf_tensor("dz", [128, 8], F32))
    lam4 = A(nc.sbuf_tensor("lam4", [128, NCC * 8], F32))
    ginv4 = A(nc.sbuf_tensor("ginv4", [128, NCC * 8], F32))
    lamc4 = A(nc.sbuf_tensor("lamc4", [128, NCC * 8], F32))
    gref4 = A(nc.sbuf_tensor("gref4", [128, NCC * 8], F32))
    f18 = A(nc.sbuf_tensor("f18", [128, 8], F32))
    ssq4_q = A(nc.sbuf_tensor("ssq4_q", [128, NCC * 8], F32))
    ssq4_k = A(nc.sbuf_tensor("ssq4_k", [128, NCC * 8], F32))
    rsq_q = A(nc.sbuf_tensor("rsq_q", [128, 8], F32))
    rsq_k = A(nc.sbuf_tensor("rsq_k", [128, 8], F32))
    fq4 = A(nc.sbuf_tensor("fq4", [128, NCC * 8], F32))
    frk4 = A(nc.sbuf_tensor("frk4", [128, NCC * 8], F32))
    fkb4 = A(nc.sbuf_tensor("fkb4", [128, NCC * 8], F32))
    xn = A(nc.sbuf_tensor("xn", [128, NC_D * TT], F32))
    xsq = A(nc.sbuf_tensor("xsq", [128, 4 * TT], BF16))
    xin = A(nc.sbuf_tensor("xin", [128, NC_D * TT], F32))
    n_bf = A(nc.sbuf_tensor("n_bf", [128, NC_D * TT], BF16))
    wab_sb = A(nc.sbuf_tensor("wab_sb", [128, NC_D * 16], BF16))
    wg_sb = A(nc.sbuf_tensor("wg_sb", [128, NC_D * 8], F32))
    wq_sb = A(nc.sbuf_tensor("wq_sb", [128, NC_D * D], BF16))
    wk_sb = A(nc.sbuf_tensor("wk_sb", [128, NC_D * D], BF16))
    wv_sb = A(nc.sbuf_tensor("wv_sb", [128, NC_D * D], BF16))
    q_b4 = A(nc.sbuf_tensor("q_b4", [128, NCC * H * DH], BF16))
    k_b4 = A(nc.sbuf_tensor("k_b4", [128, NCC * H * DH], BF16))
    v_b4 = A(nc.sbuf_tensor("v_b4", [128, NCC * H * DH], BF16))
    qt_bf = A(nc.sbuf_tensor("qt_bf", [128, H * DH], BF16))
    kn_bf = A(nc.sbuf_tensor("kn_bf", [128, H * DH], BF16))
    vt_bf = A(nc.sbuf_tensor("vt_bf", [128, H * DH], BF16))
    kb_bf = [A(nc.sbuf_tensor(f"kb_bf{i}", [128, H * DH], BF16)) for i in range(2)]
    es_a.close()

    es_b = ExitStack()
    B = es_b.enter_context
    wexp_sb = B(nc.sbuf_tensor("wexp_sb", [128, NC_D * D], BF16))
    w2_sb = B(nc.sbuf_tensor("w2_sb", [128, NC_F * D], BF16))
    w1_sb = B(nc.sbuf_tensor("w1_sb", [128, NC_D * FF], BF16))
    by1 = [B(nc.sbuf_tensor(f"by1_{i}", [128, NC_D * SUB], F32)) for i in range(2)]
    bxnb = [B(nc.sbuf_tensor(f"bxnb_{i}", [128, NC_D * SUB], BF16)) for i in range(2)]
    bst2 = [B(nc.sbuf_tensor(f"bst2_{i}", [2, SUB], F32)) for i in range(2)]
    bh2 = B(nc.sbuf_tensor("bh2", [128, NC_D * SUB], BF16))
    bg = B(nc.sbuf_tensor("bg", [128, NC_F * SUB], BF16))
    bout = B(nc.sbuf_tensor("bout", [128, NC_D * SUB], F32))
    bgb = B(nc.sbuf_tensor("bgb", [128, SUB], F32))
    bdiag = B(nc.sbuf_tensor("bdiag", [128, 128], F32))
    es_b.close()

    sems = {}
    es_sem = ExitStack()
    for s in ["pe", "vv", "sc", "gp", "st", "cc", "hz_vector", "hz_gpsimd",
              "dx", "dy1", "dst2", "dxn", "dxw", "dcb", "dcs", "dwq", "dwk", "dwv",
              "dwab", "dwo", "dw1a", "dw1b", "dw1c", "dw1d", "dw2a", "dw2b",
              "dwexa", "dwexb", "dcnt", "dcall", "dsb", "dsg",
              "dxa", "dbs", "dbb"]:
        sems[s] = es_sem.enter_context(nc.semaphore(s))

    p = Plan(hz_writers=_hz_writers)

    def w3(t, m):
        return t[:, :].rearrange("p (c m) -> p c m", m=m)

    def tile3(t, n):
        return t[:, :].rearrange("p (c n) -> p c n", n=n)

    def dram3(ap):
        return ap.rearrange("(c p) n -> p c n", p=128)

    def h3(t):
        return t[:, :].rearrange("p (h e) -> p h e", e=DH)

    def h4(t, cc):
        return t[:, :].rearrange("p (s h e) -> p s h e", s=NCC, e=DH)[:, cc, :, :]

    def s4(t, cc):
        return t[:, cc * 8:(cc + 1) * 8]

    def bcast8s(t, cc):
        return s4(t, cc).unsqueeze(2).broadcast_to([128, 8, DH])

    def maskb(cmask):
        return cmask[:, :].unsqueeze(1).broadcast_to([128, 4, DH])

    p.dma("sync", lambda s: s.dma_start(out=w3(wg_sb, 8), in_=dram3(d["wg"])), inc="dcs")
    for nm, t in [("c_iu_f", c_iu_f), ("c_half_f", c_half_f), ("c_ones_f", c_ones_f),
                  ("c_id_f", c_id_f), ("c_onescol", c_onescol),
                  ("c_sel", c_sel),
                  ("ln1w", ln1w), ("ln1b", ln1b), ("ln2w", ln2w), ("ln2b", ln2b),
                  ("b1", b1_sb), ("b2", b2_sb), ("bexp", bexp_sb), ("coremask", coremask)]:
        p.dma("sync", lambda s, t=t, nm=nm: s.dma_start(out=t[:, :], in_=d[nm]), inc="dcs")
    ds_const = p.val("dcs")

    for nm, t in [("c_negsl_b", c_negsl_b), ("c_negsu_b", c_negsu_b), ("c_iu_b", c_iu_b),
                  ("c_id_b", c_id_b), ("c_ones_b", c_ones_b)]:
        p.dma("gpsimd", lambda g, t=t, nm=nm: g.dma_start(out=t[:, :], in_=d[nm]), inc="dcb")
    p.dma("gpsimd", lambda g: g.dma_start(out=c_onescol_b[:, :], in_=d["c_onescol"]), inc="dcb")
    p.dma("gpsimd", lambda g: g.dma_start(out=c_iota999[:, :], in_=d["c_iota999"]), inc="dcb")
    dg_const = p.val("dcb")

    def load_cast(dst, src, m, sem, waits=None):
        p.dma("gpsimd", lambda g, dst=dst, src=src, m=m: g.dma_start(
            out=w3(dst, m), in_=dram3(src)), inc=sem, waits=waits)

    load_cast(wq_sb, d["wq"], D, "dwq")
    load_cast(wk_sb, d["wk"], D, "dwk")
    load_cast(wv_sb, d["wv"], D, "dwv")
    load_cast(wab_sb, d["wab"], 16, "dwab")
    load_cast(wo_sb, d["wo"], D, "dwo")

    p.step("vector", lambda v: v.tensor_reduce(wo1_sb[:, :], w3(wo_sb, D), AX.X, OP.add),
           waits=[("dwo", 16)], inc="vv")
    p.step("vector", lambda v: v.tensor_copy(wo1_bf[:, :], wo1_sb[:, :]), inc="vv")
    vv_wo1 = p.val("vv")
    p.step("vector", lambda v: v.memset(run_rep[:, :], 0.0), waits=[("dcs", ds_const)], inc="vv")
    p.step("gpsimd", lambda g: g.memset(s_st[:, :], 0.0), inc="gp")
    gp_szero = p.val("gp")

    st = {
        "pe_upd": {},
        "pe_tp": {},
        "pe_x0": {},
        "pe_proj": {},
        "sc_silu": {},
        "vv_zab": {},
        "vv_ssq": {},
        "vv_scal": {},
        "sc_n": {},
        "sc_lnexp": {},
        "ds_x": {},
        "dxn_t": {},
        "vv_ssq_prev": 0,
        "vv_folds_prev": 0,
        "sc_lnexp_prev": 0,
        "dst2_prev": 0,
        "pe_mu2_t": 0,
        "pe_ss2_t": 0,
    }

    def emit_xload(t_i):
        tk = t_i * TT
        p.dma("sync", (lambda s, tk=tk: s.dma_start(
            out=tile3(xin, TT), in_=dram3(d["xt"])[:, :, tk:tk + TT])),
            waits=p.buf_w("xin"), inc="dx")
        st["ds_x"][t_i] = p.val("dx")

    def emit_lnhead(t_i):
        tk = t_i * TT
        ds_x = st["ds_x"][t_i]
        for ic in range(NC_D):
            p.step("tensor", (lambda t, ic=ic: t.matmul(
                pb[6][0:1, 0:TT], c_onescol[:, 0:1], tile3(xin, TT)[:, ic, :],
                start=(ic == 0), stop=(ic == NC_D - 1))),
                waits=([("dx", ds_x), ("dcs", ds_const)] + p.bank_w(6)) if ic == 0 else [],
                inc="pe" if ic == NC_D - 1 else None)
        pe_mu0 = p.val("pe")
        gp_xsq = 0
        for hf in range(2):
            p.step("gpsimd", (lambda g, hf=hf: g.tensor_mul(
                tile3(xsq, TT), tile3(xin, TT)[:, hf * 4:hf * 4 + 4, :],
                tile3(xin, TT)[:, hf * 4:hf * 4 + 4, :])),
                waits=(([("dx", ds_x)] + p.buf_w("xsq")) if hf == 0
                       else [("pe", p.val("pe"))]), inc="gp")
            gp_xsq = p.val("gp")
            for ic in range(4):
                p.step("tensor", (lambda t, ic=ic, hf=hf: t.matmul(
                    pb[7][0:1, 0:TT], c_onescol_b[:, 0:1], tile3(xsq, TT)[:, ic, :],
                    start=(ic == 0 and hf == 0), stop=(ic == 3 and hf == 1))),
                    waits=([("gp", gp_xsq), ("dcb", dg_const)] +
                           (p.bank_w(7) if hf == 0 else [])) if ic == 0 else [],
                    inc="pe" if ic == 3 else None)
        pe_ss0 = p.val("pe")

        p.step("vector", lambda v: v.tensor_scalar(
            st_mu0[0:1, :], pb[6][0:1, 0:TT], 1.0 / D, None, OP.mult),
            waits=[("pe", pe_mu0), ("dst2", st["dst2_prev"])], inc="vv")
        p.step("vector", lambda v: v.tensor_scalar(
            st_ss[0:1, :], pb[7][0:1, 0:TT], 1.0 / D, None, OP.mult),
            waits=[("pe", pe_ss0)], inc="vv")
        p.bank_consumed(6, "vv")
        p.bank_consumed(7, "vv")
        p.step("vector", lambda v: v.tensor_mul(xnwo[0:1, 0:TT], st_mu0[0:1, :],
                                                st_mu0[0:1, :]), inc="vv")
        p.step("vector", lambda v: v.tensor_sub(st_ss[0:1, :], st_ss[0:1, :],
                                                xnwo[0:1, 0:TT]), inc="vv")
        vv_var0 = p.val("vv")
        p.buf_set("xnwo", ("vv", vv_var0))
        p.step("scalar", lambda sc: sc.activation(st_ss[0:1, :], st_ss[0:1, :], AF.Ln,
                                                  bias=f32(EPS_LN)),
               waits=[("vv", vv_var0)], inc="sc")
        p.step("scalar", lambda sc: sc.activation(st_ss[0:1, :], st_ss[0:1, :], AF.Exp,
                                                  scale=f32(-0.5)), inc="sc")
        sc_r0 = p.val("sc")
        p.step("vector", lambda v: v.tensor_mul(st_mu0[0:1, :], st_mu0[0:1, :],
                                                st_ss[0:1, :]),
               waits=[("sc", sc_r0)], inc="vv")
        vv_r0 = p.val("vv")

        p.step("tensor", lambda t: t.matmul(pbf(0), c_ones_f[0:1, :], st_ss[0:1, :],
                                            start=True, stop=True),
               waits=[("vv", vv_r0)] + p.bank_w(0), inc=None)
        p.step("tensor", lambda t: t.matmul(pbf(1), c_ones_f[0:1, :], st_mu0[0:1, :],
                                            start=True, stop=True),
               waits=p.bank_w(1), inc="pe")
        pe_bc0 = p.val("pe")

        def bcast_tile(bank):
            return pb[bank][:, 0:TT].unsqueeze(1).broadcast_to([128, NC_D, TT])

        p.step("vector", lambda v: v.tensor_tensor(
            tile3(xn, TT), tile3(xin, TT), bcast_tile(0), OP.mult),
            waits=[("pe", pe_bc0)] + p.buf_w("xn"), inc="vv")
        p.step("vector", lambda v: v.tensor_tensor(
            tile3(xn, TT), tile3(xn, TT), bcast_tile(1), OP.subtract), inc="vv")
        vv_xn = p.val("vv")
        p.bank_consumed(0, "vv")
        p.bank_consumed(1, "vv")
        p.dma("sync", (lambda s, tk=tk: s.dma_start(
            out=dram3(xn32_dram)[:, :, tk:tk + TT], in_=tile3(xn, TT))),
            waits=[("vv", vv_xn)], inc="dxn")
        st["dxn_t"][t_i] = p.val("dxn")

        for cc in range(NCC):
            for ic in range(NC_D):
                p.step("tensor", (lambda t, cc=cc, ic=ic: t.matmul(
                    pb[7][:, cc * 8:(cc + 1) * 8],
                    tile3(xn, TT)[:, ic, cc * CC:(cc + 1) * CC],
                    w3(wg_sb, 8)[:, ic, :],
                    start=(ic == 0), stop=(ic == NC_D - 1))),
                    waits=([("vv", vv_xn)] + p.bank_w(7)) if (cc == 0 and ic == 0) else [],
                    inc="pe" if ic == NC_D - 1 else None)
        pe_log = p.val("pe")

        tb = t_i * NCC

        def g48(t, col0=0):
            return t[:, col0:col0 + 32].rearrange("p (c e) -> p c e", e=8)

        def gb4(t, col0):
            return t[:, col0:col0 + 4].unsqueeze(2).broadcast_to([128, 4, 8])

        rt1 = ssq4_q
        rt2 = ssq4_k
        p.step("vector", (lambda v: v.tensor_copy(
            r_l[:, tb * 8:tb * 8 + 32], pb[7][:, 0:32])),
            waits=[("pe", pe_log)], inc="vv")
        p.step("vector", (lambda v: v.tensor_reduce(
            r_negm[:, tb:tb + 4], g48(r_l, tb * 8), AX.X, OP.max, negate=True)), inc="vv")
        p.step("vector", (lambda v: v.tensor_tensor(
            g48(rt1), g48(r_l, tb * 8), gb4(r_negm, tb), OP.add)),
            waits=[("sc", st["sc_lnexp_prev"])], inc="vv")
        p.step("vector", lambda v: v.tensor_scalar(
            rt1[:, :], rt1[:, :], 0.0, None, OP.is_ge), inc="vv")
        p.step("vector", lambda v: v.scalar_tensor_tensor(
            rt2[:, :], rt1[:, :], -16.0, c_iota999[:, :], OP.mult, OP.add), inc="vv")
        p.step("vector", lambda v: v.tensor_reduce(
            rt3[:, 0:4], g48(rt2), AX.X, OP.min), inc="vv")
        p.step("vector", (lambda v: v.tensor_tensor(
            g48(r_ohf, tb * 8), g48(rt2), gb4(rt3, 0), OP.is_equal)), inc="vv")
        vv_ohf = p.val("vv")
        p.step("tensor", (lambda t: t.matmul(
            pb[6][:, 0:32], c_iu_b[:, :], r_ohf[:, tb * 8:tb * 8 + 32],
            start=True, stop=True)),
            waits=[("vv", vv_ohf)] + p.bank_w(6), inc=None)
        p.step("tensor", (lambda t: t.matmul(
            pb[6][:, 32:64], c_ones_b[:, :], r_ohf[:, tb * 8:tb * 8 + 32],
            start=True, stop=True)), inc="pe")
        pe_cnt = p.val("pe")
        p.step("vector", lambda v: v.tensor_copy(rt1[:, :], pb[6][:, 0:32]),
               waits=[("pe", pe_cnt)], inc="vv")
        p.step("vector", lambda v: v.tensor_copy(rt2[:, :], pb[6][:, 32:64]), inc="vv")
        p.bank_consumed(6, "vv")
        p.step("vector", lambda v: v.tensor_copy(rt3[:, 0:8], run_rep[:, :]), inc="vv")
        p.step("vector", lambda v: v.tensor_add(rt3[:, 8:16], rt3[:, 0:8], rt2[:, 0:8]),
               inc="vv")
        p.step("vector", lambda v: v.tensor_add(rt3[:, 16:24], rt3[:, 8:16], rt2[:, 8:16]),
               inc="vv")
        p.step("vector", lambda v: v.tensor_add(rt3[:, 24:32], rt3[:, 16:24],
                                                rt2[:, 16:24]), inc="vv")
        p.step("vector", lambda v: v.tensor_add(run_rep[:, :], rt3[:, 24:32],
                                                rt2[:, 24:32]), inc="vv")
        p.step("vector", lambda v: v.tensor_add(rt1[:, :], rt1[:, :], rt3[:, :]), inc="vv")
        p.step("vector", (lambda v: v.tensor_tensor(
            g48(rt1), g48(rt1), g48(r_ohf, tb * 8), OP.mult)), inc="vv")
        p.step("vector", (lambda v: v.tensor_reduce(
            r_c[:, tb:tb + 4], g48(rt1), AX.X, OP.add)), inc="vv")

        gp_xnsq = 0
        for hf in range(2):
            p.step("gpsimd", (lambda g, hf=hf: g.tensor_mul(
                tile3(xsq, TT), tile3(xn, TT)[:, hf * 4:hf * 4 + 4, :],
                tile3(xn, TT)[:, hf * 4:hf * 4 + 4, :])),
                waits=([("vv", vv_xn), ("pe", pe_ss0)] if hf == 0
                       else [("pe", p.val("pe"))]), inc="gp")
            gp_xnsq = p.val("gp")
            for ic in range(4):
                p.step("tensor", (lambda t, ic=ic, hf=hf: t.matmul(
                    pb[7][0:1, 0:TT], c_onescol_b[:, 0:1], tile3(xsq, TT)[:, ic, :],
                    start=(ic == 0 and hf == 0), stop=(ic == 3 and hf == 1))),
                    waits=([("gp", gp_xnsq)] + (p.bank_w(7) if hf == 0 else []))
                    if ic == 0 else [],
                    inc="pe" if ic == 3 else None)
        pe_ss1 = p.val("pe")
        p.step("vector", lambda v: v.tensor_scalar(
            st_ss[0:1, :], pb[7][0:1, 0:TT], 1.0 / D, None, OP.mult),
            waits=[("pe", pe_ss1)], inc="vv")
        vv_var1 = p.val("vv")
        p.bank_consumed(7, "vv")
        p.step("scalar", lambda sc: sc.activation(st_ss[0:1, :], st_ss[0:1, :], AF.Ln,
                                                  bias=f32(EPS_LN)),
               waits=[("vv", vv_var1)], inc="sc")
        p.step("scalar", lambda sc: sc.activation(st_ss[0:1, :], st_ss[0:1, :], AF.Exp,
                                                  scale=f32(-0.5)), inc="sc")
        sc_r1 = p.val("sc")
        p.step("tensor", lambda t: t.matmul(pbf(0), c_ones_f[0:1, :], st_ss[0:1, :],
                                            start=True, stop=True),
               waits=[("sc", sc_r1)] + p.bank_w(0), inc="pe")
        pe_bc1 = p.val("pe")
        p.step("vector", lambda v: v.tensor_tensor(
            tile3(xin, TT), tile3(xn, TT), bcast_tile(0), OP.mult),
            waits=[("pe", pe_bc1), ("gp", gp_xsq)], inc="vv")
        vv_u1 = p.val("vv")
        p.bank_consumed(0, "vv")
        for ic in range(NC_D):
            p.step("scalar", (lambda sc, ic=ic: sc.activation(
                tile3(n_bf, TT)[:, ic, :], tile3(xin, TT)[:, ic, :], AF.Identity,
                bias=ln1b[:, ic:ic + 1], scale=ln1w[:, ic:ic + 1])),
                waits=([("vv", vv_u1)] + p.buf_w("n_bf")) if ic == 0 else [],
                inc="sc" if ic == NC_D - 1 else None)
        sc_n = p.val("sc")
        st["sc_n"][t_i] = sc_n
        p.buf_set("xin", ("sc", sc_n))
        p.buf_set("xn", ("dxn", st["dxn_t"][t_i]), ("pe", pe_ss1), ("pe", pe_log))
        p.buf_set("xsq", ("pe", pe_ss1))

    def emit_proj(t_i, cc):
        for ic in range(NC_D):
            w_first = ([("sc", st["sc_n"][t_i]), ("dwq", 16), ("dwk", 16), ("dwv", 16),
                        ("dwab", 16)] + p.bank_w(0, 1, 2, 3, 4, 5, 7)) if ic == 0 else []
            p.step("tensor", (lambda t, ic=ic, cc=cc: t.matmul(
                pb[0][:, :], tile3(n_bf, TT)[:, ic, cc * CC:(cc + 1) * CC],
                w3(wq_sb, D)[:, ic, 0:512],
                start=(ic == 0), stop=(ic == NC_D - 1))), waits=w_first, inc=None)
            p.step("tensor", (lambda t, ic=ic, cc=cc: t.matmul(
                pb[1][:, :], tile3(n_bf, TT)[:, ic, cc * CC:(cc + 1) * CC],
                w3(wq_sb, D)[:, ic, 512:1024],
                start=(ic == 0), stop=(ic == NC_D - 1))), inc=None)
            p.step("tensor", (lambda t, ic=ic, cc=cc: t.matmul(
                pb[2][:, :], tile3(n_bf, TT)[:, ic, cc * CC:(cc + 1) * CC],
                w3(wk_sb, D)[:, ic, 0:512],
                start=(ic == 0), stop=(ic == NC_D - 1))), inc=None)
            p.step("tensor", (lambda t, ic=ic, cc=cc: t.matmul(
                pb[3][:, :], tile3(n_bf, TT)[:, ic, cc * CC:(cc + 1) * CC],
                w3(wk_sb, D)[:, ic, 512:1024],
                start=(ic == 0), stop=(ic == NC_D - 1))), inc=None)
            p.step("tensor", (lambda t, ic=ic, cc=cc: t.matmul(
                pb[4][:, :], tile3(n_bf, TT)[:, ic, cc * CC:(cc + 1) * CC],
                w3(wv_sb, D)[:, ic, 0:512],
                start=(ic == 0), stop=(ic == NC_D - 1))), inc=None)
            p.step("tensor", (lambda t, ic=ic, cc=cc: t.matmul(
                pb[5][:, :], tile3(n_bf, TT)[:, ic, cc * CC:(cc + 1) * CC],
                w3(wv_sb, D)[:, ic, 512:1024],
                start=(ic == 0), stop=(ic == NC_D - 1))), inc=None)
            p.step("tensor", (lambda t, ic=ic, cc=cc: t.matmul(
                pb[7][:, 0:16], tile3(n_bf, TT)[:, ic, cc * CC:(cc + 1) * CC],
                w3(wab_sb, 16)[:, ic, :],
                start=(ic == 0), stop=(ic == NC_D - 1))),
                inc="pe" if ic == NC_D - 1 else None)
        pe_proj = p.val("pe")
        st["pe_proj"][(t_i, cc)] = pe_proj
        if cc == NCC - 1:
            p.buf_set("n_bf", ("pe", pe_proj))

        for dsti, (dst, bank0) in enumerate([(q_b4, 0), (k_b4, 2), (v_b4, 4)]):
            for half in range(2):
                p.step("scalar", (lambda sc, dst=dst, bank0=bank0, half=half, cc=cc:
                       sc.activation(
                           dst[:, cc * 1024 + half * 512:cc * 1024 + (half + 1) * 512],
                           pb[bank0 + half][:, :], AF.Silu)),
                       waits=([("pe", pe_proj)] + p.buf_w(f"qs{cc}"))
                       if (dsti == 0 and half == 0) else [],
                       inc="sc")
        sc_silu = p.val("sc")
        st["sc_silu"][(t_i, cc)] = sc_silu
        for b in (0, 1, 2, 3, 4, 5):
            p.bank_consumed(b, "sc")

        p.step("vector", (lambda v, cc=cc: v.tensor_copy(
            zab4[:, cc * 16:(cc + 1) * 16], pb[7][:, 0:16])),
            waits=[("pe", pe_proj)], inc="vv")
        st["vv_zab"][(t_i, cc)] = p.val("vv")
        p.bank_consumed(7, "vv")

        p.step("gpsimd", (lambda g, cc=cc: g.tensor_mul(
            sq_sc[:, :], q_b4[:, cc * 1024:(cc + 1) * 1024],
            q_b4[:, cc * 1024:(cc + 1) * 1024])),
            waits=[("sc", sc_silu), ("vv", st["vv_ssq_prev"])], inc="gp")
        p.step("vector", (lambda v, cc=cc: v.tensor_reduce(
            s4(ssq4_q, cc), h3(sq_sc), AX.X, OP.add)),
            waits=[("gp", p.val("gp")), ("sc", st["sc_lnexp_prev"])], inc="vv")
        p.step("gpsimd", (lambda g, cc=cc: g.tensor_mul(
            sq_sc[:, :], k_b4[:, cc * 1024:(cc + 1) * 1024],
            k_b4[:, cc * 1024:(cc + 1) * 1024])),
            waits=[("vv", p.val("vv"))], inc="gp")
        p.step("vector", (lambda v, cc=cc: v.tensor_reduce(
            s4(ssq4_k, cc), h3(sq_sc), AX.X, OP.add)),
            waits=[("gp", p.val("gp"))], inc="vv")
        st["vv_ssq"][(t_i, cc)] = p.val("vv")
        st["vv_ssq_prev"] = p.val("vv")

    def emit_lnexp_pre(t_i, ccs):
        for cc in ccs:
            p.step("scalar", (lambda sc, cc=cc: sc.activation(
                zab4[:, cc * 16:(cc + 1) * 16], zab4[:, cc * 16:(cc + 1) * 16],
                AF.Exp, scale=f32(-1.0))),
                waits=[("vv", st["vv_zab"][(t_i, cc)])], inc="sc")
            sc_abe = p.val("sc")
            p.step("vector", (lambda v, cc=cc: v.tensor_scalar(
                zab4[:, cc * 16:(cc + 1) * 16], zab4[:, cc * 16:(cc + 1) * 16],
                1.0, None, OP.add)),
                waits=[("sc", sc_abe)], inc="vv")
            p.step("vector", (lambda v, cc=cc: v.reciprocal(
                s4(beta4, cc), zab4[:, cc * 16:cc * 16 + 8])), inc="vv")
            vv_beta = p.val("vv")
            p.step("scalar", (lambda sc, cc=cc: sc.activation(
                s4(la8_4, cc), zab4[:, cc * 16 + 8:cc * 16 + 16], AF.Ln)),
                waits=[("vv", vv_beta)], inc="sc")
            st["sc_la"] = p.val("sc")

    def emit_lnexp(t_i, ccs):
        for cc in ccs:
            p.step("tensor", (lambda t, cc=cc: t.matmul(
                pb[6][:, 0:8], c_iu_f[:, :], s4(la8_4, cc), start=True, stop=True)),
                waits=[("sc", st["sc_la"])] + p.bank_w(6), inc=None)
            p.step("tensor", (lambda t, cc=cc: t.matmul(
                pb[6][:, 8:16], c_half_f[:, :], s4(la8_4, cc), start=True, stop=True)),
                inc=None)
            p.step("tensor", (lambda t, cc=cc: t.matmul(
                pb[6][:, 16:24], c_ones_f[:, :], s4(la8_4, cc), start=True, stop=True)),
                inc="pe")
            pe_cum = p.val("pe")
            p.step("vector", lambda v: v.tensor_copy(cumz[:, :], pb[6][:, 0:24]),
                   waits=[("pe", pe_cum)], inc="vv")
            p.bank_consumed(6, "vv")
            p.step("vector", lambda v: v.tensor_sub(dz[:, :], cumz[:, 8:16], cumz[:, 0:8]),
                   inc="vv")
            p.step("vector", lambda v: v.tensor_sub(f18[:, :], cumz[:, 8:16], cumz[:, 16:24]),
                   inc="vv")
            vv_dz = p.val("vv")
            p.step("scalar", (lambda sc, cc=cc: sc.activation(s4(lam4, cc), dz[:, :], AF.Exp)),
                   waits=[("vv", vv_dz)], inc="sc")
            p.step("scalar", (lambda sc, cc=cc: sc.activation(s4(ginv4, cc), dz[:, :],
                                                              AF.Exp, scale=f32(-1.0))),
                   inc="sc")
            p.step("scalar", (lambda sc, cc=cc: sc.activation(s4(lamc4, cc), f18[:, :],
                                                              AF.Exp)), inc="sc")
            p.step("scalar", (lambda sc, cc=cc: sc.activation(s4(gref4, cc), cumz[:, 8:16],
                                                              AF.Exp, scale=f32(-1.0))),
                   inc="sc")
            p.step("scalar", (lambda sc, cc=cc: sc.activation(rsq_q[:, :], s4(ssq4_q, cc),
                                                              AF.Ln, bias=f32(EPS_L2))),
                   waits=[("vv", st["vv_ssq"][(t_i, cc)]),
                          ("vv", st["vv_folds_prev"])], inc="sc")
            p.step("scalar", lambda sc: sc.activation(rsq_q[:, :], rsq_q[:, :], AF.Exp,
                                                      scale=f32(-0.5)), inc="sc")
            p.step("scalar", (lambda sc, cc=cc: sc.activation(rsq_k[:, :], s4(ssq4_k, cc),
                                                              AF.Ln, bias=f32(EPS_L2))),
                   inc="sc")
            p.step("scalar", lambda sc: sc.activation(rsq_k[:, :], rsq_k[:, :], AF.Exp,
                                                      scale=f32(-0.5)), inc="sc")
            sc_lx = p.val("sc")
            p.step("vector", (lambda v, cc=cc: v.tensor_mul(
                s4(fq4, cc), rsq_q[:, :], s4(lam4, cc))),
                waits=[("sc", sc_lx)], inc="vv")
            p.step("vector", (lambda v, cc=cc: v.tensor_copy(s4(frk4, cc), rsq_k[:, :])),
                   inc="vv")
            p.step("vector", (lambda v, cc=cc: v.tensor_mul(
                s4(fkb4, cc), rsq_k[:, :], s4(beta4, cc))), inc="vv")
            st["vv_folds_prev"] = p.val("vv")
            st["sc_lnexp"][(t_i, cc)] = p.val("sc")
        st["sc_lnexp_prev"] = p.val("sc")

    def emit_scalings(t_i, cc):
        par = cc % 2
        ccg = t_i * NCC + cc
        g_pe = st["pe_x0"].get(ccg - 1, 0)
        p.step("vector", (lambda v, cc=cc: v.tensor_tensor(
            h3(qt_bf), h4(q_b4, cc), bcast8s(fq4, cc), OP.mult)),
            waits=[("sc", st["sc_lnexp"][(t_i, cc)]), ("pe", g_pe)], inc="vv")
        p.step("vector", (lambda v, cc=cc: v.tensor_tensor(
            h3(kn_bf), h4(k_b4, cc), bcast8s(frk4, cc), OP.mult)), inc="vv")
        p.step("vector", (lambda v, cc=cc, par=par: v.tensor_tensor(
            h3(kb_bf[par]), h4(k_b4, cc), bcast8s(fkb4, cc), OP.mult)), inc="vv")
        p.step("vector", (lambda v, cc=cc: v.tensor_tensor(
            h3(vt_bf), h4(v_b4, cc), bcast8s(ginv4, cc), OP.mult)), inc="vv")
        st["vv_scal"][(t_i, cc)] = p.val("vv")
        p.buf_set(f"qs{cc}", ("vv", p.val("vv")))

    def emit_delta(t_i, cc):
        par = cc % 2
        ccg = t_i * NCC + cc
        vv_tm = st["vv_scal"][(t_i, cc)]
        p.step("vector", (lambda v, cc=cc: v.tensor_tensor(
            h3(sig_bf), h3(s_st), bcast8s(gref4, cc), OP.mult)),
            waits=[("gp", gp_szero), ("pe", st["pe_upd"].get(ccg - 1, 0))], inc="vv")
        vv_sig = p.val("vv")
        for srcb, bank in [(kn_bf, 0), (kb_bf[par], 1), (qt_bf, 2)]:
            for h in range(H):
                p.step("tensor", (lambda t, srcb=srcb, bank=bank, h=h: t.transpose(
                    pbb(bank)[:, h * DH:(h + 1) * DH], h3(srcb)[:, h, :], c_id_b[:, :])),
                    waits=([("vv", vv_tm)] + p.bank_w(bank)) if h == 0 else [],
                    inc="pe" if h == H - 1 else None)
        pe_tp = p.val("pe")
        st["pe_tp"][ccg] = pe_tp
        p.step("scalar", lambda sc: sc.copy(kfm[:, :], pbb(0)[:, :]),
               waits=[("pe", pe_tp)], inc="sc")
        p.step("scalar", lambda sc: sc.activation(nkfm[:, :], pbb(0)[:, :], AF.Identity,
                                                  scale=f32(-1.0)), inc="sc")
        sc_tpe = p.val("sc")
        p.step("vector", lambda v: v.tensor_copy(kbfm[:, :], pbb(1)[:, :]),
               waits=[("pe", pe_tp)], inc="vv")
        p.step("vector", lambda v: v.tensor_copy(qfm[:, :], pbb(2)[:, :]), inc="vv")
        vv_tpe = p.val("vv")
        p.bank_consumed(0, "sc")
        p.bank_consumed(1, "vv")
        p.bank_consumed(2, "vv")

        for h in range(H):
            p.step("tensor", (lambda t, h=h: t.matmul(
                pg(2 + h // 4)[:, h % 4, :], h3(kfm)[:, h, :], h3(kbfm)[:, h, :],
                start=True, stop=True)),
                waits=([("sc", sc_tpe), ("vv", vv_tpe)] + p.bank_w(2, 3)) if h == 0 else [],
                inc="pe" if h == H - 1 else None)
        for h in range(H):
            p.step("tensor", (lambda t, h=h: t.matmul(
                pg(4 + h // 4)[:, h % 4, :], h3(kbfm)[:, h, :], h3(kfm)[:, h, :],
                start=True, stop=True)),
                waits=p.bank_w(4, 5) if h == 0 else [],
                inc="pe" if h == H - 1 else None)
        for h in range(H):
            p.step("tensor", (lambda t, h=h: t.matmul(
                pg(6 + h // 4)[:, h % 4, :], h3(kbfm)[:, h, :], h3(qfm)[:, h, :],
                start=True, stop=True)),
                waits=p.bank_w(6, 7) if h == 0 else [],
                inc="pe" if h == H - 1 else None)
        pe_afam = p.val("pe")

        def mask_evac(dst, b0, cmask):
            def fn(e, dst=dst, b0=b0, cmask=cmask):
                i1 = e.tensor_tensor(h3(dst)[:, 0:4, :], pg(b0), maskb(cmask), OP.mult)
                i2 = e.tensor_tensor(h3(dst)[:, 4:8, :], pg(b0 + 1), maskb(cmask), OP.mult)
                return [i1, i2]
            return fn
        p.step("vector", mask_evac(q_cur, 2, c_negsl_b), waits=[("pe", pe_afam)], inc="vv")
        p.step("vector", mask_evac(p_cur, 4, c_negsu_b), inc="vv")
        p.step("vector", mask_evac(mt_bf, 6, c_iu_b), inc="vv")
        vv_q = p.val("vv")
        sc_p = 0
        p.bank_consumed(2, "vv")
        p.bank_consumed(3, "vv")
        p.bank_consumed(4, "vv")
        p.bank_consumed(5, "vv")
        p.bank_consumed(6, "vv")
        p.bank_consumed(7, "vv")

        for h in range(H):
            p.step("tensor", (lambda t, h=h: t.matmul(
                pg(h // 4)[:, h % 4, :], h3(nkfm)[:, h, :], h3(sig_bf)[:, h, :],
                start=True, stop=False)),
                waits=([("vv", vv_sig), ("sc", sc_tpe)] + p.bank_w(0, 1)) if h == 0 else [],
                inc=None)
            p.step("tensor", (lambda t, h=h: t.matmul(
                pg(h // 4)[:, h % 4, :], c_id_b[:, :], h3(vt_bf)[:, h, :],
                start=False, stop=True)),
                inc="pe" if h == H - 1 else None)
        pe_x0 = p.val("pe")
        st["pe_x0"][ccg] = pe_x0

        def x0_evac(v):
            i1 = v.tensor_copy(h3(x_a)[:, 0:4, :], pg(0))
            i2 = v.tensor_copy(h3(x_a)[:, 4:8, :], pg(1))
            return [i1, i2]
        p.step("vector", x0_evac, waits=[("pe", pe_x0)], inc="vv")
        vv_x = p.val("vv")
        p.bank_consumed(0, "vv")
        p.bank_consumed(1, "vv")

        xcur, xnxt = x_a, x_b
        pcur, pnxt = p_cur, p_nxt
        qcur, qnxt = q_cur, q_nxt
        vvA = vv_x
        vvB = vv_x
        scA = 0
        scB = 0
        scQA = 0
        vvQB = vv_q
        for lvl in range(LVLS):
            for g, (bank, h0) in enumerate([(0, 0), (1, 4)]):
                w0 = ([("vv", (vvA if g == 0 else vvB)), ("vv", vv_q),
                       ("sc", (scA if g == 0 else scB))] + p.bank_w(bank))
                for j in range(4):
                    p.step("tensor", (lambda t, bank=bank, j=j, h0=h0, pcur=pcur,
                           xcur=xcur: t.matmul(
                               pg(bank)[:, j, :], h3(pcur)[:, h0 + j, :],
                               h3(xcur)[:, h0 + j, :], start=True, stop=True)),
                        waits=w0 if j == 0 else [],
                        inc="pe" if j == 3 else None)
                pe_ap = p.val("pe")

                def xadd(v, bank=bank, h0=h0, xcur=xcur, xnxt=xnxt):
                    return v.tensor_add(h3(xnxt)[:, h0:h0 + 4, :], pg(bank),
                                        h3(xcur)[:, h0:h0 + 4, :])
                p.step("vector", xadd, waits=[("pe", pe_ap)], inc="vv")
                if g == 0:
                    vvA = p.val("vv")
                else:
                    vvB = p.val("vv")
                p.bank_consumed(bank, "vv")
            if lvl < LVLS - 1:
                for g, (bank, h0) in enumerate([(2, 0), (3, 4)]):
                    w0 = ([("sc", (scA if g == 0 else scB)),
                           ("sc", scQA if g == 0 else 0),
                           ("vv", vvQB if g == 1 else 0), ("vv", vv_q)] +
                          p.bank_w(bank))
                    for j in range(4):
                        p.step("tensor", (lambda t, bank=bank, j=j, h0=h0, qcur=qcur,
                               pcur=pcur: t.matmul(
                                   pg(bank)[:, j, :], h3(qcur)[:, h0 + j, :],
                                   h3(pcur)[:, h0 + j, :], start=True, stop=True)),
                            waits=w0 if j == 0 else [],
                            inc="pe" if j == 3 else None)
                    pe_p = p.val("pe")

                    def pevac(sc, bank=bank, h0=h0, pnxt=pnxt):
                        return sc.copy(h3(pnxt)[:, h0:h0 + 4, :], pg(bank))
                    p.step("scalar", pevac, waits=[("pe", pe_p)], inc="sc")
                    if g == 0:
                        scA = p.val("sc")
                    else:
                        scB = p.val("sc")
                    p.bank_consumed(bank, "sc")
            if lvl < LVLS - 2:
                for g, (bank, h0) in enumerate([(4, 0), (5, 4)]):
                    for j in range(4):
                        p.step("tensor", (lambda t, bank=bank, j=j, h0=h0, qcur=qcur,
                               pcur=pcur: t.matmul(
                                   pg(bank)[:, j, :], h3(pcur)[:, h0 + j, :],
                                   h3(qcur)[:, h0 + j, :], start=True, stop=True)),
                            waits=p.bank_w(bank) if j == 0 else [],
                            inc="pe" if j == 3 else None)
                    pe_q = p.val("pe")
                    if g == 0:
                        def qevacA(sc, h0=h0, qnxt=qnxt):
                            return sc.copy(h3(qnxt)[:, h0:h0 + 4, :], pg(4))
                        p.step("scalar", qevacA, waits=[("pe", pe_q)], inc="sc")
                        scQA = p.val("sc")
                        p.bank_consumed(4, "sc")
                    else:
                        def qevacB(v, h0=h0, qnxt=qnxt):
                            return v.tensor_copy(h3(qnxt)[:, h0:h0 + 4, :], pg(5))
                        p.step("vector", qevacB, waits=[("pe", pe_q)], inc="vv")
                        vvQB = p.val("vv")
                        p.bank_consumed(5, "vv")
            xcur, xnxt = xnxt, xcur
            if lvl < LVLS - 1:
                pcur, pnxt = pnxt, pcur
            if lvl < LVLS - 2:
                qcur, qnxt = qnxt, qcur
        rbar = xcur
        vv_rbar = max(vvA, vvB)

        for h in range(H):
            p.step("tensor", (lambda t, h=h: t.matmul(
                pg(2 + h // 4)[:, h % 4, :], h3(sig_bf)[:, h, :], h3(qfm)[:, h, :],
                start=True, stop=False)),
                waits=([("vv", vv_rbar)] + p.bank_w(2, 3)) if h == 0 else [],
                inc=None)
            p.step("tensor", (lambda t, h=h, rbar=rbar: t.matmul(
                pg(2 + h // 4)[:, h % 4, :], h3(rbar)[:, h, :], h3(mt_bf)[:, h, :],
                start=False, stop=True)),
                inc="pe" if h == H - 1 else None)
        pe_o = p.val("pe")
        for h in range(H):
            p.step("tensor", (lambda t, h=h: t.matmul(
                pg(4 + h // 4)[:, h % 4, :], c_id_b[:, :], h3(sig_bf)[:, h, :],
                start=True, stop=False)),
                waits=p.bank_w(4, 5) if h == 0 else [], inc=None)
            p.step("tensor", (lambda t, h=h, rbar=rbar, par=par: t.matmul(
                pg(4 + h // 4)[:, h % 4, :], h3(kb_bf[par])[:, h, :], h3(rbar)[:, h, :],
                start=False, stop=True)),
                inc="pe" if h == H - 1 else None)
        pe_upd = p.val("pe")
        st["pe_upd"][ccg] = pe_upd

        def oevac(sc, cc=cc):
            o3 = o_fm[:, :].rearrange("p (h n) -> p h n", n=TT)
            i1 = sc.copy(o3[:, 0:4, cc * CC:(cc + 1) * CC], pg(2))
            i2 = sc.copy(o3[:, 4:8, cc * CC:(cc + 1) * CC], pg(3))
            return [i1, i2]
        p.step("scalar", oevac,
               waits=[("pe", pe_o)] + (p.buf_w("o_fm") if cc == 0 else []), inc="sc")
        p.bank_consumed(2, "sc")
        p.bank_consumed(3, "sc")

        def sevac(v, cc=cc):
            i1 = v.tensor_tensor(h3(s_st)[:, 0:4, :], pg(4), bcast8s(lamc4, cc)[:, 0:4, :],
                                 OP.mult)
            i2 = v.tensor_tensor(h3(s_st)[:, 4:8, :], pg(5), bcast8s(lamc4, cc)[:, 4:8, :],
                                 OP.mult)
            return [i1, i2]
        p.step("vector", sevac, waits=[("pe", pe_upd)], inc="vv")
        p.bank_consumed(4, "vv")
        p.bank_consumed(5, "vv")

    def emit_wo(t_i):
        tk = t_i * TT
        sc_oall = p.val("sc")
        for half in range(2):
            for oc in range(4):
                for hv in range(NC_D):
                    p.step("tensor", (lambda t, half=half, oc=oc, hv=hv: t.matmul(
                        pb[oc][:, :],
                        w3(wo_sb, D)[:, hv, (half * 4 + oc) * 128:(half * 4 + oc + 1) * 128],
                        o_fm[:, :].rearrange("p (h n) -> p h n", n=TT)[:, hv, :],
                        start=(hv == 0), stop=(hv == NC_D - 1))),
                        waits=([("sc", sc_oall), ("dwo", 16)] + p.bank_w(0, 1, 2, 3))
                        if (hv == 0 and oc == 0) else [],
                        inc="pe" if hv == NC_D - 1 else None)
            pe_woh = p.val("pe")
            vv_y1 = 0
            for q in range(2):
                c0 = half * 4 + 2 * q
                p.dma("sync", (lambda s, tk=tk, c0=c0: s.dma_start(
                    out=tile3(xnwo, TT),
                    in_=dram3(xn32_dram)[:, c0:c0 + 2, tk:tk + TT])),
                    waits=[("dxn", st["dxn_t"][t_i])] + p.buf_w("xnwo"), inc="dxw")
                ds_xw = p.val("dxw")

                def y1add(v, q=q):
                    i1 = v.tensor_add(tile3(y1h, TT)[:, 2 * q, :], pb[2 * q][:, :],
                                      tile3(xnwo, TT)[:, 0, :])
                    i2 = v.tensor_add(tile3(y1h, TT)[:, 2 * q + 1, :], pb[2 * q + 1][:, :],
                                      tile3(xnwo, TT)[:, 1, :])
                    return [i1, i2]
                p.step("vector", y1add,
                       waits=[("pe", pe_woh), ("dxw", ds_xw)] + p.buf_w("y1h"), inc="vv")
                vv_y1 = p.val("vv")
                p.buf_set("xnwo", ("vv", vv_y1))
                p.bank_consumed(2 * q, "vv")
                p.bank_consumed(2 * q + 1, "vv")
            p.step("scalar", (lambda sc: sc.activation(
                xsq[:, 0:4 * TT], y1h[:, :], AF.Square)),
                waits=[("vv", vv_y1)] + p.buf_w("xsq"), inc="sc")
            sc_sq = p.val("sc")
            for ic in range(4):
                p.step("tensor", (lambda t, ic=ic, half=half: t.matmul(
                    pb[7][0:1, 0:TT], c_onescol_b[:, 0:1], tile3(xsq, TT)[:, ic, :],
                    start=(ic == 0 and half == 0),
                    stop=(ic == 3 and half == 1))),
                    waits=([("sc", sc_sq)] + (p.bank_w(7) if half == 0 else []))
                    if ic == 0 else [],
                    inc="pe" if ic == 3 else None)
            pe_ss2 = p.val("pe")
            p.dma("sync", (lambda s, tk=tk, half=half: s.dma_start(
                out=dram3(y1_dram)[:, half * 4:(half + 1) * 4, tk:tk + TT],
                in_=tile3(y1h, TT))),
                waits=[("vv", vv_y1)], inc="dy1")
            p.buf_set("y1h", ("dy1", p.val("dy1")), ("sc", sc_sq))
            p.buf_set("xsq", ("pe", pe_ss2))
        for hv in range(NC_D):
            p.step("tensor", (lambda t, hv=hv: t.matmul(
                pb[6][0:1, 0:TT], wo1_bf[:, hv:hv + 1],
                o_fm[:, :].rearrange("p (h n) -> p h n", n=TT)[:, hv, :],
                start=(hv == 0), stop=(hv == NC_D - 1))),
                waits=([("vv", vv_wo1), ("sc", sc_oall)] + p.bank_w(6)) if hv == 0 else [],
                inc="pe" if hv == NC_D - 1 else None)
        st["pe_mu2_t"] = p.val("pe")
        st["pe_ss2_t"] = pe_ss2
        p.buf_set("o_fm", ("pe", p.val("pe")))

    def emit_ln2fin(t_i):
        tk = t_i * TT
        p.step("vector", lambda v: v.tensor_scalar(
            st_ss[0:1, :], pb[6][0:1, 0:TT], 1.0 / D, None, OP.mult),
            waits=[("pe", st["pe_mu2_t"]), ("dst2", st["dst2_prev"])], inc="vv")
        p.step("vector", lambda v: v.tensor_scalar(
            st_mu0[0:1, :], pb[7][0:1, 0:TT], 1.0 / D, None, OP.mult),
            waits=[("pe", st["pe_ss2_t"])], inc="vv")
        p.bank_consumed(6, "vv")
        p.bank_consumed(7, "vv")
        p.step("vector", lambda v: v.tensor_mul(xnwo[0:1, 0:TT], st_ss[0:1, :],
                                                st_ss[0:1, :]), inc="vv")
        p.step("vector", lambda v: v.tensor_sub(st_mu0[0:1, :], st_mu0[0:1, :],
                                                xnwo[0:1, 0:TT]), inc="vv")
        vv_var2 = p.val("vv")
        p.buf_set("xnwo", ("vv", vv_var2))
        p.step("scalar", lambda sc: sc.activation(st_mu0[0:1, :], st_mu0[0:1, :], AF.Ln,
                                                  bias=f32(EPS_LN)),
               waits=[("vv", vv_var2)], inc="sc")
        p.step("scalar", lambda sc: sc.activation(st_mu0[0:1, :], st_mu0[0:1, :], AF.Exp,
                                                  scale=f32(-0.5)), inc="sc")
        sc_r2 = p.val("sc")
        p.step("vector", lambda v: v.tensor_mul(st_ss[0:1, :], st_ss[0:1, :],
                                                st_mu0[0:1, :]),
               waits=[("sc", sc_r2)], inc="vv")
        vv_st2 = p.val("vv")
        p.dma("sync", (lambda s, tk=tk: s.dma_start(out=st2_dram[0:1, tk:tk + TT],
                                                    in_=st_ss[0:1, :])),
              waits=[("vv", vv_st2)], inc="dst2")
        p.dma("sync", (lambda s, tk=tk: s.dma_start(out=st2_dram[1:2, tk:tk + TT],
                                                    in_=st_mu0[0:1, :])), inc="dst2")
        st["dst2_prev"] = p.val("dst2")

    emit_xload(0)
    emit_lnhead(0)
    for cc in range(NCC):
        emit_proj(0, cc)
    emit_lnexp_pre(0, [0, 1, 2, 3])
    emit_lnexp(0, [0, 1, 2, 3])
    for t_i in range(NT):
        nxt = t_i + 1
        if nxt < NT:
            emit_xload(nxt)
        emit_scalings(t_i, 0)
        emit_delta(t_i, 0)
        if nxt < NT:
            emit_lnhead(nxt)
        emit_scalings(t_i, 1)
        emit_delta(t_i, 1)
        if nxt < NT:
            emit_proj(nxt, 0)
            emit_proj(nxt, 1)
        emit_scalings(t_i, 2)
        emit_delta(t_i, 2)
        if nxt < NT:
            emit_proj(nxt, 2)
        emit_scalings(t_i, 3)
        if nxt < NT:
            emit_proj(nxt, 3)
            emit_lnexp_pre(nxt, [0, 1, 2])
        emit_delta(t_i, 3)
        if nxt < NT:
            emit_lnexp(nxt, [0, 1, 2])
        emit_wo(t_i)
        emit_ln2fin(t_i)
        if nxt < NT:
            emit_lnexp_pre(nxt, [3])
            emit_lnexp(nxt, [3])
        if t_i == (NT - 2 if EARLY_CC else NT - 1):
            p.dma("gpsimd", lambda g: g.dma_start(out=cnt_in, in_=run_rep[0:1, 0:8]),
                  waits=[("vv", p.val("vv"))], inc="dcnt")
            p.step("gpsimd", lambda g: g.collective_compute(
                "AllGather", OP.bypass, replica_groups=[list(range(8))],
                ins=[cnt_in.opt()], outs=[cnt_all.opt()]),
                waits=[("dcnt", p.val("dcnt"))], inc="cc")
            p.dma("gpsimd", lambda g: g.dma_start(out=cnt_sb[0:1, 0:64],
                                                  in_=cnt_all[0:8, 0:8]),
                  waits=[("cc", p.val("cc"))], inc="dcall")

    dg_xn_all = p.val("dxn")
    pe_afinal = p.val("pe")
    vv_afinal = p.val("vv")
    sc_afinal = p.val("sc")
    alias_w = [("pe", pe_afinal), ("vv", vv_afinal), ("sc", sc_afinal),
               ("dst2", p.val("dst2")), ("dy1", p.val("dy1")),
               ("dxw", p.val("dxw")), ("dx", p.val("dx"))]

    for ccg in range(32):
        p.step("scalar", (lambda sc, ccg=ccg: sc.activation(
            r_l[:, ccg * 8:(ccg + 1) * 8], r_l[:, ccg * 8:(ccg + 1) * 8], AF.Exp,
            bias=r_negm[:, ccg:ccg + 1])),
            waits=[("vv", vv_afinal)] if ccg == 0 else [], inc="sc")
    sc_exp = p.val("sc")
    dg_call = p.val("dcall")
    p.step("vector", lambda v: v.tensor_tensor(
        base_row[0:1, 8:72].rearrange("p (e j) -> p e j", j=8),
        cnt_sb[0:1, 0:64].rearrange("p (j e) -> p j e", e=8).transpose([0, 2, 1]),
        coremask[0:1, 0:8].unsqueeze(1).broadcast_to([1, 8, 8]),
        OP.mult), waits=[("dcall", dg_call)], inc="vv")
    p.step("vector", lambda v: v.reduce_sum(
        base_row[0:1, 0:8], base_row[0:1, 8:72].rearrange("p (e j) -> p e j", j=8), AX.X),
        inc="vv")
    vv_base = p.val("vv")

    for gq in range(8):
        p.dma("gpsimd", (lambda g, gq=gq: g.dma_start(
            out=w3(w1_sb, FF)[:, :, gq * 512:(gq + 1) * 512],
            in_=dram3(d["w1"])[:, :, gq * 512:(gq + 1) * 512])),
            waits=alias_w if gq == 0 else [], inc="dw1" + "abcd"[gq // 2])
    for gq in range(8):
        p.dma("gpsimd", (lambda g, gq=gq: g.dma_start(
            out=w3(w2_sb, D)[:, :, gq * 128:(gq + 1) * 128],
            in_=dram3(d["w2"])[:, :, gq * 128:(gq + 1) * 128])),
            inc="dw2" + "ab"[gq // 4])
    for gh in range(2):
        p.dma("gpsimd", (lambda g, gh=gh: g.dma_start(
            out=w3(wexp_sb, D)[:, :, gh * 512:(gh + 1) * 512],
            in_=dram3(d["wexp"])[:, :, gh * 512:(gh + 1) * 512])),
            inc="dwex" + "ab"[gh])

    NSUB = T // SUB

    def emit_bload(su, first=False):
        par = su % 2
        j0 = su * SUB
        p.dma("sync", (lambda s, j0=j0, par=par: s.dma_start(
            out=tile3(by1[par], SUB), in_=dram3(y1_dram)[:, :, j0:j0 + SUB])),
            waits=(alias_w if first else []) + p.buf_w(f"by1{par}") +
                  [("dy1", p.val("dy1"))], inc="dsb")
        p.dma("gpsimd", (lambda g, j0=j0, par=par: g.dma_start(
            out=tile3(bxnb[par], SUB), in_=dram3(xn32_dram)[:, :, j0:j0 + SUB])),
            waits=p.buf_w(f"bxnb{par}") + [("dxn", dg_xn_all)], inc="dsg")
        p.dma("sync", (lambda s, j0=j0, par=par: s.dma_start(
            out=bst2[par][0:2, :], in_=st2_dram[0:2, j0:j0 + SUB])),
            waits=[("dst2", p.val("dst2"))] + p.buf_w(f"bst{par}"), inc="dsb")
        return (p.val("dsb"), p.val("dsg"))

    ds_b = [(0, 0)] * (NSUB + 1)
    ds_b[0] = emit_bload(0, first=True)
    ds_b[1] = emit_bload(1, first=True)
    p.dma("sync", lambda s: s.dma_start(out=base_dram, in_=base_row[0:1, 0:8]),
          waits=[("vv", vv_base)], inc="dbs")
    p.dma("sync", lambda s: s.dma_start(
        out=base_rep[:, :],
        in_=base_dram.unsqueeze(0).broadcast_to([128, 1, 8]).rearrange("p a e -> p (a e)")),
        waits=[("dbs", p.val("dbs"))], inc="dbb")
    dg_bb = p.val("dbb")
    bh_sc = {}

    def emit_ln2head(su):
        par = su % 2
        p.step("tensor", (lambda t, par=par: t.matmul(
            pb[6][:, 0:SUB], c_sel[0:2, 128:256], bst2[par][0:2, :],
            start=True, stop=True)),
            waits=[("dsb", ds_b[su][0])] + p.bank_w(6), inc=None)
        p.step("tensor", (lambda t, par=par: t.matmul(
            pb[6][:, SUB:2 * SUB], c_sel[0:2, 0:128], bst2[par][0:2, :],
            start=True, stop=True)), inc="pe")
        pe_bc2 = p.val("pe")
        p.step("vector", (lambda v, par=par: v.tensor_tensor(
            tile3(bout, SUB), tile3(by1[par], SUB),
            pb[6][:, 0:SUB].unsqueeze(1).broadcast_to([128, NC_D, SUB]), OP.mult)),
            waits=[("pe", pe_bc2)] + p.buf_w("bout"), inc="vv")
        p.step("vector", lambda v: v.tensor_tensor(
            tile3(bout, SUB), tile3(bout, SUB),
            pb[6][:, SUB:2 * SUB].unsqueeze(1).broadcast_to([128, NC_D, SUB]), OP.subtract),
            inc="vv")
        vv_u2 = p.val("vv")
        p.bank_consumed(6, "vv")
        for ic in range(NC_D):
            p.step("scalar", (lambda sc, ic=ic: sc.activation(
                tile3(bh2, SUB)[:, ic, :], tile3(bout, SUB)[:, ic, :], AF.Identity,
                bias=ln2b[:, ic:ic + 1], scale=ln2w[:, ic:ic + 1])),
                waits=([("vv", vv_u2)] + p.buf_w("bh2")) if ic == 0 else [],
                inc="sc" if ic == NC_D - 1 else None)
        bh_sc[su] = p.val("sc")
        p.buf_set(f"bst{par}", ("pe", pe_bc2))

    emit_ln2head(0)
    for su in range(NSUB):
        par = su % 2
        j0 = su * SUB
        sc_h2 = bh_sc[su]

        for g4 in range(8):
            bp = (g4 % 2) * 2
            for j in range(4):
                oc = g4 * 4 + j
                bank = bp + j // 2
                colr = (j % 2) * SUB
                for ic in range(NC_D):
                    p.step("tensor", (lambda t, oc=oc, bank=bank, colr=colr, ic=ic: t.matmul(
                        pb[bank][:, colr:colr + SUB],
                        w3(w1_sb, FF)[:, ic, oc * 128:(oc + 1) * 128],
                        tile3(bh2, SUB)[:, ic, :],
                        start=(ic == 0), stop=(ic == NC_D - 1))),
                        waits=([("sc", sc_h2), ("dw1" + "abcd"[g4 // 2], 32)] +
                               p.bank_w(bp, bp + 1))
                        if (ic == 0 and j == 0) else [],
                        inc="pe" if (ic == NC_D - 1 and j == 3) else None)
            pe_g = p.val("pe")
            for j in range(4):
                oc = g4 * 4 + j
                bank = bp + j // 2
                colr = (j % 2) * SUB
                p.step("scalar", (lambda sc, oc=oc, bank=bank, colr=colr: sc.activation(
                    tile3(bg, SUB)[:, oc, :], pb[bank][:, colr:colr + SUB], AF.Gelu,
                    bias=b1_sb[:, oc:oc + 1])),
                    waits=([("pe", pe_g)] + (p.buf_w("bg") if oc == 0 else [])), inc="sc")
            p.bank_consumed(bp, "sc")
            p.bank_consumed(bp + 1, "sc")
        sc_gelu = p.val("sc")
        p.buf_set("bh2", ("pe", pe_g))

        for q in range(2):
            ccg = j0 // CC + q
            p.step("vector", (lambda v, ccg=ccg: v.reduce_sum(
                m1c[:, :], r_l[:, ccg * 8:(ccg + 1) * 8], AX.X)),
                waits=([("sc", sc_exp), ("dbb", dg_bb)] if (su == 0 and q == 0) else []),
                inc="vv")
            p.step("vector", (lambda v: v.reciprocal(t8a[:, 0:1], m1c[:, :])), inc="vv")
            p.step("vector", (lambda v, ccg=ccg: v.tensor_tensor(
                t8b[:, :], r_ohf[:, ccg * 8:(ccg + 1) * 8], base_rep[:, :], OP.mult)),
                inc="vv")
            p.step("vector", (lambda v: v.reduce_sum(m1c[:, :], t8b[:, :], AX.X)), inc="vv")
            p.step("vector", (lambda v, ccg=ccg: v.tensor_add(
                m1c[:, :], m1c[:, :], r_c[:, ccg:ccg + 1])), inc="vv")
            p.step("vector", (lambda v: v.tensor_scalar(
                m1c[:, :], m1c[:, :], 101.0, None, OP.is_lt)), inc="vv")
            p.step("vector", (lambda v, ccg=ccg: v.tensor_mul(
                r_g[:, ccg:ccg + 1], t8a[:, 0:1], m1c[:, :])), inc="vv")
            p.step("vector", (lambda v, ccg=ccg: v.tensor_scalar(
                bdiag[:, :], c_id_f[:, :], r_g[:, ccg:ccg + 1], None, OP.mult)),
                waits=[("pe", st.get("pe_gate_prev", 0) if q == 0 else p.val("pe"))],
                inc="vv")
            vv_dg2 = p.val("vv")
            p.step("tensor", (lambda t, q=q: t.matmul(
                pb[6][:, q * 128:(q + 1) * 128], c_ones_f[:, :], bdiag[:, :],
                start=True, stop=True)),
                waits=[("vv", vv_dg2)] + (p.bank_w(6) if q == 0 else []), inc="pe")
        pe_gb = p.val("pe")
        st["pe_gate_prev"] = pe_gb
        p.step("vector", lambda v: v.tensor_copy(bgb[:, :], pb[6][:, 0:SUB]),
               waits=[("pe", pe_gb)], inc="vv")
        vv_gb = p.val("vv")
        p.bank_consumed(6, "vv")


        for g4 in range(2):
            bp = 4 + g4 * 2
            for j in range(4):
                oc = g4 * 4 + j
                bank = bp + j // 2
                colr = (j % 2) * SUB
                for ic2 in range(NC_F):
                    p.step("tensor", (lambda t, oc=oc, bank=bank, colr=colr, ic2=ic2: t.matmul(
                        pb[bank][:, colr:colr + SUB],
                        w3(w2_sb, D)[:, ic2, oc * 128:(oc + 1) * 128],
                        tile3(bg, SUB)[:, ic2, :],
                        start=(ic2 == 0), stop=(ic2 == NC_F - 1))),
                        waits=([("sc", sc_gelu), ("dw2" + "ab"[g4], 64)] +
                               p.bank_w(bp, bp + 1))
                        if (ic2 == 0 and j == 0) else [],
                        inc="pe" if (ic2 == NC_F - 1 and j == 3) else None)
            pe_y2 = p.val("pe")
            for j in range(4):
                oc = g4 * 4 + j
                bank = bp + j // 2
                colr = (j % 2) * SUB
                p.step("vector", (lambda v, oc=oc, bank=bank, colr=colr, par=par:
                       v.scalar_tensor_tensor(
                           tile3(by1[par], SUB)[:, oc, :], pb[bank][:, colr:colr + SUB],
                           b2_sb[:, oc:oc + 1], tile3(by1[par], SUB)[:, oc, :],
                           OP.add, OP.add)),
                    waits=[("pe", pe_y2)], inc="vv")
            p.bank_consumed(bp, "vv")
            p.bank_consumed(bp + 1, "vv")
        vv_s1 = p.val("vv")

        if su + 1 < NSUB:
            emit_ln2head(su + 1)
        sc_h2n = bh_sc.get(su + 1, sc_h2)

        for g4 in range(2):
            bp = g4 * 2
            for j in range(4):
                oc = g4 * 4 + j
                bank = bp + j // 2
                colr = (j % 2) * SUB
                for ic in range(NC_D):
                    p.step("tensor", (lambda t, oc=oc, bank=bank, colr=colr, ic=ic, par=par:
                           t.matmul(
                               pb[bank][:, colr:colr + SUB],
                               w3(wexp_sb, D)[:, ic, oc * 128:(oc + 1) * 128],
                               tile3(bxnb[par], SUB)[:, ic, :],
                               start=(ic == 0), stop=(ic == NC_D - 1))),
                        waits=([("dwex" + "ab"[g4], 16), ("dsg", ds_b[su][1])] +
                               p.bank_w(bp, bp + 1))
                        if (ic == 0 and j == 0) else [],
                        inc="pe" if (ic == NC_D - 1 and j == 3) else None)
            pe_m = p.val("pe")
            for j in range(4):
                oc = g4 * 4 + j
                bank = bp + j // 2
                colr = (j % 2) * SUB
                p.step("vector", (lambda v, oc=oc, bank=bank, colr=colr: v.scalar_tensor_tensor(
                    tile3(bout, SUB)[:, oc, :], pb[bank][:, colr:colr + SUB],
                    bexp_sb[:, oc:oc + 1], bgb[:, :], OP.add, OP.mult)),
                    waits=([("pe", pe_m), ("vv", vv_gb), ("sc", sc_h2n)]
                           if (g4 == 0 and j == 0) else [("pe", pe_m)]),
                    inc="vv")
            p.bank_consumed(bp, "vv")
            p.bank_consumed(bp + 1, "vv")
        vv_m1 = p.val("vv")
        p.buf_set(f"bxnb{par}", ("pe", p.val("pe")))

        p.step("gpsimd", (lambda g, par=par: g.tensor_add(
            by1[par][:, :], by1[par][:, :], bout[:, :])),
            waits=[("vv", max(vv_m1, vv_s1))], inc="gp")
        gp_out = ("gp", p.val("gp"))
        p.dma("gpsimd", (lambda g, j0=j0, par=par: g.dma_start(
            out=tile3(by1[par], SUB), in_=dram3(d["xt"])[:, :, j0:j0 + SUB],
            accum_op=OP.add)),
            waits=[gp_out], inc="dxa")
        dg_xadd = p.val("dxa")
        p.dma("sync", (lambda s, j0=j0, par=par: s.dma_start(
            out=dram3(d["out"])[:, :, j0:j0 + SUB], in_=tile3(by1[par], SUB))),
            waits=[("dxa", dg_xadd)], inc="st")
        p.buf_set(f"by1{par}", ("st", p.val("st")))
        p.buf_set("bout", gp_out)
        p.buf_set("bg", ("pe", p.val("pe")))
        if su + 2 <= NSUB - 1:
            ds_b[su + 2] = emit_bload(su + 2)

    if DEBUG_DUMP:
        d["dbg_st2"] = nc.dram_tensor("dbg_st2", [2, T], F32, kind="ExternalOutput").ap()
        d["dbg_y1"] = nc.dram_tensor("dbg_y1", [D, T], F32, kind="ExternalOutput").ap()
        d["dbg_xn"] = nc.dram_tensor("dbg_xn", [D, T], F32, kind="ExternalOutput").ap()
        p.dma("sync", lambda s: s.dma_start(out=d["dbg_st2"], in_=st2_dram[:, :]),
              waits=[("st", p.val("st"))], inc="st")
        p.dma("sync", lambda s: s.dma_start(out=d["dbg_y1"], in_=y1_dram[:, :]), inc="st")
        p.dma("sync", lambda s: s.dma_start(out=d["dbg_xn"], in_=xn32_dram[:, :]), inc="st")
    p.step("sync", lambda s: None, waits=[("st", p.val("st"))])

    _BUILD_CACHE["dbg"] = dict(r_g=r_g, r_c=r_c, r_ohf=r_ohf, r_l=r_l, r_negm=r_negm,
                               run_rep=run_rep, base_rep=base_rep, cnt_sb=cnt_sb,
                               base_row=base_row)
    with nc.Block() as blk:
        _emit(p, sems, blk, hazard_scan=_hazard_scan, hazard_fix=_hazard_fix)
    es_sem.close()
    es_persist.close()
    return nc, p


def build_fixed():
    scan = {}
    nc1, p1 = build_program(_hazard_scan=scan)
    del nc1
    hz_writers = set()
    fix = {}
    for eng in ("vector", "gpsimd"):
        flagged = set()
        for (e, rsi), ws in scan.items():
            if e == eng:
                flagged.update(ws)
        if not flagged:
            continue
        for si in flagged:
            fn, waits, inc, inc_by = p1.steps[eng][si]
            if inc is None:
                hz_writers.add((eng, si))
        cnt = {}
        vals = {}
        for si, (fn, waits, inc, inc_by) in enumerate(p1.steps[eng]):
            if inc is None and (eng, si) in hz_writers:
                inc, inc_by = "hz_" + eng, 1
            if inc is not None:
                cnt[inc] = cnt.get(inc, 0) + inc_by
                vals[si] = (inc, cnt[inc])
            else:
                vals[si] = None
        for (e, rsi), ws in scan.items():
            if e != eng:
                continue
            for wsi in ws:
                wv = vals.get(wsi)
                if wv is not None:
                    fix.setdefault((eng, rsi), []).append(wv)
    nc2, _ = build_program(_hazard_fix=fix, _hz_writers=hz_writers)
    return nc2


def _consts():
    i = np.arange(128)
    iu = (i[:, None] <= i[None, :]).astype(np.float32)
    half = np.zeros((128, 128), np.float32)
    half[0:64, :] = 1.0
    ones = np.ones((128, 128), np.float32)
    ident = np.eye(128, dtype=np.float32)
    sl = (i[:, None] > i[None, :]).astype(np.float32)
    su = (i[:, None] < i[None, :]).astype(np.float32)
    return {
        "d_c_iu_f": iu, "d_c_half_f": half, "d_c_ones_f": ones, "d_c_id_f": ident,
        "d_c_negsl_b": -sl, "d_c_negsu_b": -su, "d_c_iu_b": iu, "d_c_id_b": ident,
        "d_c_ones_b": ones,
        "d_c_onescol": np.ones((128, 1), np.float32),
        "d_c_sel": np.concatenate([np.repeat(np.eye(2, dtype=np.float32)[:, 0:1], 128, 1),
                                   np.repeat(np.eye(2, dtype=np.float32)[:, 1:2], 128, 1)],
                                  axis=1),
        "d_c_iota999": np.broadcast_to(
            (np.arange(32) % 8 + 16.0).astype(np.float32), (128, 32)).copy(),
    }


def kernel(**inputs):
    if "prog" not in _BUILD_CACHE:
        _BUILD_CACHE["prog"] = build_fixed()
    nc = _BUILD_CACHE["prog"]

    x = np.asarray(inputs["L0_attn_x"])

    def col128(vec, ncol):
        return np.ascontiguousarray(np.asarray(vec).reshape(ncol, 128).T)

    shared = {
        "wq": np.asarray(inputs["L0_attn_Wq"]),
        "wk": np.asarray(inputs["L0_attn_Wk"]),
        "wv": np.asarray(inputs["L0_attn_Wv"]),
        "wo": np.asarray(inputs["L0_attn_Wo"]),
        "w1": np.asarray(inputs["L0_attn_W1"]),
        "w2": np.asarray(inputs["L0_attn_W2"]),
        "wexp": np.asarray(inputs["L0_ffn_w_expert"]),
        "wg": np.asarray(inputs["L0_ffn_w_gate"]),
        "wab": np.ascontiguousarray(np.concatenate(
            [np.asarray(inputs["L0_attn_Wb"]), np.asarray(inputs["L0_attn_Wa"])], axis=1)),
        "d_ln1w": col128(inputs["L0_attn_ln1_w"], 8),
        "d_ln1b": col128(inputs["L0_attn_ln1_b"], 8),
        "d_ln2w": col128(inputs["L0_attn_ln2_w"], 8),
        "d_ln2b": col128(inputs["L0_attn_ln2_b"], 8),
        "d_b1": col128(inputs["L0_attn_b1"], 32),
        "d_b2": col128(inputs["L0_attn_b2"], 8),
        "d_bexp": col128(inputs["L0_ffn_b_expert"], 8),
    }
    shared.update(_consts())

    in_maps = []
    for i in range(8):
        m = dict(shared)
        m["xt"] = np.ascontiguousarray(x[i].T)
        m["d_coremask"] = (np.arange(8) < i).astype(np.float32).reshape(1, 8)
        in_maps.append(m)

    res = run_bass_kernel_spmd(nc, in_maps, core_ids=list(range(8)))
    out = np.stack([np.ascontiguousarray(np.asarray(res.results[i]["out_t"]).T)
                    for i in range(8)])
    return out.astype(np.float32)
